# version 01
# baseline (speedup 1.0000x reference)
import numpy as np
import ml_dtypes
import concourse.bass as bass
import concourse.mybir as mybir
from concourse.bass_utils import run_bass_kernel_spmd

F32 = mybir.dt.float32
BF16 = mybir.dt.bfloat16
AF = mybir.ActivationFunctionType
ALU = mybir.AluOpType
AX = mybir.AxisListType

NCORES = 8
D = 2048
T = 2048
F32EPS = float(np.finfo(np.float32).eps)
BF16NP = ml_dtypes.bfloat16


class Buf:
    __slots__ = ("ap", "w", "r")
    def __init__(self, ap, w=None):
        self.ap = ap
        self.w = list(w) if w else []
        self.r = []

    def view(self, ap):
        v = Buf(ap, w=self.w)
        v.r = list(self.r)
        return v


class KB:
    CLASSES = {"t": 1, "v": 1, "s": 1, "p": 1}

    def __init__(self, nc):
        self.nc = nc
        self.sem = {}
        self.cnt = {}
        self.waited = {e: {} for e in ("t", "v", "s", "p", "y")}
        self.eng = {"t": nc.tensor, "v": nc.vector, "s": nc.scalar,
                    "p": nc.gpsimd, "y": nc.sync}
        self._nbsem = 0

    def init_sems(self, ctx):
        self._ctx = ctx
        for c in self.CLASSES:
            self.sem[c] = ctx.enter_context(self.nc.semaphore("sem_" + c))
            self.cnt[c] = 0

    def _buf_sem(self, buf):
        key = ("b", id(buf))
        if key not in self.sem:
            self._nbsem += 1
            self.sem[key] = self._ctx.enter_context(
                self.nc.semaphore(f"bsem{self._nbsem}"))
            self.cnt[key] = 0
        return key

    def _wait(self, engine_name, tokens):
        w = self.waited[engine_name]
        eng = self.eng[engine_name]
        need = {}
        for tok in tokens:
            if tok is None:
                continue
            c, n = tok
            if n > need.get(c, 0):
                need[c] = n
        for c, n in need.items():
            if w.get(c, 0) < n:
                eng.wait_ge(self.sem[c], n)
                w[c] = n

    def emit(self, engine_name, cls, fn, reads=(), writes=(), extra=()):
        toks = list(extra)
        for b in reads:
            toks.extend(b.w)
        for b in writes:
            toks.extend(b.w)
            toks.extend(b.r)
        self._wait(engine_name, toks)
        inst = fn(self.eng[engine_name])
        inc = 16 if isinstance(cls, tuple) else self.CLASSES[cls]
        self.cnt[cls] += inc
        tok = (cls, self.cnt[cls])
        inst.then_inc(self.sem[cls], inc)
        for b in reads:
            b.r.append(tok)
        for b in writes:
            b.w = [tok]
            b.r = []
        return tok

    def t(self, fn, reads=(), writes=(), extra=()):
        return self.emit("t", "t", fn, reads, writes, extra)

    def v(self, fn, reads=(), writes=(), extra=()):
        return self.emit("v", "v", fn, reads, writes, extra)

    def s(self, fn, reads=(), writes=(), extra=()):
        return self.emit("s", "s", fn, reads, writes, extra)

    def p(self, fn, reads=(), writes=(), extra=()):
        return self.emit("p", "p", fn, reads, writes, extra)

    def dma(self, qname, fn, reads=(), writes=(), extra=()):
        assert len(writes) == 1, "dma() wants exactly one destination Buf"
        if qname == "p":
            self._nbsem += 1
            key = ("pd", self._nbsem)
            self.sem[key] = self._ctx.enter_context(
                self.nc.semaphore(f"psem{self._nbsem}"))
            self.cnt[key] = 0
        else:
            key = self._buf_sem(writes[0])
        return self.emit(qname, key, fn, reads, writes, extra)

    def accum_matmul(self, psum_buf, lhs_buf, rhs_buf, fn_ap, start, stop):
        if stop:
            return self.t(fn_ap, reads=[lhs_buf, rhs_buf], writes=[psum_buf])
        extra = list(psum_buf.w) + list(psum_buf.r)
        return self.t(fn_ap, reads=[lhs_buf, rhs_buf], extra=extra)

    @staticmethod
    def merge_views(parent, views):
        for v in views:
            for t in v.w:
                if t not in parent.w:
                    parent.w.append(t)
            parent.r.extend(v.r)

    def const_col(self, ctx, value, npart=1, dtype=F32):
        key = ("const", float(value), npart, dtype)
        if not hasattr(self, "_consts"):
            self._consts = {}
        if key not in self._consts:
            b = Buf(ctx.enter_context(self.nc.sbuf_tensor(
                f"cst{len(self._consts)}", [npart, 1], dtype)))
            self.v(lambda e: e.memset(b.ap[:], float(value)), writes=[b])
            self._consts[key] = b
        return self._consts[key]

    def value_load(self, engine_name, buf, ap=None):
        self._wait(engine_name, list(buf.w))
        return self.eng[engine_name].value_load(ap if ap is not None else buf.ap[0:1, 0:1])

    def final_wait(self, engine_name="y"):
        toks = [(c, n) for c, n in self.cnt.items() if n > 0]
        self._wait(engine_name, toks)

    def phase_barrier(self):
        toks = [(c, n) for c, n in self.cnt.items() if n > 0]
        for e in ("t", "v", "s", "p", "y"):
            self._wait(e, toks)



def ln_partition(kb, ctx, xT_bufs, out_bufs, ntok, eps=1e-5, name="ln",
                 ones_col=None, ones_row=None, gamma=None, beta=None):
    nc = kb.nc
    nt = len(xT_bufs)
    nD = nt * 128
    E = ctx.enter_context
    xb = Buf(E(nc.sbuf_tensor(name + "_xb", [128, 512], BF16)))
    sq = Buf(E(nc.sbuf_tensor(name + "_sq", [128, 512], BF16)))
    ps_sum = Buf(E(nc.psum_tensor(name + "_psum", [1, 512], F32)))
    ps_sq = Buf(E(nc.psum_tensor(name + "_psq", [1, 512], F32)))
    mu = Buf(E(nc.sbuf_tensor(name + "_mu", [1, 512], F32)))
    rstd = Buf(E(nc.sbuf_tensor(name + "_rstd", [1, 512], F32)))
    ps_b = Buf(E(nc.psum_tensor(name + "_pb", [128, 512], F32)))
    ps_b2 = Buf(E(nc.psum_tensor(name + "_pb2", [128, 512], F32)))
    mub = Buf(E(nc.sbuf_tensor(name + "_mub", [128, 512], F32)))
    rstdb = Buf(E(nc.sbuf_tensor(name + "_rstdb", [128, 512], F32)))
    tmp = Buf(E(nc.sbuf_tensor(name + "_tmp", [128, 512], F32)))

    for c0 in range(0, ntok, 512):
        c1 = min(c0 + 512, ntok)
        w = c1 - c0
        for i, xT in enumerate(xT_bufs):
            kb.v(lambda e, xT=xT, c0=c0, c1=c1, w=w: e.tensor_copy(
                out=xb.ap[:, 0:w], in_=xT.ap[:, c0:c1]), reads=[xT], writes=[xb])
            kb.v(lambda e, w=w: e.tensor_mul(out=sq.ap[:, 0:w], in0=xb.ap[:, 0:w],
                                             in1=xb.ap[:, 0:w]), reads=[xb], writes=[sq])
            st = (i == 0)
            sp = (i == nt - 1)
            kb.accum_matmul(ps_sum, ones_col, xb,
                            lambda e, w=w, st=st, sp=sp: e.matmul(
                                ps_sum.ap[0:1, 0:w], ones_col.ap[:, 0:1], xb.ap[:, 0:w],
                                start=st, stop=sp), st, sp)
            kb.accum_matmul(ps_sq, ones_col, sq,
                            lambda e, w=w, st=st, sp=sp: e.matmul(
                                ps_sq.ap[0:1, 0:w], ones_col.ap[:, 0:1], sq.ap[:, 0:w],
                                start=st, stop=sp), st, sp)
        kb.v(lambda e, w=w: e.tensor_scalar(out=mu.ap[:, 0:w], in0=ps_sum.ap[0:1, 0:w],
                                            scalar1=1.0 / nD, scalar2=None, op0=ALU.mult),
             reads=[ps_sum], writes=[mu])
        kb.v(lambda e, w=w: e.tensor_mul(out=rstd.ap[:, 0:w], in0=mu.ap[:, 0:w],
                                         in1=mu.ap[:, 0:w]), reads=[mu], writes=[rstd])
        kb.v(lambda e, w=w: e.scalar_tensor_tensor(
            out=rstd.ap[:, 0:w], in0=ps_sq.ap[0:1, 0:w], scalar=1.0 / nD,
            in1=rstd.ap[:, 0:w], op0=ALU.mult, op1=ALU.subtract),
            reads=[ps_sq, rstd], writes=[rstd])
        kb.s(lambda e, w=w: e.activation(out=rstd.ap[:, 0:w], in_=rstd.ap[:, 0:w],
                                         func=AF.Ln, bias=float(eps)),
             reads=[rstd], writes=[rstd])
        kb.s(lambda e, w=w: e.activation(out=rstd.ap[:, 0:w], in_=rstd.ap[:, 0:w],
                                         func=AF.Exp, scale=-0.5),
             reads=[rstd], writes=[rstd])
        kb.t(lambda e, w=w: e.matmul(ps_b.ap[:, 0:w], ones_row.ap[0:1, :],
                                     mu.ap[0:1, 0:w], start=True, stop=True),
             reads=[mu, ones_row], writes=[ps_b])
        kb.v(lambda e, w=w: e.tensor_copy(out=mub.ap[:, 0:w], in_=ps_b.ap[:, 0:w]),
             reads=[ps_b], writes=[mub])
        kb.t(lambda e, w=w: e.matmul(ps_b2.ap[:, 0:w], ones_row.ap[0:1, :],
                                     rstd.ap[0:1, 0:w], start=True, stop=True),
             reads=[rstd, ones_row], writes=[ps_b2])
        kb.v(lambda e, w=w: e.tensor_copy(out=rstdb.ap[:, 0:w], in_=ps_b2.ap[:, 0:w]),
             reads=[ps_b2], writes=[rstdb])
        for i, xT in enumerate(xT_bufs):
            ob = out_bufs[i]
            kb.v(lambda e, xT=xT, c0=c0, c1=c1, w=w: e.tensor_sub(
                out=tmp.ap[:, 0:w], in0=xT.ap[:, c0:c1], in1=mub.ap[:, 0:w]),
                reads=[xT, mub], writes=[tmp])
            if gamma is None:
                kb.v(lambda e, ob=ob, c0=c0, c1=c1, w=w: e.tensor_mul(
                    out=ob.ap[:, c0:c1], in0=tmp.ap[:, 0:w], in1=rstdb.ap[:, 0:w]),
                    reads=[tmp, rstdb], writes=[ob])
            else:
                kb.v(lambda e, w=w: e.tensor_mul(
                    out=tmp.ap[:, 0:w], in0=tmp.ap[:, 0:w], in1=rstdb.ap[:, 0:w]),
                    reads=[tmp, rstdb], writes=[tmp])
                g = gamma[i]; bta = beta[i]
                kb.v(lambda e, ob=ob, g=g, bta=bta, c0=c0, c1=c1, w=w: e.tensor_scalar(
                    out=ob.ap[:, c0:c1], in0=tmp.ap[:, 0:w], scalar1=g.ap[:, 0:1],
                    scalar2=bta.ap[:, 0:1], op0=ALU.mult, op1=ALU.add),
                    reads=[tmp, g, bta], writes=[ob])



def build_launch_Q():
    nc = bass.Bass("TRN2", target_bir_lowering=False)
    WG = nc.dram_tensor("WG", [2048, 2048], F32, kind="ExternalInput").ap()
    WP = nc.dram_tensor("WP", [256, 8192], F32, kind="ExternalInput").ap()
    EYE = nc.dram_tensor("EYE", [128, 128], BF16, kind="ExternalInput").ap()
    OG = nc.dram_tensor("OG", [2048, 2048], BF16, kind="ExternalOutput").ap()
    OP = nc.dram_tensor("OP", [8192, 256], BF16, kind="ExternalOutput").ap()

    from contextlib import ExitStack
    with ExitStack() as ctx:
        kb = KB(nc)
        kb.init_sems(ctx)
        E = ctx.enter_context
        eye = Buf(E(nc.sbuf_tensor("eye", [128, 128], BF16)))
        kb.dma("y", lambda e: e.dma_start(out=eye.ap[:], in_=EYE), writes=[eye])

        def quant_one(name, W, OUT, nrows, ncols, E, outbuf):
            ng = ncols // 64
            wb = Buf(E(nc.sbuf_tensor(name + "_wb", [128, ncols], BF16)))
            gsum = Buf(E(nc.sbuf_tensor(name + "_gs", [128, ng], F32)))
            scale = Buf(E(nc.sbuf_tensor(name + "_sc", [128, ng], BF16)))
            rcp = Buf(E(nc.sbuf_tensor(name + "_rc", [128, ng], F32)))
            tq = Buf(E(nc.sbuf_tensor(name + "_tq", [128, ncols], BF16)))
            ab = Buf(E(nc.sbuf_tensor(name + "_ab", [128, ncols], BF16)))
            mask = Buf(E(nc.sbuf_tensor(name + "_mk", [128, ncols], BF16)))
            sgn = Buf(E(nc.sbuf_tensor(name + "_sg", [128, ncols], BF16)))
            what = Buf(E(nc.sbuf_tensor(name + "_wh", [128, ncols], BF16)))
            ptr = Buf(E(nc.psum_tensor(name + "_pt", [128, 128], BF16)))
            panels = [Buf(E(nc.sbuf_tensor(f"{name}_pan{cc}", [128, nrows], BF16)))
                      for cc in range(ncols // 128)]
            for r in range(nrows // 128):
                kb.dma("p", lambda e, r=r: e.dma_start(out=wb.ap[:], in_=W[r * 128:(r + 1) * 128, :]),
                       writes=[wb])
                v3 = wb.ap[:].rearrange("p (g e) -> p g e", e=64)
                kb.v(lambda e, v3=v3: e.tensor_reduce(out=gsum.ap[:], in_=v3, axis=AX.X,
                                                      op=ALU.add, apply_absolute_value=True),
                     reads=[wb], writes=[gsum])
                kb.v(lambda e: e.tensor_scalar(out=scale.ap[:], in0=gsum.ap[:],
                                               scalar1=1.0 / 64, scalar2=1e-8,
                                               op0=ALU.mult, op1=ALU.max),
                     reads=[gsum], writes=[scale])
                kb.v(lambda e: e.reciprocal(out=rcp.ap[:], in_=scale.ap[:]),
                     reads=[scale], writes=[rcp])
                rcp3 = rcp.ap[:].unsqueeze(2).broadcast_to([128, ng, 64])
                tq3 = tq.ap[:].rearrange("p (g e) -> p g e", e=64)
                kb.v(lambda e, v3=v3, rcp3=rcp3, tq3=tq3: e.tensor_tensor(
                    out=tq3, in0=v3, in1=rcp3, op=ALU.mult), reads=[wb, rcp], writes=[tq])
                kb.s(lambda e: e.activation(out=ab.ap[:], in_=tq.ap[:], func=AF.Abs),
                     reads=[tq], writes=[ab])
                kb.v(lambda e: e.tensor_scalar(out=mask.ap[:], in0=ab.ap[:], scalar1=0.5,
                                               scalar2=None, op0=ALU.is_gt),
                     reads=[ab], writes=[mask])
                kb.s(lambda e: e.sign(out=sgn.ap[:], in_=wb.ap[:]),
                     reads=[wb], writes=[sgn])
                sc3 = scale.ap[:].unsqueeze(2).broadcast_to([128, ng, 64])
                wh3 = what.ap[:].rearrange("p (g e) -> p g e", e=64)
                sg3 = sgn.ap[:].rearrange("p (g e) -> p g e", e=64)
                kb.v(lambda e, sg3=sg3, sc3=sc3, wh3=wh3: e.tensor_tensor(
                    out=wh3, in0=sg3, in1=sc3, op=ALU.mult), reads=[sgn, scale], writes=[what])
                kb.v(lambda e: e.tensor_mul(out=what.ap[:], in0=what.ap[:], in1=mask.ap[:]),
                     reads=[mask, what], writes=[what])
                for cc in range(ncols // 128):
                    kb.t(lambda e, cc=cc: e.transpose(ptr.ap[:], what.ap[:, cc * 128:(cc + 1) * 128],
                                                      eye.ap[:]),
                         reads=[what, eye], writes=[ptr])
                    kb.v(lambda e, cc=cc, r=r: e.tensor_copy(
                        out=panels[cc].ap[:, r * 128:(r + 1) * 128], in_=ptr.ap[:]),
                        reads=[ptr], writes=[panels[cc]])
            for cc in range(ncols // 128):
                kb.dma("y", lambda e, cc=cc: e.dma_start(
                    out=OUT[cc * 128:(cc + 1) * 128, :], in_=panels[cc].ap[:]),
                    reads=[panels[cc]], writes=[outbuf])

        ogb, opb = Buf(None), Buf(None)
        from contextlib import ExitStack as _ES
        with _ES() as sub:
            quant_one("g", WG, OG, 2048, 2048, sub.enter_context, ogb)
        kb.phase_barrier()
        with _ES() as sub:
            quant_one("q", WP, OP, 256, 8192, sub.enter_context, opb)
        kb.final_wait("y")
    return nc


_EYE128 = np.eye(128, dtype=BF16NP)


def host_quant_launch(inputs):
    nc = build_launch_Q()
    wg = np.ascontiguousarray(inputs["L0_ffn_w_gate_up"])
    wp = np.ascontiguousarray(inputs["L0_ffn_w_proj"])
    in_maps = []
    for c in range(NCORES):
        in_maps.append({
            "WG": np.ascontiguousarray(wg[c * 2048:(c + 1) * 2048, :]),
            "WP": np.ascontiguousarray(wp[c * 256:(c + 1) * 256, :]),
            "EYE": _EYE128,
        })
    res = run_bass_kernel_spmd(nc, in_maps, core_ids=list(range(NCORES)))
    whgT = np.concatenate([res.results[c]["OG"] for c in range(NCORES)], axis=1)
    whpT = np.concatenate([res.results[c]["OP"] for c in range(NCORES)], axis=1)
    return whgT, whpT






def build_launch_B(T_=T):
    nc = bass.Bass("TRN2", target_bir_lowering=False)
    R1 = nc.dram_tensor("R1", [D, T_], F32, kind="ExternalInput").ap()
    WHG = nc.dram_tensor("WHG", [2048, 16384], BF16, kind="ExternalInput").ap()
    WHP = nc.dram_tensor("WHP", [8192, 2048], BF16, kind="ExternalInput").ap()
    ONESC = nc.dram_tensor("ONESC", [128, 1], BF16, kind="ExternalInput").ap()
    ONESR = nc.dram_tensor("ONESR", [1, 128], F32, kind="ExternalInput").ap()
    R2 = nc.dram_tensor("R2", [D, T_], F32, kind="ExternalOutput").ap()

    from contextlib import ExitStack
    with ExitStack() as ctx:
        kb = KB(nc)
        kb.init_sems(ctx)
        E = ctx.enter_context
        ones_col = Buf(E(nc.sbuf_tensor("onesc", [128, 1], BF16)))
        ones_row = Buf(E(nc.sbuf_tensor("onesr", [1, 128], F32)))
        kb.dma("y", lambda e: e.dma_start(out=ones_col.ap[:], in_=ONESC), writes=[ones_col])
        kb.dma("y", lambda e: e.dma_start(out=ones_row.ap[:], in_=ONESR), writes=[ones_row])

        r1c = Buf(E(nc.sbuf_tensor("r1c", [128, 16 * 512], F32)))
        xlnc = Buf(E(nc.sbuf_tensor("xlnc", [128, 16 * 512], BF16)))
        hc = Buf(E(nc.sbuf_tensor("hc", [128, 64 * 512], BF16)))
        outc = Buf(E(nc.sbuf_tensor("outc", [128, 16 * 512], F32)))
        wg = [Buf(E(nc.sbuf_tensor(f"wg{i}", [128, 16 * 128], BF16))) for i in range(2)]
        wu = [Buf(E(nc.sbuf_tensor(f"wu{i}", [128, 16 * 128], BF16))) for i in range(2)]
        wp = [Buf(E(nc.sbuf_tensor(f"wp{i}", [128, 32 * 128], BF16))) for i in range(2)]
        ps_g = Buf(E(nc.psum_tensor("ps_g", [128, 512], F32)))
        ps_u = Buf(E(nc.psum_tensor("ps_u", [128, 512], F32)))
        ps_d = Buf(E(nc.psum_tensor("ps_d", [128, 512], F32)))
        ps_rms = Buf(E(nc.psum_tensor("ps_rms", [1, 512], F32)))
        ps_rb = Buf(E(nc.psum_tensor("ps_rb", [128, 512], F32)))
        silu_t = Buf(E(nc.sbuf_tensor("silu_t", [128, 512], F32)))
        sqh = Buf(E(nc.sbuf_tensor("sqh", [128, 512], BF16)))
        rmsrow = Buf(E(nc.sbuf_tensor("rmsrow", [1, 512], F32)))
        rmsb = Buf(E(nc.sbuf_tensor("rmsb", [128, 512], F32)))

        NCH = T_ // 512
        r1v = R1.rearrange("(t p) n -> p t n", p=128)
        r2v = R2.rearrange("(t p) n -> p t n", p=128)
        whgv = WHG.rearrange("(t p) f -> p t f", p=128)
        whpv = WHP.rearrange("(g p) f -> p g f", p=128)
        r2buf = Buf(None)

        for c in range(NCH):
            c0 = c * 512
            kb.dma("y", lambda e, c0=c0: e.dma_start(
                out=r1c.ap[:].rearrange("p (t n) -> p t n", n=512),
                in_=r1v[:, :, c0:c0 + 512]), writes=[r1c])
            r1tiles = [r1c.view(r1c.ap[:, i * 512:(i + 1) * 512]) for i in range(16)]
            xltiles = [xlnc.view(xlnc.ap[:, i * 512:(i + 1) * 512]) for i in range(16)]
            _ln_chunk(kb, nc, E, r1tiles, xltiles, ones_col, ones_row, 1e-5,
                      f"lnB{c}", kb_cache=_ln_cache(kb, nc, E, "lnB", 512))
            kb.merge_views(r1c, r1tiles)
            kb.merge_views(xlnc, xltiles)
            for g in range(64):
                gb = wg[g % 2]
                ub = wu[g % 2]
                kb.dma("s", lambda e, g=g, gb=gb: e.dma_start(
                    out=gb.ap[:].rearrange("p (t f) -> p t f", f=128),
                    in_=whgv[:, :, g * 128:(g + 1) * 128]), writes=[gb])
                kb.dma("s", lambda e, g=g, ub=ub: e.dma_start(
                    out=ub.ap[:].rearrange("p (t f) -> p t f", f=128),
                    in_=whgv[:, :, 8192 + g * 128:8192 + (g + 1) * 128]), writes=[ub])
                for i in range(16):
                    st, sp = (i == 0), (i == 15)
                    kb.accum_matmul(ps_g, gb, xlnc, lambda e, i=i, gb=gb, st=st, sp=sp: e.matmul(
                        ps_g.ap[:], gb.ap[:, i * 128:(i + 1) * 128],
                        xlnc.ap[:, i * 512:(i + 1) * 512], start=st, stop=sp), st, sp)
                    kb.accum_matmul(ps_u, ub, xlnc, lambda e, i=i, ub=ub, st=st, sp=sp: e.matmul(
                        ps_u.ap[:], ub.ap[:, i * 128:(i + 1) * 128],
                        xlnc.ap[:, i * 512:(i + 1) * 512], start=st, stop=sp), st, sp)
                kb.s(lambda e: e.activation(out=silu_t.ap[:], in_=ps_g.ap[:], func=AF.Silu),
                     reads=[ps_g], writes=[silu_t])
                kb.v(lambda e, g=g: e.tensor_mul(out=hc.ap[:, g * 512:(g + 1) * 512],
                                                 in0=silu_t.ap[:], in1=ps_u.ap[:]),
                     reads=[silu_t, ps_u], writes=[hc])
                kb.v(lambda e, g=g: e.tensor_mul(out=sqh.ap[:], in0=hc.ap[:, g * 512:(g + 1) * 512],
                                                 in1=hc.ap[:, g * 512:(g + 1) * 512]),
                     reads=[hc], writes=[sqh])
                st, sp = (g == 0), (g == 63)
                kb.accum_matmul(ps_rms, ones_col, sqh, lambda e, st=st, sp=sp: e.matmul(
                    ps_rms.ap[:], ones_col.ap[:, 0:1], sqh.ap[:], start=st, stop=sp), st, sp)
            epsb2 = kb.const_col(ctx, F32EPS)
            kb.s(lambda e: e.activation(out=rmsrow.ap[:], in_=ps_rms.ap[0:1, :], func=AF.Ln,
                                        bias=epsb2.ap[0:1, 0:1], scale=1.0 / 8192),
                 reads=[ps_rms, epsb2], writes=[rmsrow])
            kb.s(lambda e: e.activation(out=rmsrow.ap[:], in_=rmsrow.ap[:], func=AF.Exp,
                                        scale=-0.5), reads=[rmsrow], writes=[rmsrow])
            kb.t(lambda e: e.matmul(ps_rb.ap[:], ones_row.ap[0:1, :], rmsrow.ap[0:1, :],
                                    start=True, stop=True),
                 reads=[rmsrow, ones_row], writes=[ps_rb])
            kb.v(lambda e: e.tensor_copy(out=rmsb.ap[:], in_=ps_rb.ap[:]),
                 reads=[ps_rb], writes=[rmsb])
            for g in range(64):
                kb.v(lambda e, g=g: e.tensor_mul(out=hc.ap[:, g * 512:(g + 1) * 512],
                                                 in0=hc.ap[:, g * 512:(g + 1) * 512],
                                                 in1=rmsb.ap[:]),
                     reads=[rmsb, hc], writes=[hc])
            for j in range(16):
                for half in range(2):
                    pb = wp[half]
                    kb.dma("s", lambda e, j=j, half=half, pb=pb: e.dma_start(
                        out=pb.ap[:].rearrange("p (g f) -> p g f", f=128),
                        in_=whpv[:, half * 32:(half + 1) * 32, j * 128:(j + 1) * 128]),
                        writes=[pb])
                    for gg in range(32):
                        g = half * 32 + gg
                        st, sp = (g == 0), (g == 63)
                        kb.accum_matmul(ps_d, pb, hc, lambda e, g=g, gg=gg, pb=pb, st=st, sp=sp: e.matmul(
                            ps_d.ap[:], pb.ap[:, gg * 128:(gg + 1) * 128],
                            hc.ap[:, g * 512:(g + 1) * 512], start=st, stop=sp), st, sp)
                kb.v(lambda e, j=j: e.tensor_copy(out=outc.ap[:, j * 512:(j + 1) * 512],
                                                  in_=ps_d.ap[:]),
                     reads=[ps_d], writes=[outc])
            otiles = [outc.view(outc.ap[:, i * 512:(i + 1) * 512]) for i in range(16)]
            _ln_chunk(kb, nc, E, otiles, otiles, ones_col, ones_row, 1e-5,
                      f"lnB2{c}", kb_cache=_ln_cache(kb, nc, E, "lnB", 512))
            kb.merge_views(outc, otiles)
            kb.dma("y", lambda e, c0=c0: e.dma_start(
                out=r2v[:, :, c0:c0 + 512],
                in_=outc.ap[:].rearrange("p (t n) -> p t n", n=512)),
                reads=[outc], writes=[r2buf])
            kb.dma("p", lambda e, c0=c0: e.dma_start(
                out=r2v[:, :, c0:c0 + 512],
                in_=r1v[:, :, c0:c0 + 512], accum_op=ALU.add),
                reads=[r1c], writes=[r2buf])
        kb.final_wait("y")
    return nc


_LN_CACHES = {}


def _ln_cache(kb, nc, E, name, w, eps=1e-5):
    key = (id(kb), name, w)
    if key not in _LN_CACHES:
        class _C(dict):
            pass
        _LN_CACHES[key] = {
            "epsb": None,
            "xb": Buf(E(nc.sbuf_tensor(name + "_xb", [128, w], BF16))),
            "sq": Buf(E(nc.sbuf_tensor(name + "_sq", [128, w], BF16))),
            "ps_sum": Buf(E(nc.psum_tensor(name + "_psum", [1, w], F32))),
            "ps_sq": Buf(E(nc.psum_tensor(name + "_psq", [1, w], F32))),
            "mu": Buf(E(nc.sbuf_tensor(name + "_mu", [1, w], F32))),
            "rstd": Buf(E(nc.sbuf_tensor(name + "_rstd", [1, w], F32))),
            "ps_b": Buf(E(nc.psum_tensor(name + "_pb", [128, w], F32))),
            "mub": Buf(E(nc.sbuf_tensor(name + "_mub", [128, w], F32))),
            "rstdb": Buf(E(nc.sbuf_tensor(name + "_rstdb", [128, w], F32))),
            "tmp": Buf(E(nc.sbuf_tensor(name + "_tmp", [128, w], F32))),
        }
        b = Buf(E(nc.sbuf_tensor(name + "_eps", [1, 1], F32)))
        kb.v(lambda e: e.memset(b.ap[:], float(eps)), writes=[b])
        _LN_CACHES[key]["epsb"] = b
    return _LN_CACHES[key]


def _ln_chunk(kb, nc, E, xtiles, otiles, ones_col, ones_row, eps, name,
              kb_cache=None, gamma=None, beta=None):
    cs = kb_cache
    xb, sq = cs["xb"], cs["sq"]
    ps_sum, ps_sq = cs["ps_sum"], cs["ps_sq"]
    mu, rstd = cs["mu"], cs["rstd"]
    ps_b = ps_b2 = cs["ps_b"]
    mub, rstdb, tmp = cs["mub"], cs["rstdb"], cs["tmp"]
    nt = len(xtiles)
    nD_ = nt * 128
    w = xtiles[0].ap.shape[-1]
    for i, xT in enumerate(xtiles):
        kb.v(lambda e, xT=xT, w=w: e.tensor_copy(out=xb.ap[:, 0:w], in_=xT.ap[:]),
             reads=[xT], writes=[xb])
        kb.v(lambda e, w=w: e.tensor_mul(out=sq.ap[:, 0:w], in0=xb.ap[:, 0:w],
                                         in1=xb.ap[:, 0:w]), reads=[xb], writes=[sq])
        st, sp = (i == 0), (i == nt - 1)
        kb.accum_matmul(ps_sum, ones_col, xb, lambda e, w=w, st=st, sp=sp: e.matmul(
            ps_sum.ap[0:1, 0:w], ones_col.ap[:, 0:1], xb.ap[:, 0:w],
            start=st, stop=sp), st, sp)
        kb.accum_matmul(ps_sq, ones_col, sq, lambda e, w=w, st=st, sp=sp: e.matmul(
            ps_sq.ap[0:1, 0:w], ones_col.ap[:, 0:1], sq.ap[:, 0:w],
            start=st, stop=sp), st, sp)
    kb.v(lambda e, w=w: e.tensor_scalar(out=mu.ap[:, 0:w], in0=ps_sum.ap[0:1, 0:w],
                                        scalar1=1.0 / nD_, scalar2=None, op0=ALU.mult),
         reads=[ps_sum], writes=[mu])
    kb.v(lambda e, w=w: e.tensor_mul(out=rstd.ap[:, 0:w], in0=mu.ap[:, 0:w],
                                     in1=mu.ap[:, 0:w]), reads=[mu], writes=[rstd])
    kb.v(lambda e, w=w: e.scalar_tensor_tensor(
        out=rstd.ap[:, 0:w], in0=ps_sq.ap[0:1, 0:w], scalar=1.0 / nD_,
        in1=rstd.ap[:, 0:w], op0=ALU.mult, op1=ALU.subtract),
        reads=[ps_sq, rstd], writes=[rstd])
    epsb = cs["epsb"]
    kb.s(lambda e, w=w: e.activation(out=rstd.ap[:, 0:w], in_=rstd.ap[:, 0:w],
                                     func=AF.Ln, bias=epsb.ap[0:1, 0:1]),
         reads=[rstd, epsb], writes=[rstd])
    kb.s(lambda e, w=w: e.activation(out=rstd.ap[:, 0:w], in_=rstd.ap[:, 0:w],
                                     func=AF.Exp, scale=-0.5),
         reads=[rstd], writes=[rstd])
    kb.t(lambda e, w=w: e.matmul(ps_b.ap[:, 0:w], ones_row.ap[0:1, :],
                                 mu.ap[0:1, 0:w], start=True, stop=True),
         reads=[mu, ones_row], writes=[ps_b])
    kb.v(lambda e, w=w: e.tensor_copy(out=mub.ap[:, 0:w], in_=ps_b.ap[:, 0:w]),
         reads=[ps_b], writes=[mub])
    kb.t(lambda e, w=w: e.matmul(ps_b2.ap[:, 0:w], ones_row.ap[0:1, :],
                                 rstd.ap[0:1, 0:w], start=True, stop=True),
         reads=[rstd, ones_row], writes=[ps_b2])
    kb.v(lambda e, w=w: e.tensor_copy(out=rstdb.ap[:, 0:w], in_=ps_b2.ap[:, 0:w]),
         reads=[ps_b2], writes=[rstdb])
    for i, xT in enumerate(xtiles):
        ob = otiles[i]
        kb.v(lambda e, xT=xT, w=w: e.tensor_sub(out=tmp.ap[:, 0:w], in0=xT.ap[:],
                                                in1=mub.ap[:, 0:w]),
             reads=[xT, mub], writes=[tmp])
        if gamma is None:
            kb.v(lambda e, ob=ob, w=w: e.tensor_mul(out=ob.ap[:], in0=tmp.ap[:, 0:w],
                                                    in1=rstdb.ap[:, 0:w]),
                 reads=[tmp, rstdb], writes=[ob])
        else:
            kb.v(lambda e, w=w: e.tensor_mul(out=tmp.ap[:, 0:w], in0=tmp.ap[:, 0:w],
                                             in1=rstdb.ap[:, 0:w]),
                 reads=[tmp, rstdb], writes=[tmp])
            g_, b_ = gamma[i], beta[i]
            kb.v(lambda e, ob=ob, g_=g_, b_=b_, w=w: e.tensor_scalar(
                out=ob.ap[:], in0=tmp.ap[:, 0:w], scalar1=g_.ap[:, 0:1],
                scalar2=b_.ap[:, 0:1], op0=ALU.mult, op1=ALU.add),
                reads=[tmp, g_, b_], writes=[ob])



def build_launch_A(S_=8192, T_=T):
    nc = bass.Bass("TRN2", target_bir_lowering=False)
    XCTX = nc.dram_tensor("XCTX", [D, S_], F32, kind="ExternalInput").ap()
    WDQ = nc.dram_tensor("WDQ", [D, 512], F32, kind="ExternalInput").ap()
    WUQ = nc.dram_tensor("WUQ", [512, D], F32, kind="ExternalInput").ap()
    WDKV = nc.dram_tensor("WDKV", [D, 512], F32, kind="ExternalInput").ap()
    WUKV = nc.dram_tensor("WUKV", [512, 2 * D], F32, kind="ExternalInput").ap()
    WOT = nc.dram_tensor("WOT", [D, D], F32, kind="ExternalInput").ap()
    QLNG = nc.dram_tensor("QLNG", [512, 1], F32, kind="ExternalInput").ap()
    QLNB = nc.dram_tensor("QLNB", [512, 1], F32, kind="ExternalInput").ap()
    KVLNG = nc.dram_tensor("KVLNG", [512, 1], F32, kind="ExternalInput").ap()
    KVLNB = nc.dram_tensor("KVLNB", [512, 1], F32, kind="ExternalInput").ap()
    NJ = T_ // 512
    NI = S_ // 128
    MASKS = nc.dram_tensor("MASKS", [NJ, NI, 128, 512], BF16, kind="ExternalInput").ap()
    IDXQ = nc.dram_tensor("IDXQ", [1, 1], mybir.dt.int32, kind="ExternalInput").ap()
    ONESC = nc.dram_tensor("ONESC", [128, 1], BF16, kind="ExternalInput").ap()
    ONESR = nc.dram_tensor("ONESR", [1, 128], F32, kind="ExternalInput").ap()
    R1 = nc.dram_tensor("R1", [D, T_], F32, kind="ExternalOutput").ap()
    CQ_T = nc.dram_tensor("cq_t", [512, S_], BF16, kind="Internal").ap()
    CKV_T = nc.dram_tensor("ckv_t", [512, S_], BF16, kind="Internal").ap()
    KT_S = nc.dram_tensor("kt_s", [D, S_], BF16, kind="Internal").ap()
    VT_S = nc.dram_tensor("vt_s", [S_, D], BF16, kind="Internal").ap()
    QT_S = nc.dram_tensor("qt_s", [D, T_], BF16, kind="Internal").ap()
    AOT_S = nc.dram_tensor("aot_s", [D, T_], BF16, kind="Internal").ap()

    xctxv = XCTX.rearrange("(t p) n -> p t n", p=128)
    r1v = R1.rearrange("(t p) n -> p t n", p=128)
    cqv = CQ_T.rearrange("(t p) n -> p t n", p=128)
    ckvv = CKV_T.rearrange("(t p) n -> p t n", p=128)
    ktv = KT_S.rearrange("(t p) n -> p t n", p=128)
    vtv = VT_S.rearrange("(t p) f -> p t f", p=128)
    qtv = QT_S.rearrange("(t p) n -> p t n", p=128)
    aotv = AOT_S.rearrange("(t p) n -> p t n", p=128)
    NCH = S_ // 512
    NOC = T_ // 512

    from contextlib import ExitStack
    with ExitStack() as ctx:
        kb = KB(nc)
        kb.init_sems(ctx)
        E = ctx.enter_context
        ones_col = Buf(E(nc.sbuf_tensor("onesc", [128, 1], BF16)))
        ones_row = Buf(E(nc.sbuf_tensor("onesr", [1, 128], F32)))
        kb.dma("y", lambda e: e.dma_start(out=ones_col.ap[:], in_=ONESC), writes=[ones_col])
        kb.dma("y", lambda e: e.dma_start(out=ones_row.ap[:], in_=ONESR), writes=[ones_row])
        idxq = Buf(E(nc.sbuf_tensor("idxq", [1, 1], mybir.dt.int32)))
        kb.dma("y", lambda e: e.dma_start(out=idxq.ap[:], in_=IDXQ), writes=[idxq])

        scr_cq = Buf(None)
        scr_ckv = Buf(None)
        scr_kt = Buf(None)
        scr_vt = Buf(None)
        scr_qt = Buf(None)
        scr_aot = Buf(None)

        with ExitStack() as sub:
            SE = sub.enter_context
            wdq = Buf(SE(nc.sbuf_tensor("wdq", [128, 16 * 512], BF16)))
            wdkv = Buf(SE(nc.sbuf_tensor("wdkv", [128, 16 * 512], BF16)))
            kb.dma("p", lambda e: e.dma_start(
                out=wdq.ap[:].rearrange("p (t f) -> p t f", f=512),
                in_=WDQ.rearrange("(t p) f -> p t f", p=128)), writes=[wdq])
            kb.dma("p", lambda e: e.dma_start(
                out=wdkv.ap[:].rearrange("p (t f) -> p t f", f=512),
                in_=WDKV.rearrange("(t p) f -> p t f", p=128)), writes=[wdkv])
            qg = Buf(SE(nc.sbuf_tensor("qg", [128, 4], F32)))
            qb = Buf(SE(nc.sbuf_tensor("qb", [128, 4], F32)))
            kg = Buf(SE(nc.sbuf_tensor("kg", [128, 4], F32)))
            kbta = Buf(SE(nc.sbuf_tensor("kbta", [128, 4], F32)))
            for b, t_ in ((qg, QLNG), (qb, QLNB), (kg, KVLNG), (kbta, KVLNB)):
                kb.dma("y", lambda e, b=b, t_=t_: e.dma_start(
                    out=b.ap[:].unsqueeze(2),
                    in_=t_.rearrange("(t p) o -> p t o", p=128),
                    allow_slow_non_contiguous=True), writes=[b])
            xc = Buf(SE(nc.sbuf_tensor("xc", [128, 16 * 512], F32)))
            xlnb = Buf(SE(nc.sbuf_tensor("xlnb", [128, 16 * 512], BF16)))
            cqkv = Buf(SE(nc.sbuf_tensor("cqkv", [128, 8 * 512], F32)))
            cqkvb = Buf(SE(nc.sbuf_tensor("cqkvb", [128, 8 * 512], BF16)))
            ps = Buf(SE(nc.psum_tensor("psA1", [128, 512], F32)))
            lnc = _ln_cache(kb, nc, SE, "lnA1", 512)
            lnc5 = _ln_cache(kb, nc, SE, "lnA1b", 512, eps=1e-5)
            for ch in range(NCH):
                c0 = ch * 512
                kb.dma("y", lambda e, c0=c0: e.dma_start(
                    out=xc.ap[:].rearrange("p (t n) -> p t n", n=512),
                    in_=xctxv[:, :, c0:c0 + 512]), writes=[xc])
                xts = [xc.view(xc.ap[:, i * 512:(i + 1) * 512]) for i in range(16)]
                xlts = [xlnb.view(xlnb.ap[:, i * 512:(i + 1) * 512]) for i in range(16)]
                _ln_chunk(kb, nc, SE, xts, xlts, ones_col, ones_row, 1e-5,
                          "", kb_cache=lnc)
                kb.merge_views(xc, xts)
                kb.merge_views(xlnb, xlts)
                for half, wd in ((0, wdq), (1, wdkv)):
                    for mt in range(4):
                        for i in range(16):
                            st, sp = (i == 0), (i == 15)
                            kb.accum_matmul(ps, wd, xlnb,
                                lambda e, i=i, mt=mt, wd=wd, st=st, sp=sp: e.matmul(
                                    ps.ap[:], wd.ap[:, i * 512 + mt * 128:i * 512 + (mt + 1) * 128],
                                    xlnb.ap[:, i * 512:(i + 1) * 512], start=st, stop=sp),
                                st, sp)
                        kb.v(lambda e, half=half, mt=mt: e.tensor_copy(
                            out=cqkv.ap[:, (half * 4 + mt) * 512:(half * 4 + mt + 1) * 512],
                            in_=ps.ap[:]), reads=[ps], writes=[cqkv])
                cqts = [cqkv.view(cqkv.ap[:, i * 512:(i + 1) * 512]) for i in range(4)]
                cqos = [cqkvb.view(cqkvb.ap[:, i * 512:(i + 1) * 512]) for i in range(4)]
                ckts = [cqkv.view(cqkv.ap[:, (4 + i) * 512:(5 + i) * 512]) for i in range(4)]
                ckos = [cqkvb.view(cqkvb.ap[:, (4 + i) * 512:(5 + i) * 512]) for i in range(4)]
                qgs = [qg.view(qg.ap[:, i:i + 1]) for i in range(4)]
                qbs = [qb.view(qb.ap[:, i:i + 1]) for i in range(4)]
                kgs = [kg.view(kg.ap[:, i:i + 1]) for i in range(4)]
                kbs = [kbta.view(kbta.ap[:, i:i + 1]) for i in range(4)]
                _ln_chunk(kb, nc, SE, cqts, cqos, ones_col, ones_row, 1e-5,
                          "", kb_cache=lnc5, gamma=qgs, beta=qbs)
                _ln_chunk(kb, nc, SE, ckts, ckos, ones_col, ones_row, 1e-5,
                          "", kb_cache=lnc5, gamma=kgs, beta=kbs)
                kb.merge_views(cqkv, cqts + ckts)
                kb.merge_views(cqkvb, cqos + ckos)
                kb.dma("y", lambda e, c0=c0: e.dma_start(
                    out=cqv[:, :, c0:c0 + 512],
                    in_=cqkvb.ap[:, 0:4 * 512].rearrange("p (t n) -> p t n", n=512)),
                    reads=[cqkvb], writes=[scr_cq])
                kb.dma("y", lambda e, c0=c0: e.dma_start(
                    out=ckvv[:, :, c0:c0 + 512],
                    in_=cqkvb.ap[:, 4 * 512:8 * 512].rearrange("p (t n) -> p t n", n=512)),
                    reads=[cqkvb], writes=[scr_ckv])
        kb.phase_barrier()

        with ExitStack() as sub:
            SE = sub.enter_context
            wukv = Buf(SE(nc.sbuf_tensor("wukv", [128, 4 * 4096], BF16)))
            wuq = Buf(SE(nc.sbuf_tensor("wuq", [128, 4 * 2048], BF16)))
            kb.dma("p", lambda e: e.dma_start(
                out=wukv.ap[:].rearrange("p (t f) -> p t f", f=4096),
                in_=WUKV.rearrange("(t p) f -> p t f", p=128)), writes=[wukv])
            kb.dma("p", lambda e: e.dma_start(
                out=wuq.ap[:].rearrange("p (t f) -> p t f", f=2048),
                in_=WUQ.rearrange("(t p) f -> p t f", p=128)), writes=[wuq])
            ckvc = Buf(SE(nc.sbuf_tensor("ckvc", [128, 4 * 512], BF16)))
            ev = Buf(SE(nc.sbuf_tensor("evA2", [128, 512], BF16)))
            ps = Buf(SE(nc.psum_tensor("psA2", [128, 512], F32)))
            for ch in range(NCH):
                c0 = ch * 512
                kb.dma("y", lambda e, c0=c0: e.dma_start(
                    out=ckvc.ap[:].rearrange("p (t n) -> p t n", n=512),
                    in_=ckvv[:, :, c0:c0 + 512]), writes=[ckvc])
                for kf in range(16):
                    for i in range(4):
                        st, sp = (i == 0), (i == 3)
                        kb.accum_matmul(ps, wukv, ckvc,
                            lambda e, i=i, kf=kf, st=st, sp=sp: e.matmul(
                                ps.ap[:], wukv.ap[:, i * 4096 + kf * 128:i * 4096 + (kf + 1) * 128],
                                ckvc.ap[:, i * 512:(i + 1) * 512], start=st, stop=sp), st, sp)
                    kb.v(lambda e: e.tensor_copy(out=ev.ap[:], in_=ps.ap[:]),
                         reads=[ps], writes=[ev])
                    kb.dma("y", lambda e, kf=kf, c0=c0: e.dma_start(
                        out=ktv[:, kf, c0:c0 + 512], in_=ev.ap[:]),
                        reads=[ev], writes=[scr_kt])
                for tt in range(4):
                    for vc in range(4):
                        for i in range(4):
                            st, sp = (i == 0), (i == 3)
                            kb.accum_matmul(ps, ckvc, wukv,
                                lambda e, i=i, tt=tt, vc=vc, st=st, sp=sp: e.matmul(
                                    ps.ap[:], ckvc.ap[:, i * 512 + tt * 128:i * 512 + (tt + 1) * 128],
                                    wukv.ap[:, i * 4096 + 2048 + vc * 512:i * 4096 + 2048 + (vc + 1) * 512],
                                    start=st, stop=sp), st, sp)
                        kb.v(lambda e: e.tensor_copy(out=ev.ap[:], in_=ps.ap[:]),
                             reads=[ps], writes=[ev])
                        kb.dma("y", lambda e, ch=ch, tt=tt, vc=vc: e.dma_start(
                            out=vtv[:, ch * 4 + tt, vc * 512:(vc + 1) * 512], in_=ev.ap[:]),
                            reads=[ev], writes=[scr_vt])
            regq = kb.value_load("y", idxq)
            for oc in range(NOC):
                kb.dma("y", lambda e, oc=oc: e.dma_start(
                    out=ckvc.ap[:].rearrange("p (t n) -> p t n", n=512),
                    in_=cqv[:, :, bass.ts(regq, T_)][:, :, oc * 512:(oc + 1) * 512]),
                    reads=[idxq], writes=[ckvc], extra=list(scr_cq.w))
                for qf in range(16):
                    for i in range(4):
                        st, sp = (i == 0), (i == 3)
                        kb.accum_matmul(ps, wuq, ckvc,
                            lambda e, i=i, qf=qf, st=st, sp=sp: e.matmul(
                                ps.ap[:], wuq.ap[:, i * 2048 + qf * 128:i * 2048 + (qf + 1) * 128],
                                ckvc.ap[:, i * 512:(i + 1) * 512], start=st, stop=sp), st, sp)
                    kb.v(lambda e: e.tensor_copy(out=ev.ap[:], in_=ps.ap[:]),
                         reads=[ps], writes=[ev])
                    kb.dma("y", lambda e, qf=qf, oc=oc: e.dma_start(
                        out=qtv[:, qf, oc * 512:(oc + 1) * 512], in_=ev.ap[:]),
                        reads=[ev], writes=[scr_qt])
        kb.phase_barrier()

        with ExitStack() as sub:
            SE = sub.enter_context
            maskb = Buf(SE(nc.sbuf_tensor("maskb", [128, NI * 512], BF16)))
            ktb = Buf(SE(nc.sbuf_tensor("ktb", [128, S_], BF16)))
            vtb = Buf(SE(nc.sbuf_tensor("vtb", [128, NI * 128], BF16)))
            qjb = Buf(SE(nc.sbuf_tensor("qjb", [128, 512], BF16)))
            pt = Buf(SE(nc.sbuf_tensor("ptA", [128, 1024], BF16)))
            ps_sc = Buf(SE(nc.psum_tensor("ps_sc", [128, 1024], F32)))
            ps_pv = Buf(SE(nc.psum_tensor("ps_pv", [128, 512], F32)))
            ps_dn = Buf(SE(nc.psum_tensor("ps_dn", [1, 512], F32)))
            ps_db = Buf(SE(nc.psum_tensor("ps_db", [128, 512], F32)))
            dnrow = Buf(SE(nc.sbuf_tensor("dnrow", [1, 512], F32)))
            dnb = Buf(SE(nc.sbuf_tensor("dnb", [128, 512], F32)))
            aoj = Buf(SE(nc.sbuf_tensor("aoj", [128, 512], BF16)))
            SCALE = 1.0 / float(np.sqrt(128.0))
            for j in range(NJ):
                kb.dma("y", lambda e, j=j: e.dma_start(
                    out=maskb.ap[:].rearrange("p (i n) -> p i n", n=512),
                    in_=MASKS[j].rearrange("i p n -> p i n")), writes=[maskb])
                for h in range(16):
                    kb.dma("y", lambda e, h=h: e.dma_start(
                        out=ktb.ap[:].rearrange("p (t n) -> p t n", n=512),
                        in_=ktv[:, h, :].rearrange("p (t n) -> p t n", n=512)),
                        writes=[ktb], extra=list(scr_kt.w))
                    kb.dma("s", lambda e, h=h: e.dma_start(
                        out=vtb.ap[:].rearrange("p (t f) -> p t f", f=128),
                        in_=vtv[:, :, h * 128:(h + 1) * 128]),
                        writes=[vtb], extra=list(scr_vt.w))
                    kb.dma("s", lambda e, h=h, j=j: e.dma_start(
                        out=qjb.ap[:], in_=qtv[:, h, j * 512:(j + 1) * 512]),
                        writes=[qjb], extra=list(scr_qt.w))
                    for ip in range(NI // 2):
                        for half in range(2):
                            i = 2 * ip + half
                            kb.t(lambda e, i=i, half=half: e.matmul(
                                ps_sc.ap[:, half * 512:(half + 1) * 512],
                                ktb.ap[:, i * 128:(i + 1) * 128], qjb.ap[:],
                                start=True, stop=True),
                                reads=[ktb, qjb], writes=[ps_sc] if half == 1 else [],
                                ) if half == 1 else kb.accum_matmul(
                                ps_sc, ktb, qjb,
                                lambda e, i=i, half=half: e.matmul(
                                    ps_sc.ap[:, half * 512:(half + 1) * 512],
                                    ktb.ap[:, i * 128:(i + 1) * 128], qjb.ap[:],
                                    start=True, stop=True), True, False)
                        kb.s(lambda e: e.activation(out=pt.ap[:], in_=ps_sc.ap[:],
                                                    func=AF.Exp, scale=SCALE),
                             reads=[ps_sc], writes=[pt])
                        kb.v(lambda e, ip=ip: e.tensor_mul(
                            out=pt.ap[:], in0=pt.ap[:],
                            in1=maskb.ap[:, ip * 1024:(ip + 1) * 1024]),
                            reads=[maskb, pt], writes=[pt])
                        for half in range(2):
                            i = 2 * ip + half
                            st, sp = (i == 0), (i == NI - 1)
                            kb.accum_matmul(ps_pv, vtb, pt,
                                lambda e, i=i, half=half, st=st, sp=sp: e.matmul(
                                    ps_pv.ap[:], vtb.ap[:, i * 128:(i + 1) * 128],
                                    pt.ap[:, half * 512:(half + 1) * 512],
                                    start=st, stop=sp), st, sp)
                            kb.accum_matmul(ps_dn, ones_col, pt,
                                lambda e, half=half, st=st, sp=sp: e.matmul(
                                    ps_dn.ap[:], ones_col.ap[:, 0:1],
                                    pt.ap[:, half * 512:(half + 1) * 512],
                                    start=st, stop=sp), st, sp)
                    kb.v(lambda e: e.reciprocal(out=dnrow.ap[:], in_=ps_dn.ap[0:1, :]),
                         reads=[ps_dn], writes=[dnrow])
                    kb.t(lambda e: e.matmul(ps_db.ap[:], ones_row.ap[0:1, :],
                                            dnrow.ap[0:1, :], start=True, stop=True),
                         reads=[dnrow, ones_row], writes=[ps_db])
                    kb.v(lambda e: e.tensor_copy(out=dnb.ap[:], in_=ps_db.ap[:]),
                         reads=[ps_db], writes=[dnb])
                    kb.v(lambda e: e.tensor_mul(out=aoj.ap[:], in0=ps_pv.ap[:],
                                                in1=dnb.ap[:]),
                         reads=[ps_pv, dnb], writes=[aoj])
                    kb.dma("y", lambda e, h=h, j=j: e.dma_start(
                        out=aotv[:, h, j * 512:(j + 1) * 512], in_=aoj.ap[:]),
                        reads=[aoj], writes=[scr_aot])
        kb.phase_barrier()

        with ExitStack() as sub:
            SE = sub.enter_context
            aoc = Buf(SE(nc.sbuf_tensor("aoc", [128, 16 * 512], BF16)))
            wotb = Buf(SE(nc.sbuf_tensor("wotb", [128, 16 * 2048], BF16)))
            outc = Buf(SE(nc.sbuf_tensor("outcA", [128, 16 * 512], F32)))
            ps = Buf(SE(nc.psum_tensor("psA3", [128, 512], F32)))
            lnc = _ln_cache(kb, nc, SE, "lnA3", 512)
            wotv = WOT.rearrange("(t p) f -> p t f", p=128)
            kb.dma("p", lambda e: e.dma_start(
                out=wotb.ap[:].rearrange("p (t f) -> p t f", f=2048),
                in_=wotv), writes=[wotb])
            r1buf = Buf(None)
            for oc in range(NOC):
                c0 = oc * 512
                kb.dma("y", lambda e, c0=c0: e.dma_start(
                    out=aoc.ap[:].rearrange("p (t n) -> p t n", n=512),
                    in_=aotv[:, :, c0:c0 + 512]),
                    writes=[aoc], extra=list(scr_aot.w))
                for j in range(16):
                    for i in range(16):
                        st, sp = (i == 0), (i == 15)
                        kb.accum_matmul(ps, wotb, aoc,
                            lambda e, i=i, j=j, st=st, sp=sp: e.matmul(
                                ps.ap[:], wotb.ap[:, i * 2048 + j * 128:i * 2048 + (j + 1) * 128],
                                aoc.ap[:, i * 512:(i + 1) * 512], start=st, stop=sp), st, sp)
                    kb.v(lambda e, j=j: e.tensor_copy(out=outc.ap[:, j * 512:(j + 1) * 512],
                                                      in_=ps.ap[:]),
                         reads=[ps], writes=[outc])
                otiles = [outc.view(outc.ap[:, i * 512:(i + 1) * 512]) for i in range(16)]
                _ln_chunk(kb, nc, SE, otiles, otiles, ones_col, ones_row, 1e-5,
                          "", kb_cache=lnc)
                kb.merge_views(outc, otiles)
                kb.dma("y", lambda e, c0=c0: e.dma_start(
                    out=r1v[:, :, c0:c0 + 512],
                    in_=outc.ap[:].rearrange("p (t n) -> p t n", n=512)),
                    reads=[outc], writes=[r1buf])
                regq2 = kb.value_load("p", idxq)
                kb.dma("p", lambda e, c0=c0, regq2=regq2: e.dma_start(
                    out=r1v[:, :, c0:c0 + 512],
                    in_=xctxv[:, :, bass.ts(regq2, T_)][:, :, c0:c0 + 512],
                    accum_op=ALU.add),
                    reads=[idxq], writes=[r1buf])
        kb.final_wait("y")
    return nc



CH = 128
EXT = 128


def build_launch_C(T_=T):
    TE = T_ + EXT
    NCHT = TE // 128
    nc = bass.Bass("TRN2", target_bir_lowering=False)
    R2E = nc.dram_tensor("R2E", [D, TE], F32, kind="ExternalInput").ap()
    WINT = nc.dram_tensor("WINT", [D, 8384], F32, kind="ExternalInput").ap()
    CONVW = nc.dram_tensor("CONVW", [4224, 4], F32, kind="ExternalInput").ap()
    CONVB = nc.dram_tensor("CONVB", [4224, 1], F32, kind="ExternalInput").ap()
    DTB = nc.dram_tensor("DTB", [64, 1], F32, kind="ExternalInput").ap()
    ALOG = nc.dram_tensor("ALOG", [64, 1], F32, kind="ExternalInput").ap()
    DROW = nc.dram_tensor("DROW", [1, 4096], F32, kind="ExternalInput").ap()
    TRI = nc.dram_tensor("TRI", [128, 128], F32, kind="ExternalInput").ap()
    EYE = nc.dram_tensor("EYE", [128, 128], BF16, kind="ExternalInput").ap()
    EYEF = nc.dram_tensor("EYEF", [128, 128], F32, kind="ExternalInput").ap()
    ONESC = nc.dram_tensor("ONESC", [128, 1], BF16, kind="ExternalInput").ap()
    ONESR = nc.dram_tensor("ONESR", [1, 128], F32, kind="ExternalInput").ap()
    YP = nc.dram_tensor("YP", [T_, 4096], F32, kind="ExternalOutput").ap()
    ZS = nc.dram_tensor("ZS", [T_, 4096], BF16, kind="ExternalOutput").ap()
    CS = nc.dram_tensor("CS", [64, T_], BF16, kind="ExternalOutput").ap()
    AC = nc.dram_tensor("AC", [64, T_], F32, kind="ExternalOutput").ap()
    UEND = nc.dram_tensor("UEND", [64, 4096], F32, kind="ExternalOutput").ap()
    ATOT = nc.dram_tensor("ATOT", [64, 1], F32, kind="ExternalOutput").ap()
    XBCS = nc.dram_tensor("xbcs", [4224, TE], BF16, kind="Internal").ap()

    r2v = R2E.rearrange("(t p) n -> p t n", p=128)
    wintv = WINT.rearrange("(t p) f -> p t f", p=128)
    xbv = XBCS.rearrange("(t p) n -> p t n", p=128)
    ypv = YP.rearrange("(t p) f -> p t f", p=128)
    zsv = ZS.rearrange("(t p) f -> p t f", p=128)

    from contextlib import ExitStack
    with ExitStack() as ctx:
        kb = KB(nc)
        kb.init_sems(ctx)
        E = ctx.enter_context
        ones_col = Buf(E(nc.sbuf_tensor("onesc", [128, 1], BF16)))
        ones_row = Buf(E(nc.sbuf_tensor("onesr", [1, 128], F32)))
        tri = Buf(E(nc.sbuf_tensor("tri", [128, 128], F32)))
        eyeb = Buf(E(nc.sbuf_tensor("eyeb", [128, 128], BF16)))
        eyef = Buf(E(nc.sbuf_tensor("eyef", [128, 128], F32)))
        for b, t_ in ((ones_col, ONESC), (ones_row, ONESR), (tri, TRI),
                      (eyeb, EYE), (eyef, EYEF)):
            kb.dma("y", lambda e, b=b, t_=t_: e.dma_start(out=b.ap[:], in_=t_), writes=[b])
        dtbc = Buf(E(nc.sbuf_tensor("dtbc", [64, 1], F32)))
        alogc = Buf(E(nc.sbuf_tensor("alogc", [64, 1], F32)))
        kb.dma("y", lambda e: e.dma_start(out=dtbc.ap[:], in_=DTB), writes=[dtbc])
        kb.dma("y", lambda e: e.dma_start(out=alogc.ap[:], in_=ALOG), writes=[alogc])

        dtv = Buf(E(nc.sbuf_tensor("dtv", [64, TE], F32)))
        acum = Buf(E(nc.sbuf_tensor("acum", [64, TE], F32)))
        one64 = kb.const_col(ctx, 1.0, 64)
        zeros_chunk = Buf(E(nc.sbuf_tensor("zchk", [64, CH], F32)))
        kb.v(lambda e: e.memset(zeros_chunk.ap[:], 0.0), writes=[zeros_chunk])
        scr_xbc = Buf(None)
        out_zs = Buf(None)
        out_cs = Buf(None)
        out_ac = Buf(None)
        out_yp = Buf(None)
        out_ue = Buf(None)
        out_at = Buf(None)

        with ExitStack() as sub:
            SE = sub.enter_context
            r2c = Buf(SE(nc.sbuf_tensor("r2c", [128, 16 * 512], F32)))
            xlnT = Buf(SE(nc.sbuf_tensor("xlnT", [128, 16 * TE], BF16)))
            lnc = _ln_cache(kb, nc, SE, "lnC", 512)
            c0 = 0
            while c0 < TE:
                w = min(512, TE - c0)
                kb.dma("y", lambda e, c0=c0, w=w: e.dma_start(
                    out=r2c.ap[:, 0:16 * w].rearrange("p (t n) -> p t n", n=w),
                    in_=r2v[:, :, c0:c0 + w]), writes=[r2c])
                xts = [r2c.view(r2c.ap[:, i * w:(i + 1) * w]) for i in range(16)]
                ots = [xlnT.view(xlnT.ap[:, i * TE + c0:i * TE + c0 + w]) for i in range(16)]
                _ln_chunk(kb, nc, SE, xts, ots, ones_col, ones_row, 1e-5,
                          "", kb_cache=lnc)
                kb.merge_views(r2c, xts)
                kb.merge_views(xlnT, ots)
                c0 += w
            winb = [Buf(SE(nc.sbuf_tensor(f"winbC{i}", [128, 16 * 256], BF16))) for i in range(2)]
            ps = Buf(SE(nc.psum_tensor("psC2", [128, 512], F32)))
            evb = Buf(SE(nc.sbuf_tensor("evC2", [128, 512], BF16)))
            for zc in range(16):
                wbf = winb[zc % 2]
                kb.dma("p", lambda e, zc=zc, wbf=wbf: e.dma_start(
                    out=wbf.ap[:].rearrange("p (t f) -> p t f", f=256),
                    in_=wintv[:, :, zc * 256:(zc + 1) * 256]), writes=[wbf])
                for tt in range(T_ // 128):
                    for i in range(16):
                        st, sp = (i == 0), (i == 15)
                        kb.accum_matmul(ps, xlnT, wbf,
                            lambda e, i=i, tt=tt, wbf=wbf, st=st, sp=sp: e.matmul(
                                ps.ap[:, 0:256], xlnT.ap[:, i * TE + EXT + tt * 128:i * TE + EXT + (tt + 1) * 128],
                                wbf.ap[:, i * 256:(i + 1) * 256], start=st, stop=sp), st, sp)
                    kb.v(lambda e: e.tensor_copy(out=evb.ap[:, 0:256], in_=ps.ap[:, 0:256]),
                         reads=[ps], writes=[evb])
                    kb.dma("y", lambda e, tt=tt, zc=zc: e.dma_start(
                        out=zsv[:, tt, zc * 256:(zc + 1) * 256], in_=evb.ap[:, 0:256]),
                        reads=[evb], writes=[out_zs])
            winf = [Buf(SE(nc.sbuf_tensor(f"winfC{i}", [128, 16 * 128], BF16))) for i in range(2)]
            evf = Buf(SE(nc.sbuf_tensor("evfC", [128, TE], BF16)))
            for fc in range(33):
                wbf = winf[fc % 2]
                kb.dma("p", lambda e, fc=fc, wbf=wbf: e.dma_start(
                    out=wbf.ap[:].rearrange("p (t f) -> p t f", f=128),
                    in_=wintv[:, :, 4096 + fc * 128:4096 + (fc + 1) * 128]), writes=[wbf])
                c0 = 0
                while c0 < TE:
                    w = min(512, TE - c0)
                    for i in range(16):
                        st, sp = (i == 0), (i == 15)
                        kb.accum_matmul(ps, wbf, xlnT,
                            lambda e, i=i, c0=c0, w=w, wbf=wbf, st=st, sp=sp: e.matmul(
                                ps.ap[:, 0:w], wbf.ap[:, i * 128:(i + 1) * 128],
                                xlnT.ap[:, i * TE + c0:i * TE + c0 + w], start=st, stop=sp), st, sp)
                    kb.v(lambda e, c0=c0, w=w: e.tensor_copy(out=evf.ap[:, c0:c0 + w],
                                                             in_=ps.ap[:, 0:w]),
                         reads=[ps], writes=[evf])
                    c0 += w
                kb.dma("y", lambda e, fc=fc: e.dma_start(
                    out=xbv[:, fc, :], in_=evf.ap[:]), reads=[evf], writes=[scr_xbc])
            wdt = Buf(SE(nc.sbuf_tensor("wdtC", [128, 16 * 64], BF16)))
            kb.dma("p", lambda e: e.dma_start(
                out=wdt.ap[:].rearrange("p (t f) -> p t f", f=64),
                in_=wintv[:, :, 8320:8384]), writes=[wdt])
            ps64 = Buf(SE(nc.psum_tensor("ps64C", [64, 512], F32)))
            c0 = 0
            while c0 < TE:
                w = min(512, TE - c0)
                for i in range(16):
                    st, sp = (i == 0), (i == 15)
                    kb.accum_matmul(ps64, wdt, xlnT,
                        lambda e, i=i, c0=c0, w=w, st=st, sp=sp: e.matmul(
                            ps64.ap[:, 0:w], wdt.ap[:, i * 64:(i + 1) * 64],
                            xlnT.ap[:, i * TE + c0:i * TE + c0 + w], start=st, stop=sp), st, sp)
                kb.v(lambda e, c0=c0, w=w: e.tensor_copy(out=dtv.ap[:, c0:c0 + w],
                                                         in_=ps64.ap[:, 0:w]),
                     reads=[ps64], writes=[dtv])
                c0 += w
        kb.phase_barrier()

        with ExitStack() as sub:
            SE = sub.enter_context
            tmpa = Buf(SE(nc.sbuf_tensor("tmpaC", [64, TE], F32)))
            kb.v(lambda e: e.tensor_scalar(out=dtv.ap[:], in0=dtv.ap[:],
                                           scalar1=dtbc.ap[:, 0:1], scalar2=None,
                                           op0=ALU.add), reads=[dtbc, dtv], writes=[dtv])
            kb.s(lambda e: e.activation(out=tmpa.ap[:], in_=dtv.ap[:], func=AF.Abs),
                 reads=[dtv], writes=[tmpa])
            kb.s(lambda e: e.activation(out=tmpa.ap[:], in_=tmpa.ap[:], func=AF.Exp,
                                        scale=-1.0), reads=[tmpa], writes=[tmpa])
            kb.s(lambda e: e.activation(out=tmpa.ap[:], in_=tmpa.ap[:], func=AF.Ln,
                                        bias=one64.ap[:, 0:1]),
                 reads=[tmpa, one64], writes=[tmpa])
            kb.v(lambda e: e.tensor_scalar(out=dtv.ap[:], in0=dtv.ap[:], scalar1=0.0,
                                           scalar2=None, op0=ALU.max),
                 reads=[dtv], writes=[dtv])
            kb.v(lambda e: e.tensor_add(out=dtv.ap[:], in0=dtv.ap[:], in1=tmpa.ap[:]),
                 reads=[tmpa, dtv], writes=[dtv])
            acol = Buf(SE(nc.sbuf_tensor("acolC", [64, 1], F32)))
            kb.s(lambda e: e.activation(out=acol.ap[:], in_=alogc.ap[:], func=AF.Exp),
                 reads=[alogc], writes=[acol])
            kb.v(lambda e: e.tensor_scalar(out=acol.ap[:], in0=acol.ap[:], scalar1=-1.0,
                                           scalar2=None, op0=ALU.mult),
                 reads=[acol], writes=[acol])
            kb.v(lambda e: e.tensor_scalar(out=tmpa.ap[:], in0=dtv.ap[:],
                                           scalar1=acol.ap[:, 0:1], scalar2=None,
                                           op0=ALU.mult), reads=[dtv, acol], writes=[tmpa])
            for chk in range(NCHT):
                kb.v(lambda e, chk=chk: e.tensor_tensor_scan(
                    out=acum.ap[:, chk * CH:(chk + 1) * CH],
                    data0=tmpa.ap[:, chk * CH:(chk + 1) * CH],
                    data1=zeros_chunk.ap[:], initial=0.0,
                    op0=ALU.add, op1=ALU.add),
                    reads=[tmpa, zeros_chunk], writes=[acum])
            kb.dma("y", lambda e: e.dma_start(out=AC, in_=acum.ap[:, EXT:]),
                   reads=[acum], writes=[out_ac])
            ach = Buf(SE(nc.sbuf_tensor("achC", [64, T_ // 128], F32)))
            atotb = Buf(SE(nc.sbuf_tensor("atotC", [64, 1], F32)))
            kb.v(lambda e: e.tensor_copy(out=ach.ap[:],
                                         in_=acum.ap[:, EXT + CH - 1::CH]),
                 reads=[acum], writes=[ach])
            kb.v(lambda e: e.tensor_reduce(out=atotb.ap[:], in_=ach.ap[:],
                                           axis=AX.X, op=ALU.add),
                 reads=[ach], writes=[atotb])
            kb.dma("y", lambda e: e.dma_start(out=ATOT, in_=atotb.ap[:]),
                   reads=[atotb], writes=[out_at])
        kb.phase_barrier()

        ACRS = nc.dram_tensor("acrs", [64, CH], F32, kind="Internal").ap()
        with ExitStack() as sub:
            SE = sub.enter_context
            drow = Buf(SE(nc.sbuf_tensor("drow", [1, 4096], F32)))
            kb.dma("y", lambda e: e.dma_start(out=drow.ap[:], in_=DROW), writes=[drow])
            dbc = Buf(SE(nc.sbuf_tensor("dbc", [128, 4096], F32)))
            ps_bcD = Buf(SE(nc.psum_tensor("ps_bcC", [128, 512], F32)))
            for g in range(8):
                kb.t(lambda e, g=g: e.matmul(ps_bcD.ap[:], ones_row.ap[0:1, :],
                                             drow.ap[0:1, g * 512:(g + 1) * 512],
                                             start=True, stop=True),
                     reads=[drow, ones_row], writes=[ps_bcD])
                kb.v(lambda e, g=g: e.tensor_copy(out=dbc.ap[:, g * 512:(g + 1) * 512],
                                                  in_=ps_bcD.ap[:]), reads=[ps_bcD], writes=[dbc])
            convwb = Buf(SE(nc.sbuf_tensor("convwb", [128, 33 * 4], F32)))
            convbb = Buf(SE(nc.sbuf_tensor("convbb", [128, 33], F32)))
            kb.dma("y", lambda e: e.dma_start(
                out=convwb.ap[:].rearrange("p (t w) -> p t w", w=4),
                in_=CONVW.rearrange("(t p) w -> p t w", p=128),
                allow_slow_non_contiguous=True), writes=[convwb])
            kb.dma("y", lambda e: e.dma_start(
                out=convbb.ap[:].unsqueeze(2),
                in_=CONVB.rearrange("(t p) o -> p t o", p=128),
                allow_slow_non_contiguous=True), writes=[convbb])
            xbcw = Buf(SE(nc.sbuf_tensor("xbcw", [128, 17 * 516], BF16)))
            cvacc = Buf(SE(nc.sbuf_tensor("cvacc", [128, 512], F32)))
            convs = Buf(SE(nc.sbuf_tensor("convs", [128, 33 * 512], BF16)))
            xtokc = Buf(SE(nc.sbuf_tensor("xtokc", [128, 4096], BF16)))
            tmp512 = Buf(SE(nc.sbuf_tensor("tmp512C", [128, 512], F32)))
            ps_tr = Buf(SE(nc.psum_tensor("ps_trC", [128, 128], BF16)))
            ps_trf = Buf(SE(nc.psum_tensor("ps_trfC", [128, 128], F32)))
            ps_g = Buf(SE(nc.psum_tensor("ps_gC", [128, 128], F32)))
            ps_b2 = Buf(SE(nc.psum_tensor("ps_b2C", [128, 128], F32)))
            ps_y = Buf(SE(nc.psum_tensor("ps_yC", [128, 512], F32)))
            ps_st = Buf(SE(nc.psum_tensor("ps_stC", [64, 512], F32)))
            ps_bc2 = Buf(SE(nc.psum_tensor("ps_bc2C", [64, 512], F32)))
            dtT = Buf(SE(nc.sbuf_tensor("dtT", [128, 64], F32)))
            decS = Buf(SE(nc.sbuf_tensor("decS", [64, CH], F32)))
            decT = Buf(SE(nc.sbuf_tensor("decT", [128, 64], F32)))
            acumT = Buf(SE(nc.sbuf_tensor("acumT", [128, 64], F32)))
            acrow = Buf(SE(nc.sbuf_tensor("acrowC", [1, 8 * CH], F32)))
            exprow = Buf(SE(nc.sbuf_tensor("exprowC", [1, 8 * CH], BF16)))
            eachr = Buf(SE(nc.sbuf_tensor("eachrC", [1, 64], F32)))
            ones_rowb = Buf(SE(nc.sbuf_tensor("onesrbC", [1, 128], BF16)))
            kb.v(lambda e: e.memset(ones_rowb.ap[:], 1.0), writes=[ones_rowb])
            xdth = Buf(SE(nc.sbuf_tensor("xdth", [128, 64], BF16)))
            xdtd = Buf(SE(nc.sbuf_tensor("xdtd", [128, 64], BF16)))
            btok = Buf(SE(nc.sbuf_tensor("btok", [128, 64], BF16)))
            segS = Buf(SE(nc.sbuf_tensor("segS", [128, CH], F32)))
            wmat = Buf(SE(nc.sbuf_tensor("wmat", [128, CH], BF16)))
            cd = Buf(SE(nc.sbuf_tensor("cdC", [64, CH], BF16)))
            rtile = Buf(SE(nc.sbuf_tensor("rtile", [64, 4096], F32)))
            rbf = Buf(SE(nc.sbuf_tensor("rbfC", [64, 4096], BF16)))
            stt_s = Buf(SE(nc.sbuf_tensor("sttC", [64, 4096], F32)))
            ybuf = Buf(SE(nc.sbuf_tensor("ybuf", [128, 4096], F32)))
            cloc = Buf(SE(nc.sbuf_tensor("clocC", [64, 512], BF16)))
            acr_d = Buf(None)
            kb.v(lambda e: e.memset(rtile.ap[:], 0.0), writes=[rtile])

            for cc in range(T_ // 512):
                t0 = EXT + cc * 512
                for grp, (f0, nf) in enumerate(((0, 17), (17, 16))):
                    kb.dma("y", lambda e, t0=t0, f0=f0, nf=nf: e.dma_start(
                        out=xbcw.ap[:, 0:nf * 516].rearrange("p (t n) -> p t n", n=516),
                        in_=xbv[:, f0:f0 + nf, t0 - 4:t0 + 512]),
                        writes=[xbcw], extra=list(scr_xbc.w))
                    for lf in range(nf):
                        ft = f0 + lf
                        kb.v(lambda e, lf=lf, ft=ft: e.tensor_scalar(
                            out=cvacc.ap[:],
                            in0=xbcw.ap[:, lf * 516 + 1:lf * 516 + 513],
                            scalar1=convwb.ap[:, 4 * ft:4 * ft + 1],
                            scalar2=convbb.ap[:, ft:ft + 1],
                            op0=ALU.mult, op1=ALU.add),
                            reads=[xbcw, convwb, convbb], writes=[cvacc])
                        for w_ in range(1, 4):
                            kb.v(lambda e, lf=lf, ft=ft, w_=w_: e.scalar_tensor_tensor(
                                out=cvacc.ap[:],
                                in0=xbcw.ap[:, lf * 516 + 1 + w_:lf * 516 + 513 + w_],
                                scalar=convwb.ap[:, 4 * ft + w_:4 * ft + w_ + 1],
                                in1=cvacc.ap[:],
                                op0=ALU.mult, op1=ALU.add),
                                reads=[xbcw, convwb, cvacc], writes=[cvacc])
                        kb.s(lambda e, ft=ft: e.activation(
                            out=convs.ap[:, ft * 512:(ft + 1) * 512], in_=cvacc.ap[:],
                            func=AF.Silu), reads=[cvacc], writes=[convs])
                kb.dma("y", lambda e, cc=cc: e.dma_start(
                    out=CS[:, cc * 512:(cc + 1) * 512],
                    in_=convs.ap[64:128, 32 * 512:33 * 512]),
                    reads=[convs], writes=[out_cs])
                kb.v(lambda e: e.tensor_copy(out=cloc.ap[:],
                                             in_=convs.ap[64:128, 32 * 512:33 * 512]),
                     reads=[convs], writes=[cloc])

                for lt in range(4):
                    chg = cc * 4 + lt
                    s0 = lt * 128
                    a0 = EXT + chg * CH
                    acv = acum.view(acum.ap[:, a0:a0 + CH])
                    dtvv = dtv.view(dtv.ap[:, a0:a0 + CH])
                    bmat = convs.view(convs.ap[0:64, 32 * 512 + s0:32 * 512 + s0 + 128])
                    cmat = cloc.view(cloc.ap[:, s0:s0 + 128])
                    for ft in range(32):
                        kb.t(lambda e, ft=ft, s0=s0: e.transpose(
                            ps_tr.ap[:], convs.ap[:, ft * 512 + s0:ft * 512 + s0 + 128],
                            eyeb.ap[:]), reads=[convs, eyeb], writes=[ps_tr])
                        kb.v(lambda e, ft=ft: e.tensor_copy(
                            out=xtokc.ap[:, ft * 128:(ft + 1) * 128],
                            in_=ps_tr.ap[:]), reads=[ps_tr], writes=[xtokc])
                    xt = xtokc
                    kb.v(lambda e: e.tensor_copy(out=rbf.ap[:], in_=rtile.ap[:]),
                         reads=[rtile], writes=[rbf])
                    kb.dma("y", lambda e: e.dma_start(out=ACRS, in_=acv.ap[:]),
                           reads=[acv], writes=[acr_d])
                    kb.dma("y", lambda e: e.dma_start(
                        out=eachr.ap[:],
                        in_=ACRS.rearrange("h l -> (h l)").unsqueeze(0)[:, CH - 1::CH],
                        allow_slow_non_contiguous=True),
                        reads=[acr_d], writes=[eachr], extra=list(acr_d.w))
                    kb.s(lambda e: e.activation(out=eachr.ap[:], in_=eachr.ap[:],
                                                func=AF.Exp), reads=[eachr], writes=[eachr])
                    kb.t(lambda e: e.transpose(ps_trf.ap[:, 0:64], dtvv.ap[:], eyef.ap[0:64, 0:64]),
                         reads=[dtvv, eyef], writes=[ps_trf])
                    kb.v(lambda e: e.tensor_copy(out=dtT.ap[:], in_=ps_trf.ap[:, 0:64]),
                         reads=[ps_trf], writes=[dtT])
                    kb.t(lambda e: e.transpose(ps_trf.ap[:, 0:64], acv.ap[:], eyef.ap[0:64, 0:64]),
                         reads=[acv, eyef], writes=[ps_trf])
                    kb.v(lambda e: e.tensor_copy(out=acumT.ap[:], in_=ps_trf.ap[:, 0:64]),
                         reads=[ps_trf], writes=[acumT])
                    kb.v(lambda e: e.tensor_scalar(
                        out=decS.ap[:], in0=acv.ap[:], scalar1=acv.ap[:, CH - 1:CH],
                        scalar2=None, op0=ALU.subtract), reads=[acv], writes=[decS])
                    kb.s(lambda e: e.activation(out=decS.ap[:], in_=decS.ap[:],
                                                func=AF.Exp, scale=-1.0),
                         reads=[decS], writes=[decS])
                    kb.t(lambda e: e.transpose(ps_trf.ap[:, 0:64], decS.ap[:], eyef.ap[0:64, 0:64]),
                         reads=[decS, eyef], writes=[ps_trf])
                    kb.v(lambda e: e.tensor_copy(out=decT.ap[:], in_=ps_trf.ap[:, 0:64]),
                         reads=[ps_trf], writes=[decT])
                    kb.t(lambda e: e.transpose(ps_tr.ap[:, 0:64], bmat.ap[:], eyeb.ap[0:64, 0:64]),
                         reads=[bmat, eyeb], writes=[ps_tr])
                    kb.v(lambda e: e.tensor_copy(out=btok.ap[:], in_=ps_tr.ap[:, 0:64]),
                         reads=[ps_tr], writes=[btok])
                    kb.t(lambda e: e.matmul(ps_g.ap[:], bmat.ap[:], cmat.ap[:],
                                            start=True, stop=True),
                         reads=[bmat, cmat], writes=[ps_g])
                    for h in range(64):
                        hg = h % 8
                        if hg == 0:
                            kb.dma("y", lambda e, h=h: e.dma_start(
                                out=acrow.ap[:],
                                in_=ACRS.rearrange("h l -> (h l)").unsqueeze(0)[:, h * CH:(h + 8) * CH]),
                                reads=[acr_d], writes=[acrow], extra=list(acr_d.w))
                            kb.s(lambda e: e.activation(out=exprow.ap[:], in_=acrow.ap[:],
                                                        func=AF.Exp),
                                 reads=[acrow], writes=[exprow])
                        kb.t(lambda e, hg=hg: e.matmul(
                            ps_b2.ap[:], ones_row.ap[0:1, :],
                            acrow.ap[0:1, hg * CH:(hg + 1) * CH], start=True, stop=True),
                            reads=[acrow, ones_row], writes=[ps_b2])
                        kb.v(lambda e, h=h: e.tensor_scalar(
                            out=segS.ap[:], in0=ps_b2.ap[:], scalar1=acumT.ap[:, h:h + 1],
                            scalar2=None, op0=ALU.subtract),
                            reads=[ps_b2, acumT], writes=[segS])
                        kb.v(lambda e: e.tensor_add(out=segS.ap[:], in0=segS.ap[:],
                                                    in1=tri.ap[:]),
                             reads=[tri, segS], writes=[segS])
                        kb.s(lambda e: e.activation(out=segS.ap[:], in_=segS.ap[:],
                                                    func=AF.Exp), reads=[segS], writes=[segS])
                        kb.v(lambda e: e.tensor_tensor(out=wmat.ap[:], in0=segS.ap[:],
                                                       in1=ps_g.ap[:], op=ALU.mult),
                             reads=[segS, ps_g], writes=[wmat])
                        kb.v(lambda e, h=h: e.tensor_scalar(
                            out=xdth.ap[:], in0=xt.ap[:, h * 64:(h + 1) * 64],
                            scalar1=dtT.ap[:, h:h + 1], scalar2=None, op0=ALU.mult),
                            reads=[xt, dtT], writes=[xdth])
                        kb.v(lambda e, h=h: e.tensor_scalar(
                            out=xdtd.ap[:], in0=xdth.ap[:], scalar1=decT.ap[:, h:h + 1],
                            scalar2=None, op0=ALU.mult),
                            reads=[xdth, decT], writes=[xdtd])
                        kb.accum_matmul(ps_y, wmat, xdth,
                            lambda e, hg=hg: e.matmul(
                                ps_y.ap[:, hg * 64:(hg + 1) * 64], wmat.ap[:], xdth.ap[:],
                                start=True, stop=False), True, False)
                        kb.t(lambda e, hg=hg: e.matmul(
                            ps_bc2.ap[:, 0:CH], ones_rowb.ap[0:1, 0:64],
                            exprow.ap[0:1, hg * CH:(hg + 1) * CH],
                            start=True, stop=True),
                            reads=[exprow, ones_rowb], writes=[ps_bc2])
                        kb.v(lambda e: e.tensor_tensor(
                            out=cd.ap[:], in0=cmat.ap[:], in1=ps_bc2.ap[:, 0:CH],
                            op=ALU.mult),
                            reads=[cmat, ps_bc2], writes=[cd])
                        kb.accum_matmul(ps_y, cd, rbf,
                            lambda e, h=h, hg=hg: e.matmul(
                                ps_y.ap[:, hg * 64:(hg + 1) * 64], cd.ap[:],
                                rbf.ap[:, h * 64:(h + 1) * 64],
                                start=False, stop=True), False, True)
                        kb.t(lambda e, h=h, hg=hg: e.matmul(
                            ps_st.ap[:, hg * 64:(hg + 1) * 64], btok.ap[:], xdtd.ap[:],
                            start=True, stop=True),
                            reads=[btok, xdtd],
                            writes=[ps_st] if hg == 7 else [],
                            extra=(list(ps_st.w) + list(ps_st.r)) if hg != 7 else ())
                        if hg == 7:
                            g = h // 8
                            kb.v(lambda e, g=g: e.tensor_tensor(
                                out=tmp512.ap[:], in0=xt.ap[:, g * 512:(g + 1) * 512],
                                in1=dbc.ap[:, g * 512:(g + 1) * 512], op=ALU.mult),
                                reads=[xt, dbc], writes=[tmp512])
                            kb.v(lambda e, g=g: e.tensor_add(
                                out=ybuf.ap[:, g * 512:(g + 1) * 512],
                                in0=tmp512.ap[:], in1=ps_y.ap[:]),
                                reads=[tmp512, ps_y], writes=[ybuf])
                            kb.v(lambda e, g=g: e.tensor_copy(
                                out=stt_s.ap[:, g * 512:(g + 1) * 512], in_=ps_st.ap[:]),
                                reads=[ps_st], writes=[stt_s])
                    kb.dma("y", lambda e, chg=chg: e.dma_start(
                        out=ypv[:, chg, :], in_=ybuf.ap[:]), reads=[ybuf], writes=[out_yp])
                    for g in range(8):
                        rhs_bc = eachr.ap[0:1, g * 8:(g + 1) * 8].unsqueeze(2).broadcast_to([1, 8, 64])
                        kb.t(lambda e, g=g, rhs_bc=rhs_bc: e.matmul(
                            ps_bc2.ap[:], ones_row.ap[0:1, 0:64], rhs_bc,
                            start=True, stop=True),
                            reads=[eachr, ones_row], writes=[ps_bc2])
                        kb.v(lambda e, g=g: e.scalar_tensor_tensor(
                            out=rtile.ap[:, g * 512:(g + 1) * 512],
                            in0=rtile.ap[:, g * 512:(g + 1) * 512], scalar=1.0,
                            in1=ps_bc2.ap[:], op0=ALU.mult, op1=ALU.mult),
                            reads=[ps_bc2, rtile], writes=[rtile])
                        kb.v(lambda e, g=g: e.tensor_add(
                            out=rtile.ap[:, g * 512:(g + 1) * 512],
                            in0=rtile.ap[:, g * 512:(g + 1) * 512],
                            in1=stt_s.ap[:, g * 512:(g + 1) * 512]),
                            reads=[stt_s, rtile], writes=[rtile])
            kb.dma("y", lambda e: e.dma_start(out=UEND, in_=rtile.ap[:]),
                   reads=[rtile], writes=[out_ue])

        kb.final_wait("y")
    return nc



def build_launch_D(T_=T):
    NCH_ = T_ // CH
    nc = bass.Bass("TRN2", target_bir_lowering=False)
    YP = nc.dram_tensor("YP", [T_, 4096], F32, kind="ExternalInput").ap()
    ZS = nc.dram_tensor("ZS", [T_, 4096], BF16, kind="ExternalInput").ap()
    CS = nc.dram_tensor("CS", [64, T_], BF16, kind="ExternalInput").ap()
    AC = nc.dram_tensor("AC", [64, T_], F32, kind="ExternalInput").ap()
    UENDS = nc.dram_tensor("UENDS", [3, 64, 4096], F32, kind="ExternalInput").ap()
    ATOTS = nc.dram_tensor("ATOTS", [64, 3], F32, kind="ExternalInput").ap()
    R2 = nc.dram_tensor("R2", [D, T_], F32, kind="ExternalInput").ap()
    WOUTT = nc.dram_tensor("WOUTT", [4096, D], F32, kind="ExternalInput").ap()
    NORMW = nc.dram_tensor("NORMW", [1, 4096], F32, kind="ExternalInput").ap()
    EYEB = nc.dram_tensor("EYE", [128, 128], BF16, kind="ExternalInput").ap()
    ONESC = nc.dram_tensor("ONESC", [128, 1], BF16, kind="ExternalInput").ap()
    ONESR = nc.dram_tensor("ONESR", [1, 128], F32, kind="ExternalInput").ap()
    R3 = nc.dram_tensor("R3", [D, T_], F32, kind="ExternalOutput").ap()
    ACRS = nc.dram_tensor("acrsD", [64, CH], F32, kind="Internal").ap()

    ypv = YP.rearrange("(t p) f -> p t f", p=128)
    zsv = ZS.rearrange("(t p) f -> p t f", p=128)
    r2v = R2.rearrange("(t p) n -> p t n", p=128)
    r3v = R3.rearrange("(t p) n -> p t n", p=128)
    woutv = WOUTT.rearrange("(t p) f -> p t f", p=128)

    from contextlib import ExitStack
    with ExitStack() as ctx:
        kb = KB(nc)
        kb.init_sems(ctx)
        E = ctx.enter_context
        ones_col = Buf(E(nc.sbuf_tensor("onesc", [128, 1], BF16)))
        ones_row = Buf(E(nc.sbuf_tensor("onesr", [1, 128], F32)))
        ones_rowb = Buf(E(nc.sbuf_tensor("onesrb", [1, 128], BF16)))
        eyeb = Buf(E(nc.sbuf_tensor("eyeb", [128, 128], BF16)))
        kb.dma("y", lambda e: e.dma_start(out=ones_col.ap[:], in_=ONESC), writes=[ones_col])
        kb.dma("y", lambda e: e.dma_start(out=ones_row.ap[:], in_=ONESR), writes=[ones_row])
        kb.dma("y", lambda e: e.dma_start(out=eyeb.ap[:], in_=EYEB), writes=[eyeb])
        kb.v(lambda e: e.memset(ones_rowb.ap[:], 1.0), writes=[ones_rowb])
        epsb = kb.const_col(ctx, 1e-5, 128)

        nwb = Buf(E(nc.sbuf_tensor("nwb", [128, 4096], F32)))
        inst = Buf(E(nc.sbuf_tensor("instD", [64, 4096], F32)))
        with ExitStack() as sub0:
            S0 = sub0.enter_context
            normw = Buf(S0(nc.sbuf_tensor("normw", [1, 4096], F32)))
            kb.dma("y", lambda e: e.dma_start(out=normw.ap[:], in_=NORMW), writes=[normw])
            ps_b = Buf(S0(nc.psum_tensor("ps_bD", [128, 512], F32)))
            for g in range(8):
                kb.t(lambda e, g=g: e.matmul(ps_b.ap[:], ones_row.ap[0:1, :],
                                             normw.ap[0:1, g * 512:(g + 1) * 512],
                                             start=True, stop=True),
                     reads=[normw, ones_row], writes=[ps_b])
                kb.v(lambda e, g=g: e.tensor_copy(out=nwb.ap[:, g * 512:(g + 1) * 512],
                                                  in_=ps_b.ap[:]), reads=[ps_b], writes=[nwb])
            uends = Buf(S0(nc.sbuf_tensor("uends", [64, 3 * 4096], F32)))
            kb.dma("y", lambda e: e.dma_start(
                out=uends.ap[:].rearrange("p (k f) -> p k f", f=4096), in_=UENDS.transpose([1, 0, 2])),
                writes=[uends])
            atots = Buf(S0(nc.sbuf_tensor("atots", [64, 3], F32)))
            kb.dma("y", lambda e: e.dma_start(out=atots.ap[:], in_=ATOTS), writes=[atots])
            ein = Buf(S0(nc.sbuf_tensor("einD", [64, 2], F32)))
            kb.s(lambda e: e.activation(out=ein.ap[:, 0:1], in_=atots.ap[:, 0:1], func=AF.Exp),
                 reads=[atots], writes=[ein])
            kb.v(lambda e: e.tensor_add(out=ein.ap[:, 1:2], in0=atots.ap[:, 0:1],
                                        in1=atots.ap[:, 1:2]), reads=[atots], writes=[ein])
            kb.s(lambda e: e.activation(out=ein.ap[:, 1:2], in_=ein.ap[:, 1:2], func=AF.Exp),
                 reads=[ein], writes=[ein])
            kb.v(lambda e: e.scalar_tensor_tensor(
                out=inst.ap[:], in0=uends.ap[:, 4096:2 * 4096], scalar=ein.ap[:, 0:1],
                in1=uends.ap[:, 0:4096], op0=ALU.mult, op1=ALU.add),
                reads=[uends, ein], writes=[inst])
            kb.v(lambda e: e.scalar_tensor_tensor(
                out=inst.ap[:], in0=uends.ap[:, 2 * 4096:3 * 4096], scalar=ein.ap[:, 1:2],
                in1=inst.ap[:], op0=ALU.mult, op1=ALU.add),
                reads=[uends, ein, inst], writes=[inst])
            kb.phase_barrier()
        achk = Buf(E(nc.sbuf_tensor("achkD", [64, NCH_], F32)))
        acs = Buf(E(nc.sbuf_tensor("acsD", [64, NCH_], F32)))
        acbuf = Buf(E(nc.sbuf_tensor("acbufD", [64, T_], F32)))
        kb.dma("y", lambda e: e.dma_start(out=acbuf.ap[:], in_=AC), writes=[acbuf])
        kb.v(lambda e: e.tensor_copy(out=achk.ap[:], in_=acbuf.ap[:, CH - 1::CH]),
             reads=[acbuf], writes=[achk])
        zch = Buf(E(nc.sbuf_tensor("zchD", [64, NCH_], F32)))
        kb.v(lambda e: e.memset(zch.ap[:], 0.0), writes=[zch])
        kb.v(lambda e: e.tensor_tensor_scan(out=acs.ap[:], data0=achk.ap[:],
                                            data1=zch.ap[:], initial=0.0,
                                            op0=ALU.add, op1=ALU.add),
             reads=[achk, zch], writes=[acs])
        eacs = Buf(E(nc.sbuf_tensor("eacsD", [64, NCH_], F32)))
        kb.v(lambda e: e.memset(eacs.ap[:, 0:1], 0.0), writes=[eacs])
        kb.v(lambda e: e.tensor_copy(out=eacs.ap[:, 1:NCH_], in_=acs.ap[:, 0:NCH_ - 1]),
             reads=[acs], writes=[eacs])
        kb.s(lambda e: e.activation(out=eacs.ap[:], in_=eacs.ap[:], func=AF.Exp),
             reads=[eacs], writes=[eacs])

        with ExitStack() as sub:
            SE = sub.enter_context
            csb = Buf(SE(nc.sbuf_tensor("csbD", [64, T_], BF16)))
            kb.dma("y", lambda e: e.dma_start(out=csb.ap[:], in_=CS), writes=[csb])
            pin = Buf(SE(nc.sbuf_tensor("pinD", [64, 4096], BF16)))
            cd = Buf(SE(nc.sbuf_tensor("cdD", [64, CH], BF16)))
            acrow = Buf(SE(nc.sbuf_tensor("acrowD", [1, 8 * CH], F32)))
            exprow = Buf(SE(nc.sbuf_tensor("exprowD", [1, 8 * CH], BF16)))
            ypc = Buf(SE(nc.sbuf_tensor("ypcD", [128, 4096], F32)))
            zc_ = Buf(SE(nc.sbuf_tensor("zcD", [128, 4096], BF16)))
            sz = Buf(SE(nc.sbuf_tensor("szD", [128, 4096], F32)))
            sq = sz
            rcol = Buf(SE(nc.sbuf_tensor("rcolD", [128, 1], F32)))
            hnb = Buf(SE(nc.sbuf_tensor("hnbD", [128, 4096], BF16)))
            hnT = Buf(SE(nc.sbuf_tensor("hnTD", [128, 32 * 512], BF16)))
            wout = [Buf(SE(nc.sbuf_tensor(f"woutD{i}", [128, 32 * 128], BF16))) for i in range(2)]
            outc = Buf(SE(nc.sbuf_tensor("outcD", [128, 16 * 512], F32)))
            ps_y = Buf(SE(nc.psum_tensor("ps_yD", [128, 512], F32)))
            ps_bc2 = Buf(SE(nc.psum_tensor("ps_bc2D", [64, 512], F32)))
            ps_tr = Buf(SE(nc.psum_tensor("ps_trD", [128, 128], BF16)))
            ps_o = Buf(SE(nc.psum_tensor("ps_oD", [128, 512], F32)))
            lnc = _ln_cache(kb, nc, SE, "lnD", 512)
            acr_d = Buf(None)
            r3buf = Buf(None)
            for quad in range(T_ // 512):
                for lt in range(4):
                    ch = quad * 4 + lt
                    kb.v(lambda e, ch=ch: e.tensor_scalar(
                        out=pin.ap[:], in0=inst.ap[:], scalar1=eacs.ap[:, ch:ch + 1],
                        scalar2=None, op0=ALU.mult),
                        reads=[inst, eacs], writes=[pin])
                    kb.dma("y", lambda e, ch=ch: e.dma_start(
                        out=ACRS, in_=acbuf.ap[:, ch * CH:(ch + 1) * CH]),
                        reads=[acbuf], writes=[acr_d])
                    kb.dma("y", lambda e, ch=ch: e.dma_start(
                        out=ypc.ap[:], in_=ypv[:, ch, :]), writes=[ypc])
                    for h in range(64):
                        hg = h % 8
                        if hg == 0:
                            kb.dma("y", lambda e, h=h: e.dma_start(
                                out=acrow.ap[:],
                                in_=ACRS.rearrange("h l -> (h l)").unsqueeze(0)[:, h * CH:(h + 8) * CH]),
                                reads=[acr_d], writes=[acrow], extra=list(acr_d.w))
                            kb.s(lambda e: e.activation(out=exprow.ap[:], in_=acrow.ap[:],
                                                        func=AF.Exp),
                                 reads=[acrow], writes=[exprow])
                        kb.t(lambda e, hg=hg: e.matmul(
                            ps_bc2.ap[:, 0:CH], ones_rowb.ap[0:1, 0:64],
                            exprow.ap[0:1, hg * CH:(hg + 1) * CH],
                            start=True, stop=True),
                            reads=[exprow, ones_rowb], writes=[ps_bc2])
                        kb.v(lambda e, ch=ch: e.tensor_tensor(
                            out=cd.ap[:], in0=csb.ap[:, ch * CH:(ch + 1) * CH],
                            in1=ps_bc2.ap[:, 0:CH], op=ALU.mult),
                            reads=[csb, ps_bc2], writes=[cd])
                        kb.t(lambda e, h=h, hg=hg: e.matmul(
                            ps_y.ap[:, hg * 64:(hg + 1) * 64], cd.ap[:],
                            pin.ap[:, h * 64:(h + 1) * 64],
                            start=True, stop=True),
                            reads=[cd, pin],
                            writes=[ps_y] if hg == 7 else [],
                            extra=(list(ps_y.w) + list(ps_y.r)) if hg != 7 else ())
                        if hg == 7:
                            g = h // 8
                            kb.v(lambda e, g=g: e.tensor_add(
                                out=ypc.ap[:, g * 512:(g + 1) * 512],
                                in0=ypc.ap[:, g * 512:(g + 1) * 512], in1=ps_y.ap[:]),
                                reads=[ps_y, ypc], writes=[ypc])
                    kb.dma("y", lambda e, ch=ch: e.dma_start(
                        out=zc_.ap[:], in_=zsv[:, ch, :]), writes=[zc_])
                    kb.s(lambda e: e.activation(out=sz.ap[:], in_=zc_.ap[:], func=AF.Silu),
                         reads=[zc_], writes=[sz])
                    kb.v(lambda e: e.tensor_mul(out=ypc.ap[:], in0=ypc.ap[:], in1=sz.ap[:]),
                         reads=[sz, ypc], writes=[ypc])
                    kb.s(lambda e: e.square(out=sq.ap[:], in_=ypc.ap[:]),
                         reads=[ypc], writes=[sq])
                    kb.v(lambda e: e.reduce_sum(out=rcol.ap[:], in_=sq.ap[:], axis=AX.X),
                         reads=[sq], writes=[rcol])
                    kb.v(lambda e: e.tensor_scalar(out=rcol.ap[:], in0=rcol.ap[:],
                                                   scalar1=1.0 / 4096, scalar2=epsb.ap[:, 0:1],
                                                   op0=ALU.mult, op1=ALU.add),
                         reads=[rcol, epsb], writes=[rcol])
                    kb.s(lambda e: e.sqrt(out=rcol.ap[:], in_=rcol.ap[:]),
                         reads=[rcol], writes=[rcol])
                    kb.v(lambda e: e.reciprocal(out=rcol.ap[:], in_=rcol.ap[:]),
                         reads=[rcol], writes=[rcol])
                    kb.v(lambda e: e.tensor_scalar(out=ypc.ap[:], in0=ypc.ap[:],
                                                   scalar1=rcol.ap[:, 0:1], scalar2=None,
                                                   op0=ALU.mult),
                         reads=[rcol, ypc], writes=[ypc])
                    kb.v(lambda e: e.tensor_tensor(out=hnb.ap[:], in0=ypc.ap[:],
                                                   in1=nwb.ap[:], op=ALU.mult),
                         reads=[nwb, ypc], writes=[hnb])
                    for ft in range(32):
                        kb.t(lambda e, ft=ft: e.transpose(
                            ps_tr.ap[:], hnb.ap[:, ft * 128:(ft + 1) * 128], eyeb.ap[:]),
                            reads=[hnb, eyeb], writes=[ps_tr])
                        kb.v(lambda e, ft=ft, lt=lt: e.tensor_copy(
                            out=hnT.ap[:, ft * 512 + lt * 128:ft * 512 + (lt + 1) * 128],
                            in_=ps_tr.ap[:]), reads=[ps_tr], writes=[hnT])
                for j in range(16):
                    wb_ = wout[j % 2]
                    kb.dma("p", lambda e, j=j, wb_=wb_: e.dma_start(
                        out=wb_.ap[:].rearrange("p (t f) -> p t f", f=128),
                        in_=woutv[:, :, j * 128:(j + 1) * 128]), writes=[wb_])
                    for i in range(32):
                        st, sp = (i == 0), (i == 31)
                        kb.accum_matmul(ps_o, wb_, hnT,
                            lambda e, i=i, wb_=wb_, st=st, sp=sp: e.matmul(
                                ps_o.ap[:], wb_.ap[:, i * 128:(i + 1) * 128],
                                hnT.ap[:, i * 512:(i + 1) * 512], start=st, stop=sp), st, sp)
                    kb.v(lambda e, j=j: e.tensor_copy(out=outc.ap[:, j * 512:(j + 1) * 512],
                                                      in_=ps_o.ap[:]),
                         reads=[ps_o], writes=[outc])
                otiles = [outc.view(outc.ap[:, i * 512:(i + 1) * 512]) for i in range(16)]
                _ln_chunk(kb, nc, SE, otiles, otiles, ones_col, ones_row, 1e-5,
                          "", kb_cache=lnc)
                kb.merge_views(outc, otiles)
                kb.dma("y", lambda e, quad=quad: e.dma_start(
                    out=r3v[:, :, quad * 512:(quad + 1) * 512],
                    in_=outc.ap[:].rearrange("p (t n) -> p t n", n=512)),
                    reads=[outc], writes=[r3buf])
                kb.dma("p", lambda e, quad=quad: e.dma_start(
                    out=r3v[:, :, quad * 512:(quad + 1) * 512],
                    in_=r2v[:, :, quad * 512:(quad + 1) * 512],
                    accum_op=ALU.add), reads=[], writes=[r3buf])
        kb.final_wait("y")
    return nc



def ln_free(kb, E_, nc, _a, _b, w, eps, name, epsb):
    mu = Buf(E_(nc.sbuf_tensor(name + "_mu", [128, 1], F32)))
    var = Buf(E_(nc.sbuf_tensor(name + "_var", [128, 1], F32)))
    var4 = Buf(E_(nc.sbuf_tensor(name + "_var4", [128, w // 512], F32)))
    sq = Buf(E_(nc.sbuf_tensor(name + "_sq", [128, 512], F32)))

    def run(xv, ov):
        kb.v(lambda e: e.reduce_sum(out=mu.ap[:], in_=xv.ap[:], axis=AX.X),
             reads=[xv], writes=[mu])
        kb.v(lambda e: e.tensor_scalar(out=mu.ap[:], in0=mu.ap[:], scalar1=1.0 / w,
                                       scalar2=None, op0=ALU.mult), reads=[mu], writes=[mu])
        for q in range(w // 512):
            kb.s(lambda e, q=q: e.square(out=sq.ap[:], in_=xv.ap[:, q * 512:(q + 1) * 512]),
                 reads=[xv], writes=[sq])
            kb.v(lambda e, q=q: e.reduce_sum(out=var4.ap[:, q:q + 1], in_=sq.ap[:], axis=AX.X),
                 reads=[sq], writes=[var4])
        kb.v(lambda e: e.reduce_sum(out=var.ap[:], in_=var4.ap[:], axis=AX.X),
             reads=[var4], writes=[var])
        kb.v(lambda e: e.tensor_scalar(out=var.ap[:], in0=var.ap[:], scalar1=1.0 / w,
                                       scalar2=epsb.ap[:, 0:1], op0=ALU.mult, op1=ALU.add),
             reads=[var, epsb], writes=[var])
        kb.v(lambda e: e.scalar_tensor_tensor(out=var.ap[:], in0=mu.ap[:],
                                              scalar=mu.ap[:, 0:1], in1=var.ap[:],
                                              op0=ALU.mult, op1=ALU.subtract_rev
                                              if hasattr(ALU, "subtract_rev") else ALU.subtract),
             reads=[mu, var], writes=[var]) if False else None
        kb.v(lambda e: e.tensor_mul(out=sq.ap[:, 0:1], in0=mu.ap[:], in1=mu.ap[:]),
             reads=[mu], writes=[sq])
        kb.v(lambda e: e.tensor_tensor(out=var.ap[:], in0=var.ap[:], in1=sq.ap[:, 0:1],
                                       op=ALU.subtract), reads=[sq, var], writes=[var])
        kb.s(lambda e: e.sqrt(out=var.ap[:], in_=var.ap[:]), reads=[var], writes=[var])
        kb.v(lambda e: e.reciprocal(out=var.ap[:], in_=var.ap[:]), reads=[var], writes=[var])
        kb.v(lambda e: e.tensor_scalar(out=ov.ap[:], in0=xv.ap[:], scalar1=mu.ap[:, 0:1],
                                       scalar2=var.ap[:, 0:1], op0=ALU.subtract,
                                       op1=ALU.mult),
             reads=[xv, mu, var], writes=[ov])
    return run


def build_launch_E(T_=T):
    nc = bass.Bass("TRN2", target_bir_lowering=False)
    R3 = nc.dram_tensor("R3", [D, T_], F32, kind="ExternalInput").ap()
    GWT = nc.dram_tensor("GWT", [D, 8], F32, kind="ExternalInput").ap()
    EFC1T = nc.dram_tensor("EFC1T", [8, D, 1408], F32, kind="ExternalInput").ap()
    EFC2T = nc.dram_tensor("EFC2T", [8, D, 1408], F32, kind="ExternalInput").ap()
    EPROJT = nc.dram_tensor("EPROJT", [8, 1408, D], F32, kind="ExternalInput").ap()
    SFC1T = nc.dram_tensor("SFC1T", [D, 5632], F32, kind="ExternalInput").ap()
    SFC2T = nc.dram_tensor("SFC2T", [D, 5632], F32, kind="ExternalInput").ap()
    SPROJT = nc.dram_tensor("SPROJT", [5632, D], F32, kind="ExternalInput").ap()
    SGWT = nc.dram_tensor("SGWT", [D, 1], F32, kind="ExternalInput").ap()
    EYEF = nc.dram_tensor("EYEF", [128, 128], F32, kind="ExternalInput").ap()
    ONESC = nc.dram_tensor("ONESC", [128, 1], BF16, kind="ExternalInput").ap()
    ONESR = nc.dram_tensor("ONESR", [1, 128], F32, kind="ExternalInput").ap()
    XOUT = nc.dram_tensor("XOUT", [T_, D], F32, kind="ExternalOutput").ap()
    BFC1 = nc.dram_tensor("bfc1", [8, D, 1408], BF16, kind="Internal").ap()
    BFC2 = nc.dram_tensor("bfc2", [8, D, 1408], BF16, kind="Internal").ap()
    BPRJ = nc.dram_tensor("bprj", [8, 1408, D], BF16, kind="Internal").ap()
    BSF1 = nc.dram_tensor("bsf1", [D, 5632], BF16, kind="Internal").ap()
    BSF2 = nc.dram_tensor("bsf2", [D, 5632], BF16, kind="Internal").ap()
    BSPJ = nc.dram_tensor("bspj", [5632, D], BF16, kind="Internal").ap()

    r3v = R3.rearrange("(t p) n -> p t n", p=128)
    xoutv = XOUT.rearrange("(t p) f -> p t f", p=128)

    from contextlib import ExitStack
    with ExitStack() as ctx:
        kb = KB(nc)
        kb.init_sems(ctx)
        E = ctx.enter_context
        ones_col = Buf(E(nc.sbuf_tensor("onesc", [128, 1], BF16)))
        ones_row = Buf(E(nc.sbuf_tensor("onesr", [1, 128], F32)))
        eyef = Buf(E(nc.sbuf_tensor("eyef", [128, 128], F32)))
        kb.dma("y", lambda e: e.dma_start(out=ones_col.ap[:], in_=ONESC), writes=[ones_col])
        kb.dma("y", lambda e: e.dma_start(out=ones_row.ap[:], in_=ONESR), writes=[ones_row])
        kb.dma("y", lambda e: e.dma_start(out=eyef.ap[:], in_=EYEF), writes=[eyef])
        epsb = kb.const_col(ctx, 1e-5, 128)
        wcast = Buf(None)
        for e_ in range(8):
            kb.dma("p", lambda e, e_=e_: e.dma_start(out=BFC1[e_], in_=EFC1T[e_]), writes=[wcast])
            kb.dma("p", lambda e, e_=e_: e.dma_start(out=BFC2[e_], in_=EFC2T[e_]), writes=[wcast])
            kb.dma("p", lambda e, e_=e_: e.dma_start(out=BPRJ[e_], in_=EPROJT[e_]), writes=[wcast])
        kb.dma("p", lambda e: e.dma_start(out=BSF1, in_=SFC1T), writes=[wcast])
        kb.dma("p", lambda e: e.dma_start(out=BSF2, in_=SFC2T), writes=[wcast])
        kb.dma("p", lambda e: e.dma_start(out=BSPJ, in_=SPROJT), writes=[wcast])
        gwb = Buf(E(nc.sbuf_tensor("gwb", [128, 16 * 8], BF16)))
        kb.dma("p", lambda e: e.dma_start(
            out=gwb.ap[:].rearrange("p (t f) -> p t f", f=8),
            in_=GWT.rearrange("(t p) f -> p t f", p=128)), writes=[gwb])
        sgwb = Buf(E(nc.sbuf_tensor("sgwb", [128, 16], BF16)))
        kb.dma("p", lambda e: e.dma_start(
            out=sgwb.ap[:].unsqueeze(2),
            in_=SGWT.rearrange("(t p) f -> p t f", p=128),
            allow_slow_non_contiguous=True), writes=[sgwb])

        bf1v = [BFC1[e_].rearrange("(t p) f -> p t f", p=128) for e_ in range(8)]
        bf2v = [BFC2[e_].rearrange("(t p) f -> p t f", p=128) for e_ in range(8)]
        bpv = [BPRJ[e_].rearrange("(t p) f -> p t f", p=128) for e_ in range(8)]
        bs1v = BSF1.rearrange("(t p) f -> p t f", p=128)
        bs2v = BSF2.rearrange("(t p) f -> p t f", p=128)
        bspv = BSPJ.rearrange("(t p) f -> p t f", p=128)

        with ExitStack() as sub:
            SE = sub.enter_context
            r3c = Buf(SE(nc.sbuf_tensor("r3cE", [128, 16 * 512], BF16)))
            xlnc = Buf(SE(nc.sbuf_tensor("xlncE", [128, 16 * 512], BF16)))
            yacc = Buf(SE(nc.sbuf_tensor("yaccE", [128, 4 * 2048], F32)))
            hT = Buf(SE(nc.sbuf_tensor("hTE", [128, 44 * 512], BF16)))
            wf1 = [Buf(SE(nc.sbuf_tensor(f"wf1E{i}", [128, 16 * 128], BF16))) for i in range(2)]
            wf2 = [Buf(SE(nc.sbuf_tensor(f"wf2E{i}", [128, 16 * 128], BF16))) for i in range(2)]
            wpj = Buf(SE(nc.sbuf_tensor("wpjE", [128, 44 * 512], BF16)))
            combT = Buf(SE(nc.sbuf_tensor("combTE", [128, 4 * 8], F32)))
            gcol = Buf(SE(nc.sbuf_tensor("gcolE", [128, 4], F32)))
            logr = Buf(SE(nc.sbuf_tensor("logrE", [8, 512], F32)))
            logt = Buf(SE(nc.sbuf_tensor("logtE", [128, 8], F32)))
            m1 = Buf(SE(nc.sbuf_tensor("m1E", [128, 1], F32)))
            m2 = Buf(SE(nc.sbuf_tensor("m2E", [128, 1], F32)))
            msk = Buf(SE(nc.sbuf_tensor("mskE", [128, 8], F32)))
            grow = Buf(SE(nc.sbuf_tensor("growE", [1, 512], F32)))
            silu_t = Buf(SE(nc.sbuf_tensor("siluE", [128, 512], F32)))
            r3tok = Buf(SE(nc.sbuf_tensor("r3tokE", [128, 2048], F32)))
            ps_1 = Buf(SE(nc.psum_tensor("ps1E", [128, 512], F32)))
            ps_2 = Buf(SE(nc.psum_tensor("ps2E", [128, 512], F32)))
            ps_r = Buf(SE(nc.psum_tensor("psrE", [8, 512], F32)))
            ps_t = Buf(SE(nc.psum_tensor("pstE", [128, 128], F32)))
            ps_g = Buf(SE(nc.psum_tensor("psgE", [1, 512], F32)))
            lnc = _ln_cache(kb, nc, SE, "lnE", 512)
            lnfree = ln_free(kb, SE, nc, None, None, 2048, 1e-5, "lnfE", epsb)
            xout_b = Buf(None)
            for cc in range(T_ // 512):
                c0 = cc * 512
                kb.dma("p", lambda e, c0=c0: e.dma_start(
                    out=r3c.ap[:].rearrange("p (t n) -> p t n", n=512),
                    in_=r3v[:, :, c0:c0 + 512]), writes=[r3c])
                xts = [r3c.view(r3c.ap[:, i * 512:(i + 1) * 512]) for i in range(16)]
                ots = [xlnc.view(xlnc.ap[:, i * 512:(i + 1) * 512]) for i in range(16)]
                _ln_chunk(kb, nc, SE, xts, ots, ones_col, ones_row, 1e-5, "", kb_cache=lnc)
                kb.merge_views(r3c, xts)
                kb.merge_views(xlnc, ots)
                for i in range(16):
                    st, sp = (i == 0), (i == 15)
                    kb.accum_matmul(ps_r, gwb, xlnc,
                        lambda e, i=i, st=st, sp=sp: e.matmul(
                            ps_r.ap[:], gwb.ap[:, i * 8:(i + 1) * 8],
                            xlnc.ap[:, i * 512:(i + 1) * 512], start=st, stop=sp), st, sp)
                kb.v(lambda e: e.tensor_copy(out=logr.ap[:], in_=ps_r.ap[:]),
                     reads=[ps_r], writes=[logr])
                for tt in range(4):
                    kb.t(lambda e, tt=tt: e.transpose(
                        ps_t.ap[:, 0:8], logr.ap[:, tt * 128:(tt + 1) * 128],
                        eyef.ap[0:8, 0:8]), reads=[logr, eyef], writes=[ps_t])
                    kb.v(lambda e: e.tensor_copy(out=logt.ap[:], in_=ps_t.ap[:, 0:8]),
                         reads=[ps_t], writes=[logt])
                    kb.v(lambda e: e.reduce_max(out=m1.ap[:], in_=logt.ap[:], axis=AX.X),
                         reads=[logt], writes=[m1])
                    kb.v(lambda e: e.tensor_scalar(out=logt.ap[:], in0=logt.ap[:],
                                                   scalar1=m1.ap[:, 0:1], scalar2=None,
                                                   op0=ALU.subtract),
                         reads=[m1, logt], writes=[logt])
                    kb.s(lambda e: e.activation(out=logt.ap[:], in_=logt.ap[:], func=AF.Exp),
                         reads=[logt], writes=[logt])
                    kb.v(lambda e: e.reduce_sum(out=m1.ap[:], in_=logt.ap[:], axis=AX.X),
                         reads=[logt], writes=[m1])
                    kb.v(lambda e: e.reciprocal(out=m1.ap[:], in_=m1.ap[:]),
                         reads=[m1], writes=[m1])
                    kb.v(lambda e: e.tensor_scalar(out=logt.ap[:], in0=logt.ap[:],
                                                   scalar1=m1.ap[:, 0:1], scalar2=None,
                                                   op0=ALU.mult),
                         reads=[m1, logt], writes=[logt])
                    kb.v(lambda e: e.reduce_max(out=m1.ap[:], in_=logt.ap[:], axis=AX.X),
                         reads=[logt], writes=[m1])
                    kb.v(lambda e: e.tensor_scalar(out=msk.ap[:], in0=logt.ap[:],
                                                   scalar1=m1.ap[:, 0:1], scalar2=None,
                                                   op0=ALU.is_ge),
                         reads=[logt, m1], writes=[msk])
                    kb.v(lambda e: e.scalar_tensor_tensor(out=msk.ap[:], in0=msk.ap[:],
                                                          scalar=-2.0, in1=logt.ap[:],
                                                          op0=ALU.mult, op1=ALU.add),
                         reads=[logt, msk], writes=[msk])
                    kb.v(lambda e: e.reduce_max(out=m2.ap[:], in_=msk.ap[:], axis=AX.X),
                         reads=[msk], writes=[m2])
                    kb.v(lambda e: e.tensor_scalar(out=msk.ap[:], in0=logt.ap[:],
                                                   scalar1=m2.ap[:, 0:1], scalar2=None,
                                                   op0=ALU.is_ge),
                         reads=[logt, m2], writes=[msk])
                    kb.v(lambda e, tt=tt: e.tensor_tensor(
                        out=combT.ap[:, tt * 8:(tt + 1) * 8], in0=logt.ap[:],
                        in1=msk.ap[:], op=ALU.mult),
                        reads=[logt, msk], writes=[combT])
                for i in range(16):
                    st, sp = (i == 0), (i == 15)
                    kb.accum_matmul(ps_g, sgwb, xlnc,
                        lambda e, i=i, st=st, sp=sp: e.matmul(
                            ps_g.ap[:], sgwb.ap[:, i:i + 1],
                            xlnc.ap[:, i * 512:(i + 1) * 512], start=st, stop=sp), st, sp)
                kb.s(lambda e: e.activation(out=grow.ap[:], in_=ps_g.ap[0:1, :],
                                            func=AF.Sigmoid), reads=[ps_g], writes=[grow])
                for tt in range(4):
                    kb.t(lambda e, tt=tt: e.transpose(
                        ps_t.ap[:, 0:1], grow.ap[:, tt * 128:(tt + 1) * 128],
                        eyef.ap[0:1, 0:1]), reads=[grow, eyef], writes=[ps_t])
                    kb.v(lambda e, tt=tt: e.tensor_copy(out=gcol.ap[:, tt:tt + 1],
                                                        in_=ps_t.ap[:, 0:1]),
                         reads=[ps_t], writes=[gcol])
                kb.v(lambda e: e.memset(yacc.ap[:], 0.0), writes=[yacc])
                for e_ in range(9):
                    NF = 11 if e_ < 8 else 44
                    f1v = bf1v[e_] if e_ < 8 else bs1v
                    f2v = bf2v[e_] if e_ < 8 else bs2v
                    pv = bpv[e_] if e_ < 8 else bspv
                    for ft in range(NF):
                        w1 = wf1[ft % 2]
                        w2 = wf2[ft % 2]
                        kb.dma("y", lambda e, ft=ft, w1=w1, f1v=f1v: e.dma_start(
                            out=w1.ap[:].rearrange("p (t f) -> p t f", f=128),
                            in_=f1v[:, :, ft * 128:(ft + 1) * 128]),
                            writes=[w1], extra=list(wcast.w))
                        kb.dma("s", lambda e, ft=ft, w2=w2, f2v=f2v: e.dma_start(
                            out=w2.ap[:].rearrange("p (t f) -> p t f", f=128),
                            in_=f2v[:, :, ft * 128:(ft + 1) * 128]),
                            writes=[w2], extra=list(wcast.w))
                        for i in range(16):
                            st, sp = (i == 0), (i == 15)
                            kb.accum_matmul(ps_1, w1, xlnc,
                                lambda e, i=i, w1=w1, st=st, sp=sp: e.matmul(
                                    ps_1.ap[:], w1.ap[:, i * 128:(i + 1) * 128],
                                    xlnc.ap[:, i * 512:(i + 1) * 512], start=st, stop=sp), st, sp)
                            kb.accum_matmul(ps_2, w2, xlnc,
                                lambda e, i=i, w2=w2, st=st, sp=sp: e.matmul(
                                    ps_2.ap[:], w2.ap[:, i * 128:(i + 1) * 128],
                                    xlnc.ap[:, i * 512:(i + 1) * 512], start=st, stop=sp), st, sp)
                        kb.s(lambda e: e.activation(out=silu_t.ap[:], in_=ps_1.ap[:],
                                                    func=AF.Silu), reads=[ps_1], writes=[silu_t])
                        kb.v(lambda e, ft=ft: e.tensor_mul(
                            out=hT.ap[:, ft * 512:(ft + 1) * 512], in0=silu_t.ap[:],
                            in1=ps_2.ap[:]), reads=[silu_t, ps_2], writes=[hT])
                    for dchunk in range(4):
                        kb.dma("y", lambda e, dchunk=dchunk, pv=pv, NF=NF: e.dma_start(
                            out=wpj.ap[:, 0:NF * 512].rearrange("p (t f) -> p t f", f=512),
                            in_=pv[:, :, dchunk * 512:(dchunk + 1) * 512]),
                            writes=[wpj], extra=list(wcast.w))
                        for tt in range(4):
                            for i in range(NF):
                                st, sp = (i == 0), (i == NF - 1)
                                kb.accum_matmul(ps_1, hT, wpj,
                                    lambda e, i=i, tt=tt, st=st, sp=sp: e.matmul(
                                        ps_1.ap[:], hT.ap[:, i * 512 + tt * 128:i * 512 + (tt + 1) * 128],
                                        wpj.ap[:, i * 512:(i + 1) * 512], start=st, stop=sp), st, sp)
                            ccol = (combT.ap[:, tt * 8 + e_:tt * 8 + e_ + 1] if e_ < 8
                                    else gcol.ap[:, tt:tt + 1])
                            csrc = combT if e_ < 8 else gcol
                            kb.v(lambda e, tt=tt, dchunk=dchunk, ccol=ccol: e.scalar_tensor_tensor(
                                out=yacc.ap[:, tt * 2048 + dchunk * 512:tt * 2048 + (dchunk + 1) * 512],
                                in0=ps_1.ap[:], scalar=ccol,
                                in1=yacc.ap[:, tt * 2048 + dchunk * 512:tt * 2048 + (dchunk + 1) * 512],
                                op0=ALU.mult, op1=ALU.add),
                                reads=[ps_1, csrc, yacc], writes=[yacc])
                for tt in range(4):
                    yv = yacc.view(yacc.ap[:, tt * 2048:(tt + 1) * 2048])
                    ov = yacc.view(yacc.ap[:, tt * 2048:(tt + 1) * 2048])
                    lnfree(yv, ov)
                    kb.merge_views(yacc, [yv, ov])
                    for i in range(16):
                        kb.dma("y", lambda e, i=i, tt=tt, c0=c0: e.dma_start(
                            out=silu_t.ap[:, 0:128],
                            in_=r3v[:, i, c0 + tt * 128:c0 + (tt + 1) * 128]),
                            writes=[silu_t])
                        kb.t(lambda e: e.transpose(ps_t.ap[:], silu_t.ap[:, 0:128],
                                                   eyef.ap[:]),
                             reads=[silu_t, eyef], writes=[ps_t])
                        kb.v(lambda e, i=i: e.tensor_copy(
                            out=r3tok.ap[:, i * 128:(i + 1) * 128], in_=ps_t.ap[:]),
                            reads=[ps_t], writes=[r3tok])
                    kb.v(lambda e, tt=tt: e.tensor_add(
                        out=r3tok.ap[:], in0=r3tok.ap[:],
                        in1=yacc.ap[:, tt * 2048:(tt + 1) * 2048]),
                        reads=[yacc, r3tok], writes=[r3tok])
                    kb.dma("y", lambda e, cc=cc, tt=tt: e.dma_start(
                        out=xoutv[:, cc * 4 + tt, :], in_=r3tok.ap[:]),
                        reads=[r3tok], writes=[xout_b])
        kb.final_wait("y")
    return nc



_ONESC = np.ones((128, 1), dtype=BF16NP)
_ONESR = np.ones((1, 128), dtype=np.float32)
_EYEB = np.eye(128, dtype=BF16NP)
_EYEF = np.eye(128, dtype=np.float32)
_TRIADD = np.where(np.tril(np.ones((128, 128))).T > 0, 0.0, -1e30).astype(np.float32)


def _attn_masks(own0, S_=8192, T_=T):
    NJ, NI = T_ // 512, S_ // 128
    m = np.zeros((NJ, NI, 128, 512), dtype=BF16NP)
    kk = np.arange(128)[:, None]
    qq = np.arange(512)[None, :]
    for j in range(NJ):
        q0 = own0 + 512 * j
        for i in range(NI):
            m[j, i] = ((128 * i + kk) <= (q0 + qq)).astype(BF16NP)
    return m


_MASK_CACHE = {}


def kernel(**inputs):
    x = np.asarray(inputs["L0_attn_x"], dtype=np.float32)
    B0, S_, D_ = x.shape
    cores = list(range(NCORES))

    whgT, whpT = host_quant_launch(inputs)

    ncA = build_launch_A(S_, T)
    wo_t = np.ascontiguousarray(np.asarray(inputs["L0_attn_W_o"]).T)
    xT_b = [np.ascontiguousarray(x[b].T) for b in range(B0)]
    in_maps = []
    for c in cores:
        b, q = c // 4, c % 4
        if q not in _MASK_CACHE:
            _MASK_CACHE[q] = _attn_masks(q * T, S_, T)
        in_maps.append({
            "XCTX": xT_b[b],
            "WDQ": np.asarray(inputs["L0_attn_W_dq"]),
            "WUQ": np.asarray(inputs["L0_attn_W_uq"]),
            "WDKV": np.asarray(inputs["L0_attn_W_dkv"]),
            "WUKV": np.asarray(inputs["L0_attn_W_ukv"]),
            "WOT": wo_t,
            "QLNG": np.asarray(inputs["L0_attn_q_ln_g"])[:, None],
            "QLNB": np.asarray(inputs["L0_attn_q_ln_b"])[:, None],
            "KVLNG": np.asarray(inputs["L0_attn_kv_ln_g"])[:, None],
            "KVLNB": np.asarray(inputs["L0_attn_kv_ln_b"])[:, None],
            "MASKS": _MASK_CACHE[q],
            "IDXQ": np.array([[q]], dtype=np.int32),
            "ONESC": _ONESC, "ONESR": _ONESR,
        })
    resA = run_bass_kernel_spmd(ncA, in_maps, core_ids=cores)
    R1 = [resA.results[c]["R1"] for c in cores]

    ncB = build_launch_B(T)
    in_maps = [{
        "R1": R1[c], "WHG": whgT, "WHP": whpT,
        "ONESC": _ONESC, "ONESR": _ONESR,
    } for c in cores]
    resB = run_bass_kernel_spmd(ncB, in_maps, core_ids=cores)
    R2 = [resB.results[c]["R2"] for c in cores]

    ncC = build_launch_C(T)
    win_t = np.ascontiguousarray(np.asarray(inputs["L1_attn_W_in"]).T)
    drow = np.repeat(np.asarray(inputs["L1_attn_D"]), 64)[None, :].astype(np.float32)
    in_maps = []
    for c in cores:
        q = c % 4
        ext = (np.zeros((D_, EXT), np.float32) if q == 0
               else np.ascontiguousarray(R2[c - 1][:, -EXT:]))
        r2e = np.concatenate([ext, R2[c]], axis=1)
        in_maps.append({
            "R2E": np.ascontiguousarray(r2e),
            "WINT": win_t,
            "CONVW": np.asarray(inputs["L1_attn_conv_w"]),
            "CONVB": np.asarray(inputs["L1_attn_conv_b"])[:, None],
            "DTB": np.asarray(inputs["L1_attn_dt_bias"])[:, None],
            "ALOG": np.asarray(inputs["L1_attn_A_log"])[:, None],
            "DROW": drow, "TRI": _TRIADD, "EYE": _EYEB, "EYEF": _EYEF,
            "ONESC": _ONESC, "ONESR": _ONESR,
        })
    resC = run_bass_kernel_spmd(ncC, in_maps, core_ids=cores)
    outsC = [resC.results[c] for c in cores]

    ncD = build_launch_D(T)
    wout_t = np.ascontiguousarray(np.asarray(inputs["L1_attn_W_out"]).T)
    normw = np.asarray(inputs["L1_attn_norm_w"])[None, :].astype(np.float32)
    in_maps = []
    for c in cores:
        q = c % 4
        uends = np.zeros((3, 64, 4096), np.float32)
        atots = np.zeros((64, 3), np.float32)
        for k in range(3):
            p = c - 1 - k
            if p // 4 == c // 4 and p >= 0:
                uends[k] = outsC[p]["UEND"]
                atots[:, k] = outsC[p]["ATOT"][:, 0]
        in_maps.append({
            "YP": outsC[c]["YP"], "ZS": outsC[c]["ZS"], "CS": outsC[c]["CS"],
            "AC": outsC[c]["AC"], "UENDS": uends, "ATOTS": atots,
            "R2": R2[c], "WOUTT": wout_t, "NORMW": normw,
            "EYE": _EYEB, "ONESC": _ONESC, "ONESR": _ONESR,
        })
    resD = run_bass_kernel_spmd(ncD, in_maps, core_ids=cores)
    R3 = [resD.results[c]["R3"] for c in cores]

    ncE = build_launch_E(T)
    gwt = np.ascontiguousarray(np.asarray(inputs["L1_ffn_gate_w"]).T)
    efc1t = np.ascontiguousarray(np.asarray(inputs["L1_ffn_expert_fc1"]).transpose(0, 2, 1))
    efc2t = np.ascontiguousarray(np.asarray(inputs["L1_ffn_expert_fc2"]).transpose(0, 2, 1))
    eprojt = np.ascontiguousarray(np.asarray(inputs["L1_ffn_expert_proj"]).transpose(0, 2, 1))
    sfc1t = np.ascontiguousarray(np.asarray(inputs["L1_ffn_shared_fc1"]).T)
    sfc2t = np.ascontiguousarray(np.asarray(inputs["L1_ffn_shared_fc2"]).T)
    sprojt = np.ascontiguousarray(np.asarray(inputs["L1_ffn_shared_proj"]).T)
    sgwt = np.ascontiguousarray(np.asarray(inputs["L1_ffn_shared_gate_w"]).T)
    in_maps = [{
        "R3": R3[c], "GWT": gwt, "EFC1T": efc1t, "EFC2T": efc2t,
        "EPROJT": eprojt, "SFC1T": sfc1t, "SFC2T": sfc2t, "SPROJT": sprojt,
        "SGWT": sgwt, "EYEF": _EYEF, "ONESC": _ONESC, "ONESR": _ONESR,
    } for c in cores]
    resE = run_bass_kernel_spmd(ncE, in_maps, core_ids=cores)

    out = np.empty((B0, S_, D_), dtype=np.float32)
    for c in cores:
        b, q = c // 4, c % 4
        out[b, q * T:(q + 1) * T] = resE.results[c]["XOUT"]
    return out


# version 02
# speedup vs baseline: 1.1969x; 1.1969x over previous
import numpy as np
import ml_dtypes
import concourse.bass as bass
import concourse.mybir as mybir
from concourse.bass_utils import run_bass_kernel_spmd

F32 = mybir.dt.float32
BF16 = mybir.dt.bfloat16
AF = mybir.ActivationFunctionType
ALU = mybir.AluOpType
AX = mybir.AxisListType

NCORES = 8
D = 2048
T = 2048
F32EPS = float(np.finfo(np.float32).eps)
BF16NP = ml_dtypes.bfloat16


class Buf:
    __slots__ = ("ap", "w", "r")
    def __init__(self, ap, w=None):
        self.ap = ap
        self.w = list(w) if w else []
        self.r = []

    def view(self, ap):
        v = Buf(ap, w=self.w)
        v.r = list(self.r)
        return v


class KB:
    CLASSES = {"t": 1, "v": 1, "s": 1, "p": 1}

    def __init__(self, nc):
        self.nc = nc
        self.sem = {}
        self.cnt = {}
        self.waited = {e: {} for e in ("t", "v", "s", "p", "y")}
        self.eng = {"t": nc.tensor, "v": nc.vector, "s": nc.scalar,
                    "p": nc.gpsimd, "y": nc.sync}
        self._nbsem = 0

    def init_sems(self, ctx):
        self._ctx = ctx
        for c in self.CLASSES:
            self.sem[c] = ctx.enter_context(self.nc.semaphore("sem_" + c))
            self.cnt[c] = 0

    def _buf_sem(self, buf):
        key = ("b", id(buf))
        if key not in self.sem:
            self._nbsem += 1
            self.sem[key] = self._ctx.enter_context(
                self.nc.semaphore(f"bsem{self._nbsem}"))
            self.cnt[key] = 0
        return key

    def _wait(self, engine_name, tokens):
        w = self.waited[engine_name]
        eng = self.eng[engine_name]
        need = {}
        for tok in tokens:
            if tok is None:
                continue
            c, n = tok
            if n > need.get(c, 0):
                need[c] = n
        for c, n in need.items():
            if w.get(c, 0) < n:
                eng.wait_ge(self.sem[c], n)
                w[c] = n

    def emit(self, engine_name, cls, fn, reads=(), writes=(), extra=()):
        toks = list(extra)
        for b in reads:
            toks.extend(b.w)
        for b in writes:
            toks.extend(b.w)
            toks.extend(b.r)
        self._wait(engine_name, toks)
        inst = fn(self.eng[engine_name])
        inc = 16 if isinstance(cls, tuple) else self.CLASSES[cls]
        self.cnt[cls] += inc
        tok = (cls, self.cnt[cls])
        inst.then_inc(self.sem[cls], inc)
        for b in reads:
            b.r.append(tok)
        for b in writes:
            b.w = [tok]
            b.r = []
        return tok

    def t(self, fn, reads=(), writes=(), extra=()):
        return self.emit("t", "t", fn, reads, writes, extra)

    def v(self, fn, reads=(), writes=(), extra=()):
        return self.emit("v", "v", fn, reads, writes, extra)

    def s(self, fn, reads=(), writes=(), extra=()):
        return self.emit("s", "s", fn, reads, writes, extra)

    def p(self, fn, reads=(), writes=(), extra=()):
        return self.emit("p", "p", fn, reads, writes, extra)

    def dma(self, qname, fn, reads=(), writes=(), extra=()):
        assert len(writes) == 1, "dma() wants exactly one destination Buf"
        if qname == "p":
            self._nbsem += 1
            key = ("pd", self._nbsem)
            self.sem[key] = self._ctx.enter_context(
                self.nc.semaphore(f"psem{self._nbsem}"))
            self.cnt[key] = 0
        else:
            key = self._buf_sem(writes[0])
        return self.emit(qname, key, fn, reads, writes, extra)

    def accum_matmul(self, psum_buf, lhs_buf, rhs_buf, fn_ap, start, stop):
        if stop:
            return self.t(fn_ap, reads=[lhs_buf, rhs_buf], writes=[psum_buf])
        extra = list(psum_buf.w) + list(psum_buf.r)
        return self.t(fn_ap, reads=[lhs_buf, rhs_buf], extra=extra)

    @staticmethod
    def merge_views(parent, views):
        for v in views:
            for t in v.w:
                if t not in parent.w:
                    parent.w.append(t)
            parent.r.extend(v.r)

    def const_col(self, ctx, value, npart=1, dtype=F32):
        key = ("const", float(value), npart, dtype)
        if not hasattr(self, "_consts"):
            self._consts = {}
        if key not in self._consts:
            b = Buf(ctx.enter_context(self.nc.sbuf_tensor(
                f"cst{len(self._consts)}", [npart, 1], dtype)))
            self.v(lambda e: e.memset(b.ap[:], float(value)), writes=[b])
            self._consts[key] = b
        return self._consts[key]

    def value_load(self, engine_name, buf, ap=None):
        self._wait(engine_name, list(buf.w))
        return self.eng[engine_name].value_load(ap if ap is not None else buf.ap[0:1, 0:1])

    def final_wait(self, engine_name="y"):
        toks = [(c, n) for c, n in self.cnt.items() if n > 0]
        self._wait(engine_name, toks)

    def phase_barrier(self):
        toks = [(c, n) for c, n in self.cnt.items() if n > 0]
        for e in ("t", "v", "s", "p", "y"):
            self._wait(e, toks)



def ln_partition(kb, ctx, xT_bufs, out_bufs, ntok, eps=1e-5, name="ln",
                 ones_col=None, ones_row=None, gamma=None, beta=None):
    nc = kb.nc
    nt = len(xT_bufs)
    nD = nt * 128
    E = ctx.enter_context
    xb = Buf(E(nc.sbuf_tensor(name + "_xb", [128, 512], BF16)))
    sq = Buf(E(nc.sbuf_tensor(name + "_sq", [128, 512], BF16)))
    ps_sum = Buf(E(nc.psum_tensor(name + "_psum", [1, 512], F32)))
    ps_sq = Buf(E(nc.psum_tensor(name + "_psq", [1, 512], F32)))
    mu = Buf(E(nc.sbuf_tensor(name + "_mu", [1, 512], F32)))
    rstd = Buf(E(nc.sbuf_tensor(name + "_rstd", [1, 512], F32)))
    ps_b = Buf(E(nc.psum_tensor(name + "_pb", [128, 512], F32)))
    ps_b2 = Buf(E(nc.psum_tensor(name + "_pb2", [128, 512], F32)))
    mub = Buf(E(nc.sbuf_tensor(name + "_mub", [128, 512], F32)))
    rstdb = Buf(E(nc.sbuf_tensor(name + "_rstdb", [128, 512], F32)))
    tmp = Buf(E(nc.sbuf_tensor(name + "_tmp", [128, 512], F32)))

    for c0 in range(0, ntok, 512):
        c1 = min(c0 + 512, ntok)
        w = c1 - c0
        for i, xT in enumerate(xT_bufs):
            kb.v(lambda e, xT=xT, c0=c0, c1=c1, w=w: e.tensor_copy(
                out=xb.ap[:, 0:w], in_=xT.ap[:, c0:c1]), reads=[xT], writes=[xb])
            kb.v(lambda e, w=w: e.tensor_mul(out=sq.ap[:, 0:w], in0=xb.ap[:, 0:w],
                                             in1=xb.ap[:, 0:w]), reads=[xb], writes=[sq])
            st = (i == 0)
            sp = (i == nt - 1)
            kb.accum_matmul(ps_sum, ones_col, xb,
                            lambda e, w=w, st=st, sp=sp: e.matmul(
                                ps_sum.ap[0:1, 0:w], ones_col.ap[:, 0:1], xb.ap[:, 0:w],
                                start=st, stop=sp), st, sp)
            kb.accum_matmul(ps_sq, ones_col, sq,
                            lambda e, w=w, st=st, sp=sp: e.matmul(
                                ps_sq.ap[0:1, 0:w], ones_col.ap[:, 0:1], sq.ap[:, 0:w],
                                start=st, stop=sp), st, sp)
        kb.v(lambda e, w=w: e.tensor_scalar(out=mu.ap[:, 0:w], in0=ps_sum.ap[0:1, 0:w],
                                            scalar1=1.0 / nD, scalar2=None, op0=ALU.mult),
             reads=[ps_sum], writes=[mu])
        kb.v(lambda e, w=w: e.tensor_mul(out=rstd.ap[:, 0:w], in0=mu.ap[:, 0:w],
                                         in1=mu.ap[:, 0:w]), reads=[mu], writes=[rstd])
        kb.v(lambda e, w=w: e.scalar_tensor_tensor(
            out=rstd.ap[:, 0:w], in0=ps_sq.ap[0:1, 0:w], scalar=1.0 / nD,
            in1=rstd.ap[:, 0:w], op0=ALU.mult, op1=ALU.subtract),
            reads=[ps_sq, rstd], writes=[rstd])
        kb.s(lambda e, w=w: e.activation(out=rstd.ap[:, 0:w], in_=rstd.ap[:, 0:w],
                                         func=AF.Ln, bias=float(eps)),
             reads=[rstd], writes=[rstd])
        kb.s(lambda e, w=w: e.activation(out=rstd.ap[:, 0:w], in_=rstd.ap[:, 0:w],
                                         func=AF.Exp, scale=-0.5),
             reads=[rstd], writes=[rstd])
        kb.t(lambda e, w=w: e.matmul(ps_b.ap[:, 0:w], ones_row.ap[0:1, :],
                                     mu.ap[0:1, 0:w], start=True, stop=True),
             reads=[mu, ones_row], writes=[ps_b])
        kb.v(lambda e, w=w: e.tensor_copy(out=mub.ap[:, 0:w], in_=ps_b.ap[:, 0:w]),
             reads=[ps_b], writes=[mub])
        kb.t(lambda e, w=w: e.matmul(ps_b2.ap[:, 0:w], ones_row.ap[0:1, :],
                                     rstd.ap[0:1, 0:w], start=True, stop=True),
             reads=[rstd, ones_row], writes=[ps_b2])
        kb.v(lambda e, w=w: e.tensor_copy(out=rstdb.ap[:, 0:w], in_=ps_b2.ap[:, 0:w]),
             reads=[ps_b2], writes=[rstdb])
        for i, xT in enumerate(xT_bufs):
            ob = out_bufs[i]
            kb.v(lambda e, xT=xT, c0=c0, c1=c1, w=w: e.tensor_sub(
                out=tmp.ap[:, 0:w], in0=xT.ap[:, c0:c1], in1=mub.ap[:, 0:w]),
                reads=[xT, mub], writes=[tmp])
            if gamma is None:
                kb.v(lambda e, ob=ob, c0=c0, c1=c1, w=w: e.tensor_mul(
                    out=ob.ap[:, c0:c1], in0=tmp.ap[:, 0:w], in1=rstdb.ap[:, 0:w]),
                    reads=[tmp, rstdb], writes=[ob])
            else:
                kb.v(lambda e, w=w: e.tensor_mul(
                    out=tmp.ap[:, 0:w], in0=tmp.ap[:, 0:w], in1=rstdb.ap[:, 0:w]),
                    reads=[tmp, rstdb], writes=[tmp])
                g = gamma[i]; bta = beta[i]
                kb.v(lambda e, ob=ob, g=g, bta=bta, c0=c0, c1=c1, w=w: e.tensor_scalar(
                    out=ob.ap[:, c0:c1], in0=tmp.ap[:, 0:w], scalar1=g.ap[:, 0:1],
                    scalar2=bta.ap[:, 0:1], op0=ALU.mult, op1=ALU.add),
                    reads=[tmp, g, bta], writes=[ob])



def build_launch_Q():
    nc = bass.Bass("TRN2", target_bir_lowering=False)
    WG = nc.dram_tensor("WG", [2048, 2048], F32, kind="ExternalInput").ap()
    WP = nc.dram_tensor("WP", [256, 8192], F32, kind="ExternalInput").ap()
    EYE = nc.dram_tensor("EYE", [128, 128], BF16, kind="ExternalInput").ap()
    OG = nc.dram_tensor("OG", [2048, 2048], BF16, kind="ExternalOutput").ap()
    OP = nc.dram_tensor("OP", [8192, 256], BF16, kind="ExternalOutput").ap()

    from contextlib import ExitStack
    with ExitStack() as ctx:
        kb = KB(nc)
        kb.init_sems(ctx)
        E = ctx.enter_context
        eye = Buf(E(nc.sbuf_tensor("eye", [128, 128], BF16)))
        kb.dma("y", lambda e: e.dma_start(out=eye.ap[:], in_=EYE), writes=[eye])

        def quant_one(name, W, OUT, nrows, ncols, E, outbuf):
            ng = ncols // 64
            wb = Buf(E(nc.sbuf_tensor(name + "_wb", [128, ncols], BF16)))
            gsum = Buf(E(nc.sbuf_tensor(name + "_gs", [128, ng], F32)))
            scale = Buf(E(nc.sbuf_tensor(name + "_sc", [128, ng], BF16)))
            rcp = Buf(E(nc.sbuf_tensor(name + "_rc", [128, ng], F32)))
            tq = Buf(E(nc.sbuf_tensor(name + "_tq", [128, ncols], BF16)))
            ab = Buf(E(nc.sbuf_tensor(name + "_ab", [128, ncols], BF16)))
            mask = Buf(E(nc.sbuf_tensor(name + "_mk", [128, ncols], BF16)))
            sgn = Buf(E(nc.sbuf_tensor(name + "_sg", [128, ncols], BF16)))
            what = Buf(E(nc.sbuf_tensor(name + "_wh", [128, ncols], BF16)))
            ptr = Buf(E(nc.psum_tensor(name + "_pt", [128, 128], BF16)))
            panels = [Buf(E(nc.sbuf_tensor(f"{name}_pan{cc}", [128, nrows], BF16)))
                      for cc in range(ncols // 128)]
            for r in range(nrows // 128):
                kb.dma("p", lambda e, r=r: e.dma_start(out=wb.ap[:], in_=W[r * 128:(r + 1) * 128, :]),
                       writes=[wb])
                v3 = wb.ap[:].rearrange("p (g e) -> p g e", e=64)
                kb.v(lambda e, v3=v3: e.tensor_reduce(out=gsum.ap[:], in_=v3, axis=AX.X,
                                                      op=ALU.add, apply_absolute_value=True),
                     reads=[wb], writes=[gsum])
                kb.v(lambda e: e.tensor_scalar(out=scale.ap[:], in0=gsum.ap[:],
                                               scalar1=1.0 / 64, scalar2=1e-8,
                                               op0=ALU.mult, op1=ALU.max),
                     reads=[gsum], writes=[scale])
                kb.v(lambda e: e.reciprocal(out=rcp.ap[:], in_=scale.ap[:]),
                     reads=[scale], writes=[rcp])
                rcp3 = rcp.ap[:].unsqueeze(2).broadcast_to([128, ng, 64])
                tq3 = tq.ap[:].rearrange("p (g e) -> p g e", e=64)
                kb.v(lambda e, v3=v3, rcp3=rcp3, tq3=tq3: e.tensor_tensor(
                    out=tq3, in0=v3, in1=rcp3, op=ALU.mult), reads=[wb, rcp], writes=[tq])
                kb.s(lambda e: e.activation(out=ab.ap[:], in_=tq.ap[:], func=AF.Abs),
                     reads=[tq], writes=[ab])
                kb.v(lambda e: e.tensor_scalar(out=mask.ap[:], in0=ab.ap[:], scalar1=0.5,
                                               scalar2=None, op0=ALU.is_gt),
                     reads=[ab], writes=[mask])
                kb.s(lambda e: e.sign(out=sgn.ap[:], in_=wb.ap[:]),
                     reads=[wb], writes=[sgn])
                sc3 = scale.ap[:].unsqueeze(2).broadcast_to([128, ng, 64])
                wh3 = what.ap[:].rearrange("p (g e) -> p g e", e=64)
                sg3 = sgn.ap[:].rearrange("p (g e) -> p g e", e=64)
                kb.v(lambda e, sg3=sg3, sc3=sc3, wh3=wh3: e.tensor_tensor(
                    out=wh3, in0=sg3, in1=sc3, op=ALU.mult), reads=[sgn, scale], writes=[what])
                kb.v(lambda e: e.tensor_mul(out=what.ap[:], in0=what.ap[:], in1=mask.ap[:]),
                     reads=[mask, what], writes=[what])
                for cc in range(ncols // 128):
                    kb.t(lambda e, cc=cc: e.transpose(ptr.ap[:], what.ap[:, cc * 128:(cc + 1) * 128],
                                                      eye.ap[:]),
                         reads=[what, eye], writes=[ptr])
                    kb.v(lambda e, cc=cc, r=r: e.tensor_copy(
                        out=panels[cc].ap[:, r * 128:(r + 1) * 128], in_=ptr.ap[:]),
                        reads=[ptr], writes=[panels[cc]])
            for cc in range(ncols // 128):
                kb.dma("y", lambda e, cc=cc: e.dma_start(
                    out=OUT[cc * 128:(cc + 1) * 128, :], in_=panels[cc].ap[:]),
                    reads=[panels[cc]], writes=[outbuf])

        ogb, opb = Buf(None), Buf(None)
        from contextlib import ExitStack as _ES
        with _ES() as sub:
            quant_one("g", WG, OG, 2048, 2048, sub.enter_context, ogb)
        kb.phase_barrier()
        with _ES() as sub:
            quant_one("q", WP, OP, 256, 8192, sub.enter_context, opb)
        kb.final_wait("y")
    return nc


_EYE128 = np.eye(128, dtype=BF16NP)


def host_quant_launch(inputs):
    nc = build_launch_Q()
    wg = np.ascontiguousarray(inputs["L0_ffn_w_gate_up"])
    wp = np.ascontiguousarray(inputs["L0_ffn_w_proj"])
    in_maps = []
    for c in range(NCORES):
        in_maps.append({
            "WG": np.ascontiguousarray(wg[c * 2048:(c + 1) * 2048, :]),
            "WP": np.ascontiguousarray(wp[c * 256:(c + 1) * 256, :]),
            "EYE": _EYE128,
        })
    res = run_bass_kernel_spmd(nc, in_maps, core_ids=list(range(NCORES)))
    whgT = np.concatenate([res.results[c]["OG"] for c in range(NCORES)], axis=1)
    whpT = np.concatenate([res.results[c]["OP"] for c in range(NCORES)], axis=1)
    return whgT, whpT






def build_launch_B(T_=T):
    nc = bass.Bass("TRN2", target_bir_lowering=False)
    R1 = nc.dram_tensor("R1", [D, T_], F32, kind="ExternalInput").ap()
    WHG = nc.dram_tensor("WHG", [2048, 16384], BF16, kind="ExternalInput").ap()
    WHP = nc.dram_tensor("WHP", [8192, 2048], BF16, kind="ExternalInput").ap()
    ONESC = nc.dram_tensor("ONESC", [128, 1], BF16, kind="ExternalInput").ap()
    ONESR = nc.dram_tensor("ONESR", [1, 128], F32, kind="ExternalInput").ap()
    R2 = nc.dram_tensor("R2", [D, T_], F32, kind="ExternalOutput").ap()

    from contextlib import ExitStack
    with ExitStack() as ctx:
        kb = KB(nc)
        kb.init_sems(ctx)
        E = ctx.enter_context
        ones_col = Buf(E(nc.sbuf_tensor("onesc", [128, 1], BF16)))
        ones_row = Buf(E(nc.sbuf_tensor("onesr", [1, 128], F32)))
        kb.dma("y", lambda e: e.dma_start(out=ones_col.ap[:], in_=ONESC), writes=[ones_col])
        kb.dma("y", lambda e: e.dma_start(out=ones_row.ap[:], in_=ONESR), writes=[ones_row])

        r1c = Buf(E(nc.sbuf_tensor("r1c", [128, 16 * 512], F32)))
        xlnc = Buf(E(nc.sbuf_tensor("xlnc", [128, 16 * 512], BF16)))
        hc = Buf(E(nc.sbuf_tensor("hc", [128, 64 * 512], BF16)))
        outc = Buf(E(nc.sbuf_tensor("outc", [128, 16 * 512], F32)))
        wg = [Buf(E(nc.sbuf_tensor(f"wg{i}", [128, 16 * 128], BF16))) for i in range(2)]
        wu = [Buf(E(nc.sbuf_tensor(f"wu{i}", [128, 16 * 128], BF16))) for i in range(2)]
        wp = [Buf(E(nc.sbuf_tensor(f"wp{i}", [128, 32 * 128], BF16))) for i in range(2)]
        ps_g = Buf(E(nc.psum_tensor("ps_g", [128, 512], F32)))
        ps_u = Buf(E(nc.psum_tensor("ps_u", [128, 512], F32)))
        ps_d = Buf(E(nc.psum_tensor("ps_d", [128, 512], F32)))
        ps_rms = Buf(E(nc.psum_tensor("ps_rms", [1, 512], F32)))
        ps_rb = Buf(E(nc.psum_tensor("ps_rb", [128, 512], F32)))
        silu_t = Buf(E(nc.sbuf_tensor("silu_t", [128, 512], F32)))
        sqh = Buf(E(nc.sbuf_tensor("sqh", [128, 512], BF16)))
        rmsrow = Buf(E(nc.sbuf_tensor("rmsrow", [1, 512], F32)))
        rmsb = Buf(E(nc.sbuf_tensor("rmsb", [128, 512], F32)))

        NCH = T_ // 512
        r1v = R1.rearrange("(t p) n -> p t n", p=128)
        r2v = R2.rearrange("(t p) n -> p t n", p=128)
        whgv = WHG.rearrange("(t p) f -> p t f", p=128)
        whpv = WHP.rearrange("(g p) f -> p g f", p=128)
        r2buf = Buf(None)

        for c in range(NCH):
            c0 = c * 512
            kb.dma("y", lambda e, c0=c0: e.dma_start(
                out=r1c.ap[:].rearrange("p (t n) -> p t n", n=512),
                in_=r1v[:, :, c0:c0 + 512]), writes=[r1c])
            r1tiles = [r1c.view(r1c.ap[:, i * 512:(i + 1) * 512]) for i in range(16)]
            xltiles = [xlnc.view(xlnc.ap[:, i * 512:(i + 1) * 512]) for i in range(16)]
            _ln_chunk(kb, nc, E, r1tiles, xltiles, ones_col, ones_row, 1e-5,
                      f"lnB{c}", kb_cache=_ln_cache(kb, nc, E, "lnB", 512))
            kb.merge_views(r1c, r1tiles)
            kb.merge_views(xlnc, xltiles)
            for g in range(64):
                gb = wg[g % 2]
                ub = wu[g % 2]
                kb.dma("s", lambda e, g=g, gb=gb: e.dma_start(
                    out=gb.ap[:].rearrange("p (t f) -> p t f", f=128),
                    in_=whgv[:, :, g * 128:(g + 1) * 128]), writes=[gb])
                kb.dma("s", lambda e, g=g, ub=ub: e.dma_start(
                    out=ub.ap[:].rearrange("p (t f) -> p t f", f=128),
                    in_=whgv[:, :, 8192 + g * 128:8192 + (g + 1) * 128]), writes=[ub])
                for i in range(16):
                    st, sp = (i == 0), (i == 15)
                    kb.accum_matmul(ps_g, gb, xlnc, lambda e, i=i, gb=gb, st=st, sp=sp: e.matmul(
                        ps_g.ap[:], gb.ap[:, i * 128:(i + 1) * 128],
                        xlnc.ap[:, i * 512:(i + 1) * 512], start=st, stop=sp), st, sp)
                    kb.accum_matmul(ps_u, ub, xlnc, lambda e, i=i, ub=ub, st=st, sp=sp: e.matmul(
                        ps_u.ap[:], ub.ap[:, i * 128:(i + 1) * 128],
                        xlnc.ap[:, i * 512:(i + 1) * 512], start=st, stop=sp), st, sp)
                kb.s(lambda e: e.activation(out=silu_t.ap[:], in_=ps_g.ap[:], func=AF.Silu),
                     reads=[ps_g], writes=[silu_t])
                kb.v(lambda e, g=g: e.tensor_mul(out=hc.ap[:, g * 512:(g + 1) * 512],
                                                 in0=silu_t.ap[:], in1=ps_u.ap[:]),
                     reads=[silu_t, ps_u], writes=[hc])
                kb.v(lambda e, g=g: e.tensor_mul(out=sqh.ap[:], in0=hc.ap[:, g * 512:(g + 1) * 512],
                                                 in1=hc.ap[:, g * 512:(g + 1) * 512]),
                     reads=[hc], writes=[sqh])
                st, sp = (g == 0), (g == 63)
                kb.accum_matmul(ps_rms, ones_col, sqh, lambda e, st=st, sp=sp: e.matmul(
                    ps_rms.ap[:], ones_col.ap[:, 0:1], sqh.ap[:], start=st, stop=sp), st, sp)
            epsb2 = kb.const_col(ctx, F32EPS)
            kb.s(lambda e: e.activation(out=rmsrow.ap[:], in_=ps_rms.ap[0:1, :], func=AF.Ln,
                                        bias=epsb2.ap[0:1, 0:1], scale=1.0 / 8192),
                 reads=[ps_rms, epsb2], writes=[rmsrow])
            kb.s(lambda e: e.activation(out=rmsrow.ap[:], in_=rmsrow.ap[:], func=AF.Exp,
                                        scale=-0.5), reads=[rmsrow], writes=[rmsrow])
            kb.t(lambda e: e.matmul(ps_rb.ap[:], ones_row.ap[0:1, :], rmsrow.ap[0:1, :],
                                    start=True, stop=True),
                 reads=[rmsrow, ones_row], writes=[ps_rb])
            kb.v(lambda e: e.tensor_copy(out=rmsb.ap[:], in_=ps_rb.ap[:]),
                 reads=[ps_rb], writes=[rmsb])
            for g in range(64):
                kb.v(lambda e, g=g: e.tensor_mul(out=hc.ap[:, g * 512:(g + 1) * 512],
                                                 in0=hc.ap[:, g * 512:(g + 1) * 512],
                                                 in1=rmsb.ap[:]),
                     reads=[rmsb, hc], writes=[hc])
            for j in range(16):
                for half in range(2):
                    pb = wp[half]
                    kb.dma("s", lambda e, j=j, half=half, pb=pb: e.dma_start(
                        out=pb.ap[:].rearrange("p (g f) -> p g f", f=128),
                        in_=whpv[:, half * 32:(half + 1) * 32, j * 128:(j + 1) * 128]),
                        writes=[pb])
                    for gg in range(32):
                        g = half * 32 + gg
                        st, sp = (g == 0), (g == 63)
                        kb.accum_matmul(ps_d, pb, hc, lambda e, g=g, gg=gg, pb=pb, st=st, sp=sp: e.matmul(
                            ps_d.ap[:], pb.ap[:, gg * 128:(gg + 1) * 128],
                            hc.ap[:, g * 512:(g + 1) * 512], start=st, stop=sp), st, sp)
                kb.v(lambda e, j=j: e.tensor_copy(out=outc.ap[:, j * 512:(j + 1) * 512],
                                                  in_=ps_d.ap[:]),
                     reads=[ps_d], writes=[outc])
            otiles = [outc.view(outc.ap[:, i * 512:(i + 1) * 512]) for i in range(16)]
            _ln_chunk(kb, nc, E, otiles, otiles, ones_col, ones_row, 1e-5,
                      f"lnB2{c}", kb_cache=_ln_cache(kb, nc, E, "lnB", 512))
            kb.merge_views(outc, otiles)
            kb.dma("y", lambda e, c0=c0: e.dma_start(
                out=r2v[:, :, c0:c0 + 512],
                in_=outc.ap[:].rearrange("p (t n) -> p t n", n=512)),
                reads=[outc], writes=[r2buf])
            kb.dma("p", lambda e, c0=c0: e.dma_start(
                out=r2v[:, :, c0:c0 + 512],
                in_=r1v[:, :, c0:c0 + 512], accum_op=ALU.add),
                reads=[r1c], writes=[r2buf])
        kb.final_wait("y")
    return nc


_LN_CACHES = {}


def _ln_cache(kb, nc, E, name, w, eps=1e-5):
    key = (id(kb), name, w)
    if key not in _LN_CACHES:
        class _C(dict):
            pass
        _LN_CACHES[key] = {
            "epsb": None,
            "xb": Buf(E(nc.sbuf_tensor(name + "_xb", [128, w], BF16))),
            "sq": Buf(E(nc.sbuf_tensor(name + "_sq", [128, w], BF16))),
            "ps_sum": Buf(E(nc.psum_tensor(name + "_psum", [1, w], F32))),
            "ps_sq": Buf(E(nc.psum_tensor(name + "_psq", [1, w], F32))),
            "mu": Buf(E(nc.sbuf_tensor(name + "_mu", [1, w], F32))),
            "rstd": Buf(E(nc.sbuf_tensor(name + "_rstd", [1, w], F32))),
            "ps_b": Buf(E(nc.psum_tensor(name + "_pb", [128, w], F32))),
            "mub": Buf(E(nc.sbuf_tensor(name + "_mub", [128, w], F32))),
            "rstdb": Buf(E(nc.sbuf_tensor(name + "_rstdb", [128, w], F32))),
            "tmp": Buf(E(nc.sbuf_tensor(name + "_tmp", [128, w], F32))),
        }
        b = Buf(E(nc.sbuf_tensor(name + "_eps", [1, 1], F32)))
        kb.v(lambda e: e.memset(b.ap[:], float(eps)), writes=[b])
        _LN_CACHES[key]["epsb"] = b
    return _LN_CACHES[key]


def _ln_chunk(kb, nc, E, xtiles, otiles, ones_col, ones_row, eps, name,
              kb_cache=None, gamma=None, beta=None):
    cs = kb_cache
    xb, sq = cs["xb"], cs["sq"]
    ps_sum, ps_sq = cs["ps_sum"], cs["ps_sq"]
    mu, rstd = cs["mu"], cs["rstd"]
    ps_b = ps_b2 = cs["ps_b"]
    mub, rstdb, tmp = cs["mub"], cs["rstdb"], cs["tmp"]
    nt = len(xtiles)
    nD_ = nt * 128
    w = xtiles[0].ap.shape[-1]
    for i, xT in enumerate(xtiles):
        kb.v(lambda e, xT=xT, w=w: e.tensor_copy(out=xb.ap[:, 0:w], in_=xT.ap[:]),
             reads=[xT], writes=[xb])
        kb.v(lambda e, w=w: e.tensor_mul(out=sq.ap[:, 0:w], in0=xb.ap[:, 0:w],
                                         in1=xb.ap[:, 0:w]), reads=[xb], writes=[sq])
        st, sp = (i == 0), (i == nt - 1)
        kb.accum_matmul(ps_sum, ones_col, xb, lambda e, w=w, st=st, sp=sp: e.matmul(
            ps_sum.ap[0:1, 0:w], ones_col.ap[:, 0:1], xb.ap[:, 0:w],
            start=st, stop=sp), st, sp)
        kb.accum_matmul(ps_sq, ones_col, sq, lambda e, w=w, st=st, sp=sp: e.matmul(
            ps_sq.ap[0:1, 0:w], ones_col.ap[:, 0:1], sq.ap[:, 0:w],
            start=st, stop=sp), st, sp)
    kb.v(lambda e, w=w: e.tensor_scalar(out=mu.ap[:, 0:w], in0=ps_sum.ap[0:1, 0:w],
                                        scalar1=1.0 / nD_, scalar2=None, op0=ALU.mult),
         reads=[ps_sum], writes=[mu])
    kb.v(lambda e, w=w: e.tensor_mul(out=rstd.ap[:, 0:w], in0=mu.ap[:, 0:w],
                                     in1=mu.ap[:, 0:w]), reads=[mu], writes=[rstd])
    kb.v(lambda e, w=w: e.scalar_tensor_tensor(
        out=rstd.ap[:, 0:w], in0=ps_sq.ap[0:1, 0:w], scalar=1.0 / nD_,
        in1=rstd.ap[:, 0:w], op0=ALU.mult, op1=ALU.subtract),
        reads=[ps_sq, rstd], writes=[rstd])
    epsb = cs["epsb"]
    kb.s(lambda e, w=w: e.activation(out=rstd.ap[:, 0:w], in_=rstd.ap[:, 0:w],
                                     func=AF.Ln, bias=epsb.ap[0:1, 0:1]),
         reads=[rstd, epsb], writes=[rstd])
    kb.s(lambda e, w=w: e.activation(out=rstd.ap[:, 0:w], in_=rstd.ap[:, 0:w],
                                     func=AF.Exp, scale=-0.5),
         reads=[rstd], writes=[rstd])
    kb.t(lambda e, w=w: e.matmul(ps_b.ap[:, 0:w], ones_row.ap[0:1, :],
                                 mu.ap[0:1, 0:w], start=True, stop=True),
         reads=[mu, ones_row], writes=[ps_b])
    kb.v(lambda e, w=w: e.tensor_copy(out=mub.ap[:, 0:w], in_=ps_b.ap[:, 0:w]),
         reads=[ps_b], writes=[mub])
    kb.t(lambda e, w=w: e.matmul(ps_b2.ap[:, 0:w], ones_row.ap[0:1, :],
                                 rstd.ap[0:1, 0:w], start=True, stop=True),
         reads=[rstd, ones_row], writes=[ps_b2])
    kb.v(lambda e, w=w: e.tensor_copy(out=rstdb.ap[:, 0:w], in_=ps_b2.ap[:, 0:w]),
         reads=[ps_b2], writes=[rstdb])
    for i, xT in enumerate(xtiles):
        ob = otiles[i]
        kb.v(lambda e, xT=xT, w=w: e.tensor_sub(out=tmp.ap[:, 0:w], in0=xT.ap[:],
                                                in1=mub.ap[:, 0:w]),
             reads=[xT, mub], writes=[tmp])
        if gamma is None:
            kb.v(lambda e, ob=ob, w=w: e.tensor_mul(out=ob.ap[:], in0=tmp.ap[:, 0:w],
                                                    in1=rstdb.ap[:, 0:w]),
                 reads=[tmp, rstdb], writes=[ob])
        else:
            kb.v(lambda e, w=w: e.tensor_mul(out=tmp.ap[:, 0:w], in0=tmp.ap[:, 0:w],
                                             in1=rstdb.ap[:, 0:w]),
                 reads=[tmp, rstdb], writes=[tmp])
            g_, b_ = gamma[i], beta[i]
            kb.v(lambda e, ob=ob, g_=g_, b_=b_, w=w: e.tensor_scalar(
                out=ob.ap[:], in0=tmp.ap[:, 0:w], scalar1=g_.ap[:, 0:1],
                scalar2=b_.ap[:, 0:1], op0=ALU.mult, op1=ALU.add),
                reads=[tmp, g_, b_], writes=[ob])



def build_launch_A(S_=8192, T_=T):
    nc = bass.Bass("TRN2", target_bir_lowering=False)
    XCTX = nc.dram_tensor("XCTX", [D, S_], F32, kind="ExternalInput").ap()
    WDQ = nc.dram_tensor("WDQ", [D, 512], F32, kind="ExternalInput").ap()
    WUQ = nc.dram_tensor("WUQ", [512, D], F32, kind="ExternalInput").ap()
    WDKV = nc.dram_tensor("WDKV", [D, 512], F32, kind="ExternalInput").ap()
    WUKV = nc.dram_tensor("WUKV", [512, 2 * D], F32, kind="ExternalInput").ap()
    WOT = nc.dram_tensor("WOT", [D, D], F32, kind="ExternalInput").ap()
    QLNG = nc.dram_tensor("QLNG", [512, 1], F32, kind="ExternalInput").ap()
    QLNB = nc.dram_tensor("QLNB", [512, 1], F32, kind="ExternalInput").ap()
    KVLNG = nc.dram_tensor("KVLNG", [512, 1], F32, kind="ExternalInput").ap()
    KVLNB = nc.dram_tensor("KVLNB", [512, 1], F32, kind="ExternalInput").ap()
    NJ = T_ // 512
    NI = S_ // 128
    MASKS = nc.dram_tensor("MASKS", [NJ, NI, 128, 512], BF16, kind="ExternalInput").ap()
    IDXQ = nc.dram_tensor("IDXQ", [1, 1], mybir.dt.int32, kind="ExternalInput").ap()
    ONESC = nc.dram_tensor("ONESC", [128, 1], BF16, kind="ExternalInput").ap()
    ONESR = nc.dram_tensor("ONESR", [1, 128], F32, kind="ExternalInput").ap()
    R1 = nc.dram_tensor("R1", [D, T_], F32, kind="ExternalOutput").ap()
    CQ_T = nc.dram_tensor("cq_t", [512, S_], BF16, kind="Internal").ap()
    CKV_T = nc.dram_tensor("ckv_t", [512, S_], BF16, kind="Internal").ap()
    KT_S = nc.dram_tensor("kt_s", [D, S_], BF16, kind="Internal").ap()
    VT_S = nc.dram_tensor("vt_s", [S_, D], BF16, kind="Internal").ap()
    QT_S = nc.dram_tensor("qt_s", [D, T_], BF16, kind="Internal").ap()
    AOT_S = nc.dram_tensor("aot_s", [D, T_], BF16, kind="Internal").ap()

    xctxv = XCTX.rearrange("(t p) n -> p t n", p=128)
    r1v = R1.rearrange("(t p) n -> p t n", p=128)
    cqv = CQ_T.rearrange("(t p) n -> p t n", p=128)
    ckvv = CKV_T.rearrange("(t p) n -> p t n", p=128)
    ktv = KT_S.rearrange("(t p) n -> p t n", p=128)
    vtv = VT_S.rearrange("(t p) f -> p t f", p=128)
    qtv = QT_S.rearrange("(t p) n -> p t n", p=128)
    aotv = AOT_S.rearrange("(t p) n -> p t n", p=128)
    NCH = S_ // 512
    NOC = T_ // 512

    from contextlib import ExitStack
    with ExitStack() as ctx:
        kb = KB(nc)
        kb.init_sems(ctx)
        E = ctx.enter_context
        ones_col = Buf(E(nc.sbuf_tensor("onesc", [128, 1], BF16)))
        ones_row = Buf(E(nc.sbuf_tensor("onesr", [1, 128], F32)))
        kb.dma("y", lambda e: e.dma_start(out=ones_col.ap[:], in_=ONESC), writes=[ones_col])
        kb.dma("y", lambda e: e.dma_start(out=ones_row.ap[:], in_=ONESR), writes=[ones_row])
        idxq = Buf(E(nc.sbuf_tensor("idxq", [1, 1], mybir.dt.int32)))
        kb.dma("y", lambda e: e.dma_start(out=idxq.ap[:], in_=IDXQ), writes=[idxq])

        scr_cq = Buf(None)
        scr_ckv = Buf(None)
        scr_kt = Buf(None)
        scr_vt = Buf(None)
        scr_qt = Buf(None)
        scr_aot = Buf(None)

        with ExitStack() as sub:
            SE = sub.enter_context
            wdq = Buf(SE(nc.sbuf_tensor("wdq", [128, 16 * 512], BF16)))
            wdkv = Buf(SE(nc.sbuf_tensor("wdkv", [128, 16 * 512], BF16)))
            kb.dma("p", lambda e: e.dma_start(
                out=wdq.ap[:].rearrange("p (t f) -> p t f", f=512),
                in_=WDQ.rearrange("(t p) f -> p t f", p=128)), writes=[wdq])
            kb.dma("p", lambda e: e.dma_start(
                out=wdkv.ap[:].rearrange("p (t f) -> p t f", f=512),
                in_=WDKV.rearrange("(t p) f -> p t f", p=128)), writes=[wdkv])
            qg = Buf(SE(nc.sbuf_tensor("qg", [128, 4], F32)))
            qb = Buf(SE(nc.sbuf_tensor("qb", [128, 4], F32)))
            kg = Buf(SE(nc.sbuf_tensor("kg", [128, 4], F32)))
            kbta = Buf(SE(nc.sbuf_tensor("kbta", [128, 4], F32)))
            for b, t_ in ((qg, QLNG), (qb, QLNB), (kg, KVLNG), (kbta, KVLNB)):
                kb.dma("y", lambda e, b=b, t_=t_: e.dma_start(
                    out=b.ap[:].unsqueeze(2),
                    in_=t_.rearrange("(t p) o -> p t o", p=128),
                    allow_slow_non_contiguous=True), writes=[b])
            xc = Buf(SE(nc.sbuf_tensor("xc", [128, 16 * 512], F32)))
            xlnb = Buf(SE(nc.sbuf_tensor("xlnb", [128, 16 * 512], BF16)))
            cqkv = Buf(SE(nc.sbuf_tensor("cqkv", [128, 8 * 512], F32)))
            cqkvb = Buf(SE(nc.sbuf_tensor("cqkvb", [128, 8 * 512], BF16)))
            ps = Buf(SE(nc.psum_tensor("psA1", [128, 512], F32)))
            lnc = _ln_cache(kb, nc, SE, "lnA1", 512)
            lnc5 = _ln_cache(kb, nc, SE, "lnA1b", 512, eps=1e-5)
            for ch in range(NCH):
                c0 = ch * 512
                kb.dma("y", lambda e, c0=c0: e.dma_start(
                    out=xc.ap[:].rearrange("p (t n) -> p t n", n=512),
                    in_=xctxv[:, :, c0:c0 + 512]), writes=[xc])
                xts = [xc.view(xc.ap[:, i * 512:(i + 1) * 512]) for i in range(16)]
                xlts = [xlnb.view(xlnb.ap[:, i * 512:(i + 1) * 512]) for i in range(16)]
                _ln_chunk(kb, nc, SE, xts, xlts, ones_col, ones_row, 1e-5,
                          "", kb_cache=lnc)
                kb.merge_views(xc, xts)
                kb.merge_views(xlnb, xlts)
                for half, wd in ((0, wdq), (1, wdkv)):
                    for mt in range(4):
                        for i in range(16):
                            st, sp = (i == 0), (i == 15)
                            kb.accum_matmul(ps, wd, xlnb,
                                lambda e, i=i, mt=mt, wd=wd, st=st, sp=sp: e.matmul(
                                    ps.ap[:], wd.ap[:, i * 512 + mt * 128:i * 512 + (mt + 1) * 128],
                                    xlnb.ap[:, i * 512:(i + 1) * 512], start=st, stop=sp),
                                st, sp)
                        kb.v(lambda e, half=half, mt=mt: e.tensor_copy(
                            out=cqkv.ap[:, (half * 4 + mt) * 512:(half * 4 + mt + 1) * 512],
                            in_=ps.ap[:]), reads=[ps], writes=[cqkv])
                cqts = [cqkv.view(cqkv.ap[:, i * 512:(i + 1) * 512]) for i in range(4)]
                cqos = [cqkvb.view(cqkvb.ap[:, i * 512:(i + 1) * 512]) for i in range(4)]
                ckts = [cqkv.view(cqkv.ap[:, (4 + i) * 512:(5 + i) * 512]) for i in range(4)]
                ckos = [cqkvb.view(cqkvb.ap[:, (4 + i) * 512:(5 + i) * 512]) for i in range(4)]
                qgs = [qg.view(qg.ap[:, i:i + 1]) for i in range(4)]
                qbs = [qb.view(qb.ap[:, i:i + 1]) for i in range(4)]
                kgs = [kg.view(kg.ap[:, i:i + 1]) for i in range(4)]
                kbs = [kbta.view(kbta.ap[:, i:i + 1]) for i in range(4)]
                _ln_chunk(kb, nc, SE, cqts, cqos, ones_col, ones_row, 1e-5,
                          "", kb_cache=lnc5, gamma=qgs, beta=qbs)
                _ln_chunk(kb, nc, SE, ckts, ckos, ones_col, ones_row, 1e-5,
                          "", kb_cache=lnc5, gamma=kgs, beta=kbs)
                kb.merge_views(cqkv, cqts + ckts)
                kb.merge_views(cqkvb, cqos + ckos)
                kb.dma("y", lambda e, c0=c0: e.dma_start(
                    out=cqv[:, :, c0:c0 + 512],
                    in_=cqkvb.ap[:, 0:4 * 512].rearrange("p (t n) -> p t n", n=512)),
                    reads=[cqkvb], writes=[scr_cq])
                kb.dma("y", lambda e, c0=c0: e.dma_start(
                    out=ckvv[:, :, c0:c0 + 512],
                    in_=cqkvb.ap[:, 4 * 512:8 * 512].rearrange("p (t n) -> p t n", n=512)),
                    reads=[cqkvb], writes=[scr_ckv])
        kb.phase_barrier()

        with ExitStack() as sub:
            SE = sub.enter_context
            wukv = Buf(SE(nc.sbuf_tensor("wukv", [128, 4 * 4096], BF16)))
            wuq = Buf(SE(nc.sbuf_tensor("wuq", [128, 4 * 2048], BF16)))
            kb.dma("p", lambda e: e.dma_start(
                out=wukv.ap[:].rearrange("p (t f) -> p t f", f=4096),
                in_=WUKV.rearrange("(t p) f -> p t f", p=128)), writes=[wukv])
            kb.dma("p", lambda e: e.dma_start(
                out=wuq.ap[:].rearrange("p (t f) -> p t f", f=2048),
                in_=WUQ.rearrange("(t p) f -> p t f", p=128)), writes=[wuq])
            ckvc = Buf(SE(nc.sbuf_tensor("ckvc", [128, 4 * 512], BF16)))
            ev = Buf(SE(nc.sbuf_tensor("evA2", [128, 512], BF16)))
            ps = Buf(SE(nc.psum_tensor("psA2", [128, 512], F32)))
            for ch in range(NCH):
                c0 = ch * 512
                kb.dma("y", lambda e, c0=c0: e.dma_start(
                    out=ckvc.ap[:].rearrange("p (t n) -> p t n", n=512),
                    in_=ckvv[:, :, c0:c0 + 512]), writes=[ckvc])
                for kf in range(16):
                    for i in range(4):
                        st, sp = (i == 0), (i == 3)
                        kb.accum_matmul(ps, wukv, ckvc,
                            lambda e, i=i, kf=kf, st=st, sp=sp: e.matmul(
                                ps.ap[:], wukv.ap[:, i * 4096 + kf * 128:i * 4096 + (kf + 1) * 128],
                                ckvc.ap[:, i * 512:(i + 1) * 512], start=st, stop=sp), st, sp)
                    kb.v(lambda e: e.tensor_copy(out=ev.ap[:], in_=ps.ap[:]),
                         reads=[ps], writes=[ev])
                    kb.dma("y", lambda e, kf=kf, c0=c0: e.dma_start(
                        out=ktv[:, kf, c0:c0 + 512], in_=ev.ap[:]),
                        reads=[ev], writes=[scr_kt])
                for tt in range(4):
                    for vc in range(4):
                        for i in range(4):
                            st, sp = (i == 0), (i == 3)
                            kb.accum_matmul(ps, ckvc, wukv,
                                lambda e, i=i, tt=tt, vc=vc, st=st, sp=sp: e.matmul(
                                    ps.ap[:], ckvc.ap[:, i * 512 + tt * 128:i * 512 + (tt + 1) * 128],
                                    wukv.ap[:, i * 4096 + 2048 + vc * 512:i * 4096 + 2048 + (vc + 1) * 512],
                                    start=st, stop=sp), st, sp)
                        kb.v(lambda e: e.tensor_copy(out=ev.ap[:], in_=ps.ap[:]),
                             reads=[ps], writes=[ev])
                        kb.dma("y", lambda e, ch=ch, tt=tt, vc=vc: e.dma_start(
                            out=vtv[:, ch * 4 + tt, vc * 512:(vc + 1) * 512], in_=ev.ap[:]),
                            reads=[ev], writes=[scr_vt])
            regq = kb.value_load("y", idxq)
            for oc in range(NOC):
                kb.dma("y", lambda e, oc=oc: e.dma_start(
                    out=ckvc.ap[:].rearrange("p (t n) -> p t n", n=512),
                    in_=cqv[:, :, bass.ts(regq, T_)][:, :, oc * 512:(oc + 1) * 512]),
                    reads=[idxq], writes=[ckvc], extra=list(scr_cq.w))
                for qf in range(16):
                    for i in range(4):
                        st, sp = (i == 0), (i == 3)
                        kb.accum_matmul(ps, wuq, ckvc,
                            lambda e, i=i, qf=qf, st=st, sp=sp: e.matmul(
                                ps.ap[:], wuq.ap[:, i * 2048 + qf * 128:i * 2048 + (qf + 1) * 128],
                                ckvc.ap[:, i * 512:(i + 1) * 512], start=st, stop=sp), st, sp)
                    kb.v(lambda e: e.tensor_copy(out=ev.ap[:], in_=ps.ap[:]),
                         reads=[ps], writes=[ev])
                    kb.dma("y", lambda e, qf=qf, oc=oc: e.dma_start(
                        out=qtv[:, qf, oc * 512:(oc + 1) * 512], in_=ev.ap[:]),
                        reads=[ev], writes=[scr_qt])
        kb.phase_barrier()

        with ExitStack() as sub:
            SE = sub.enter_context
            maskb = Buf(SE(nc.sbuf_tensor("maskb", [128, NI * 512], BF16)))
            ktb = Buf(SE(nc.sbuf_tensor("ktb", [128, S_], BF16)))
            vtb = Buf(SE(nc.sbuf_tensor("vtb", [128, NI * 128], BF16)))
            qjb = Buf(SE(nc.sbuf_tensor("qjb", [128, 512], BF16)))
            pt = Buf(SE(nc.sbuf_tensor("ptA", [128, 1024], BF16)))
            ps_sc = Buf(SE(nc.psum_tensor("ps_sc", [128, 1024], F32)))
            ps_pv = Buf(SE(nc.psum_tensor("ps_pv", [128, 512], F32)))
            ps_dn = Buf(SE(nc.psum_tensor("ps_dn", [1, 512], F32)))
            ps_db = Buf(SE(nc.psum_tensor("ps_db", [128, 512], F32)))
            dnrow = Buf(SE(nc.sbuf_tensor("dnrow", [1, 512], F32)))
            dnb = Buf(SE(nc.sbuf_tensor("dnb", [128, 512], F32)))
            aoj = Buf(SE(nc.sbuf_tensor("aoj", [128, 512], BF16)))
            SCALE = 1.0 / float(np.sqrt(128.0))
            for j in range(NJ):
                kb.dma("y", lambda e, j=j: e.dma_start(
                    out=maskb.ap[:].rearrange("p (i n) -> p i n", n=512),
                    in_=MASKS[j].rearrange("i p n -> p i n")), writes=[maskb])
                for h in range(16):
                    kb.dma("y", lambda e, h=h: e.dma_start(
                        out=ktb.ap[:].rearrange("p (t n) -> p t n", n=512),
                        in_=ktv[:, h, :].rearrange("p (t n) -> p t n", n=512)),
                        writes=[ktb], extra=list(scr_kt.w))
                    kb.dma("s", lambda e, h=h: e.dma_start(
                        out=vtb.ap[:].rearrange("p (t f) -> p t f", f=128),
                        in_=vtv[:, :, h * 128:(h + 1) * 128]),
                        writes=[vtb], extra=list(scr_vt.w))
                    kb.dma("s", lambda e, h=h, j=j: e.dma_start(
                        out=qjb.ap[:], in_=qtv[:, h, j * 512:(j + 1) * 512]),
                        writes=[qjb], extra=list(scr_qt.w))
                    for ip in range(NI // 2):
                        for half in range(2):
                            i = 2 * ip + half
                            kb.t(lambda e, i=i, half=half: e.matmul(
                                ps_sc.ap[:, half * 512:(half + 1) * 512],
                                ktb.ap[:, i * 128:(i + 1) * 128], qjb.ap[:],
                                start=True, stop=True),
                                reads=[ktb, qjb], writes=[ps_sc] if half == 1 else [],
                                ) if half == 1 else kb.accum_matmul(
                                ps_sc, ktb, qjb,
                                lambda e, i=i, half=half: e.matmul(
                                    ps_sc.ap[:, half * 512:(half + 1) * 512],
                                    ktb.ap[:, i * 128:(i + 1) * 128], qjb.ap[:],
                                    start=True, stop=True), True, False)
                        kb.s(lambda e: e.activation(out=pt.ap[:], in_=ps_sc.ap[:],
                                                    func=AF.Exp, scale=SCALE),
                             reads=[ps_sc], writes=[pt])
                        kb.v(lambda e, ip=ip: e.tensor_mul(
                            out=pt.ap[:], in0=pt.ap[:],
                            in1=maskb.ap[:, ip * 1024:(ip + 1) * 1024]),
                            reads=[maskb, pt], writes=[pt])
                        for half in range(2):
                            i = 2 * ip + half
                            st, sp = (i == 0), (i == NI - 1)
                            kb.accum_matmul(ps_pv, vtb, pt,
                                lambda e, i=i, half=half, st=st, sp=sp: e.matmul(
                                    ps_pv.ap[:], vtb.ap[:, i * 128:(i + 1) * 128],
                                    pt.ap[:, half * 512:(half + 1) * 512],
                                    start=st, stop=sp), st, sp)
                            kb.accum_matmul(ps_dn, ones_col, pt,
                                lambda e, half=half, st=st, sp=sp: e.matmul(
                                    ps_dn.ap[:], ones_col.ap[:, 0:1],
                                    pt.ap[:, half * 512:(half + 1) * 512],
                                    start=st, stop=sp), st, sp)
                    kb.v(lambda e: e.reciprocal(out=dnrow.ap[:], in_=ps_dn.ap[0:1, :]),
                         reads=[ps_dn], writes=[dnrow])
                    kb.t(lambda e: e.matmul(ps_db.ap[:], ones_row.ap[0:1, :],
                                            dnrow.ap[0:1, :], start=True, stop=True),
                         reads=[dnrow, ones_row], writes=[ps_db])
                    kb.v(lambda e: e.tensor_copy(out=dnb.ap[:], in_=ps_db.ap[:]),
                         reads=[ps_db], writes=[dnb])
                    kb.v(lambda e: e.tensor_mul(out=aoj.ap[:], in0=ps_pv.ap[:],
                                                in1=dnb.ap[:]),
                         reads=[ps_pv, dnb], writes=[aoj])
                    kb.dma("y", lambda e, h=h, j=j: e.dma_start(
                        out=aotv[:, h, j * 512:(j + 1) * 512], in_=aoj.ap[:]),
                        reads=[aoj], writes=[scr_aot])
        kb.phase_barrier()

        with ExitStack() as sub:
            SE = sub.enter_context
            aoc = Buf(SE(nc.sbuf_tensor("aoc", [128, 16 * 512], BF16)))
            wotb = Buf(SE(nc.sbuf_tensor("wotb", [128, 16 * 2048], BF16)))
            outc = Buf(SE(nc.sbuf_tensor("outcA", [128, 16 * 512], F32)))
            ps = Buf(SE(nc.psum_tensor("psA3", [128, 512], F32)))
            lnc = _ln_cache(kb, nc, SE, "lnA3", 512)
            wotv = WOT.rearrange("(t p) f -> p t f", p=128)
            kb.dma("p", lambda e: e.dma_start(
                out=wotb.ap[:].rearrange("p (t f) -> p t f", f=2048),
                in_=wotv), writes=[wotb])
            r1buf = Buf(None)
            for oc in range(NOC):
                c0 = oc * 512
                kb.dma("y", lambda e, c0=c0: e.dma_start(
                    out=aoc.ap[:].rearrange("p (t n) -> p t n", n=512),
                    in_=aotv[:, :, c0:c0 + 512]),
                    writes=[aoc], extra=list(scr_aot.w))
                for j in range(16):
                    for i in range(16):
                        st, sp = (i == 0), (i == 15)
                        kb.accum_matmul(ps, wotb, aoc,
                            lambda e, i=i, j=j, st=st, sp=sp: e.matmul(
                                ps.ap[:], wotb.ap[:, i * 2048 + j * 128:i * 2048 + (j + 1) * 128],
                                aoc.ap[:, i * 512:(i + 1) * 512], start=st, stop=sp), st, sp)
                    kb.v(lambda e, j=j: e.tensor_copy(out=outc.ap[:, j * 512:(j + 1) * 512],
                                                      in_=ps.ap[:]),
                         reads=[ps], writes=[outc])
                otiles = [outc.view(outc.ap[:, i * 512:(i + 1) * 512]) for i in range(16)]
                _ln_chunk(kb, nc, SE, otiles, otiles, ones_col, ones_row, 1e-5,
                          "", kb_cache=lnc)
                kb.merge_views(outc, otiles)
                kb.dma("y", lambda e, c0=c0: e.dma_start(
                    out=r1v[:, :, c0:c0 + 512],
                    in_=outc.ap[:].rearrange("p (t n) -> p t n", n=512)),
                    reads=[outc], writes=[r1buf])
                regq2 = kb.value_load("p", idxq)
                kb.dma("p", lambda e, c0=c0, regq2=regq2: e.dma_start(
                    out=r1v[:, :, c0:c0 + 512],
                    in_=xctxv[:, :, bass.ts(regq2, T_)][:, :, c0:c0 + 512],
                    accum_op=ALU.add),
                    reads=[idxq], writes=[r1buf])
        kb.final_wait("y")
    return nc



CH = 128
EXT = 128


def build_launch_C(T_=T):
    TE = T_ + EXT
    NCHT = TE // 128
    nc = bass.Bass("TRN2", target_bir_lowering=False)
    R2E = nc.dram_tensor("R2E", [D, TE], F32, kind="ExternalInput").ap()
    WINT = nc.dram_tensor("WINT", [D, 8384], F32, kind="ExternalInput").ap()
    CONVW = nc.dram_tensor("CONVW", [4224, 4], F32, kind="ExternalInput").ap()
    CONVB = nc.dram_tensor("CONVB", [4224, 1], F32, kind="ExternalInput").ap()
    DTB = nc.dram_tensor("DTB", [64, 1], F32, kind="ExternalInput").ap()
    ALOG = nc.dram_tensor("ALOG", [64, 1], F32, kind="ExternalInput").ap()
    DROW = nc.dram_tensor("DROW", [1, 4096], F32, kind="ExternalInput").ap()
    TRI = nc.dram_tensor("TRI", [128, 128], F32, kind="ExternalInput").ap()
    EYE = nc.dram_tensor("EYE", [128, 128], BF16, kind="ExternalInput").ap()
    EYEF = nc.dram_tensor("EYEF", [128, 128], F32, kind="ExternalInput").ap()
    ONESC = nc.dram_tensor("ONESC", [128, 1], BF16, kind="ExternalInput").ap()
    ONESR = nc.dram_tensor("ONESR", [1, 128], F32, kind="ExternalInput").ap()
    YP = nc.dram_tensor("YP", [T_, 4096], F32, kind="ExternalOutput").ap()
    ZS = nc.dram_tensor("ZS", [T_, 4096], BF16, kind="ExternalOutput").ap()
    CS = nc.dram_tensor("CS", [64, T_], BF16, kind="ExternalOutput").ap()
    AC = nc.dram_tensor("AC", [64, T_], F32, kind="ExternalOutput").ap()
    UEND = nc.dram_tensor("UEND", [64, 4096], F32, kind="ExternalOutput").ap()
    ATOT = nc.dram_tensor("ATOT", [64, 1], F32, kind="ExternalOutput").ap()
    XBCS = nc.dram_tensor("xbcs", [4224, TE], BF16, kind="Internal").ap()

    r2v = R2E.rearrange("(t p) n -> p t n", p=128)
    wintv = WINT.rearrange("(t p) f -> p t f", p=128)
    xbv = XBCS.rearrange("(t p) n -> p t n", p=128)
    ypv = YP.rearrange("(t p) f -> p t f", p=128)
    zsv = ZS.rearrange("(t p) f -> p t f", p=128)

    from contextlib import ExitStack
    with ExitStack() as ctx:
        kb = KB(nc)
        kb.init_sems(ctx)
        E = ctx.enter_context
        ones_col = Buf(E(nc.sbuf_tensor("onesc", [128, 1], BF16)))
        ones_row = Buf(E(nc.sbuf_tensor("onesr", [1, 128], F32)))
        tri = Buf(E(nc.sbuf_tensor("tri", [128, 128], F32)))
        eyeb = Buf(E(nc.sbuf_tensor("eyeb", [128, 128], BF16)))
        eyef = Buf(E(nc.sbuf_tensor("eyef", [128, 128], F32)))
        for b, t_ in ((ones_col, ONESC), (ones_row, ONESR), (tri, TRI),
                      (eyeb, EYE), (eyef, EYEF)):
            kb.dma("y", lambda e, b=b, t_=t_: e.dma_start(out=b.ap[:], in_=t_), writes=[b])
        dtbc = Buf(E(nc.sbuf_tensor("dtbc", [64, 1], F32)))
        alogc = Buf(E(nc.sbuf_tensor("alogc", [64, 1], F32)))
        kb.dma("y", lambda e: e.dma_start(out=dtbc.ap[:], in_=DTB), writes=[dtbc])
        kb.dma("y", lambda e: e.dma_start(out=alogc.ap[:], in_=ALOG), writes=[alogc])

        dtv = Buf(E(nc.sbuf_tensor("dtv", [64, TE], F32)))
        acum = Buf(E(nc.sbuf_tensor("acum", [64, TE], F32)))
        one64 = kb.const_col(ctx, 1.0, 64)
        zeros_chunk = Buf(E(nc.sbuf_tensor("zchk", [64, CH], F32)))
        kb.v(lambda e: e.memset(zeros_chunk.ap[:], 0.0), writes=[zeros_chunk])
        scr_xbc = Buf(None)
        out_zs = Buf(None)
        out_cs = Buf(None)
        out_ac = Buf(None)
        out_yp = Buf(None)
        out_ue = Buf(None)
        out_at = Buf(None)

        with ExitStack() as sub:
            SE = sub.enter_context
            r2c = Buf(SE(nc.sbuf_tensor("r2c", [128, 16 * 512], F32)))
            xlnT = Buf(SE(nc.sbuf_tensor("xlnT", [128, 16 * TE], BF16)))
            lnc = _ln_cache(kb, nc, SE, "lnC", 512)
            c0 = 0
            while c0 < TE:
                w = min(512, TE - c0)
                kb.dma("y", lambda e, c0=c0, w=w: e.dma_start(
                    out=r2c.ap[:, 0:16 * w].rearrange("p (t n) -> p t n", n=w),
                    in_=r2v[:, :, c0:c0 + w]), writes=[r2c])
                xts = [r2c.view(r2c.ap[:, i * w:(i + 1) * w]) for i in range(16)]
                ots = [xlnT.view(xlnT.ap[:, i * TE + c0:i * TE + c0 + w]) for i in range(16)]
                _ln_chunk(kb, nc, SE, xts, ots, ones_col, ones_row, 1e-5,
                          "", kb_cache=lnc)
                kb.merge_views(r2c, xts)
                kb.merge_views(xlnT, ots)
                c0 += w
            winb = [Buf(SE(nc.sbuf_tensor(f"winbC{i}", [128, 16 * 256], BF16))) for i in range(2)]
            ps = Buf(SE(nc.psum_tensor("psC2", [128, 512], F32)))
            evb = Buf(SE(nc.sbuf_tensor("evC2", [128, 512], BF16)))
            for zc in range(16):
                wbf = winb[zc % 2]
                kb.dma("p", lambda e, zc=zc, wbf=wbf: e.dma_start(
                    out=wbf.ap[:].rearrange("p (t f) -> p t f", f=256),
                    in_=wintv[:, :, zc * 256:(zc + 1) * 256]), writes=[wbf])
                for tt in range(T_ // 128):
                    for i in range(16):
                        st, sp = (i == 0), (i == 15)
                        kb.accum_matmul(ps, xlnT, wbf,
                            lambda e, i=i, tt=tt, wbf=wbf, st=st, sp=sp: e.matmul(
                                ps.ap[:, 0:256], xlnT.ap[:, i * TE + EXT + tt * 128:i * TE + EXT + (tt + 1) * 128],
                                wbf.ap[:, i * 256:(i + 1) * 256], start=st, stop=sp), st, sp)
                    kb.v(lambda e: e.tensor_copy(out=evb.ap[:, 0:256], in_=ps.ap[:, 0:256]),
                         reads=[ps], writes=[evb])
                    kb.dma("y", lambda e, tt=tt, zc=zc: e.dma_start(
                        out=zsv[:, tt, zc * 256:(zc + 1) * 256], in_=evb.ap[:, 0:256]),
                        reads=[evb], writes=[out_zs])
            winf = [Buf(SE(nc.sbuf_tensor(f"winfC{i}", [128, 16 * 128], BF16))) for i in range(2)]
            evf = Buf(SE(nc.sbuf_tensor("evfC", [128, TE], BF16)))
            for fc in range(33):
                wbf = winf[fc % 2]
                kb.dma("p", lambda e, fc=fc, wbf=wbf: e.dma_start(
                    out=wbf.ap[:].rearrange("p (t f) -> p t f", f=128),
                    in_=wintv[:, :, 4096 + fc * 128:4096 + (fc + 1) * 128]), writes=[wbf])
                c0 = 0
                while c0 < TE:
                    w = min(512, TE - c0)
                    for i in range(16):
                        st, sp = (i == 0), (i == 15)
                        kb.accum_matmul(ps, wbf, xlnT,
                            lambda e, i=i, c0=c0, w=w, wbf=wbf, st=st, sp=sp: e.matmul(
                                ps.ap[:, 0:w], wbf.ap[:, i * 128:(i + 1) * 128],
                                xlnT.ap[:, i * TE + c0:i * TE + c0 + w], start=st, stop=sp), st, sp)
                    kb.v(lambda e, c0=c0, w=w: e.tensor_copy(out=evf.ap[:, c0:c0 + w],
                                                             in_=ps.ap[:, 0:w]),
                         reads=[ps], writes=[evf])
                    c0 += w
                kb.dma("y", lambda e, fc=fc: e.dma_start(
                    out=xbv[:, fc, :], in_=evf.ap[:]), reads=[evf], writes=[scr_xbc])
            wdt = Buf(SE(nc.sbuf_tensor("wdtC", [128, 16 * 64], BF16)))
            kb.dma("p", lambda e: e.dma_start(
                out=wdt.ap[:].rearrange("p (t f) -> p t f", f=64),
                in_=wintv[:, :, 8320:8384]), writes=[wdt])
            ps64 = Buf(SE(nc.psum_tensor("ps64C", [64, 512], F32)))
            c0 = 0
            while c0 < TE:
                w = min(512, TE - c0)
                for i in range(16):
                    st, sp = (i == 0), (i == 15)
                    kb.accum_matmul(ps64, wdt, xlnT,
                        lambda e, i=i, c0=c0, w=w, st=st, sp=sp: e.matmul(
                            ps64.ap[:, 0:w], wdt.ap[:, i * 64:(i + 1) * 64],
                            xlnT.ap[:, i * TE + c0:i * TE + c0 + w], start=st, stop=sp), st, sp)
                kb.v(lambda e, c0=c0, w=w: e.tensor_copy(out=dtv.ap[:, c0:c0 + w],
                                                         in_=ps64.ap[:, 0:w]),
                     reads=[ps64], writes=[dtv])
                c0 += w
        kb.phase_barrier()

        with ExitStack() as sub:
            SE = sub.enter_context
            tmpa = Buf(SE(nc.sbuf_tensor("tmpaC", [64, TE], F32)))
            kb.v(lambda e: e.tensor_scalar(out=dtv.ap[:], in0=dtv.ap[:],
                                           scalar1=dtbc.ap[:, 0:1], scalar2=None,
                                           op0=ALU.add), reads=[dtbc, dtv], writes=[dtv])
            kb.s(lambda e: e.activation(out=tmpa.ap[:], in_=dtv.ap[:], func=AF.Abs),
                 reads=[dtv], writes=[tmpa])
            kb.s(lambda e: e.activation(out=tmpa.ap[:], in_=tmpa.ap[:], func=AF.Exp,
                                        scale=-1.0), reads=[tmpa], writes=[tmpa])
            kb.s(lambda e: e.activation(out=tmpa.ap[:], in_=tmpa.ap[:], func=AF.Ln,
                                        bias=one64.ap[:, 0:1]),
                 reads=[tmpa, one64], writes=[tmpa])
            kb.v(lambda e: e.tensor_scalar(out=dtv.ap[:], in0=dtv.ap[:], scalar1=0.0,
                                           scalar2=None, op0=ALU.max),
                 reads=[dtv], writes=[dtv])
            kb.v(lambda e: e.tensor_add(out=dtv.ap[:], in0=dtv.ap[:], in1=tmpa.ap[:]),
                 reads=[tmpa, dtv], writes=[dtv])
            acol = Buf(SE(nc.sbuf_tensor("acolC", [64, 1], F32)))
            kb.s(lambda e: e.activation(out=acol.ap[:], in_=alogc.ap[:], func=AF.Exp),
                 reads=[alogc], writes=[acol])
            kb.v(lambda e: e.tensor_scalar(out=acol.ap[:], in0=acol.ap[:], scalar1=-1.0,
                                           scalar2=None, op0=ALU.mult),
                 reads=[acol], writes=[acol])
            kb.v(lambda e: e.tensor_scalar(out=tmpa.ap[:], in0=dtv.ap[:],
                                           scalar1=acol.ap[:, 0:1], scalar2=None,
                                           op0=ALU.mult), reads=[dtv, acol], writes=[tmpa])
            for chk in range(NCHT):
                kb.v(lambda e, chk=chk: e.tensor_tensor_scan(
                    out=acum.ap[:, chk * CH:(chk + 1) * CH],
                    data0=tmpa.ap[:, chk * CH:(chk + 1) * CH],
                    data1=zeros_chunk.ap[:], initial=0.0,
                    op0=ALU.add, op1=ALU.add),
                    reads=[tmpa, zeros_chunk], writes=[acum])
            kb.dma("y", lambda e: e.dma_start(out=AC, in_=acum.ap[:, EXT:]),
                   reads=[acum], writes=[out_ac])
            ach = Buf(SE(nc.sbuf_tensor("achC", [64, T_ // 128], F32)))
            atotb = Buf(SE(nc.sbuf_tensor("atotC", [64, 1], F32)))
            kb.v(lambda e: e.tensor_copy(out=ach.ap[:],
                                         in_=acum.ap[:, EXT + CH - 1::CH]),
                 reads=[acum], writes=[ach])
            kb.v(lambda e: e.tensor_reduce(out=atotb.ap[:], in_=ach.ap[:],
                                           axis=AX.X, op=ALU.add),
                 reads=[ach], writes=[atotb])
            kb.dma("y", lambda e: e.dma_start(out=ATOT, in_=atotb.ap[:]),
                   reads=[atotb], writes=[out_at])
        kb.phase_barrier()

        ACRS = nc.dram_tensor("acrs", [64, CH], F32, kind="Internal").ap()
        with ExitStack() as sub:
            SE = sub.enter_context
            drow = Buf(SE(nc.sbuf_tensor("drow", [1, 4096], F32)))
            kb.dma("y", lambda e: e.dma_start(out=drow.ap[:], in_=DROW), writes=[drow])
            dbc = Buf(SE(nc.sbuf_tensor("dbc", [128, 4096], F32)))
            ps_bcD = Buf(SE(nc.psum_tensor("ps_bcC", [128, 512], F32)))
            for g in range(8):
                kb.t(lambda e, g=g: e.matmul(ps_bcD.ap[:], ones_row.ap[0:1, :],
                                             drow.ap[0:1, g * 512:(g + 1) * 512],
                                             start=True, stop=True),
                     reads=[drow, ones_row], writes=[ps_bcD])
                kb.v(lambda e, g=g: e.tensor_copy(out=dbc.ap[:, g * 512:(g + 1) * 512],
                                                  in_=ps_bcD.ap[:]), reads=[ps_bcD], writes=[dbc])
            convwb = Buf(SE(nc.sbuf_tensor("convwb", [128, 33 * 4], F32)))
            convbb = Buf(SE(nc.sbuf_tensor("convbb", [128, 33], F32)))
            kb.dma("y", lambda e: e.dma_start(
                out=convwb.ap[:].rearrange("p (t w) -> p t w", w=4),
                in_=CONVW.rearrange("(t p) w -> p t w", p=128),
                allow_slow_non_contiguous=True), writes=[convwb])
            kb.dma("y", lambda e: e.dma_start(
                out=convbb.ap[:].unsqueeze(2),
                in_=CONVB.rearrange("(t p) o -> p t o", p=128),
                allow_slow_non_contiguous=True), writes=[convbb])
            xbcw = Buf(SE(nc.sbuf_tensor("xbcw", [128, 17 * 516], BF16)))
            cvacc = Buf(SE(nc.sbuf_tensor("cvacc", [128, 512], F32)))
            convs = Buf(SE(nc.sbuf_tensor("convs", [128, 33 * 512], BF16)))
            xtokc = Buf(SE(nc.sbuf_tensor("xtokc", [128, 4096], BF16)))
            tmp512 = Buf(SE(nc.sbuf_tensor("tmp512C", [128, 512], F32)))
            ps_tr = Buf(SE(nc.psum_tensor("ps_trC", [128, 128], BF16)))
            ps_trf = Buf(SE(nc.psum_tensor("ps_trfC", [128, 128], F32)))
            ps_g = Buf(SE(nc.psum_tensor("ps_gC", [128, 128], F32)))
            ps_b2 = Buf(SE(nc.psum_tensor("ps_b2C", [128, 128], F32)))
            ps_y = Buf(SE(nc.psum_tensor("ps_yC", [128, 512], F32)))
            ps_st = Buf(SE(nc.psum_tensor("ps_stC", [64, 512], F32)))
            ps_bc2 = Buf(SE(nc.psum_tensor("ps_bc2C", [64, 512], F32)))
            dtT = Buf(SE(nc.sbuf_tensor("dtT", [128, 64], F32)))
            decS = Buf(SE(nc.sbuf_tensor("decS", [64, CH], F32)))
            decT = Buf(SE(nc.sbuf_tensor("decT", [128, 64], F32)))
            acumT = Buf(SE(nc.sbuf_tensor("acumT", [128, 64], F32)))
            acrow = Buf(SE(nc.sbuf_tensor("acrowC", [1, 8 * CH], F32)))
            exprow = Buf(SE(nc.sbuf_tensor("exprowC", [1, 8 * CH], BF16)))
            eachr = Buf(SE(nc.sbuf_tensor("eachrC", [1, 64], F32)))
            ones_rowb = Buf(SE(nc.sbuf_tensor("onesrbC", [1, 128], BF16)))
            kb.v(lambda e: e.memset(ones_rowb.ap[:], 1.0), writes=[ones_rowb])
            xdth = Buf(SE(nc.sbuf_tensor("xdth", [128, 64], BF16)))
            xdtd = Buf(SE(nc.sbuf_tensor("xdtd", [128, 64], BF16)))
            btok = Buf(SE(nc.sbuf_tensor("btok", [128, 64], BF16)))
            segS = Buf(SE(nc.sbuf_tensor("segS", [128, CH], F32)))
            wmat = Buf(SE(nc.sbuf_tensor("wmat", [128, CH], BF16)))
            cd = Buf(SE(nc.sbuf_tensor("cdC", [64, CH], BF16)))
            rtile = Buf(SE(nc.sbuf_tensor("rtile", [64, 4096], F32)))
            rbf = Buf(SE(nc.sbuf_tensor("rbfC", [64, 4096], BF16)))
            stt_s = Buf(SE(nc.sbuf_tensor("sttC", [64, 4096], F32)))
            ybuf = Buf(SE(nc.sbuf_tensor("ybuf", [128, 4096], F32)))
            cloc = Buf(SE(nc.sbuf_tensor("clocC", [64, 512], BF16)))
            acr_d = Buf(None)
            kb.v(lambda e: e.memset(rtile.ap[:], 0.0), writes=[rtile])

            for cc in range(T_ // 512):
                t0 = EXT + cc * 512
                for grp, (f0, nf) in enumerate(((0, 17), (17, 16))):
                    kb.dma("y", lambda e, t0=t0, f0=f0, nf=nf: e.dma_start(
                        out=xbcw.ap[:, 0:nf * 516].rearrange("p (t n) -> p t n", n=516),
                        in_=xbv[:, f0:f0 + nf, t0 - 4:t0 + 512]),
                        writes=[xbcw], extra=list(scr_xbc.w))
                    for lf in range(nf):
                        ft = f0 + lf
                        kb.v(lambda e, lf=lf, ft=ft: e.tensor_scalar(
                            out=cvacc.ap[:],
                            in0=xbcw.ap[:, lf * 516 + 1:lf * 516 + 513],
                            scalar1=convwb.ap[:, 4 * ft:4 * ft + 1],
                            scalar2=convbb.ap[:, ft:ft + 1],
                            op0=ALU.mult, op1=ALU.add),
                            reads=[xbcw, convwb, convbb], writes=[cvacc])
                        for w_ in range(1, 4):
                            kb.v(lambda e, lf=lf, ft=ft, w_=w_: e.scalar_tensor_tensor(
                                out=cvacc.ap[:],
                                in0=xbcw.ap[:, lf * 516 + 1 + w_:lf * 516 + 513 + w_],
                                scalar=convwb.ap[:, 4 * ft + w_:4 * ft + w_ + 1],
                                in1=cvacc.ap[:],
                                op0=ALU.mult, op1=ALU.add),
                                reads=[xbcw, convwb, cvacc], writes=[cvacc])
                        kb.s(lambda e, ft=ft: e.activation(
                            out=convs.ap[:, ft * 512:(ft + 1) * 512], in_=cvacc.ap[:],
                            func=AF.Silu), reads=[cvacc], writes=[convs])
                kb.dma("y", lambda e, cc=cc: e.dma_start(
                    out=CS[:, cc * 512:(cc + 1) * 512],
                    in_=convs.ap[64:128, 32 * 512:33 * 512]),
                    reads=[convs], writes=[out_cs])
                kb.v(lambda e: e.tensor_copy(out=cloc.ap[:],
                                             in_=convs.ap[64:128, 32 * 512:33 * 512]),
                     reads=[convs], writes=[cloc])

                for lt in range(4):
                    chg = cc * 4 + lt
                    s0 = lt * 128
                    a0 = EXT + chg * CH
                    acv = acum.view(acum.ap[:, a0:a0 + CH])
                    dtvv = dtv.view(dtv.ap[:, a0:a0 + CH])
                    bmat = convs.view(convs.ap[0:64, 32 * 512 + s0:32 * 512 + s0 + 128])
                    cmat = cloc.view(cloc.ap[:, s0:s0 + 128])
                    for ft in range(32):
                        kb.t(lambda e, ft=ft, s0=s0: e.transpose(
                            ps_tr.ap[:], convs.ap[:, ft * 512 + s0:ft * 512 + s0 + 128],
                            eyeb.ap[:]), reads=[convs, eyeb], writes=[ps_tr])
                        kb.v(lambda e, ft=ft: e.tensor_copy(
                            out=xtokc.ap[:, ft * 128:(ft + 1) * 128],
                            in_=ps_tr.ap[:]), reads=[ps_tr], writes=[xtokc])
                    xt = xtokc
                    kb.v(lambda e: e.tensor_copy(out=rbf.ap[:], in_=rtile.ap[:]),
                         reads=[rtile], writes=[rbf])
                    kb.dma("y", lambda e: e.dma_start(out=ACRS, in_=acv.ap[:]),
                           reads=[acv], writes=[acr_d])
                    kb.dma("y", lambda e: e.dma_start(
                        out=eachr.ap[:],
                        in_=ACRS.rearrange("h l -> (h l)").unsqueeze(0)[:, CH - 1::CH],
                        allow_slow_non_contiguous=True),
                        reads=[acr_d], writes=[eachr], extra=list(acr_d.w))
                    kb.s(lambda e: e.activation(out=eachr.ap[:], in_=eachr.ap[:],
                                                func=AF.Exp), reads=[eachr], writes=[eachr])
                    kb.t(lambda e: e.transpose(ps_trf.ap[:, 0:64], dtvv.ap[:], eyef.ap[0:64, 0:64]),
                         reads=[dtvv, eyef], writes=[ps_trf])
                    kb.v(lambda e: e.tensor_copy(out=dtT.ap[:], in_=ps_trf.ap[:, 0:64]),
                         reads=[ps_trf], writes=[dtT])
                    kb.t(lambda e: e.transpose(ps_trf.ap[:, 0:64], acv.ap[:], eyef.ap[0:64, 0:64]),
                         reads=[acv, eyef], writes=[ps_trf])
                    kb.v(lambda e: e.tensor_copy(out=acumT.ap[:], in_=ps_trf.ap[:, 0:64]),
                         reads=[ps_trf], writes=[acumT])
                    kb.v(lambda e: e.tensor_scalar(
                        out=decS.ap[:], in0=acv.ap[:], scalar1=acv.ap[:, CH - 1:CH],
                        scalar2=None, op0=ALU.subtract), reads=[acv], writes=[decS])
                    kb.s(lambda e: e.activation(out=decS.ap[:], in_=decS.ap[:],
                                                func=AF.Exp, scale=-1.0),
                         reads=[decS], writes=[decS])
                    kb.t(lambda e: e.transpose(ps_trf.ap[:, 0:64], decS.ap[:], eyef.ap[0:64, 0:64]),
                         reads=[decS, eyef], writes=[ps_trf])
                    kb.v(lambda e: e.tensor_copy(out=decT.ap[:], in_=ps_trf.ap[:, 0:64]),
                         reads=[ps_trf], writes=[decT])
                    kb.t(lambda e: e.transpose(ps_tr.ap[:, 0:64], bmat.ap[:], eyeb.ap[0:64, 0:64]),
                         reads=[bmat, eyeb], writes=[ps_tr])
                    kb.v(lambda e: e.tensor_copy(out=btok.ap[:], in_=ps_tr.ap[:, 0:64]),
                         reads=[ps_tr], writes=[btok])
                    kb.t(lambda e: e.matmul(ps_g.ap[:], bmat.ap[:], cmat.ap[:],
                                            start=True, stop=True),
                         reads=[bmat, cmat], writes=[ps_g])
                    for h in range(64):
                        hg = h % 8
                        if hg == 0:
                            kb.dma("y", lambda e, h=h: e.dma_start(
                                out=acrow.ap[:],
                                in_=ACRS.rearrange("h l -> (h l)").unsqueeze(0)[:, h * CH:(h + 8) * CH]),
                                reads=[acr_d], writes=[acrow], extra=list(acr_d.w))
                            kb.s(lambda e: e.activation(out=exprow.ap[:], in_=acrow.ap[:],
                                                        func=AF.Exp),
                                 reads=[acrow], writes=[exprow])
                        kb.t(lambda e, hg=hg: e.matmul(
                            ps_b2.ap[:], ones_row.ap[0:1, :],
                            acrow.ap[0:1, hg * CH:(hg + 1) * CH], start=True, stop=True),
                            reads=[acrow, ones_row], writes=[ps_b2])
                        kb.v(lambda e, h=h: e.tensor_scalar(
                            out=segS.ap[:], in0=ps_b2.ap[:], scalar1=acumT.ap[:, h:h + 1],
                            scalar2=None, op0=ALU.subtract),
                            reads=[ps_b2, acumT], writes=[segS])
                        kb.v(lambda e: e.tensor_add(out=segS.ap[:], in0=segS.ap[:],
                                                    in1=tri.ap[:]),
                             reads=[tri, segS], writes=[segS])
                        kb.s(lambda e: e.activation(out=segS.ap[:], in_=segS.ap[:],
                                                    func=AF.Exp), reads=[segS], writes=[segS])
                        kb.v(lambda e: e.tensor_tensor(out=wmat.ap[:], in0=segS.ap[:],
                                                       in1=ps_g.ap[:], op=ALU.mult),
                             reads=[segS, ps_g], writes=[wmat])
                        kb.v(lambda e, h=h: e.tensor_scalar(
                            out=xdth.ap[:], in0=xt.ap[:, h * 64:(h + 1) * 64],
                            scalar1=dtT.ap[:, h:h + 1], scalar2=None, op0=ALU.mult),
                            reads=[xt, dtT], writes=[xdth])
                        kb.v(lambda e, h=h: e.tensor_scalar(
                            out=xdtd.ap[:], in0=xdth.ap[:], scalar1=decT.ap[:, h:h + 1],
                            scalar2=None, op0=ALU.mult),
                            reads=[xdth, decT], writes=[xdtd])
                        kb.accum_matmul(ps_y, wmat, xdth,
                            lambda e, hg=hg: e.matmul(
                                ps_y.ap[:, hg * 64:(hg + 1) * 64], wmat.ap[:], xdth.ap[:],
                                start=True, stop=False), True, False)
                        kb.t(lambda e, hg=hg: e.matmul(
                            ps_bc2.ap[:, 0:CH], ones_rowb.ap[0:1, 0:64],
                            exprow.ap[0:1, hg * CH:(hg + 1) * CH],
                            start=True, stop=True),
                            reads=[exprow, ones_rowb], writes=[ps_bc2])
                        kb.v(lambda e: e.tensor_tensor(
                            out=cd.ap[:], in0=cmat.ap[:], in1=ps_bc2.ap[:, 0:CH],
                            op=ALU.mult),
                            reads=[cmat, ps_bc2], writes=[cd])
                        kb.accum_matmul(ps_y, cd, rbf,
                            lambda e, h=h, hg=hg: e.matmul(
                                ps_y.ap[:, hg * 64:(hg + 1) * 64], cd.ap[:],
                                rbf.ap[:, h * 64:(h + 1) * 64],
                                start=False, stop=True), False, True)
                        kb.t(lambda e, h=h, hg=hg: e.matmul(
                            ps_st.ap[:, hg * 64:(hg + 1) * 64], btok.ap[:], xdtd.ap[:],
                            start=True, stop=True),
                            reads=[btok, xdtd],
                            writes=[ps_st] if hg == 7 else [],
                            extra=(list(ps_st.w) + list(ps_st.r)) if hg != 7 else ())
                        if hg == 7:
                            g = h // 8
                            kb.v(lambda e, g=g: e.tensor_tensor(
                                out=tmp512.ap[:], in0=xt.ap[:, g * 512:(g + 1) * 512],
                                in1=dbc.ap[:, g * 512:(g + 1) * 512], op=ALU.mult),
                                reads=[xt, dbc], writes=[tmp512])
                            kb.v(lambda e, g=g: e.tensor_add(
                                out=ybuf.ap[:, g * 512:(g + 1) * 512],
                                in0=tmp512.ap[:], in1=ps_y.ap[:]),
                                reads=[tmp512, ps_y], writes=[ybuf])
                            kb.v(lambda e, g=g: e.tensor_copy(
                                out=stt_s.ap[:, g * 512:(g + 1) * 512], in_=ps_st.ap[:]),
                                reads=[ps_st], writes=[stt_s])
                    kb.dma("y", lambda e, chg=chg: e.dma_start(
                        out=ypv[:, chg, :], in_=ybuf.ap[:]), reads=[ybuf], writes=[out_yp])
                    for g in range(8):
                        rhs_bc = eachr.ap[0:1, g * 8:(g + 1) * 8].unsqueeze(2).broadcast_to([1, 8, 64])
                        kb.t(lambda e, g=g, rhs_bc=rhs_bc: e.matmul(
                            ps_bc2.ap[:], ones_row.ap[0:1, 0:64], rhs_bc,
                            start=True, stop=True),
                            reads=[eachr, ones_row], writes=[ps_bc2])
                        kb.v(lambda e, g=g: e.scalar_tensor_tensor(
                            out=rtile.ap[:, g * 512:(g + 1) * 512],
                            in0=rtile.ap[:, g * 512:(g + 1) * 512], scalar=1.0,
                            in1=ps_bc2.ap[:], op0=ALU.mult, op1=ALU.mult),
                            reads=[ps_bc2, rtile], writes=[rtile])
                        kb.v(lambda e, g=g: e.tensor_add(
                            out=rtile.ap[:, g * 512:(g + 1) * 512],
                            in0=rtile.ap[:, g * 512:(g + 1) * 512],
                            in1=stt_s.ap[:, g * 512:(g + 1) * 512]),
                            reads=[stt_s, rtile], writes=[rtile])
            kb.dma("y", lambda e: e.dma_start(out=UEND, in_=rtile.ap[:]),
                   reads=[rtile], writes=[out_ue])

        kb.final_wait("y")
    return nc



def build_launch_D(T_=T):
    NCH_ = T_ // CH
    nc = bass.Bass("TRN2", target_bir_lowering=False)
    YP = nc.dram_tensor("YP", [T_, 4096], F32, kind="ExternalInput").ap()
    ZS = nc.dram_tensor("ZS", [T_, 4096], BF16, kind="ExternalInput").ap()
    CS = nc.dram_tensor("CS", [64, T_], BF16, kind="ExternalInput").ap()
    AC = nc.dram_tensor("AC", [64, T_], F32, kind="ExternalInput").ap()
    UENDS = nc.dram_tensor("UENDS", [3, 64, 4096], F32, kind="ExternalInput").ap()
    ATOTS = nc.dram_tensor("ATOTS", [64, 3], F32, kind="ExternalInput").ap()
    R2 = nc.dram_tensor("R2", [D, T_], F32, kind="ExternalInput").ap()
    WOUTT = nc.dram_tensor("WOUTT", [4096, D], F32, kind="ExternalInput").ap()
    NORMW = nc.dram_tensor("NORMW", [1, 4096], F32, kind="ExternalInput").ap()
    EYEB = nc.dram_tensor("EYE", [128, 128], BF16, kind="ExternalInput").ap()
    ONESC = nc.dram_tensor("ONESC", [128, 1], BF16, kind="ExternalInput").ap()
    ONESR = nc.dram_tensor("ONESR", [1, 128], F32, kind="ExternalInput").ap()
    R3 = nc.dram_tensor("R3", [D, T_], F32, kind="ExternalOutput").ap()
    ACRS = nc.dram_tensor("acrsD", [64, CH], F32, kind="Internal").ap()

    ypv = YP.rearrange("(t p) f -> p t f", p=128)
    zsv = ZS.rearrange("(t p) f -> p t f", p=128)
    r2v = R2.rearrange("(t p) n -> p t n", p=128)
    r3v = R3.rearrange("(t p) n -> p t n", p=128)
    woutv = WOUTT.rearrange("(t p) f -> p t f", p=128)

    from contextlib import ExitStack
    with ExitStack() as ctx:
        kb = KB(nc)
        kb.init_sems(ctx)
        E = ctx.enter_context
        ones_col = Buf(E(nc.sbuf_tensor("onesc", [128, 1], BF16)))
        ones_row = Buf(E(nc.sbuf_tensor("onesr", [1, 128], F32)))
        ones_rowb = Buf(E(nc.sbuf_tensor("onesrb", [1, 128], BF16)))
        eyeb = Buf(E(nc.sbuf_tensor("eyeb", [128, 128], BF16)))
        kb.dma("y", lambda e: e.dma_start(out=ones_col.ap[:], in_=ONESC), writes=[ones_col])
        kb.dma("y", lambda e: e.dma_start(out=ones_row.ap[:], in_=ONESR), writes=[ones_row])
        kb.dma("y", lambda e: e.dma_start(out=eyeb.ap[:], in_=EYEB), writes=[eyeb])
        kb.v(lambda e: e.memset(ones_rowb.ap[:], 1.0), writes=[ones_rowb])
        epsb = kb.const_col(ctx, 1e-5, 128)

        nwb = Buf(E(nc.sbuf_tensor("nwb", [128, 4096], F32)))
        inst = Buf(E(nc.sbuf_tensor("instD", [64, 4096], F32)))
        with ExitStack() as sub0:
            S0 = sub0.enter_context
            normw = Buf(S0(nc.sbuf_tensor("normw", [1, 4096], F32)))
            kb.dma("y", lambda e: e.dma_start(out=normw.ap[:], in_=NORMW), writes=[normw])
            ps_b = Buf(S0(nc.psum_tensor("ps_bD", [128, 512], F32)))
            for g in range(8):
                kb.t(lambda e, g=g: e.matmul(ps_b.ap[:], ones_row.ap[0:1, :],
                                             normw.ap[0:1, g * 512:(g + 1) * 512],
                                             start=True, stop=True),
                     reads=[normw, ones_row], writes=[ps_b])
                kb.v(lambda e, g=g: e.tensor_copy(out=nwb.ap[:, g * 512:(g + 1) * 512],
                                                  in_=ps_b.ap[:]), reads=[ps_b], writes=[nwb])
            uends = Buf(S0(nc.sbuf_tensor("uends", [64, 3 * 4096], F32)))
            kb.dma("y", lambda e: e.dma_start(
                out=uends.ap[:].rearrange("p (k f) -> p k f", f=4096), in_=UENDS.transpose([1, 0, 2])),
                writes=[uends])
            atots = Buf(S0(nc.sbuf_tensor("atots", [64, 3], F32)))
            kb.dma("y", lambda e: e.dma_start(out=atots.ap[:], in_=ATOTS), writes=[atots])
            ein = Buf(S0(nc.sbuf_tensor("einD", [64, 2], F32)))
            kb.s(lambda e: e.activation(out=ein.ap[:, 0:1], in_=atots.ap[:, 0:1], func=AF.Exp),
                 reads=[atots], writes=[ein])
            kb.v(lambda e: e.tensor_add(out=ein.ap[:, 1:2], in0=atots.ap[:, 0:1],
                                        in1=atots.ap[:, 1:2]), reads=[atots], writes=[ein])
            kb.s(lambda e: e.activation(out=ein.ap[:, 1:2], in_=ein.ap[:, 1:2], func=AF.Exp),
                 reads=[ein], writes=[ein])
            kb.v(lambda e: e.scalar_tensor_tensor(
                out=inst.ap[:], in0=uends.ap[:, 4096:2 * 4096], scalar=ein.ap[:, 0:1],
                in1=uends.ap[:, 0:4096], op0=ALU.mult, op1=ALU.add),
                reads=[uends, ein], writes=[inst])
            kb.v(lambda e: e.scalar_tensor_tensor(
                out=inst.ap[:], in0=uends.ap[:, 2 * 4096:3 * 4096], scalar=ein.ap[:, 1:2],
                in1=inst.ap[:], op0=ALU.mult, op1=ALU.add),
                reads=[uends, ein, inst], writes=[inst])
            kb.phase_barrier()
        achk = Buf(E(nc.sbuf_tensor("achkD", [64, NCH_], F32)))
        acs = Buf(E(nc.sbuf_tensor("acsD", [64, NCH_], F32)))
        acbuf = Buf(E(nc.sbuf_tensor("acbufD", [64, T_], F32)))
        kb.dma("y", lambda e: e.dma_start(out=acbuf.ap[:], in_=AC), writes=[acbuf])
        kb.v(lambda e: e.tensor_copy(out=achk.ap[:], in_=acbuf.ap[:, CH - 1::CH]),
             reads=[acbuf], writes=[achk])
        zch = Buf(E(nc.sbuf_tensor("zchD", [64, NCH_], F32)))
        kb.v(lambda e: e.memset(zch.ap[:], 0.0), writes=[zch])
        kb.v(lambda e: e.tensor_tensor_scan(out=acs.ap[:], data0=achk.ap[:],
                                            data1=zch.ap[:], initial=0.0,
                                            op0=ALU.add, op1=ALU.add),
             reads=[achk, zch], writes=[acs])
        eacs = Buf(E(nc.sbuf_tensor("eacsD", [64, NCH_], F32)))
        kb.v(lambda e: e.memset(eacs.ap[:, 0:1], 0.0), writes=[eacs])
        kb.v(lambda e: e.tensor_copy(out=eacs.ap[:, 1:NCH_], in_=acs.ap[:, 0:NCH_ - 1]),
             reads=[acs], writes=[eacs])
        kb.s(lambda e: e.activation(out=eacs.ap[:], in_=eacs.ap[:], func=AF.Exp),
             reads=[eacs], writes=[eacs])

        with ExitStack() as sub:
            SE = sub.enter_context
            csb = Buf(SE(nc.sbuf_tensor("csbD", [64, T_], BF16)))
            kb.dma("y", lambda e: e.dma_start(out=csb.ap[:], in_=CS), writes=[csb])
            pin = Buf(SE(nc.sbuf_tensor("pinD", [64, 4096], BF16)))
            cd = Buf(SE(nc.sbuf_tensor("cdD", [64, CH], BF16)))
            acrow = Buf(SE(nc.sbuf_tensor("acrowD", [1, 8 * CH], F32)))
            exprow = Buf(SE(nc.sbuf_tensor("exprowD", [1, 8 * CH], BF16)))
            ypc = Buf(SE(nc.sbuf_tensor("ypcD", [128, 4096], F32)))
            zc_ = Buf(SE(nc.sbuf_tensor("zcD", [128, 4096], BF16)))
            sz = Buf(SE(nc.sbuf_tensor("szD", [128, 4096], F32)))
            sq = sz
            rcol = Buf(SE(nc.sbuf_tensor("rcolD", [128, 1], F32)))
            hnb = Buf(SE(nc.sbuf_tensor("hnbD", [128, 4096], BF16)))
            hnT = Buf(SE(nc.sbuf_tensor("hnTD", [128, 32 * 512], BF16)))
            wout = [Buf(SE(nc.sbuf_tensor(f"woutD{i}", [128, 32 * 128], BF16))) for i in range(2)]
            outc = Buf(SE(nc.sbuf_tensor("outcD", [128, 16 * 512], F32)))
            ps_y = Buf(SE(nc.psum_tensor("ps_yD", [128, 512], F32)))
            ps_bc2 = Buf(SE(nc.psum_tensor("ps_bc2D", [64, 512], F32)))
            ps_tr = Buf(SE(nc.psum_tensor("ps_trD", [128, 128], BF16)))
            ps_o = Buf(SE(nc.psum_tensor("ps_oD", [128, 512], F32)))
            lnc = _ln_cache(kb, nc, SE, "lnD", 512)
            acr_d = Buf(None)
            r3buf = Buf(None)
            for quad in range(T_ // 512):
                for lt in range(4):
                    ch = quad * 4 + lt
                    kb.v(lambda e, ch=ch: e.tensor_scalar(
                        out=pin.ap[:], in0=inst.ap[:], scalar1=eacs.ap[:, ch:ch + 1],
                        scalar2=None, op0=ALU.mult),
                        reads=[inst, eacs], writes=[pin])
                    kb.dma("y", lambda e, ch=ch: e.dma_start(
                        out=ACRS, in_=acbuf.ap[:, ch * CH:(ch + 1) * CH]),
                        reads=[acbuf], writes=[acr_d])
                    kb.dma("y", lambda e, ch=ch: e.dma_start(
                        out=ypc.ap[:], in_=ypv[:, ch, :]), writes=[ypc])
                    for h in range(64):
                        hg = h % 8
                        if hg == 0:
                            kb.dma("y", lambda e, h=h: e.dma_start(
                                out=acrow.ap[:],
                                in_=ACRS.rearrange("h l -> (h l)").unsqueeze(0)[:, h * CH:(h + 8) * CH]),
                                reads=[acr_d], writes=[acrow], extra=list(acr_d.w))
                            kb.s(lambda e: e.activation(out=exprow.ap[:], in_=acrow.ap[:],
                                                        func=AF.Exp),
                                 reads=[acrow], writes=[exprow])
                        kb.t(lambda e, hg=hg: e.matmul(
                            ps_bc2.ap[:, 0:CH], ones_rowb.ap[0:1, 0:64],
                            exprow.ap[0:1, hg * CH:(hg + 1) * CH],
                            start=True, stop=True),
                            reads=[exprow, ones_rowb], writes=[ps_bc2])
                        kb.v(lambda e, ch=ch: e.tensor_tensor(
                            out=cd.ap[:], in0=csb.ap[:, ch * CH:(ch + 1) * CH],
                            in1=ps_bc2.ap[:, 0:CH], op=ALU.mult),
                            reads=[csb, ps_bc2], writes=[cd])
                        kb.t(lambda e, h=h, hg=hg: e.matmul(
                            ps_y.ap[:, hg * 64:(hg + 1) * 64], cd.ap[:],
                            pin.ap[:, h * 64:(h + 1) * 64],
                            start=True, stop=True),
                            reads=[cd, pin],
                            writes=[ps_y] if hg == 7 else [],
                            extra=(list(ps_y.w) + list(ps_y.r)) if hg != 7 else ())
                        if hg == 7:
                            g = h // 8
                            kb.v(lambda e, g=g: e.tensor_add(
                                out=ypc.ap[:, g * 512:(g + 1) * 512],
                                in0=ypc.ap[:, g * 512:(g + 1) * 512], in1=ps_y.ap[:]),
                                reads=[ps_y, ypc], writes=[ypc])
                    kb.dma("y", lambda e, ch=ch: e.dma_start(
                        out=zc_.ap[:], in_=zsv[:, ch, :]), writes=[zc_])
                    kb.s(lambda e: e.activation(out=sz.ap[:], in_=zc_.ap[:], func=AF.Silu),
                         reads=[zc_], writes=[sz])
                    kb.v(lambda e: e.tensor_mul(out=ypc.ap[:], in0=ypc.ap[:], in1=sz.ap[:]),
                         reads=[sz, ypc], writes=[ypc])
                    kb.s(lambda e: e.square(out=sq.ap[:], in_=ypc.ap[:]),
                         reads=[ypc], writes=[sq])
                    kb.v(lambda e: e.reduce_sum(out=rcol.ap[:], in_=sq.ap[:], axis=AX.X),
                         reads=[sq], writes=[rcol])
                    kb.v(lambda e: e.tensor_scalar(out=rcol.ap[:], in0=rcol.ap[:],
                                                   scalar1=1.0 / 4096, scalar2=epsb.ap[:, 0:1],
                                                   op0=ALU.mult, op1=ALU.add),
                         reads=[rcol, epsb], writes=[rcol])
                    kb.s(lambda e: e.sqrt(out=rcol.ap[:], in_=rcol.ap[:]),
                         reads=[rcol], writes=[rcol])
                    kb.v(lambda e: e.reciprocal(out=rcol.ap[:], in_=rcol.ap[:]),
                         reads=[rcol], writes=[rcol])
                    kb.v(lambda e: e.tensor_scalar(out=ypc.ap[:], in0=ypc.ap[:],
                                                   scalar1=rcol.ap[:, 0:1], scalar2=None,
                                                   op0=ALU.mult),
                         reads=[rcol, ypc], writes=[ypc])
                    kb.v(lambda e: e.tensor_tensor(out=hnb.ap[:], in0=ypc.ap[:],
                                                   in1=nwb.ap[:], op=ALU.mult),
                         reads=[nwb, ypc], writes=[hnb])
                    for ft in range(32):
                        kb.t(lambda e, ft=ft: e.transpose(
                            ps_tr.ap[:], hnb.ap[:, ft * 128:(ft + 1) * 128], eyeb.ap[:]),
                            reads=[hnb, eyeb], writes=[ps_tr])
                        kb.v(lambda e, ft=ft, lt=lt: e.tensor_copy(
                            out=hnT.ap[:, ft * 512 + lt * 128:ft * 512 + (lt + 1) * 128],
                            in_=ps_tr.ap[:]), reads=[ps_tr], writes=[hnT])
                for j in range(16):
                    wb_ = wout[j % 2]
                    kb.dma("p", lambda e, j=j, wb_=wb_: e.dma_start(
                        out=wb_.ap[:].rearrange("p (t f) -> p t f", f=128),
                        in_=woutv[:, :, j * 128:(j + 1) * 128]), writes=[wb_])
                    for i in range(32):
                        st, sp = (i == 0), (i == 31)
                        kb.accum_matmul(ps_o, wb_, hnT,
                            lambda e, i=i, wb_=wb_, st=st, sp=sp: e.matmul(
                                ps_o.ap[:], wb_.ap[:, i * 128:(i + 1) * 128],
                                hnT.ap[:, i * 512:(i + 1) * 512], start=st, stop=sp), st, sp)
                    kb.v(lambda e, j=j: e.tensor_copy(out=outc.ap[:, j * 512:(j + 1) * 512],
                                                      in_=ps_o.ap[:]),
                         reads=[ps_o], writes=[outc])
                otiles = [outc.view(outc.ap[:, i * 512:(i + 1) * 512]) for i in range(16)]
                _ln_chunk(kb, nc, SE, otiles, otiles, ones_col, ones_row, 1e-5,
                          "", kb_cache=lnc)
                kb.merge_views(outc, otiles)
                kb.dma("y", lambda e, quad=quad: e.dma_start(
                    out=r3v[:, :, quad * 512:(quad + 1) * 512],
                    in_=outc.ap[:].rearrange("p (t n) -> p t n", n=512)),
                    reads=[outc], writes=[r3buf])
                kb.dma("p", lambda e, quad=quad: e.dma_start(
                    out=r3v[:, :, quad * 512:(quad + 1) * 512],
                    in_=r2v[:, :, quad * 512:(quad + 1) * 512],
                    accum_op=ALU.add), reads=[], writes=[r3buf])
        kb.final_wait("y")
    return nc



def ln_free(kb, E_, nc, _a, _b, w, eps, name, epsb):
    mu = Buf(E_(nc.sbuf_tensor(name + "_mu", [128, 1], F32)))
    var = Buf(E_(nc.sbuf_tensor(name + "_var", [128, 1], F32)))
    var4 = Buf(E_(nc.sbuf_tensor(name + "_var4", [128, w // 512], F32)))
    sq = Buf(E_(nc.sbuf_tensor(name + "_sq", [128, 512], F32)))

    def run(xv, ov):
        kb.v(lambda e: e.reduce_sum(out=mu.ap[:], in_=xv.ap[:], axis=AX.X),
             reads=[xv], writes=[mu])
        kb.v(lambda e: e.tensor_scalar(out=mu.ap[:], in0=mu.ap[:], scalar1=1.0 / w,
                                       scalar2=None, op0=ALU.mult), reads=[mu], writes=[mu])
        for q in range(w // 512):
            kb.s(lambda e, q=q: e.square(out=sq.ap[:], in_=xv.ap[:, q * 512:(q + 1) * 512]),
                 reads=[xv], writes=[sq])
            kb.v(lambda e, q=q: e.reduce_sum(out=var4.ap[:, q:q + 1], in_=sq.ap[:], axis=AX.X),
                 reads=[sq], writes=[var4])
        kb.v(lambda e: e.reduce_sum(out=var.ap[:], in_=var4.ap[:], axis=AX.X),
             reads=[var4], writes=[var])
        kb.v(lambda e: e.tensor_scalar(out=var.ap[:], in0=var.ap[:], scalar1=1.0 / w,
                                       scalar2=epsb.ap[:, 0:1], op0=ALU.mult, op1=ALU.add),
             reads=[var, epsb], writes=[var])
        kb.v(lambda e: e.scalar_tensor_tensor(out=var.ap[:], in0=mu.ap[:],
                                              scalar=mu.ap[:, 0:1], in1=var.ap[:],
                                              op0=ALU.mult, op1=ALU.subtract_rev
                                              if hasattr(ALU, "subtract_rev") else ALU.subtract),
             reads=[mu, var], writes=[var]) if False else None
        kb.v(lambda e: e.tensor_mul(out=sq.ap[:, 0:1], in0=mu.ap[:], in1=mu.ap[:]),
             reads=[mu], writes=[sq])
        kb.v(lambda e: e.tensor_tensor(out=var.ap[:], in0=var.ap[:], in1=sq.ap[:, 0:1],
                                       op=ALU.subtract), reads=[sq, var], writes=[var])
        kb.s(lambda e: e.sqrt(out=var.ap[:], in_=var.ap[:]), reads=[var], writes=[var])
        kb.v(lambda e: e.reciprocal(out=var.ap[:], in_=var.ap[:]), reads=[var], writes=[var])
        kb.v(lambda e: e.tensor_scalar(out=ov.ap[:], in0=xv.ap[:], scalar1=mu.ap[:, 0:1],
                                       scalar2=var.ap[:, 0:1], op0=ALU.subtract,
                                       op1=ALU.mult),
             reads=[xv, mu, var], writes=[ov])
    return run


def build_launch_E(T_=T):
    nc = bass.Bass("TRN2", target_bir_lowering=False)
    R3 = nc.dram_tensor("R3", [D, T_], F32, kind="ExternalInput").ap()
    GWT = nc.dram_tensor("GWT", [D, 8], F32, kind="ExternalInput").ap()
    EFC1T = nc.dram_tensor("EFC1T", [8, D, 1408], F32, kind="ExternalInput").ap()
    EFC2T = nc.dram_tensor("EFC2T", [8, D, 1408], F32, kind="ExternalInput").ap()
    EPROJT = nc.dram_tensor("EPROJT", [8, 1408, D], F32, kind="ExternalInput").ap()
    SFC1T = nc.dram_tensor("SFC1T", [D, 5632], F32, kind="ExternalInput").ap()
    SFC2T = nc.dram_tensor("SFC2T", [D, 5632], F32, kind="ExternalInput").ap()
    SPROJT = nc.dram_tensor("SPROJT", [5632, D], F32, kind="ExternalInput").ap()
    SGWT = nc.dram_tensor("SGWT", [D, 1], F32, kind="ExternalInput").ap()
    EYEF = nc.dram_tensor("EYEF", [128, 128], F32, kind="ExternalInput").ap()
    ONESC = nc.dram_tensor("ONESC", [128, 1], BF16, kind="ExternalInput").ap()
    ONESR = nc.dram_tensor("ONESR", [1, 128], F32, kind="ExternalInput").ap()
    XOUT = nc.dram_tensor("XOUT", [T_, D], F32, kind="ExternalOutput").ap()

    r3v = R3.rearrange("(t p) n -> p t n", p=128)
    xoutv = XOUT.rearrange("(t p) f -> p t f", p=128)
    f1vs = [EFC1T[e_].rearrange("(t p) f -> p t f", p=128) for e_ in range(8)]
    f2vs = [EFC2T[e_].rearrange("(t p) f -> p t f", p=128) for e_ in range(8)]
    pjvs = [EPROJT[e_].rearrange("(i p) d -> p i d", p=128) for e_ in range(8)]
    bs1v = SFC1T.rearrange("(t p) f -> p t f", p=128)
    bs2v = SFC2T.rearrange("(t p) f -> p t f", p=128)
    spjv = SPROJT.rearrange("(i p) d -> p i d", p=128)

    NF = 11
    NPASS = 2
    TOKP = T_ // NPASS
    NCC = TOKP // 512
    NTT = TOKP // 128
    experts = [("r", e_) for e_ in range(8)] + [("sq", q) for q in range(4)]

    from contextlib import ExitStack
    with ExitStack() as ctx:
        kb = KB(nc)
        kb.init_sems(ctx)
        E = ctx.enter_context
        ones_col = Buf(E(nc.sbuf_tensor("onesc", [128, 1], BF16)))
        ones_row = Buf(E(nc.sbuf_tensor("onesr", [1, 128], F32)))
        eyef = Buf(E(nc.sbuf_tensor("eyef", [128, 128], F32)))
        kb.dma("y", lambda e: e.dma_start(out=ones_col.ap[:], in_=ONESC), writes=[ones_col])
        kb.dma("y", lambda e: e.dma_start(out=ones_row.ap[:], in_=ONESR), writes=[ones_row])
        kb.dma("y", lambda e: e.dma_start(out=eyef.ap[:], in_=EYEF), writes=[eyef])
        epsb = kb.const_col(ctx, 1e-5, 128)
        gwb = Buf(E(nc.sbuf_tensor("gwb", [128, 16 * 8], BF16)))
        kb.dma("p", lambda e: e.dma_start(
            out=gwb.ap[:].rearrange("p (t f) -> p t f", f=8),
            in_=GWT.rearrange("(t p) f -> p t f", p=128)), writes=[gwb])
        sgwb = Buf(E(nc.sbuf_tensor("sgwb", [128, 16], BF16)))
        kb.dma("p", lambda e: e.dma_start(
            out=sgwb.ap[:].unsqueeze(2),
            in_=SGWT.rearrange("(t p) f -> p t f", p=128),
            allow_slow_non_contiguous=True), writes=[sgwb])

        for p_ in range(NPASS):
            p0 = p_ * TOKP
            with ExitStack() as psc:
                PE_ = psc.enter_context
                xln = Buf(PE_(nc.sbuf_tensor(f"xlnE{p_}", [128, NCC * 16 * 512], BF16)))
                yacc = Buf(PE_(nc.sbuf_tensor(f"yaccE{p_}", [128, NTT * 2048], F32)))
                combT = Buf(PE_(nc.sbuf_tensor(f"combTE{p_}", [128, NTT * 8], F32)))
                gcol = Buf(PE_(nc.sbuf_tensor(f"gcolE{p_}", [128, NTT], F32)))

                with ExitStack() as p1c:
                    P1 = p1c.enter_context
                    r3c = Buf(P1(nc.sbuf_tensor(f"r3cE{p_}", [128, 16 * 512], F32)))
                    logr = Buf(P1(nc.sbuf_tensor(f"logrE{p_}", [8, 512], F32)))
                    logt = Buf(P1(nc.sbuf_tensor(f"logtE{p_}", [128, 8], F32)))
                    m1 = Buf(P1(nc.sbuf_tensor(f"m1E{p_}", [128, 1], F32)))
                    m2 = Buf(P1(nc.sbuf_tensor(f"m2E{p_}", [128, 1], F32)))
                    msk = Buf(P1(nc.sbuf_tensor(f"mskE{p_}", [128, 8], F32)))
                    grow = Buf(P1(nc.sbuf_tensor(f"growE{p_}", [1, 512], F32)))
                    ps_r = Buf(P1(nc.psum_tensor(f"psrE{p_}", [8, 512], F32)))
                    ps_t = Buf(P1(nc.psum_tensor(f"psrtE{p_}", [128, 128], F32)))
                    ps_g = Buf(P1(nc.psum_tensor(f"psgE{p_}", [1, 512], F32)))
                    lnc = _ln_cache(kb, nc, P1, f"lnE{p_}", 512)
                    for cc in range(NCC):
                        c0 = p0 + cc * 512
                        kb.dma("y", lambda e, c0=c0: e.dma_start(
                            out=r3c.ap[:].rearrange("p (t n) -> p t n", n=512),
                            in_=r3v[:, :, c0:c0 + 512]), writes=[r3c])
                        xts = [r3c.view(r3c.ap[:, i * 512:(i + 1) * 512]) for i in range(16)]
                        ots = [xln.view(xln.ap[:, (cc * 16 + i) * 512:(cc * 16 + i + 1) * 512])
                               for i in range(16)]
                        _ln_chunk(kb, nc, P1, xts, ots, ones_col, ones_row, 1e-5, "",
                                  kb_cache=lnc)
                        kb.merge_views(r3c, xts)
                        kb.merge_views(xln, ots)
                        for i in range(16):
                            st, sp = (i == 0), (i == 15)
                            kb.accum_matmul(ps_r, gwb, xln,
                                lambda e, i=i, cc=cc, st=st, sp=sp: e.matmul(
                                    ps_r.ap[:], gwb.ap[:, i * 8:(i + 1) * 8],
                                    xln.ap[:, (cc * 16 + i) * 512:(cc * 16 + i + 1) * 512],
                                    start=st, stop=sp), st, sp)
                        kb.v(lambda e: e.tensor_copy(out=logr.ap[:], in_=ps_r.ap[:]),
                             reads=[ps_r], writes=[logr])
                        for t4 in range(4):
                            tt = cc * 4 + t4
                            kb.t(lambda e, t4=t4: e.transpose(
                                ps_t.ap[:, 0:8], logr.ap[:, t4 * 128:(t4 + 1) * 128],
                                eyef.ap[0:8, 0:8]), reads=[logr, eyef], writes=[ps_t])
                            kb.v(lambda e: e.tensor_copy(out=logt.ap[:], in_=ps_t.ap[:, 0:8]),
                                 reads=[ps_t], writes=[logt])
                            kb.v(lambda e: e.reduce_max(out=m1.ap[:], in_=logt.ap[:], axis=AX.X),
                                 reads=[logt], writes=[m1])
                            kb.v(lambda e: e.tensor_scalar(out=logt.ap[:], in0=logt.ap[:],
                                                           scalar1=m1.ap[:, 0:1], scalar2=None,
                                                           op0=ALU.subtract),
                                 reads=[m1, logt], writes=[logt])
                            kb.s(lambda e: e.activation(out=logt.ap[:], in_=logt.ap[:], func=AF.Exp),
                                 reads=[logt], writes=[logt])
                            kb.v(lambda e: e.reduce_sum(out=m1.ap[:], in_=logt.ap[:], axis=AX.X),
                                 reads=[logt], writes=[m1])
                            kb.v(lambda e: e.reciprocal(out=m1.ap[:], in_=m1.ap[:]),
                                 reads=[m1], writes=[m1])
                            kb.v(lambda e: e.tensor_scalar(out=logt.ap[:], in0=logt.ap[:],
                                                           scalar1=m1.ap[:, 0:1], scalar2=None,
                                                           op0=ALU.mult),
                                 reads=[m1, logt], writes=[logt])
                            kb.v(lambda e: e.reduce_max(out=m1.ap[:], in_=logt.ap[:], axis=AX.X),
                                 reads=[logt], writes=[m1])
                            kb.v(lambda e: e.tensor_scalar(out=msk.ap[:], in0=logt.ap[:],
                                                           scalar1=m1.ap[:, 0:1], scalar2=None,
                                                           op0=ALU.is_ge),
                                 reads=[logt, m1], writes=[msk])
                            kb.v(lambda e: e.scalar_tensor_tensor(out=msk.ap[:], in0=msk.ap[:],
                                                                  scalar=-2.0, in1=logt.ap[:],
                                                                  op0=ALU.mult, op1=ALU.add),
                                 reads=[logt, msk], writes=[msk])
                            kb.v(lambda e: e.reduce_max(out=m2.ap[:], in_=msk.ap[:], axis=AX.X),
                                 reads=[msk], writes=[m2])
                            kb.v(lambda e: e.tensor_scalar(out=msk.ap[:], in0=logt.ap[:],
                                                           scalar1=m2.ap[:, 0:1], scalar2=None,
                                                           op0=ALU.is_ge),
                                 reads=[logt, m2], writes=[msk])
                            kb.v(lambda e, tt=tt: e.tensor_tensor(
                                out=combT.ap[:, tt * 8:(tt + 1) * 8], in0=logt.ap[:],
                                in1=msk.ap[:], op=ALU.mult),
                                reads=[logt, msk], writes=[combT])
                        for i in range(16):
                            st, sp = (i == 0), (i == 15)
                            kb.accum_matmul(ps_g, sgwb, xln,
                                lambda e, i=i, cc=cc, st=st, sp=sp: e.matmul(
                                    ps_g.ap[:], sgwb.ap[:, i:i + 1],
                                    xln.ap[:, (cc * 16 + i) * 512:(cc * 16 + i + 1) * 512],
                                    start=st, stop=sp), st, sp)
                        kb.s(lambda e: e.activation(out=grow.ap[:], in_=ps_g.ap[0:1, :],
                                                    func=AF.Sigmoid), reads=[ps_g], writes=[grow])
                        for t4 in range(4):
                            tt = cc * 4 + t4
                            kb.t(lambda e, t4=t4: e.transpose(
                                ps_t.ap[:, 0:1], grow.ap[:, t4 * 128:(t4 + 1) * 128],
                                eyef.ap[0:1, 0:1]), reads=[grow, eyef], writes=[ps_t])
                            kb.v(lambda e, tt=tt: e.tensor_copy(out=gcol.ap[:, tt:tt + 1],
                                                                in_=ps_t.ap[:, 0:1]),
                                 reads=[ps_t], writes=[gcol])
                    kb.phase_barrier()

                with ExitStack() as p2c:
                    P2 = p2c.enter_context
                    hT = Buf(P2(nc.sbuf_tensor(f"hTE{p_}", [128, NF * TOKP], BF16)))
                    stg1 = [Buf(P2(nc.sbuf_tensor(f"sg1E{p_}{i}", [128, 1024], F32)))
                            for i in range(2)]
                    stg2 = [Buf(P2(nc.sbuf_tensor(f"sg2E{p_}{i}", [128, 1024], F32)))
                            for i in range(2)]
                    w1b = [Buf(P2(nc.sbuf_tensor(f"w1bE{p_}{i}", [128, 2048], BF16)))
                           for i in range(2)]
                    w2b = [Buf(P2(nc.sbuf_tensor(f"w2bE{p_}{i}", [128, 2048], BF16)))
                           for i in range(2)]
                    wpjs = [Buf(P2(nc.sbuf_tensor(f"wpsE{p_}0", [128, NF * 512], F32)))] * 2
                    wpjb = [Buf(P2(nc.sbuf_tensor(f"wpbE{p_}{i}", [128, NF * 512], BF16)))
                            for i in range(2)]
                    silu_t = [Buf(P2(nc.sbuf_tensor(f"silE{p_}{i}", [128, TOKP], F32)))
                              for i in range(2)]
                    ps1 = [Buf(P2(nc.psum_tensor(f"ps1E{p_}{i}", [128, TOKP], F32)))
                           for i in range(2)]
                    ps2 = [Buf(P2(nc.psum_tensor(f"ps2E{p_}{i}", [128, TOKP], F32)))
                           for i in range(2)]

                    def fc_srcs(ei, ft):
                        kind, idx = experts[ei]
                        if kind == "r":
                            return (f1vs[idx][:, :, ft * 128:(ft + 1) * 128],
                                    f2vs[idx][:, :, ft * 128:(ft + 1) * 128])
                        qf0 = idx * 1408
                        return (bs1v[:, :, qf0 + ft * 128:qf0 + (ft + 1) * 128],
                                bs2v[:, :, qf0 + ft * 128:qf0 + (ft + 1) * 128])

                    def fc_load(gft):
                        if gft >= len(experts) * NF:
                            return
                        s1, s2 = fc_srcs(gft // NF, gft % NF)
                        b = gft % 2
                        for h in range(2):
                            kb.dma("y", lambda e, s1=s1, h=h: e.dma_start(
                                out=stg1[h].ap[:].rearrange("p (t f) -> p t f", f=128),
                                in_=s1[:, h * 8:(h + 1) * 8, :]), writes=[stg1[h]])
                            kb.dma("y", lambda e, s2=s2, h=h: e.dma_start(
                                out=stg2[h].ap[:].rearrange("p (t f) -> p t f", f=128),
                                in_=s2[:, h * 8:(h + 1) * 8, :]), writes=[stg2[h]])
                            kb.s(lambda e, b=b, h=h: e.copy(
                                out=w1b[b].ap[:, h * 1024:(h + 1) * 1024], in_=stg1[h].ap[:]),
                                reads=[stg1[h]], writes=[w1b[b]])
                            kb.v(lambda e, b=b, h=h: e.tensor_copy(
                                out=w2b[b].ap[:, h * 1024:(h + 1) * 1024], in_=stg2[h].ap[:]),
                                reads=[stg2[h]], writes=[w2b[b]])

                    def pj_dma(ei, dchunk):
                        if dchunk >= 4:
                            return
                        kind, idx = experts[ei]
                        i0 = idx * NF if kind == "sq" else 0
                        pv = spjv if kind == "sq" else pjvs[idx]
                        sb = wpjs[dchunk % 2]
                        kb.dma("y", lambda e, pv=pv, i0=i0, dchunk=dchunk, sb=sb: e.dma_start(
                            out=sb.ap[:].rearrange("p (i d) -> p i d", d=512),
                            in_=pv[:, i0:i0 + NF, dchunk * 512:(dchunk + 1) * 512]),
                            writes=[sb])

                    def pj_cast(dchunk):
                        if dchunk >= 4:
                            return
                        sb = wpjs[dchunk % 2]
                        wb = wpjb[dchunk % 2]
                        kb.v(lambda e, sb=sb, wb=wb: e.tensor_copy(
                            out=wb.ap[:], in_=sb.ap[:]), reads=[sb], writes=[wb])

                    fc_load(0)
                    for ei, (kind, idx) in enumerate(experts):
                        for ft in range(NF):
                            gft = ei * NF + ft
                            fc_load(gft + 1)
                            if ft == NF - 3:
                                pj_dma(ei, 0)
                            b = gft % 2
                            p1, p2 = ps1[b], ps2[b]
                            for i in range(16):
                                for nch in range(NCC):
                                    st = (i == 0)
                                    sp = (i == 15 and nch == NCC - 1)
                                    xsl = (nch * 16 + i) * 512
                                    kb.accum_matmul(p1, w1b[b], xln,
                                        lambda e, i=i, nch=nch, b=b, xsl=xsl, st=st, sp=False,
                                               p1=p1: e.matmul(
                                            p1.ap[:, nch * 512:(nch + 1) * 512],
                                            w1b[b].ap[:, i * 128:(i + 1) * 128],
                                            xln.ap[:, xsl:xsl + 512],
                                            start=st, stop=(i == 15)), st, sp)
                                    kb.accum_matmul(p2, w2b[b], xln,
                                        lambda e, i=i, nch=nch, b=b, xsl=xsl, st=st,
                                               p2=p2: e.matmul(
                                            p2.ap[:, nch * 512:(nch + 1) * 512],
                                            w2b[b].ap[:, i * 128:(i + 1) * 128],
                                            xln.ap[:, xsl:xsl + 512],
                                            start=st, stop=(i == 15)), st, sp)
                            kb.s(lambda e, b=b, p1=p1: e.activation(
                                out=silu_t[b].ap[:], in_=p1.ap[:], func=AF.Silu),
                                reads=[p1], writes=[silu_t[b]])
                            kb.v(lambda e, ft=ft, b=b, p2=p2: e.tensor_mul(
                                out=hT.ap[:, ft * TOKP:(ft + 1) * TOKP], in0=silu_t[b].ap[:],
                                in1=p2.ap[:]), reads=[silu_t[b], p2], writes=[hT])
                            if ft == NF - 2:
                                pj_cast(0)
                        for dchunk in range(4):
                            wb = wpjb[dchunk % 2]
                            pj_dma(ei, dchunk + 1)
                            for tt in range(NTT):
                                pp = ps2[tt % 2]
                                for i in range(NF):
                                    st, sp = (i == 0), (i == NF - 1)
                                    kb.accum_matmul(pp, hT, wb,
                                        lambda e, i=i, tt=tt, pp=pp, wb=wb, st=st, sp=sp: e.matmul(
                                            pp.ap[:, 0:512],
                                            hT.ap[:, i * TOKP + tt * 128:i * TOKP + (tt + 1) * 128],
                                            wb.ap[:, i * 512:(i + 1) * 512],
                                            start=st, stop=sp), st, sp)
                                if kind == "r":
                                    ccol = combT.ap[:, tt * 8 + idx:tt * 8 + idx + 1]
                                    csrc = combT
                                else:
                                    ccol = gcol.ap[:, tt:tt + 1]
                                    csrc = gcol
                                y0 = tt * 2048 + dchunk * 512
                                if ei == 0:
                                    kb.v(lambda e, y0=y0, pp=pp, ccol=ccol: e.tensor_scalar(
                                        out=yacc.ap[:, y0:y0 + 512], in0=pp.ap[:, 0:512],
                                        scalar1=ccol, scalar2=None, op0=ALU.mult),
                                        reads=[pp, csrc], writes=[yacc])
                                else:
                                    kb.v(lambda e, y0=y0, pp=pp, ccol=ccol: e.scalar_tensor_tensor(
                                        out=yacc.ap[:, y0:y0 + 512], in0=pp.ap[:, 0:512],
                                        scalar=ccol, in1=yacc.ap[:, y0:y0 + 512],
                                        op0=ALU.mult, op1=ALU.add),
                                        reads=[pp, csrc, yacc], writes=[yacc])
                                if tt == 1:
                                    pj_cast(dchunk + 1)
                    kb.phase_barrier()

                with ExitStack() as p3c:
                    P3 = p3c.enter_context
                    r3t = [Buf(P3(nc.sbuf_tensor(f"r3tE{p_}{i}", [128, 2048], F32)))
                           for i in range(2)]
                    r3tok = [Buf(P3(nc.sbuf_tensor(f"r3kE{p_}{i}", [128, 2048], F32)))
                             for i in range(2)]
                    ps_t2 = [Buf(P3(nc.psum_tensor(f"pstE{p_}{i}", [128, 128], F32)))
                             for i in range(4)]
                    lnfree = ln_free(kb, P3, nc, None, None, 2048, 1e-5, f"lnfE{p_}", epsb)
                    xout_b = Buf(None)

                    def r3_load(tt):
                        if tt >= NTT:
                            return
                        tok0 = p0 + tt * 128
                        kb.dma("y", lambda e, tok0=tok0, tt=tt: e.dma_start(
                            out=r3t[tt % 2].ap[:].rearrange("p (t n) -> p t n", n=128),
                            in_=r3v[:, :, tok0:tok0 + 128]), writes=[r3t[tt % 2]])

                    r3_load(0)
                    for tt in range(NTT):
                        r3_load(tt + 1)
                        yv = yacc.view(yacc.ap[:, tt * 2048:(tt + 1) * 2048])
                        ov = yacc.view(yacc.ap[:, tt * 2048:(tt + 1) * 2048])
                        lnfree(yv, ov)
                        kb.merge_views(yacc, [yv, ov])
                        rt = r3t[tt % 2]
                        rk = r3tok[tt % 2]
                        for i in range(16):
                            pst = ps_t2[i % 4]
                            kb.t(lambda e, i=i, rt=rt, pst=pst: e.transpose(
                                pst.ap[:], rt.ap[:, i * 128:(i + 1) * 128], eyef.ap[:]),
                                reads=[rt, eyef], writes=[pst])
                            cpeng = kb.v if i % 2 == 0 else kb.s
                            if i % 2 == 0:
                                cpeng(lambda e, i=i, pst=pst, rk=rk: e.tensor_copy(
                                    out=rk.ap[:, i * 128:(i + 1) * 128], in_=pst.ap[:]),
                                    reads=[pst], writes=[rk])
                            else:
                                cpeng(lambda e, i=i, pst=pst, rk=rk: e.copy(
                                    out=rk.ap[:, i * 128:(i + 1) * 128], in_=pst.ap[:]),
                                    reads=[pst], writes=[rk])
                        kb.v(lambda e, tt=tt, rk=rk: e.tensor_add(
                            out=rk.ap[:], in0=rk.ap[:],
                            in1=yacc.ap[:, tt * 2048:(tt + 1) * 2048]),
                            reads=[yacc, rk], writes=[rk])
                        gtt = p_ * NTT + tt
                        kb.dma("s", lambda e, gtt=gtt, rk=rk: e.dma_start(
                            out=xoutv[:, gtt, :], in_=rk.ap[:]),
                            reads=[rk], writes=[xout_b])
                    kb.phase_barrier()
        kb.final_wait("y")
    return nc



_ONESC = np.ones((128, 1), dtype=BF16NP)
_ONESR = np.ones((1, 128), dtype=np.float32)
_EYEB = np.eye(128, dtype=BF16NP)
_EYEF = np.eye(128, dtype=np.float32)
_TRIADD = np.where(np.tril(np.ones((128, 128))).T > 0, 0.0, -1e30).astype(np.float32)


def _attn_masks(own0, S_=8192, T_=T):
    NJ, NI = T_ // 512, S_ // 128
    m = np.zeros((NJ, NI, 128, 512), dtype=BF16NP)
    kk = np.arange(128)[:, None]
    qq = np.arange(512)[None, :]
    for j in range(NJ):
        q0 = own0 + 512 * j
        for i in range(NI):
            m[j, i] = ((128 * i + kk) <= (q0 + qq)).astype(BF16NP)
    return m


_MASK_CACHE = {}


def kernel(**inputs):
    x = np.asarray(inputs["L0_attn_x"], dtype=np.float32)
    B0, S_, D_ = x.shape
    cores = list(range(NCORES))

    whgT, whpT = host_quant_launch(inputs)

    ncA = build_launch_A(S_, T)
    wo_t = np.ascontiguousarray(np.asarray(inputs["L0_attn_W_o"]).T)
    xT_b = [np.ascontiguousarray(x[b].T) for b in range(B0)]
    in_maps = []
    for c in cores:
        b, q = c // 4, c % 4
        if q not in _MASK_CACHE:
            _MASK_CACHE[q] = _attn_masks(q * T, S_, T)
        in_maps.append({
            "XCTX": xT_b[b],
            "WDQ": np.asarray(inputs["L0_attn_W_dq"]),
            "WUQ": np.asarray(inputs["L0_attn_W_uq"]),
            "WDKV": np.asarray(inputs["L0_attn_W_dkv"]),
            "WUKV": np.asarray(inputs["L0_attn_W_ukv"]),
            "WOT": wo_t,
            "QLNG": np.asarray(inputs["L0_attn_q_ln_g"])[:, None],
            "QLNB": np.asarray(inputs["L0_attn_q_ln_b"])[:, None],
            "KVLNG": np.asarray(inputs["L0_attn_kv_ln_g"])[:, None],
            "KVLNB": np.asarray(inputs["L0_attn_kv_ln_b"])[:, None],
            "MASKS": _MASK_CACHE[q],
            "IDXQ": np.array([[q]], dtype=np.int32),
            "ONESC": _ONESC, "ONESR": _ONESR,
        })
    resA = run_bass_kernel_spmd(ncA, in_maps, core_ids=cores)
    R1 = [resA.results[c]["R1"] for c in cores]

    ncB = build_launch_B(T)
    in_maps = [{
        "R1": R1[c], "WHG": whgT, "WHP": whpT,
        "ONESC": _ONESC, "ONESR": _ONESR,
    } for c in cores]
    resB = run_bass_kernel_spmd(ncB, in_maps, core_ids=cores)
    R2 = [resB.results[c]["R2"] for c in cores]

    ncC = build_launch_C(T)
    win_t = np.ascontiguousarray(np.asarray(inputs["L1_attn_W_in"]).T)
    drow = np.repeat(np.asarray(inputs["L1_attn_D"]), 64)[None, :].astype(np.float32)
    in_maps = []
    for c in cores:
        q = c % 4
        ext = (np.zeros((D_, EXT), np.float32) if q == 0
               else np.ascontiguousarray(R2[c - 1][:, -EXT:]))
        r2e = np.concatenate([ext, R2[c]], axis=1)
        in_maps.append({
            "R2E": np.ascontiguousarray(r2e),
            "WINT": win_t,
            "CONVW": np.asarray(inputs["L1_attn_conv_w"]),
            "CONVB": np.asarray(inputs["L1_attn_conv_b"])[:, None],
            "DTB": np.asarray(inputs["L1_attn_dt_bias"])[:, None],
            "ALOG": np.asarray(inputs["L1_attn_A_log"])[:, None],
            "DROW": drow, "TRI": _TRIADD, "EYE": _EYEB, "EYEF": _EYEF,
            "ONESC": _ONESC, "ONESR": _ONESR,
        })
    resC = run_bass_kernel_spmd(ncC, in_maps, core_ids=cores)
    outsC = [resC.results[c] for c in cores]

    ncD = build_launch_D(T)
    wout_t = np.ascontiguousarray(np.asarray(inputs["L1_attn_W_out"]).T)
    normw = np.asarray(inputs["L1_attn_norm_w"])[None, :].astype(np.float32)
    in_maps = []
    for c in cores:
        q = c % 4
        uends = np.zeros((3, 64, 4096), np.float32)
        atots = np.zeros((64, 3), np.float32)
        for k in range(3):
            p = c - 1 - k
            if p // 4 == c // 4 and p >= 0:
                uends[k] = outsC[p]["UEND"]
                atots[:, k] = outsC[p]["ATOT"][:, 0]
        in_maps.append({
            "YP": outsC[c]["YP"], "ZS": outsC[c]["ZS"], "CS": outsC[c]["CS"],
            "AC": outsC[c]["AC"], "UENDS": uends, "ATOTS": atots,
            "R2": R2[c], "WOUTT": wout_t, "NORMW": normw,
            "EYE": _EYEB, "ONESC": _ONESC, "ONESR": _ONESR,
        })
    resD = run_bass_kernel_spmd(ncD, in_maps, core_ids=cores)
    R3 = [resD.results[c]["R3"] for c in cores]

    ncE = build_launch_E(T)
    gwt = np.ascontiguousarray(np.asarray(inputs["L1_ffn_gate_w"]).T)
    efc1t = np.ascontiguousarray(np.asarray(inputs["L1_ffn_expert_fc1"]).transpose(0, 2, 1))
    efc2t = np.ascontiguousarray(np.asarray(inputs["L1_ffn_expert_fc2"]).transpose(0, 2, 1))
    eprojt = np.ascontiguousarray(np.asarray(inputs["L1_ffn_expert_proj"]).transpose(0, 2, 1))
    sfc1t = np.ascontiguousarray(np.asarray(inputs["L1_ffn_shared_fc1"]).T)
    sfc2t = np.ascontiguousarray(np.asarray(inputs["L1_ffn_shared_fc2"]).T)
    sprojt = np.ascontiguousarray(np.asarray(inputs["L1_ffn_shared_proj"]).T)
    sgwt = np.ascontiguousarray(np.asarray(inputs["L1_ffn_shared_gate_w"]).T)
    in_maps = [{
        "R3": R3[c], "GWT": gwt, "EFC1T": efc1t, "EFC2T": efc2t,
        "EPROJT": eprojt, "SFC1T": sfc1t, "SFC2T": sfc2t, "SPROJT": sprojt,
        "SGWT": sgwt, "EYEF": _EYEF, "ONESC": _ONESC, "ONESR": _ONESR,
    } for c in cores]
    resE = run_bass_kernel_spmd(ncE, in_maps, core_ids=cores)

    out = np.empty((B0, S_, D_), dtype=np.float32)
    for c in cores:
        b, q = c // 4, c % 4
        out[b, q * T:(q + 1) * T] = resE.results[c]["XOUT"]
    return out


# version 03
# speedup vs baseline: 1.5004x; 1.2536x over previous
import numpy as np
import ml_dtypes
import concourse.bass as bass
import concourse.mybir as mybir
from concourse.bass_utils import run_bass_kernel_spmd

F32 = mybir.dt.float32
BF16 = mybir.dt.bfloat16
AF = mybir.ActivationFunctionType
ALU = mybir.AluOpType
AX = mybir.AxisListType

NCORES = 8
D = 2048
T = 2048
F32EPS = float(np.finfo(np.float32).eps)
BF16NP = ml_dtypes.bfloat16


class Buf:
    __slots__ = ("ap", "w", "r")
    def __init__(self, ap, w=None):
        self.ap = ap
        self.w = list(w) if w else []
        self.r = []

    def view(self, ap):
        v = Buf(ap, w=self.w)
        v.r = list(self.r)
        return v


class KB:
    CLASSES = {"t": 1, "v": 1, "s": 1, "p": 1}

    def __init__(self, nc):
        self.nc = nc
        self.sem = {}
        self.cnt = {}
        self.waited = {e: {} for e in ("t", "v", "s", "p", "y")}
        self.eng = {"t": nc.tensor, "v": nc.vector, "s": nc.scalar,
                    "p": nc.gpsimd, "y": nc.sync}
        self._nbsem = 0

    def init_sems(self, ctx):
        self._ctx = ctx
        for c in self.CLASSES:
            self.sem[c] = ctx.enter_context(self.nc.semaphore("sem_" + c))
            self.cnt[c] = 0

    def _buf_sem(self, buf):
        key = ("b", id(buf))
        if key not in self.sem:
            self._nbsem += 1
            self.sem[key] = self._ctx.enter_context(
                self.nc.semaphore(f"bsem{self._nbsem}"))
            self.cnt[key] = 0
        return key

    def _wait(self, engine_name, tokens):
        w = self.waited[engine_name]
        eng = self.eng[engine_name]
        need = {}
        for tok in tokens:
            if tok is None:
                continue
            c, n = tok
            if n > need.get(c, 0):
                need[c] = n
        for c, n in need.items():
            if w.get(c, 0) < n:
                eng.wait_ge(self.sem[c], n)
                w[c] = n

    def emit(self, engine_name, cls, fn, reads=(), writes=(), extra=()):
        toks = list(extra)
        for b in reads:
            toks.extend(b.w)
        for b in writes:
            toks.extend(b.w)
            toks.extend(b.r)
        self._wait(engine_name, toks)
        inst = fn(self.eng[engine_name])
        inc = 16 if isinstance(cls, tuple) else self.CLASSES[cls]
        self.cnt[cls] += inc
        tok = (cls, self.cnt[cls])
        inst.then_inc(self.sem[cls], inc)
        for b in reads:
            b.r.append(tok)
        for b in writes:
            b.w = [tok]
            b.r = []
        return tok

    def t(self, fn, reads=(), writes=(), extra=()):
        return self.emit("t", "t", fn, reads, writes, extra)

    def v(self, fn, reads=(), writes=(), extra=()):
        return self.emit("v", "v", fn, reads, writes, extra)

    def s(self, fn, reads=(), writes=(), extra=()):
        return self.emit("s", "s", fn, reads, writes, extra)

    def p(self, fn, reads=(), writes=(), extra=()):
        return self.emit("p", "p", fn, reads, writes, extra)

    def dma(self, qname, fn, reads=(), writes=(), extra=()):
        assert len(writes) == 1, "dma() wants exactly one destination Buf"
        if qname == "p":
            self._nbsem += 1
            key = ("pd", self._nbsem)
            self.sem[key] = self._ctx.enter_context(
                self.nc.semaphore(f"psem{self._nbsem}"))
            self.cnt[key] = 0
        else:
            key = self._buf_sem(writes[0])
        return self.emit(qname, key, fn, reads, writes, extra)

    def accum_matmul(self, psum_buf, lhs_buf, rhs_buf, fn_ap, start, stop):
        if stop:
            return self.t(fn_ap, reads=[lhs_buf, rhs_buf], writes=[psum_buf])
        extra = list(psum_buf.w) + list(psum_buf.r)
        return self.t(fn_ap, reads=[lhs_buf, rhs_buf], extra=extra)

    @staticmethod
    def merge_views(parent, views):
        for v in views:
            for t in v.w:
                if t not in parent.w:
                    parent.w.append(t)
            parent.r.extend(v.r)

    def const_col(self, ctx, value, npart=1, dtype=F32):
        key = ("const", float(value), npart, dtype)
        if not hasattr(self, "_consts"):
            self._consts = {}
        if key not in self._consts:
            b = Buf(ctx.enter_context(self.nc.sbuf_tensor(
                f"cst{len(self._consts)}", [npart, 1], dtype)))
            self.v(lambda e: e.memset(b.ap[:], float(value)), writes=[b])
            self._consts[key] = b
        return self._consts[key]

    def value_load(self, engine_name, buf, ap=None):
        self._wait(engine_name, list(buf.w))
        return self.eng[engine_name].value_load(ap if ap is not None else buf.ap[0:1, 0:1])

    def final_wait(self, engine_name="y"):
        toks = [(c, n) for c, n in self.cnt.items() if n > 0]
        self._wait(engine_name, toks)

    def phase_barrier(self):
        toks = [(c, n) for c, n in self.cnt.items() if n > 0]
        for e in ("t", "v", "s", "p", "y"):
            self._wait(e, toks)



def ln_partition(kb, ctx, xT_bufs, out_bufs, ntok, eps=1e-5, name="ln",
                 ones_col=None, ones_row=None, gamma=None, beta=None):
    nc = kb.nc
    nt = len(xT_bufs)
    nD = nt * 128
    E = ctx.enter_context
    xb = Buf(E(nc.sbuf_tensor(name + "_xb", [128, 512], BF16)))
    sq = Buf(E(nc.sbuf_tensor(name + "_sq", [128, 512], BF16)))
    ps_sum = Buf(E(nc.psum_tensor(name + "_psum", [1, 512], F32)))
    ps_sq = Buf(E(nc.psum_tensor(name + "_psq", [1, 512], F32)))
    mu = Buf(E(nc.sbuf_tensor(name + "_mu", [1, 512], F32)))
    rstd = Buf(E(nc.sbuf_tensor(name + "_rstd", [1, 512], F32)))
    ps_b = Buf(E(nc.psum_tensor(name + "_pb", [128, 512], F32)))
    ps_b2 = Buf(E(nc.psum_tensor(name + "_pb2", [128, 512], F32)))
    mub = Buf(E(nc.sbuf_tensor(name + "_mub", [128, 512], F32)))
    rstdb = Buf(E(nc.sbuf_tensor(name + "_rstdb", [128, 512], F32)))
    tmp = Buf(E(nc.sbuf_tensor(name + "_tmp", [128, 512], F32)))

    for c0 in range(0, ntok, 512):
        c1 = min(c0 + 512, ntok)
        w = c1 - c0
        for i, xT in enumerate(xT_bufs):
            kb.v(lambda e, xT=xT, c0=c0, c1=c1, w=w: e.tensor_copy(
                out=xb.ap[:, 0:w], in_=xT.ap[:, c0:c1]), reads=[xT], writes=[xb])
            kb.v(lambda e, w=w: e.tensor_mul(out=sq.ap[:, 0:w], in0=xb.ap[:, 0:w],
                                             in1=xb.ap[:, 0:w]), reads=[xb], writes=[sq])
            st = (i == 0)
            sp = (i == nt - 1)
            kb.accum_matmul(ps_sum, ones_col, xb,
                            lambda e, w=w, st=st, sp=sp: e.matmul(
                                ps_sum.ap[0:1, 0:w], ones_col.ap[:, 0:1], xb.ap[:, 0:w],
                                start=st, stop=sp), st, sp)
            kb.accum_matmul(ps_sq, ones_col, sq,
                            lambda e, w=w, st=st, sp=sp: e.matmul(
                                ps_sq.ap[0:1, 0:w], ones_col.ap[:, 0:1], sq.ap[:, 0:w],
                                start=st, stop=sp), st, sp)
        kb.v(lambda e, w=w: e.tensor_scalar(out=mu.ap[:, 0:w], in0=ps_sum.ap[0:1, 0:w],
                                            scalar1=1.0 / nD, scalar2=None, op0=ALU.mult),
             reads=[ps_sum], writes=[mu])
        kb.v(lambda e, w=w: e.tensor_mul(out=rstd.ap[:, 0:w], in0=mu.ap[:, 0:w],
                                         in1=mu.ap[:, 0:w]), reads=[mu], writes=[rstd])
        kb.v(lambda e, w=w: e.scalar_tensor_tensor(
            out=rstd.ap[:, 0:w], in0=ps_sq.ap[0:1, 0:w], scalar=1.0 / nD,
            in1=rstd.ap[:, 0:w], op0=ALU.mult, op1=ALU.subtract),
            reads=[ps_sq, rstd], writes=[rstd])
        kb.s(lambda e, w=w: e.activation(out=rstd.ap[:, 0:w], in_=rstd.ap[:, 0:w],
                                         func=AF.Ln, bias=float(eps)),
             reads=[rstd], writes=[rstd])
        kb.s(lambda e, w=w: e.activation(out=rstd.ap[:, 0:w], in_=rstd.ap[:, 0:w],
                                         func=AF.Exp, scale=-0.5),
             reads=[rstd], writes=[rstd])
        kb.t(lambda e, w=w: e.matmul(ps_b.ap[:, 0:w], ones_row.ap[0:1, :],
                                     mu.ap[0:1, 0:w], start=True, stop=True),
             reads=[mu, ones_row], writes=[ps_b])
        kb.v(lambda e, w=w: e.tensor_copy(out=mub.ap[:, 0:w], in_=ps_b.ap[:, 0:w]),
             reads=[ps_b], writes=[mub])
        kb.t(lambda e, w=w: e.matmul(ps_b2.ap[:, 0:w], ones_row.ap[0:1, :],
                                     rstd.ap[0:1, 0:w], start=True, stop=True),
             reads=[rstd, ones_row], writes=[ps_b2])
        kb.v(lambda e, w=w: e.tensor_copy(out=rstdb.ap[:, 0:w], in_=ps_b2.ap[:, 0:w]),
             reads=[ps_b2], writes=[rstdb])
        for i, xT in enumerate(xT_bufs):
            ob = out_bufs[i]
            kb.v(lambda e, xT=xT, c0=c0, c1=c1, w=w: e.tensor_sub(
                out=tmp.ap[:, 0:w], in0=xT.ap[:, c0:c1], in1=mub.ap[:, 0:w]),
                reads=[xT, mub], writes=[tmp])
            if gamma is None:
                kb.v(lambda e, ob=ob, c0=c0, c1=c1, w=w: e.tensor_mul(
                    out=ob.ap[:, c0:c1], in0=tmp.ap[:, 0:w], in1=rstdb.ap[:, 0:w]),
                    reads=[tmp, rstdb], writes=[ob])
            else:
                kb.v(lambda e, w=w: e.tensor_mul(
                    out=tmp.ap[:, 0:w], in0=tmp.ap[:, 0:w], in1=rstdb.ap[:, 0:w]),
                    reads=[tmp, rstdb], writes=[tmp])
                g = gamma[i]; bta = beta[i]
                kb.v(lambda e, ob=ob, g=g, bta=bta, c0=c0, c1=c1, w=w: e.tensor_scalar(
                    out=ob.ap[:, c0:c1], in0=tmp.ap[:, 0:w], scalar1=g.ap[:, 0:1],
                    scalar2=bta.ap[:, 0:1], op0=ALU.mult, op1=ALU.add),
                    reads=[tmp, g, bta], writes=[ob])



def build_launch_Q():
    nc = bass.Bass("TRN2", target_bir_lowering=False)
    WG = nc.dram_tensor("WG", [2048, 2048], F32, kind="ExternalInput").ap()
    WP = nc.dram_tensor("WP", [256, 8192], F32, kind="ExternalInput").ap()
    EYE = nc.dram_tensor("EYE", [128, 128], BF16, kind="ExternalInput").ap()
    OG = nc.dram_tensor("OG", [2048, 2048], BF16, kind="ExternalOutput").ap()
    OP = nc.dram_tensor("OP", [8192, 256], BF16, kind="ExternalOutput").ap()

    from contextlib import ExitStack
    with ExitStack() as ctx:
        kb = KB(nc)
        kb.init_sems(ctx)
        E = ctx.enter_context
        eye = Buf(E(nc.sbuf_tensor("eye", [128, 128], BF16)))
        kb.dma("y", lambda e: e.dma_start(out=eye.ap[:], in_=EYE), writes=[eye])

        def quant_one(name, W, OUT, nrows, ncols, E, outbuf):
            ng = ncols // 64
            wb = Buf(E(nc.sbuf_tensor(name + "_wb", [128, ncols], BF16)))
            gsum = Buf(E(nc.sbuf_tensor(name + "_gs", [128, ng], F32)))
            scale = Buf(E(nc.sbuf_tensor(name + "_sc", [128, ng], BF16)))
            rcp = Buf(E(nc.sbuf_tensor(name + "_rc", [128, ng], F32)))
            tq = Buf(E(nc.sbuf_tensor(name + "_tq", [128, ncols], BF16)))
            ab = Buf(E(nc.sbuf_tensor(name + "_ab", [128, ncols], BF16)))
            mask = Buf(E(nc.sbuf_tensor(name + "_mk", [128, ncols], BF16)))
            sgn = Buf(E(nc.sbuf_tensor(name + "_sg", [128, ncols], BF16)))
            what = Buf(E(nc.sbuf_tensor(name + "_wh", [128, ncols], BF16)))
            ptr = Buf(E(nc.psum_tensor(name + "_pt", [128, 128], BF16)))
            panels = [Buf(E(nc.sbuf_tensor(f"{name}_pan{cc}", [128, nrows], BF16)))
                      for cc in range(ncols // 128)]
            for r in range(nrows // 128):
                kb.dma("p", lambda e, r=r: e.dma_start(out=wb.ap[:], in_=W[r * 128:(r + 1) * 128, :]),
                       writes=[wb])
                v3 = wb.ap[:].rearrange("p (g e) -> p g e", e=64)
                kb.v(lambda e, v3=v3: e.tensor_reduce(out=gsum.ap[:], in_=v3, axis=AX.X,
                                                      op=ALU.add, apply_absolute_value=True),
                     reads=[wb], writes=[gsum])
                kb.v(lambda e: e.tensor_scalar(out=scale.ap[:], in0=gsum.ap[:],
                                               scalar1=1.0 / 64, scalar2=1e-8,
                                               op0=ALU.mult, op1=ALU.max),
                     reads=[gsum], writes=[scale])
                kb.v(lambda e: e.reciprocal(out=rcp.ap[:], in_=scale.ap[:]),
                     reads=[scale], writes=[rcp])
                rcp3 = rcp.ap[:].unsqueeze(2).broadcast_to([128, ng, 64])
                tq3 = tq.ap[:].rearrange("p (g e) -> p g e", e=64)
                kb.v(lambda e, v3=v3, rcp3=rcp3, tq3=tq3: e.tensor_tensor(
                    out=tq3, in0=v3, in1=rcp3, op=ALU.mult), reads=[wb, rcp], writes=[tq])
                kb.s(lambda e: e.activation(out=ab.ap[:], in_=tq.ap[:], func=AF.Abs),
                     reads=[tq], writes=[ab])
                kb.v(lambda e: e.tensor_scalar(out=mask.ap[:], in0=ab.ap[:], scalar1=0.5,
                                               scalar2=None, op0=ALU.is_gt),
                     reads=[ab], writes=[mask])
                kb.s(lambda e: e.sign(out=sgn.ap[:], in_=wb.ap[:]),
                     reads=[wb], writes=[sgn])
                sc3 = scale.ap[:].unsqueeze(2).broadcast_to([128, ng, 64])
                wh3 = what.ap[:].rearrange("p (g e) -> p g e", e=64)
                sg3 = sgn.ap[:].rearrange("p (g e) -> p g e", e=64)
                kb.v(lambda e, sg3=sg3, sc3=sc3, wh3=wh3: e.tensor_tensor(
                    out=wh3, in0=sg3, in1=sc3, op=ALU.mult), reads=[sgn, scale], writes=[what])
                kb.v(lambda e: e.tensor_mul(out=what.ap[:], in0=what.ap[:], in1=mask.ap[:]),
                     reads=[mask, what], writes=[what])
                for cc in range(ncols // 128):
                    kb.t(lambda e, cc=cc: e.transpose(ptr.ap[:], what.ap[:, cc * 128:(cc + 1) * 128],
                                                      eye.ap[:]),
                         reads=[what, eye], writes=[ptr])
                    kb.v(lambda e, cc=cc, r=r: e.tensor_copy(
                        out=panels[cc].ap[:, r * 128:(r + 1) * 128], in_=ptr.ap[:]),
                        reads=[ptr], writes=[panels[cc]])
            for cc in range(ncols // 128):
                kb.dma("y", lambda e, cc=cc: e.dma_start(
                    out=OUT[cc * 128:(cc + 1) * 128, :], in_=panels[cc].ap[:]),
                    reads=[panels[cc]], writes=[outbuf])

        ogb, opb = Buf(None), Buf(None)
        from contextlib import ExitStack as _ES
        with _ES() as sub:
            quant_one("g", WG, OG, 2048, 2048, sub.enter_context, ogb)
        kb.phase_barrier()
        with _ES() as sub:
            quant_one("q", WP, OP, 256, 8192, sub.enter_context, opb)
        kb.final_wait("y")
    return nc


_EYE128 = np.eye(128, dtype=BF16NP)


def host_quant_launch(inputs):
    nc = build_launch_Q()
    wg = np.ascontiguousarray(inputs["L0_ffn_w_gate_up"])
    wp = np.ascontiguousarray(inputs["L0_ffn_w_proj"])
    in_maps = []
    for c in range(NCORES):
        in_maps.append({
            "WG": np.ascontiguousarray(wg[c * 2048:(c + 1) * 2048, :]),
            "WP": np.ascontiguousarray(wp[c * 256:(c + 1) * 256, :]),
            "EYE": _EYE128,
        })
    res = run_bass_kernel_spmd(nc, in_maps, core_ids=list(range(NCORES)))
    whgT = np.concatenate([res.results[c]["OG"] for c in range(NCORES)], axis=1)
    whpT = np.concatenate([res.results[c]["OP"] for c in range(NCORES)], axis=1)
    return whgT, whpT






def build_launch_B(T_=T):
    nc = bass.Bass("TRN2", target_bir_lowering=False)
    R1 = nc.dram_tensor("R1", [D, T_], F32, kind="ExternalInput").ap()
    WHG = nc.dram_tensor("WHG", [2048, 16384], BF16, kind="ExternalInput").ap()
    WHP = nc.dram_tensor("WHP", [8192, 2048], BF16, kind="ExternalInput").ap()
    ONESC = nc.dram_tensor("ONESC", [128, 1], BF16, kind="ExternalInput").ap()
    ONESR = nc.dram_tensor("ONESR", [1, 128], F32, kind="ExternalInput").ap()
    R2 = nc.dram_tensor("R2", [D, T_], F32, kind="ExternalOutput").ap()

    from contextlib import ExitStack
    with ExitStack() as ctx:
        kb = KB(nc)
        kb.init_sems(ctx)
        E = ctx.enter_context
        ones_col = Buf(E(nc.sbuf_tensor("onesc", [128, 1], BF16)))
        ones_row = Buf(E(nc.sbuf_tensor("onesr", [1, 128], F32)))
        kb.dma("y", lambda e: e.dma_start(out=ones_col.ap[:], in_=ONESC), writes=[ones_col])
        kb.dma("y", lambda e: e.dma_start(out=ones_row.ap[:], in_=ONESR), writes=[ones_row])

        r1c = Buf(E(nc.sbuf_tensor("r1c", [128, 16 * 512], F32)))
        xlnc = Buf(E(nc.sbuf_tensor("xlnc", [128, 16 * 512], BF16)))
        hc = Buf(E(nc.sbuf_tensor("hc", [128, 64 * 512], BF16)))
        outc = Buf(E(nc.sbuf_tensor("outc", [128, 16 * 512], F32)))
        wg = [Buf(E(nc.sbuf_tensor(f"wg{i}", [128, 16 * 128], BF16))) for i in range(2)]
        wu = [Buf(E(nc.sbuf_tensor(f"wu{i}", [128, 16 * 128], BF16))) for i in range(2)]
        wp = [Buf(E(nc.sbuf_tensor(f"wp{i}", [128, 32 * 128], BF16))) for i in range(2)]
        ps_g = Buf(E(nc.psum_tensor("ps_g", [128, 512], F32)))
        ps_u = Buf(E(nc.psum_tensor("ps_u", [128, 512], F32)))
        ps_d = Buf(E(nc.psum_tensor("ps_d", [128, 512], F32)))
        ps_rms = Buf(E(nc.psum_tensor("ps_rms", [1, 512], F32)))
        ps_rb = Buf(E(nc.psum_tensor("ps_rb", [128, 512], F32)))
        silu_t = Buf(E(nc.sbuf_tensor("silu_t", [128, 512], F32)))
        sqh = Buf(E(nc.sbuf_tensor("sqh", [128, 512], BF16)))
        rmsrow = Buf(E(nc.sbuf_tensor("rmsrow", [1, 512], F32)))
        rmsb = Buf(E(nc.sbuf_tensor("rmsb", [128, 512], F32)))

        NCH = T_ // 512
        r1v = R1.rearrange("(t p) n -> p t n", p=128)
        r2v = R2.rearrange("(t p) n -> p t n", p=128)
        whgv = WHG.rearrange("(t p) f -> p t f", p=128)
        whpv = WHP.rearrange("(g p) f -> p g f", p=128)
        r2buf = Buf(None)

        for c in range(NCH):
            c0 = c * 512
            kb.dma("y", lambda e, c0=c0: e.dma_start(
                out=r1c.ap[:].rearrange("p (t n) -> p t n", n=512),
                in_=r1v[:, :, c0:c0 + 512]), writes=[r1c])
            r1tiles = [r1c.view(r1c.ap[:, i * 512:(i + 1) * 512]) for i in range(16)]
            xltiles = [xlnc.view(xlnc.ap[:, i * 512:(i + 1) * 512]) for i in range(16)]
            _ln_chunk(kb, nc, E, r1tiles, xltiles, ones_col, ones_row, 1e-5,
                      f"lnB{c}", kb_cache=_ln_cache(kb, nc, E, "lnB", 512))
            kb.merge_views(r1c, r1tiles)
            kb.merge_views(xlnc, xltiles)
            for g in range(64):
                gb = wg[g % 2]
                ub = wu[g % 2]
                kb.dma("s", lambda e, g=g, gb=gb: e.dma_start(
                    out=gb.ap[:].rearrange("p (t f) -> p t f", f=128),
                    in_=whgv[:, :, g * 128:(g + 1) * 128]), writes=[gb])
                kb.dma("s", lambda e, g=g, ub=ub: e.dma_start(
                    out=ub.ap[:].rearrange("p (t f) -> p t f", f=128),
                    in_=whgv[:, :, 8192 + g * 128:8192 + (g + 1) * 128]), writes=[ub])
                for i in range(16):
                    st, sp = (i == 0), (i == 15)
                    kb.accum_matmul(ps_g, gb, xlnc, lambda e, i=i, gb=gb, st=st, sp=sp: e.matmul(
                        ps_g.ap[:], gb.ap[:, i * 128:(i + 1) * 128],
                        xlnc.ap[:, i * 512:(i + 1) * 512], start=st, stop=sp), st, sp)
                    kb.accum_matmul(ps_u, ub, xlnc, lambda e, i=i, ub=ub, st=st, sp=sp: e.matmul(
                        ps_u.ap[:], ub.ap[:, i * 128:(i + 1) * 128],
                        xlnc.ap[:, i * 512:(i + 1) * 512], start=st, stop=sp), st, sp)
                kb.s(lambda e: e.activation(out=silu_t.ap[:], in_=ps_g.ap[:], func=AF.Silu),
                     reads=[ps_g], writes=[silu_t])
                kb.v(lambda e, g=g: e.tensor_mul(out=hc.ap[:, g * 512:(g + 1) * 512],
                                                 in0=silu_t.ap[:], in1=ps_u.ap[:]),
                     reads=[silu_t, ps_u], writes=[hc])
                kb.v(lambda e, g=g: e.tensor_mul(out=sqh.ap[:], in0=hc.ap[:, g * 512:(g + 1) * 512],
                                                 in1=hc.ap[:, g * 512:(g + 1) * 512]),
                     reads=[hc], writes=[sqh])
                st, sp = (g == 0), (g == 63)
                kb.accum_matmul(ps_rms, ones_col, sqh, lambda e, st=st, sp=sp: e.matmul(
                    ps_rms.ap[:], ones_col.ap[:, 0:1], sqh.ap[:], start=st, stop=sp), st, sp)
            epsb2 = kb.const_col(ctx, F32EPS)
            kb.s(lambda e: e.activation(out=rmsrow.ap[:], in_=ps_rms.ap[0:1, :], func=AF.Ln,
                                        bias=epsb2.ap[0:1, 0:1], scale=1.0 / 8192),
                 reads=[ps_rms, epsb2], writes=[rmsrow])
            kb.s(lambda e: e.activation(out=rmsrow.ap[:], in_=rmsrow.ap[:], func=AF.Exp,
                                        scale=-0.5), reads=[rmsrow], writes=[rmsrow])
            kb.t(lambda e: e.matmul(ps_rb.ap[:], ones_row.ap[0:1, :], rmsrow.ap[0:1, :],
                                    start=True, stop=True),
                 reads=[rmsrow, ones_row], writes=[ps_rb])
            kb.v(lambda e: e.tensor_copy(out=rmsb.ap[:], in_=ps_rb.ap[:]),
                 reads=[ps_rb], writes=[rmsb])
            for g in range(64):
                kb.v(lambda e, g=g: e.tensor_mul(out=hc.ap[:, g * 512:(g + 1) * 512],
                                                 in0=hc.ap[:, g * 512:(g + 1) * 512],
                                                 in1=rmsb.ap[:]),
                     reads=[rmsb, hc], writes=[hc])
            for j in range(16):
                for half in range(2):
                    pb = wp[half]
                    kb.dma("s", lambda e, j=j, half=half, pb=pb: e.dma_start(
                        out=pb.ap[:].rearrange("p (g f) -> p g f", f=128),
                        in_=whpv[:, half * 32:(half + 1) * 32, j * 128:(j + 1) * 128]),
                        writes=[pb])
                    for gg in range(32):
                        g = half * 32 + gg
                        st, sp = (g == 0), (g == 63)
                        kb.accum_matmul(ps_d, pb, hc, lambda e, g=g, gg=gg, pb=pb, st=st, sp=sp: e.matmul(
                            ps_d.ap[:], pb.ap[:, gg * 128:(gg + 1) * 128],
                            hc.ap[:, g * 512:(g + 1) * 512], start=st, stop=sp), st, sp)
                kb.v(lambda e, j=j: e.tensor_copy(out=outc.ap[:, j * 512:(j + 1) * 512],
                                                  in_=ps_d.ap[:]),
                     reads=[ps_d], writes=[outc])
            otiles = [outc.view(outc.ap[:, i * 512:(i + 1) * 512]) for i in range(16)]
            _ln_chunk(kb, nc, E, otiles, otiles, ones_col, ones_row, 1e-5,
                      f"lnB2{c}", kb_cache=_ln_cache(kb, nc, E, "lnB", 512))
            kb.merge_views(outc, otiles)
            kb.dma("y", lambda e, c0=c0: e.dma_start(
                out=r2v[:, :, c0:c0 + 512],
                in_=outc.ap[:].rearrange("p (t n) -> p t n", n=512)),
                reads=[outc], writes=[r2buf])
            kb.dma("p", lambda e, c0=c0: e.dma_start(
                out=r2v[:, :, c0:c0 + 512],
                in_=r1v[:, :, c0:c0 + 512], accum_op=ALU.add),
                reads=[r1c], writes=[r2buf])
        kb.final_wait("y")
    return nc


_LN_CACHES = {}


def _ln_cache(kb, nc, E, name, w, eps=1e-5):
    key = (id(kb), name, w)
    if key not in _LN_CACHES:
        class _C(dict):
            pass
        _LN_CACHES[key] = {
            "epsb": None,
            "xb": Buf(E(nc.sbuf_tensor(name + "_xb", [128, w], BF16))),
            "sq": Buf(E(nc.sbuf_tensor(name + "_sq", [128, w], BF16))),
            "ps_sum": Buf(E(nc.psum_tensor(name + "_psum", [1, w], F32))),
            "ps_sq": Buf(E(nc.psum_tensor(name + "_psq", [1, w], F32))),
            "mu": Buf(E(nc.sbuf_tensor(name + "_mu", [1, w], F32))),
            "rstd": Buf(E(nc.sbuf_tensor(name + "_rstd", [1, w], F32))),
            "ps_b": Buf(E(nc.psum_tensor(name + "_pb", [128, w], F32))),
            "mub": Buf(E(nc.sbuf_tensor(name + "_mub", [128, w], F32))),
            "rstdb": Buf(E(nc.sbuf_tensor(name + "_rstdb", [128, w], F32))),
            "tmp": Buf(E(nc.sbuf_tensor(name + "_tmp", [128, w], F32))),
        }
        b = Buf(E(nc.sbuf_tensor(name + "_eps", [1, 1], F32)))
        kb.v(lambda e: e.memset(b.ap[:], float(eps)), writes=[b])
        _LN_CACHES[key]["epsb"] = b
    return _LN_CACHES[key]


def _ln_chunk(kb, nc, E, xtiles, otiles, ones_col, ones_row, eps, name,
              kb_cache=None, gamma=None, beta=None):
    cs = kb_cache
    xb, sq = cs["xb"], cs["sq"]
    ps_sum, ps_sq = cs["ps_sum"], cs["ps_sq"]
    mu, rstd = cs["mu"], cs["rstd"]
    ps_b = ps_b2 = cs["ps_b"]
    mub, rstdb, tmp = cs["mub"], cs["rstdb"], cs["tmp"]
    nt = len(xtiles)
    nD_ = nt * 128
    w = xtiles[0].ap.shape[-1]
    for i, xT in enumerate(xtiles):
        kb.v(lambda e, xT=xT, w=w: e.tensor_copy(out=xb.ap[:, 0:w], in_=xT.ap[:]),
             reads=[xT], writes=[xb])
        kb.v(lambda e, w=w: e.tensor_mul(out=sq.ap[:, 0:w], in0=xb.ap[:, 0:w],
                                         in1=xb.ap[:, 0:w]), reads=[xb], writes=[sq])
        st, sp = (i == 0), (i == nt - 1)
        kb.accum_matmul(ps_sum, ones_col, xb, lambda e, w=w, st=st, sp=sp: e.matmul(
            ps_sum.ap[0:1, 0:w], ones_col.ap[:, 0:1], xb.ap[:, 0:w],
            start=st, stop=sp), st, sp)
        kb.accum_matmul(ps_sq, ones_col, sq, lambda e, w=w, st=st, sp=sp: e.matmul(
            ps_sq.ap[0:1, 0:w], ones_col.ap[:, 0:1], sq.ap[:, 0:w],
            start=st, stop=sp), st, sp)
    kb.v(lambda e, w=w: e.tensor_scalar(out=mu.ap[:, 0:w], in0=ps_sum.ap[0:1, 0:w],
                                        scalar1=1.0 / nD_, scalar2=None, op0=ALU.mult),
         reads=[ps_sum], writes=[mu])
    kb.v(lambda e, w=w: e.tensor_mul(out=rstd.ap[:, 0:w], in0=mu.ap[:, 0:w],
                                     in1=mu.ap[:, 0:w]), reads=[mu], writes=[rstd])
    kb.v(lambda e, w=w: e.scalar_tensor_tensor(
        out=rstd.ap[:, 0:w], in0=ps_sq.ap[0:1, 0:w], scalar=1.0 / nD_,
        in1=rstd.ap[:, 0:w], op0=ALU.mult, op1=ALU.subtract),
        reads=[ps_sq, rstd], writes=[rstd])
    epsb = cs["epsb"]
    kb.s(lambda e, w=w: e.activation(out=rstd.ap[:, 0:w], in_=rstd.ap[:, 0:w],
                                     func=AF.Ln, bias=epsb.ap[0:1, 0:1]),
         reads=[rstd, epsb], writes=[rstd])
    kb.s(lambda e, w=w: e.activation(out=rstd.ap[:, 0:w], in_=rstd.ap[:, 0:w],
                                     func=AF.Exp, scale=-0.5),
         reads=[rstd], writes=[rstd])
    kb.t(lambda e, w=w: e.matmul(ps_b.ap[:, 0:w], ones_row.ap[0:1, :],
                                 mu.ap[0:1, 0:w], start=True, stop=True),
         reads=[mu, ones_row], writes=[ps_b])
    kb.v(lambda e, w=w: e.tensor_copy(out=mub.ap[:, 0:w], in_=ps_b.ap[:, 0:w]),
         reads=[ps_b], writes=[mub])
    kb.t(lambda e, w=w: e.matmul(ps_b2.ap[:, 0:w], ones_row.ap[0:1, :],
                                 rstd.ap[0:1, 0:w], start=True, stop=True),
         reads=[rstd, ones_row], writes=[ps_b2])
    kb.v(lambda e, w=w: e.tensor_copy(out=rstdb.ap[:, 0:w], in_=ps_b2.ap[:, 0:w]),
         reads=[ps_b2], writes=[rstdb])
    for i, xT in enumerate(xtiles):
        ob = otiles[i]
        kb.v(lambda e, xT=xT, w=w: e.tensor_sub(out=tmp.ap[:, 0:w], in0=xT.ap[:],
                                                in1=mub.ap[:, 0:w]),
             reads=[xT, mub], writes=[tmp])
        if gamma is None:
            kb.v(lambda e, ob=ob, w=w: e.tensor_mul(out=ob.ap[:], in0=tmp.ap[:, 0:w],
                                                    in1=rstdb.ap[:, 0:w]),
                 reads=[tmp, rstdb], writes=[ob])
        else:
            kb.v(lambda e, w=w: e.tensor_mul(out=tmp.ap[:, 0:w], in0=tmp.ap[:, 0:w],
                                             in1=rstdb.ap[:, 0:w]),
                 reads=[tmp, rstdb], writes=[tmp])
            g_, b_ = gamma[i], beta[i]
            kb.v(lambda e, ob=ob, g_=g_, b_=b_, w=w: e.tensor_scalar(
                out=ob.ap[:], in0=tmp.ap[:, 0:w], scalar1=g_.ap[:, 0:1],
                scalar2=b_.ap[:, 0:1], op0=ALU.mult, op1=ALU.add),
                reads=[tmp, g_, b_], writes=[ob])



def build_launch_A(S_=8192, T_=T):
    nc = bass.Bass("TRN2", target_bir_lowering=False)
    XCTX = nc.dram_tensor("XCTX", [D, S_], F32, kind="ExternalInput").ap()
    WDQ = nc.dram_tensor("WDQ", [D, 512], F32, kind="ExternalInput").ap()
    WUQ = nc.dram_tensor("WUQ", [512, D], F32, kind="ExternalInput").ap()
    WDKV = nc.dram_tensor("WDKV", [D, 512], F32, kind="ExternalInput").ap()
    WUKV = nc.dram_tensor("WUKV", [512, 2 * D], F32, kind="ExternalInput").ap()
    WOT = nc.dram_tensor("WOT", [D, D], F32, kind="ExternalInput").ap()
    QLNG = nc.dram_tensor("QLNG", [512, 1], F32, kind="ExternalInput").ap()
    QLNB = nc.dram_tensor("QLNB", [512, 1], F32, kind="ExternalInput").ap()
    KVLNG = nc.dram_tensor("KVLNG", [512, 1], F32, kind="ExternalInput").ap()
    KVLNB = nc.dram_tensor("KVLNB", [512, 1], F32, kind="ExternalInput").ap()
    NJ = T_ // 512
    NI = S_ // 128
    MASKS = nc.dram_tensor("MASKS", [NJ, NI, 128, 512], BF16, kind="ExternalInput").ap()
    IDXQ = nc.dram_tensor("IDXQ", [1, 1], mybir.dt.int32, kind="ExternalInput").ap()
    ONESC = nc.dram_tensor("ONESC", [128, 1], BF16, kind="ExternalInput").ap()
    ONESR = nc.dram_tensor("ONESR", [1, 128], F32, kind="ExternalInput").ap()
    R1 = nc.dram_tensor("R1", [D, T_], F32, kind="ExternalOutput").ap()
    CQ_T = nc.dram_tensor("cq_t", [512, S_], BF16, kind="Internal").ap()
    CKV_T = nc.dram_tensor("ckv_t", [512, S_], BF16, kind="Internal").ap()
    KT_S = nc.dram_tensor("kt_s", [D, S_], BF16, kind="Internal").ap()
    VT_S = nc.dram_tensor("vt_s", [S_, D], BF16, kind="Internal").ap()
    QT_S = nc.dram_tensor("qt_s", [D, T_], BF16, kind="Internal").ap()
    AOT_S = nc.dram_tensor("aot_s", [D, T_], BF16, kind="Internal").ap()

    xctxv = XCTX.rearrange("(t p) n -> p t n", p=128)
    r1v = R1.rearrange("(t p) n -> p t n", p=128)
    cqv = CQ_T.rearrange("(t p) n -> p t n", p=128)
    ckvv = CKV_T.rearrange("(t p) n -> p t n", p=128)
    ktv = KT_S.rearrange("(t p) n -> p t n", p=128)
    vtv = VT_S.rearrange("(t p) f -> p t f", p=128)
    qtv = QT_S.rearrange("(t p) n -> p t n", p=128)
    aotv = AOT_S.rearrange("(t p) n -> p t n", p=128)
    NCH = S_ // 512
    NOC = T_ // 512

    from contextlib import ExitStack
    with ExitStack() as ctx:
        kb = KB(nc)
        kb.init_sems(ctx)
        E = ctx.enter_context
        ones_col = Buf(E(nc.sbuf_tensor("onesc", [128, 1], BF16)))
        ones_row = Buf(E(nc.sbuf_tensor("onesr", [1, 128], F32)))
        kb.dma("y", lambda e: e.dma_start(out=ones_col.ap[:], in_=ONESC), writes=[ones_col])
        kb.dma("y", lambda e: e.dma_start(out=ones_row.ap[:], in_=ONESR), writes=[ones_row])
        idxq = Buf(E(nc.sbuf_tensor("idxq", [1, 1], mybir.dt.int32)))
        kb.dma("y", lambda e: e.dma_start(out=idxq.ap[:], in_=IDXQ), writes=[idxq])

        scr_cq = Buf(None)
        scr_ckv = Buf(None)
        scr_kt = Buf(None)
        scr_vt = Buf(None)
        scr_qt = Buf(None)
        scr_aot = Buf(None)

        with ExitStack() as sub:
            SE = sub.enter_context
            wdq = Buf(SE(nc.sbuf_tensor("wdq", [128, 16 * 512], BF16)))
            wdkv = Buf(SE(nc.sbuf_tensor("wdkv", [128, 16 * 512], BF16)))
            kb.dma("p", lambda e: e.dma_start(
                out=wdq.ap[:].rearrange("p (t f) -> p t f", f=512),
                in_=WDQ.rearrange("(t p) f -> p t f", p=128)), writes=[wdq])
            kb.dma("p", lambda e: e.dma_start(
                out=wdkv.ap[:].rearrange("p (t f) -> p t f", f=512),
                in_=WDKV.rearrange("(t p) f -> p t f", p=128)), writes=[wdkv])
            qg = Buf(SE(nc.sbuf_tensor("qg", [128, 4], F32)))
            qb = Buf(SE(nc.sbuf_tensor("qb", [128, 4], F32)))
            kg = Buf(SE(nc.sbuf_tensor("kg", [128, 4], F32)))
            kbta = Buf(SE(nc.sbuf_tensor("kbta", [128, 4], F32)))
            for b, t_ in ((qg, QLNG), (qb, QLNB), (kg, KVLNG), (kbta, KVLNB)):
                kb.dma("y", lambda e, b=b, t_=t_: e.dma_start(
                    out=b.ap[:].unsqueeze(2),
                    in_=t_.rearrange("(t p) o -> p t o", p=128),
                    allow_slow_non_contiguous=True), writes=[b])
            xc = Buf(SE(nc.sbuf_tensor("xc", [128, 16 * 512], F32)))
            xlnb = Buf(SE(nc.sbuf_tensor("xlnb", [128, 16 * 512], BF16)))
            cqkv = Buf(SE(nc.sbuf_tensor("cqkv", [128, 8 * 512], F32)))
            cqkvb = Buf(SE(nc.sbuf_tensor("cqkvb", [128, 8 * 512], BF16)))
            ps = Buf(SE(nc.psum_tensor("psA1", [128, 512], F32)))
            lnc = _ln_cache(kb, nc, SE, "lnA1", 512)
            lnc5 = _ln_cache(kb, nc, SE, "lnA1b", 512, eps=1e-5)
            for ch in range(NCH):
                c0 = ch * 512
                kb.dma("y", lambda e, c0=c0: e.dma_start(
                    out=xc.ap[:].rearrange("p (t n) -> p t n", n=512),
                    in_=xctxv[:, :, c0:c0 + 512]), writes=[xc])
                xts = [xc.view(xc.ap[:, i * 512:(i + 1) * 512]) for i in range(16)]
                xlts = [xlnb.view(xlnb.ap[:, i * 512:(i + 1) * 512]) for i in range(16)]
                _ln_chunk(kb, nc, SE, xts, xlts, ones_col, ones_row, 1e-5,
                          "", kb_cache=lnc)
                kb.merge_views(xc, xts)
                kb.merge_views(xlnb, xlts)
                for half, wd in ((0, wdq), (1, wdkv)):
                    for mt in range(4):
                        for i in range(16):
                            st, sp = (i == 0), (i == 15)
                            kb.accum_matmul(ps, wd, xlnb,
                                lambda e, i=i, mt=mt, wd=wd, st=st, sp=sp: e.matmul(
                                    ps.ap[:], wd.ap[:, i * 512 + mt * 128:i * 512 + (mt + 1) * 128],
                                    xlnb.ap[:, i * 512:(i + 1) * 512], start=st, stop=sp),
                                st, sp)
                        kb.v(lambda e, half=half, mt=mt: e.tensor_copy(
                            out=cqkv.ap[:, (half * 4 + mt) * 512:(half * 4 + mt + 1) * 512],
                            in_=ps.ap[:]), reads=[ps], writes=[cqkv])
                cqts = [cqkv.view(cqkv.ap[:, i * 512:(i + 1) * 512]) for i in range(4)]
                cqos = [cqkvb.view(cqkvb.ap[:, i * 512:(i + 1) * 512]) for i in range(4)]
                ckts = [cqkv.view(cqkv.ap[:, (4 + i) * 512:(5 + i) * 512]) for i in range(4)]
                ckos = [cqkvb.view(cqkvb.ap[:, (4 + i) * 512:(5 + i) * 512]) for i in range(4)]
                qgs = [qg.view(qg.ap[:, i:i + 1]) for i in range(4)]
                qbs = [qb.view(qb.ap[:, i:i + 1]) for i in range(4)]
                kgs = [kg.view(kg.ap[:, i:i + 1]) for i in range(4)]
                kbs = [kbta.view(kbta.ap[:, i:i + 1]) for i in range(4)]
                _ln_chunk(kb, nc, SE, cqts, cqos, ones_col, ones_row, 1e-5,
                          "", kb_cache=lnc5, gamma=qgs, beta=qbs)
                _ln_chunk(kb, nc, SE, ckts, ckos, ones_col, ones_row, 1e-5,
                          "", kb_cache=lnc5, gamma=kgs, beta=kbs)
                kb.merge_views(cqkv, cqts + ckts)
                kb.merge_views(cqkvb, cqos + ckos)
                kb.dma("y", lambda e, c0=c0: e.dma_start(
                    out=cqv[:, :, c0:c0 + 512],
                    in_=cqkvb.ap[:, 0:4 * 512].rearrange("p (t n) -> p t n", n=512)),
                    reads=[cqkvb], writes=[scr_cq])
                kb.dma("y", lambda e, c0=c0: e.dma_start(
                    out=ckvv[:, :, c0:c0 + 512],
                    in_=cqkvb.ap[:, 4 * 512:8 * 512].rearrange("p (t n) -> p t n", n=512)),
                    reads=[cqkvb], writes=[scr_ckv])
        kb.phase_barrier()

        with ExitStack() as sub:
            SE = sub.enter_context
            wukv = Buf(SE(nc.sbuf_tensor("wukv", [128, 4 * 4096], BF16)))
            wuq = Buf(SE(nc.sbuf_tensor("wuq", [128, 4 * 2048], BF16)))
            kb.dma("p", lambda e: e.dma_start(
                out=wukv.ap[:].rearrange("p (t f) -> p t f", f=4096),
                in_=WUKV.rearrange("(t p) f -> p t f", p=128)), writes=[wukv])
            kb.dma("p", lambda e: e.dma_start(
                out=wuq.ap[:].rearrange("p (t f) -> p t f", f=2048),
                in_=WUQ.rearrange("(t p) f -> p t f", p=128)), writes=[wuq])
            ckvc = [Buf(SE(nc.sbuf_tensor(f"ckvc{x}", [128, 4 * 512], BF16)))
                    for x in range(2)]
            ev = [Buf(SE(nc.sbuf_tensor(f"evA2{x}", [128, 512], BF16))) for x in range(2)]
            ps2 = [Buf(SE(nc.psum_tensor(f"psA2{x}", [128, 512], F32))) for x in range(2)]
            gctr = [0]

            def evac_store(store_fn):
                g = gctr[0]
                pse, eve = ps2[g % 2], ev[g % 2]
                kb.v(lambda e, pse=pse, eve=eve: e.tensor_copy(out=eve.ap[:], in_=pse.ap[:]),
                     reads=[pse], writes=[eve])
                kb.dma("y", lambda e, eve=eve: store_fn(e, eve), reads=[eve],
                       writes=[store_fn.scr])
                gctr[0] += 1

            def ckv_load(ch):
                if ch >= NCH:
                    return
                c0 = ch * 512
                kb.dma("s", lambda e, c0=c0, ch=ch: e.dma_start(
                    out=ckvc[ch % 2].ap[:].rearrange("p (t n) -> p t n", n=512),
                    in_=ckvv[:, :, c0:c0 + 512]), writes=[ckvc[ch % 2]])

            ckv_load(0)
            for ch in range(NCH):
                c0 = ch * 512
                ckv_load(ch + 1)
                cv = ckvc[ch % 2]
                for kf in range(16):
                    pse = ps2[gctr[0] % 2]
                    for i in range(4):
                        st, sp = (i == 0), (i == 3)
                        kb.accum_matmul(pse, wukv, cv,
                            lambda e, i=i, kf=kf, cv=cv, pse=pse, st=st, sp=sp: e.matmul(
                                pse.ap[:], wukv.ap[:, i * 4096 + kf * 128:i * 4096 + (kf + 1) * 128],
                                cv.ap[:, i * 512:(i + 1) * 512], start=st, stop=sp), st, sp)

                    def stf(e, eve, kf=kf, c0=c0):
                        return e.dma_start(out=ktv[:, kf, c0:c0 + 512], in_=eve.ap[:])
                    stf.scr = scr_kt
                    evac_store(stf)
                for tt in range(4):
                    for vc in range(4):
                        pse = ps2[gctr[0] % 2]
                        for i in range(4):
                            st, sp = (i == 0), (i == 3)
                            kb.accum_matmul(pse, cv, wukv,
                                lambda e, i=i, tt=tt, vc=vc, cv=cv, pse=pse, st=st, sp=sp: e.matmul(
                                    pse.ap[:], cv.ap[:, i * 512 + tt * 128:i * 512 + (tt + 1) * 128],
                                    wukv.ap[:, i * 4096 + 2048 + vc * 512:i * 4096 + 2048 + (vc + 1) * 512],
                                    start=st, stop=sp), st, sp)

                        def stf(e, eve, ch=ch, tt=tt, vc=vc):
                            return e.dma_start(out=vtv[:, ch * 4 + tt, vc * 512:(vc + 1) * 512],
                                               in_=eve.ap[:])
                        stf.scr = scr_vt
                        evac_store(stf)
            regq = kb.value_load("s", idxq)
            for oc in range(NOC):
                kb.dma("s", lambda e, oc=oc: e.dma_start(
                    out=ckvc[oc % 2].ap[:].rearrange("p (t n) -> p t n", n=512),
                    in_=cqv[:, :, bass.ts(regq, T_)][:, :, oc * 512:(oc + 1) * 512]),
                    reads=[idxq], writes=[ckvc[oc % 2]], extra=list(scr_cq.w))
                cv = ckvc[oc % 2]
                for qf in range(16):
                    pse = ps2[gctr[0] % 2]
                    for i in range(4):
                        st, sp = (i == 0), (i == 3)
                        kb.accum_matmul(pse, wuq, cv,
                            lambda e, i=i, qf=qf, cv=cv, pse=pse, st=st, sp=sp: e.matmul(
                                pse.ap[:], wuq.ap[:, i * 2048 + qf * 128:i * 2048 + (qf + 1) * 128],
                                cv.ap[:, i * 512:(i + 1) * 512], start=st, stop=sp), st, sp)

                    def stf(e, eve, qf=qf, oc=oc):
                        return e.dma_start(out=qtv[:, qf, oc * 512:(oc + 1) * 512],
                                           in_=eve.ap[:])
                    stf.scr = scr_qt
                    evac_store(stf)
        kb.phase_barrier()

        with ExitStack() as sub:
            SE = sub.enter_context
            maskb = [Buf(SE(nc.sbuf_tensor(f"maskb{x}", [128, NI * 512], BF16)))
                     for x in range(2)]
            ktb = [Buf(SE(nc.sbuf_tensor(f"ktb{x}", [128, S_], BF16))) for x in range(2)]
            vtb = [Buf(SE(nc.sbuf_tensor(f"vtb{x}", [128, NI * 128], BF16))) for x in range(2)]
            qjb = [Buf(SE(nc.sbuf_tensor(f"qjb{x}", [128, 512], BF16))) for x in range(2)]
            pt = [Buf(SE(nc.sbuf_tensor(f"ptA{x}", [128, 1024], BF16))) for x in range(3)]
            ps_sc = [Buf(SE(nc.psum_tensor(f"ps_sc{x}", [128, 1024], F32))) for x in range(2)]
            ps_pv = [Buf(SE(nc.psum_tensor(f"ps_pv{x}", [128, 512], F32))) for x in range(2)]
            ps_dn = Buf(SE(nc.psum_tensor("ps_dn", [1, 512], F32)))
            ps_db = Buf(SE(nc.psum_tensor("ps_db", [128, 512], F32)))
            dnrow = Buf(SE(nc.sbuf_tensor("dnrow", [1, 512], F32)))
            dnb = Buf(SE(nc.sbuf_tensor("dnb", [128, 512], F32)))
            aoj = Buf(SE(nc.sbuf_tensor("aoj", [128, 512], BF16)))
            SCALE = 1.0 / float(np.sqrt(128.0))

            def load_mask(j):
                if j >= NJ:
                    return
                kb.dma("y", lambda e, j=j: e.dma_start(
                    out=maskb[j % 2].ap[:].rearrange("p (i n) -> p i n", n=512),
                    in_=MASKS[j].rearrange("i p n -> p i n")), writes=[maskb[j % 2]])

            def load_h(j, h):
                if h >= 16:
                    j, h = j + 1, 0
                if j >= NJ:
                    return
                hb = h % 2
                kb.dma("y", lambda e, h=h, hb=hb: e.dma_start(
                    out=ktb[hb].ap[:].rearrange("p (t n) -> p t n", n=512),
                    in_=ktv[:, h, :].rearrange("p (t n) -> p t n", n=512)),
                    writes=[ktb[hb]], extra=list(scr_kt.w))
                kb.dma("s", lambda e, h=h, hb=hb: e.dma_start(
                    out=vtb[hb].ap[:].rearrange("p (t f) -> p t f", f=128),
                    in_=vtv[:, :, h * 128:(h + 1) * 128]),
                    writes=[vtb[hb]], extra=list(scr_vt.w))
                kb.dma("s", lambda e, h=h, j=j, hb=hb: e.dma_start(
                    out=qjb[hb].ap[:], in_=qtv[:, h, j * 512:(j + 1) * 512]),
                    writes=[qjb[hb]], extra=list(scr_qt.w))

            load_mask(0)
            load_h(0, 0)
            for j in range(NJ):
                load_mask(j + 1)
                mb = maskb[j % 2]
                for h in range(16):
                    hb = h % 2
                    load_h(j, h + 1)
                    kt, vt, qj = ktb[hb], vtb[hb], qjb[hb]
                    pv_ = ps_pv[hb]

                    def emit_pv(ipp):
                        bp = ipp % 3
                        for half in range(2):
                            i = 2 * ipp + half
                            st, sp = (i == 0), (i == NI - 1)
                            kb.accum_matmul(pv_, vt, pt[bp],
                                lambda e, i=i, half=half, bp=bp, st=st, sp=sp,
                                       vt=vt, pv_=pv_: e.matmul(
                                    pv_.ap[:], vt.ap[:, i * 128:(i + 1) * 128],
                                    pt[bp].ap[:, half * 512:(half + 1) * 512],
                                    start=st, stop=sp), st, sp)
                            kb.accum_matmul(ps_dn, ones_col, pt[bp],
                                lambda e, half=half, bp=bp, st=st, sp=sp: e.matmul(
                                    ps_dn.ap[:], ones_col.ap[:, 0:1],
                                    pt[bp].ap[:, half * 512:(half + 1) * 512],
                                    start=st, stop=sp), st, sp)

                    for ip in range(NI // 2):
                        b = ip % 3
                        sc = ps_sc[ip % 2]
                        for half in range(2):
                            i = 2 * ip + half
                            kb.accum_matmul(sc, kt, qj,
                                lambda e, i=i, half=half, sc=sc, kt=kt, qj=qj: e.matmul(
                                    sc.ap[:, half * 512:(half + 1) * 512],
                                    kt.ap[:, i * 128:(i + 1) * 128], qj.ap[:],
                                    start=True, stop=True), half == 0, half == 1)
                        kb.s(lambda e, b=b, sc=sc: e.activation(
                            out=pt[b].ap[:], in_=sc.ap[:], func=AF.Exp, scale=SCALE),
                            reads=[sc], writes=[pt[b]])
                        kb.v(lambda e, ip=ip, b=b, mb=mb: e.tensor_mul(
                            out=pt[b].ap[:], in0=pt[b].ap[:],
                            in1=mb.ap[:, ip * 1024:(ip + 1) * 1024]),
                            reads=[mb, pt[b]], writes=[pt[b]])
                        if ip >= 2:
                            emit_pv(ip - 2)
                    emit_pv(NI // 2 - 2)
                    emit_pv(NI // 2 - 1)
                    kb.v(lambda e: e.reciprocal(out=dnrow.ap[:], in_=ps_dn.ap[0:1, :]),
                         reads=[ps_dn], writes=[dnrow])
                    kb.t(lambda e: e.matmul(ps_db.ap[:], ones_row.ap[0:1, :],
                                            dnrow.ap[0:1, :], start=True, stop=True),
                         reads=[dnrow, ones_row], writes=[ps_db])
                    kb.v(lambda e: e.tensor_copy(out=dnb.ap[:], in_=ps_db.ap[:]),
                         reads=[ps_db], writes=[dnb])
                    kb.v(lambda e, pv_=pv_: e.tensor_mul(out=aoj.ap[:], in0=pv_.ap[:],
                                                         in1=dnb.ap[:]),
                         reads=[pv_, dnb], writes=[aoj])
                    kb.dma("y", lambda e, h=h, j=j: e.dma_start(
                        out=aotv[:, h, j * 512:(j + 1) * 512], in_=aoj.ap[:]),
                        reads=[aoj], writes=[scr_aot])
        kb.phase_barrier()

        with ExitStack() as sub:
            SE = sub.enter_context
            aoc = Buf(SE(nc.sbuf_tensor("aoc", [128, 16 * 512], BF16)))
            wotb = Buf(SE(nc.sbuf_tensor("wotb", [128, 16 * 2048], BF16)))
            outc = Buf(SE(nc.sbuf_tensor("outcA", [128, 16 * 512], F32)))
            ps = Buf(SE(nc.psum_tensor("psA3", [128, 512], F32)))
            lnc = _ln_cache(kb, nc, SE, "lnA3", 512)
            wotv = WOT.rearrange("(t p) f -> p t f", p=128)
            kb.dma("p", lambda e: e.dma_start(
                out=wotb.ap[:].rearrange("p (t f) -> p t f", f=2048),
                in_=wotv), writes=[wotb])
            r1buf = Buf(None)
            for oc in range(NOC):
                c0 = oc * 512
                kb.dma("y", lambda e, c0=c0: e.dma_start(
                    out=aoc.ap[:].rearrange("p (t n) -> p t n", n=512),
                    in_=aotv[:, :, c0:c0 + 512]),
                    writes=[aoc], extra=list(scr_aot.w))
                for j in range(16):
                    for i in range(16):
                        st, sp = (i == 0), (i == 15)
                        kb.accum_matmul(ps, wotb, aoc,
                            lambda e, i=i, j=j, st=st, sp=sp: e.matmul(
                                ps.ap[:], wotb.ap[:, i * 2048 + j * 128:i * 2048 + (j + 1) * 128],
                                aoc.ap[:, i * 512:(i + 1) * 512], start=st, stop=sp), st, sp)
                    kb.v(lambda e, j=j: e.tensor_copy(out=outc.ap[:, j * 512:(j + 1) * 512],
                                                      in_=ps.ap[:]),
                         reads=[ps], writes=[outc])
                otiles = [outc.view(outc.ap[:, i * 512:(i + 1) * 512]) for i in range(16)]
                _ln_chunk(kb, nc, SE, otiles, otiles, ones_col, ones_row, 1e-5,
                          "", kb_cache=lnc)
                kb.merge_views(outc, otiles)
                kb.dma("y", lambda e, c0=c0: e.dma_start(
                    out=r1v[:, :, c0:c0 + 512],
                    in_=outc.ap[:].rearrange("p (t n) -> p t n", n=512)),
                    reads=[outc], writes=[r1buf])
                regq2 = kb.value_load("p", idxq)
                kb.dma("p", lambda e, c0=c0, regq2=regq2: e.dma_start(
                    out=r1v[:, :, c0:c0 + 512],
                    in_=xctxv[:, :, bass.ts(regq2, T_)][:, :, c0:c0 + 512],
                    accum_op=ALU.add),
                    reads=[idxq], writes=[r1buf])
        kb.final_wait("y")
    return nc



CH = 128
EXT = 128


def build_launch_C(T_=T):
    TE = T_ + EXT
    NCHT = TE // 128
    nc = bass.Bass("TRN2", target_bir_lowering=False)
    R2E = nc.dram_tensor("R2E", [D, TE], F32, kind="ExternalInput").ap()
    WINT = nc.dram_tensor("WINT", [D, 8384], F32, kind="ExternalInput").ap()
    CONVW = nc.dram_tensor("CONVW", [4224, 4], F32, kind="ExternalInput").ap()
    CONVB = nc.dram_tensor("CONVB", [4224, 1], F32, kind="ExternalInput").ap()
    DTB = nc.dram_tensor("DTB", [64, 1], F32, kind="ExternalInput").ap()
    ALOG = nc.dram_tensor("ALOG", [64, 1], F32, kind="ExternalInput").ap()
    DROW = nc.dram_tensor("DROW", [1, 4096], F32, kind="ExternalInput").ap()
    TRI = nc.dram_tensor("TRI", [128, 128], F32, kind="ExternalInput").ap()
    EYE = nc.dram_tensor("EYE", [128, 128], BF16, kind="ExternalInput").ap()
    EYEF = nc.dram_tensor("EYEF", [128, 128], F32, kind="ExternalInput").ap()
    ONESC = nc.dram_tensor("ONESC", [128, 1], BF16, kind="ExternalInput").ap()
    ONESR = nc.dram_tensor("ONESR", [1, 128], F32, kind="ExternalInput").ap()
    YP = nc.dram_tensor("YP", [T_, 4096], F32, kind="ExternalOutput").ap()
    ZS = nc.dram_tensor("ZS", [T_, 4096], BF16, kind="ExternalOutput").ap()
    CS = nc.dram_tensor("CS", [64, T_], BF16, kind="ExternalOutput").ap()
    AC = nc.dram_tensor("AC", [64, T_], F32, kind="ExternalOutput").ap()
    UEND = nc.dram_tensor("UEND", [64, 4096], F32, kind="ExternalOutput").ap()
    ATOT = nc.dram_tensor("ATOT", [64, 1], F32, kind="ExternalOutput").ap()
    XBCS = nc.dram_tensor("xbcs", [4224, TE], BF16, kind="Internal").ap()

    r2v = R2E.rearrange("(t p) n -> p t n", p=128)
    wintv = WINT.rearrange("(t p) f -> p t f", p=128)
    xbv = XBCS.rearrange("(t p) n -> p t n", p=128)
    ypv = YP.rearrange("(t p) f -> p t f", p=128)
    zsv = ZS.rearrange("(t p) f -> p t f", p=128)

    from contextlib import ExitStack
    with ExitStack() as ctx:
        kb = KB(nc)
        kb.init_sems(ctx)
        E = ctx.enter_context
        ones_col = Buf(E(nc.sbuf_tensor("onesc", [128, 1], BF16)))
        ones_row = Buf(E(nc.sbuf_tensor("onesr", [1, 128], F32)))
        tri = Buf(E(nc.sbuf_tensor("tri", [128, 128], F32)))
        eyeb = Buf(E(nc.sbuf_tensor("eyeb", [128, 128], BF16)))
        eyef = Buf(E(nc.sbuf_tensor("eyef", [128, 128], F32)))
        for b, t_ in ((ones_col, ONESC), (ones_row, ONESR), (tri, TRI),
                      (eyeb, EYE), (eyef, EYEF)):
            kb.dma("y", lambda e, b=b, t_=t_: e.dma_start(out=b.ap[:], in_=t_), writes=[b])
        dtbc = Buf(E(nc.sbuf_tensor("dtbc", [64, 1], F32)))
        alogc = Buf(E(nc.sbuf_tensor("alogc", [64, 1], F32)))
        kb.dma("y", lambda e: e.dma_start(out=dtbc.ap[:], in_=DTB), writes=[dtbc])
        kb.dma("y", lambda e: e.dma_start(out=alogc.ap[:], in_=ALOG), writes=[alogc])

        dtv = Buf(E(nc.sbuf_tensor("dtv", [64, TE], F32)))
        acum = Buf(E(nc.sbuf_tensor("acum", [64, TE], F32)))
        one64 = kb.const_col(ctx, 1.0, 64)
        zeros_chunk = Buf(E(nc.sbuf_tensor("zchk", [64, CH], F32)))
        kb.v(lambda e: e.memset(zeros_chunk.ap[:], 0.0), writes=[zeros_chunk])
        scr_xbc = Buf(None)
        out_zs = Buf(None)
        out_cs = Buf(None)
        out_ac = Buf(None)
        out_yp = Buf(None)
        out_ue = Buf(None)
        out_at = Buf(None)

        with ExitStack() as sub:
            SE = sub.enter_context
            r2c = Buf(SE(nc.sbuf_tensor("r2c", [128, 16 * 512], F32)))
            xlnT = Buf(SE(nc.sbuf_tensor("xlnT", [128, 16 * TE], BF16)))
            lnc = _ln_cache(kb, nc, SE, "lnC", 512)
            c0 = 0
            while c0 < TE:
                w = min(512, TE - c0)
                kb.dma("y", lambda e, c0=c0, w=w: e.dma_start(
                    out=r2c.ap[:, 0:16 * w].rearrange("p (t n) -> p t n", n=w),
                    in_=r2v[:, :, c0:c0 + w]), writes=[r2c])
                xts = [r2c.view(r2c.ap[:, i * w:(i + 1) * w]) for i in range(16)]
                ots = [xlnT.view(xlnT.ap[:, i * TE + c0:i * TE + c0 + w]) for i in range(16)]
                _ln_chunk(kb, nc, SE, xts, ots, ones_col, ones_row, 1e-5,
                          "", kb_cache=lnc)
                kb.merge_views(r2c, xts)
                kb.merge_views(xlnT, ots)
                c0 += w
            winb = [Buf(SE(nc.sbuf_tensor(f"winbC{i}", [128, 16 * 256], BF16))) for i in range(2)]
            ps = Buf(SE(nc.psum_tensor("psC2", [128, 512], F32)))
            evb = Buf(SE(nc.sbuf_tensor("evC2", [128, 512], BF16)))
            for zc in range(16):
                wbf = winb[zc % 2]
                kb.dma("p", lambda e, zc=zc, wbf=wbf: e.dma_start(
                    out=wbf.ap[:].rearrange("p (t f) -> p t f", f=256),
                    in_=wintv[:, :, zc * 256:(zc + 1) * 256]), writes=[wbf])
                for tt in range(T_ // 128):
                    for i in range(16):
                        st, sp = (i == 0), (i == 15)
                        kb.accum_matmul(ps, xlnT, wbf,
                            lambda e, i=i, tt=tt, wbf=wbf, st=st, sp=sp: e.matmul(
                                ps.ap[:, 0:256], xlnT.ap[:, i * TE + EXT + tt * 128:i * TE + EXT + (tt + 1) * 128],
                                wbf.ap[:, i * 256:(i + 1) * 256], start=st, stop=sp), st, sp)
                    kb.v(lambda e: e.tensor_copy(out=evb.ap[:, 0:256], in_=ps.ap[:, 0:256]),
                         reads=[ps], writes=[evb])
                    kb.dma("y", lambda e, tt=tt, zc=zc: e.dma_start(
                        out=zsv[:, tt, zc * 256:(zc + 1) * 256], in_=evb.ap[:, 0:256]),
                        reads=[evb], writes=[out_zs])
            winf = [Buf(SE(nc.sbuf_tensor(f"winfC{i}", [128, 16 * 128], BF16))) for i in range(2)]
            evf = Buf(SE(nc.sbuf_tensor("evfC", [128, TE], BF16)))
            for fc in range(33):
                wbf = winf[fc % 2]
                kb.dma("p", lambda e, fc=fc, wbf=wbf: e.dma_start(
                    out=wbf.ap[:].rearrange("p (t f) -> p t f", f=128),
                    in_=wintv[:, :, 4096 + fc * 128:4096 + (fc + 1) * 128]), writes=[wbf])
                c0 = 0
                while c0 < TE:
                    w = min(512, TE - c0)
                    for i in range(16):
                        st, sp = (i == 0), (i == 15)
                        kb.accum_matmul(ps, wbf, xlnT,
                            lambda e, i=i, c0=c0, w=w, wbf=wbf, st=st, sp=sp: e.matmul(
                                ps.ap[:, 0:w], wbf.ap[:, i * 128:(i + 1) * 128],
                                xlnT.ap[:, i * TE + c0:i * TE + c0 + w], start=st, stop=sp), st, sp)
                    kb.v(lambda e, c0=c0, w=w: e.tensor_copy(out=evf.ap[:, c0:c0 + w],
                                                             in_=ps.ap[:, 0:w]),
                         reads=[ps], writes=[evf])
                    c0 += w
                kb.dma("y", lambda e, fc=fc: e.dma_start(
                    out=xbv[:, fc, :], in_=evf.ap[:]), reads=[evf], writes=[scr_xbc])
            wdt = Buf(SE(nc.sbuf_tensor("wdtC", [128, 16 * 64], BF16)))
            kb.dma("p", lambda e: e.dma_start(
                out=wdt.ap[:].rearrange("p (t f) -> p t f", f=64),
                in_=wintv[:, :, 8320:8384]), writes=[wdt])
            ps64 = Buf(SE(nc.psum_tensor("ps64C", [64, 512], F32)))
            c0 = 0
            while c0 < TE:
                w = min(512, TE - c0)
                for i in range(16):
                    st, sp = (i == 0), (i == 15)
                    kb.accum_matmul(ps64, wdt, xlnT,
                        lambda e, i=i, c0=c0, w=w, st=st, sp=sp: e.matmul(
                            ps64.ap[:, 0:w], wdt.ap[:, i * 64:(i + 1) * 64],
                            xlnT.ap[:, i * TE + c0:i * TE + c0 + w], start=st, stop=sp), st, sp)
                kb.v(lambda e, c0=c0, w=w: e.tensor_copy(out=dtv.ap[:, c0:c0 + w],
                                                         in_=ps64.ap[:, 0:w]),
                     reads=[ps64], writes=[dtv])
                c0 += w
        kb.phase_barrier()

        with ExitStack() as sub:
            SE = sub.enter_context
            tmpa = Buf(SE(nc.sbuf_tensor("tmpaC", [64, TE], F32)))
            kb.v(lambda e: e.tensor_scalar(out=dtv.ap[:], in0=dtv.ap[:],
                                           scalar1=dtbc.ap[:, 0:1], scalar2=None,
                                           op0=ALU.add), reads=[dtbc, dtv], writes=[dtv])
            kb.s(lambda e: e.activation(out=tmpa.ap[:], in_=dtv.ap[:], func=AF.Abs),
                 reads=[dtv], writes=[tmpa])
            kb.s(lambda e: e.activation(out=tmpa.ap[:], in_=tmpa.ap[:], func=AF.Exp,
                                        scale=-1.0), reads=[tmpa], writes=[tmpa])
            kb.s(lambda e: e.activation(out=tmpa.ap[:], in_=tmpa.ap[:], func=AF.Ln,
                                        bias=one64.ap[:, 0:1]),
                 reads=[tmpa, one64], writes=[tmpa])
            kb.v(lambda e: e.tensor_scalar(out=dtv.ap[:], in0=dtv.ap[:], scalar1=0.0,
                                           scalar2=None, op0=ALU.max),
                 reads=[dtv], writes=[dtv])
            kb.v(lambda e: e.tensor_add(out=dtv.ap[:], in0=dtv.ap[:], in1=tmpa.ap[:]),
                 reads=[tmpa, dtv], writes=[dtv])
            acol = Buf(SE(nc.sbuf_tensor("acolC", [64, 1], F32)))
            kb.s(lambda e: e.activation(out=acol.ap[:], in_=alogc.ap[:], func=AF.Exp),
                 reads=[alogc], writes=[acol])
            kb.v(lambda e: e.tensor_scalar(out=acol.ap[:], in0=acol.ap[:], scalar1=-1.0,
                                           scalar2=None, op0=ALU.mult),
                 reads=[acol], writes=[acol])
            kb.v(lambda e: e.tensor_scalar(out=tmpa.ap[:], in0=dtv.ap[:],
                                           scalar1=acol.ap[:, 0:1], scalar2=None,
                                           op0=ALU.mult), reads=[dtv, acol], writes=[tmpa])
            for chk in range(NCHT):
                kb.v(lambda e, chk=chk: e.tensor_tensor_scan(
                    out=acum.ap[:, chk * CH:(chk + 1) * CH],
                    data0=tmpa.ap[:, chk * CH:(chk + 1) * CH],
                    data1=zeros_chunk.ap[:], initial=0.0,
                    op0=ALU.add, op1=ALU.add),
                    reads=[tmpa, zeros_chunk], writes=[acum])
            kb.dma("y", lambda e: e.dma_start(out=AC, in_=acum.ap[:, EXT:]),
                   reads=[acum], writes=[out_ac])
            ach = Buf(SE(nc.sbuf_tensor("achC", [64, T_ // 128], F32)))
            atotb = Buf(SE(nc.sbuf_tensor("atotC", [64, 1], F32)))
            kb.v(lambda e: e.tensor_copy(out=ach.ap[:],
                                         in_=acum.ap[:, EXT + CH - 1::CH]),
                 reads=[acum], writes=[ach])
            kb.v(lambda e: e.tensor_reduce(out=atotb.ap[:], in_=ach.ap[:],
                                           axis=AX.X, op=ALU.add),
                 reads=[ach], writes=[atotb])
            kb.dma("y", lambda e: e.dma_start(out=ATOT, in_=atotb.ap[:]),
                   reads=[atotb], writes=[out_at])
        kb.phase_barrier()

        ACRS = nc.dram_tensor("acrs", [64, CH], F32, kind="Internal").ap()
        with ExitStack() as sub:
            SE = sub.enter_context
            drow = Buf(SE(nc.sbuf_tensor("drow", [1, 4096], F32)))
            kb.dma("y", lambda e: e.dma_start(out=drow.ap[:], in_=DROW), writes=[drow])
            dbc = Buf(SE(nc.sbuf_tensor("dbc", [128, 4096], F32)))
            ps_bcD = Buf(SE(nc.psum_tensor("ps_bcC", [128, 512], F32)))
            for g in range(8):
                kb.t(lambda e, g=g: e.matmul(ps_bcD.ap[:], ones_row.ap[0:1, :],
                                             drow.ap[0:1, g * 512:(g + 1) * 512],
                                             start=True, stop=True),
                     reads=[drow, ones_row], writes=[ps_bcD])
                kb.v(lambda e, g=g: e.tensor_copy(out=dbc.ap[:, g * 512:(g + 1) * 512],
                                                  in_=ps_bcD.ap[:]), reads=[ps_bcD], writes=[dbc])
            convwb = Buf(SE(nc.sbuf_tensor("convwb", [128, 33 * 4], F32)))
            convbb = Buf(SE(nc.sbuf_tensor("convbb", [128, 33], F32)))
            kb.dma("y", lambda e: e.dma_start(
                out=convwb.ap[:].rearrange("p (t w) -> p t w", w=4),
                in_=CONVW.rearrange("(t p) w -> p t w", p=128),
                allow_slow_non_contiguous=True), writes=[convwb])
            kb.dma("y", lambda e: e.dma_start(
                out=convbb.ap[:].unsqueeze(2),
                in_=CONVB.rearrange("(t p) o -> p t o", p=128),
                allow_slow_non_contiguous=True), writes=[convbb])
            xbcw = Buf(SE(nc.sbuf_tensor("xbcw", [128, 17 * 516], BF16)))
            cvacc = Buf(SE(nc.sbuf_tensor("cvacc", [128, 512], F32)))
            convs = Buf(SE(nc.sbuf_tensor("convs", [128, 33 * 512], BF16)))
            xtokc = Buf(SE(nc.sbuf_tensor("xtokc", [128, 4096], BF16)))
            tmp512 = Buf(SE(nc.sbuf_tensor("tmp512C", [128, 512], F32)))
            ps_tr = Buf(SE(nc.psum_tensor("ps_trC", [128, 128], BF16)))
            ps_trf = Buf(SE(nc.psum_tensor("ps_trfC", [128, 128], F32)))
            ps_g = Buf(SE(nc.psum_tensor("ps_gC", [128, 128], F32)))
            ps_b2 = Buf(SE(nc.psum_tensor("ps_b2C", [128, 128], F32)))
            ps_y = Buf(SE(nc.psum_tensor("ps_yC", [128, 512], F32)))
            ps_st = Buf(SE(nc.psum_tensor("ps_stC", [64, 512], F32)))
            ps_bc2 = Buf(SE(nc.psum_tensor("ps_bc2C", [64, 512], F32)))
            dtT = Buf(SE(nc.sbuf_tensor("dtT", [128, 64], F32)))
            decS = Buf(SE(nc.sbuf_tensor("decS", [64, CH], F32)))
            decT = Buf(SE(nc.sbuf_tensor("decT", [128, 64], F32)))
            acumT = Buf(SE(nc.sbuf_tensor("acumT", [128, 64], F32)))
            acrow = Buf(SE(nc.sbuf_tensor("acrowC", [1, 8 * CH], F32)))
            exprow = Buf(SE(nc.sbuf_tensor("exprowC", [1, 8 * CH], BF16)))
            eachr = Buf(SE(nc.sbuf_tensor("eachrC", [1, 64], F32)))
            ones_rowb = Buf(SE(nc.sbuf_tensor("onesrbC", [1, 128], BF16)))
            kb.v(lambda e: e.memset(ones_rowb.ap[:], 1.0), writes=[ones_rowb])
            xdth = Buf(SE(nc.sbuf_tensor("xdth", [128, 64], BF16)))
            xdtd = Buf(SE(nc.sbuf_tensor("xdtd", [128, 64], BF16)))
            btok = Buf(SE(nc.sbuf_tensor("btok", [128, 64], BF16)))
            segS = Buf(SE(nc.sbuf_tensor("segS", [128, CH], F32)))
            wmat = Buf(SE(nc.sbuf_tensor("wmat", [128, CH], BF16)))
            cd = Buf(SE(nc.sbuf_tensor("cdC", [64, CH], BF16)))
            rtile = Buf(SE(nc.sbuf_tensor("rtile", [64, 4096], F32)))
            rbf = Buf(SE(nc.sbuf_tensor("rbfC", [64, 4096], BF16)))
            stt_s = Buf(SE(nc.sbuf_tensor("sttC", [64, 4096], F32)))
            ybuf = Buf(SE(nc.sbuf_tensor("ybuf", [128, 4096], F32)))
            cloc = Buf(SE(nc.sbuf_tensor("clocC", [64, 512], BF16)))
            acr_d = Buf(None)
            kb.v(lambda e: e.memset(rtile.ap[:], 0.0), writes=[rtile])

            for cc in range(T_ // 512):
                t0 = EXT + cc * 512
                for grp, (f0, nf) in enumerate(((0, 17), (17, 16))):
                    kb.dma("y", lambda e, t0=t0, f0=f0, nf=nf: e.dma_start(
                        out=xbcw.ap[:, 0:nf * 516].rearrange("p (t n) -> p t n", n=516),
                        in_=xbv[:, f0:f0 + nf, t0 - 4:t0 + 512]),
                        writes=[xbcw], extra=list(scr_xbc.w))
                    for lf in range(nf):
                        ft = f0 + lf
                        kb.v(lambda e, lf=lf, ft=ft: e.tensor_scalar(
                            out=cvacc.ap[:],
                            in0=xbcw.ap[:, lf * 516 + 1:lf * 516 + 513],
                            scalar1=convwb.ap[:, 4 * ft:4 * ft + 1],
                            scalar2=convbb.ap[:, ft:ft + 1],
                            op0=ALU.mult, op1=ALU.add),
                            reads=[xbcw, convwb, convbb], writes=[cvacc])
                        for w_ in range(1, 4):
                            kb.v(lambda e, lf=lf, ft=ft, w_=w_: e.scalar_tensor_tensor(
                                out=cvacc.ap[:],
                                in0=xbcw.ap[:, lf * 516 + 1 + w_:lf * 516 + 513 + w_],
                                scalar=convwb.ap[:, 4 * ft + w_:4 * ft + w_ + 1],
                                in1=cvacc.ap[:],
                                op0=ALU.mult, op1=ALU.add),
                                reads=[xbcw, convwb, cvacc], writes=[cvacc])
                        kb.s(lambda e, ft=ft: e.activation(
                            out=convs.ap[:, ft * 512:(ft + 1) * 512], in_=cvacc.ap[:],
                            func=AF.Silu), reads=[cvacc], writes=[convs])
                kb.dma("y", lambda e, cc=cc: e.dma_start(
                    out=CS[:, cc * 512:(cc + 1) * 512],
                    in_=convs.ap[64:128, 32 * 512:33 * 512]),
                    reads=[convs], writes=[out_cs])
                kb.v(lambda e: e.tensor_copy(out=cloc.ap[:],
                                             in_=convs.ap[64:128, 32 * 512:33 * 512]),
                     reads=[convs], writes=[cloc])

                for lt in range(4):
                    chg = cc * 4 + lt
                    s0 = lt * 128
                    a0 = EXT + chg * CH
                    acv = acum.view(acum.ap[:, a0:a0 + CH])
                    dtvv = dtv.view(dtv.ap[:, a0:a0 + CH])
                    bmat = convs.view(convs.ap[0:64, 32 * 512 + s0:32 * 512 + s0 + 128])
                    cmat = cloc.view(cloc.ap[:, s0:s0 + 128])
                    for ft in range(32):
                        kb.t(lambda e, ft=ft, s0=s0: e.transpose(
                            ps_tr.ap[:], convs.ap[:, ft * 512 + s0:ft * 512 + s0 + 128],
                            eyeb.ap[:]), reads=[convs, eyeb], writes=[ps_tr])
                        kb.v(lambda e, ft=ft: e.tensor_copy(
                            out=xtokc.ap[:, ft * 128:(ft + 1) * 128],
                            in_=ps_tr.ap[:]), reads=[ps_tr], writes=[xtokc])
                    xt = xtokc
                    kb.v(lambda e: e.tensor_copy(out=rbf.ap[:], in_=rtile.ap[:]),
                         reads=[rtile], writes=[rbf])
                    kb.dma("y", lambda e: e.dma_start(out=ACRS, in_=acv.ap[:]),
                           reads=[acv], writes=[acr_d])
                    kb.dma("y", lambda e: e.dma_start(
                        out=eachr.ap[:],
                        in_=ACRS.rearrange("h l -> (h l)").unsqueeze(0)[:, CH - 1::CH],
                        allow_slow_non_contiguous=True),
                        reads=[acr_d], writes=[eachr], extra=list(acr_d.w))
                    kb.s(lambda e: e.activation(out=eachr.ap[:], in_=eachr.ap[:],
                                                func=AF.Exp), reads=[eachr], writes=[eachr])
                    kb.t(lambda e: e.transpose(ps_trf.ap[:, 0:64], dtvv.ap[:], eyef.ap[0:64, 0:64]),
                         reads=[dtvv, eyef], writes=[ps_trf])
                    kb.v(lambda e: e.tensor_copy(out=dtT.ap[:], in_=ps_trf.ap[:, 0:64]),
                         reads=[ps_trf], writes=[dtT])
                    kb.t(lambda e: e.transpose(ps_trf.ap[:, 0:64], acv.ap[:], eyef.ap[0:64, 0:64]),
                         reads=[acv, eyef], writes=[ps_trf])
                    kb.v(lambda e: e.tensor_copy(out=acumT.ap[:], in_=ps_trf.ap[:, 0:64]),
                         reads=[ps_trf], writes=[acumT])
                    kb.v(lambda e: e.tensor_scalar(
                        out=decS.ap[:], in0=acv.ap[:], scalar1=acv.ap[:, CH - 1:CH],
                        scalar2=None, op0=ALU.subtract), reads=[acv], writes=[decS])
                    kb.s(lambda e: e.activation(out=decS.ap[:], in_=decS.ap[:],
                                                func=AF.Exp, scale=-1.0),
                         reads=[decS], writes=[decS])
                    kb.t(lambda e: e.transpose(ps_trf.ap[:, 0:64], decS.ap[:], eyef.ap[0:64, 0:64]),
                         reads=[decS, eyef], writes=[ps_trf])
                    kb.v(lambda e: e.tensor_copy(out=decT.ap[:], in_=ps_trf.ap[:, 0:64]),
                         reads=[ps_trf], writes=[decT])
                    kb.t(lambda e: e.transpose(ps_tr.ap[:, 0:64], bmat.ap[:], eyeb.ap[0:64, 0:64]),
                         reads=[bmat, eyeb], writes=[ps_tr])
                    kb.v(lambda e: e.tensor_copy(out=btok.ap[:], in_=ps_tr.ap[:, 0:64]),
                         reads=[ps_tr], writes=[btok])
                    kb.t(lambda e: e.matmul(ps_g.ap[:], bmat.ap[:], cmat.ap[:],
                                            start=True, stop=True),
                         reads=[bmat, cmat], writes=[ps_g])
                    for h in range(64):
                        hg = h % 8
                        if hg == 0:
                            kb.dma("y", lambda e, h=h: e.dma_start(
                                out=acrow.ap[:],
                                in_=ACRS.rearrange("h l -> (h l)").unsqueeze(0)[:, h * CH:(h + 8) * CH]),
                                reads=[acr_d], writes=[acrow], extra=list(acr_d.w))
                            kb.s(lambda e: e.activation(out=exprow.ap[:], in_=acrow.ap[:],
                                                        func=AF.Exp),
                                 reads=[acrow], writes=[exprow])
                        kb.t(lambda e, hg=hg: e.matmul(
                            ps_b2.ap[:], ones_row.ap[0:1, :],
                            acrow.ap[0:1, hg * CH:(hg + 1) * CH], start=True, stop=True),
                            reads=[acrow, ones_row], writes=[ps_b2])
                        kb.v(lambda e, h=h: e.tensor_scalar(
                            out=segS.ap[:], in0=ps_b2.ap[:], scalar1=acumT.ap[:, h:h + 1],
                            scalar2=None, op0=ALU.subtract),
                            reads=[ps_b2, acumT], writes=[segS])
                        kb.v(lambda e: e.tensor_add(out=segS.ap[:], in0=segS.ap[:],
                                                    in1=tri.ap[:]),
                             reads=[tri, segS], writes=[segS])
                        kb.s(lambda e: e.activation(out=segS.ap[:], in_=segS.ap[:],
                                                    func=AF.Exp), reads=[segS], writes=[segS])
                        kb.v(lambda e: e.tensor_tensor(out=wmat.ap[:], in0=segS.ap[:],
                                                       in1=ps_g.ap[:], op=ALU.mult),
                             reads=[segS, ps_g], writes=[wmat])
                        kb.v(lambda e, h=h: e.tensor_scalar(
                            out=xdth.ap[:], in0=xt.ap[:, h * 64:(h + 1) * 64],
                            scalar1=dtT.ap[:, h:h + 1], scalar2=None, op0=ALU.mult),
                            reads=[xt, dtT], writes=[xdth])
                        kb.v(lambda e, h=h: e.tensor_scalar(
                            out=xdtd.ap[:], in0=xdth.ap[:], scalar1=decT.ap[:, h:h + 1],
                            scalar2=None, op0=ALU.mult),
                            reads=[xdth, decT], writes=[xdtd])
                        kb.accum_matmul(ps_y, wmat, xdth,
                            lambda e, hg=hg: e.matmul(
                                ps_y.ap[:, hg * 64:(hg + 1) * 64], wmat.ap[:], xdth.ap[:],
                                start=True, stop=False), True, False)
                        kb.t(lambda e, hg=hg: e.matmul(
                            ps_bc2.ap[:, 0:CH], ones_rowb.ap[0:1, 0:64],
                            exprow.ap[0:1, hg * CH:(hg + 1) * CH],
                            start=True, stop=True),
                            reads=[exprow, ones_rowb], writes=[ps_bc2])
                        kb.v(lambda e: e.tensor_tensor(
                            out=cd.ap[:], in0=cmat.ap[:], in1=ps_bc2.ap[:, 0:CH],
                            op=ALU.mult),
                            reads=[cmat, ps_bc2], writes=[cd])
                        kb.accum_matmul(ps_y, cd, rbf,
                            lambda e, h=h, hg=hg: e.matmul(
                                ps_y.ap[:, hg * 64:(hg + 1) * 64], cd.ap[:],
                                rbf.ap[:, h * 64:(h + 1) * 64],
                                start=False, stop=True), False, True)
                        kb.t(lambda e, h=h, hg=hg: e.matmul(
                            ps_st.ap[:, hg * 64:(hg + 1) * 64], btok.ap[:], xdtd.ap[:],
                            start=True, stop=True),
                            reads=[btok, xdtd],
                            writes=[ps_st] if hg == 7 else [],
                            extra=(list(ps_st.w) + list(ps_st.r)) if hg != 7 else ())
                        if hg == 7:
                            g = h // 8
                            kb.v(lambda e, g=g: e.tensor_tensor(
                                out=tmp512.ap[:], in0=xt.ap[:, g * 512:(g + 1) * 512],
                                in1=dbc.ap[:, g * 512:(g + 1) * 512], op=ALU.mult),
                                reads=[xt, dbc], writes=[tmp512])
                            kb.v(lambda e, g=g: e.tensor_add(
                                out=ybuf.ap[:, g * 512:(g + 1) * 512],
                                in0=tmp512.ap[:], in1=ps_y.ap[:]),
                                reads=[tmp512, ps_y], writes=[ybuf])
                            kb.v(lambda e, g=g: e.tensor_copy(
                                out=stt_s.ap[:, g * 512:(g + 1) * 512], in_=ps_st.ap[:]),
                                reads=[ps_st], writes=[stt_s])
                    kb.dma("y", lambda e, chg=chg: e.dma_start(
                        out=ypv[:, chg, :], in_=ybuf.ap[:]), reads=[ybuf], writes=[out_yp])
                    for g in range(8):
                        rhs_bc = eachr.ap[0:1, g * 8:(g + 1) * 8].unsqueeze(2).broadcast_to([1, 8, 64])
                        kb.t(lambda e, g=g, rhs_bc=rhs_bc: e.matmul(
                            ps_bc2.ap[:], ones_row.ap[0:1, 0:64], rhs_bc,
                            start=True, stop=True),
                            reads=[eachr, ones_row], writes=[ps_bc2])
                        kb.v(lambda e, g=g: e.scalar_tensor_tensor(
                            out=rtile.ap[:, g * 512:(g + 1) * 512],
                            in0=rtile.ap[:, g * 512:(g + 1) * 512], scalar=1.0,
                            in1=ps_bc2.ap[:], op0=ALU.mult, op1=ALU.mult),
                            reads=[ps_bc2, rtile], writes=[rtile])
                        kb.v(lambda e, g=g: e.tensor_add(
                            out=rtile.ap[:, g * 512:(g + 1) * 512],
                            in0=rtile.ap[:, g * 512:(g + 1) * 512],
                            in1=stt_s.ap[:, g * 512:(g + 1) * 512]),
                            reads=[stt_s, rtile], writes=[rtile])
            kb.dma("y", lambda e: e.dma_start(out=UEND, in_=rtile.ap[:]),
                   reads=[rtile], writes=[out_ue])

        kb.final_wait("y")
    return nc



def build_launch_D(T_=T):
    NCH_ = T_ // CH
    nc = bass.Bass("TRN2", target_bir_lowering=False)
    YP = nc.dram_tensor("YP", [T_, 4096], F32, kind="ExternalInput").ap()
    ZS = nc.dram_tensor("ZS", [T_, 4096], BF16, kind="ExternalInput").ap()
    CS = nc.dram_tensor("CS", [64, T_], BF16, kind="ExternalInput").ap()
    AC = nc.dram_tensor("AC", [64, T_], F32, kind="ExternalInput").ap()
    UENDS = nc.dram_tensor("UENDS", [3, 64, 4096], F32, kind="ExternalInput").ap()
    ATOTS = nc.dram_tensor("ATOTS", [64, 3], F32, kind="ExternalInput").ap()
    R2 = nc.dram_tensor("R2", [D, T_], F32, kind="ExternalInput").ap()
    WOUTT = nc.dram_tensor("WOUTT", [4096, D], F32, kind="ExternalInput").ap()
    NORMW = nc.dram_tensor("NORMW", [1, 4096], F32, kind="ExternalInput").ap()
    EYEB = nc.dram_tensor("EYE", [128, 128], BF16, kind="ExternalInput").ap()
    ONESC = nc.dram_tensor("ONESC", [128, 1], BF16, kind="ExternalInput").ap()
    ONESR = nc.dram_tensor("ONESR", [1, 128], F32, kind="ExternalInput").ap()
    R3 = nc.dram_tensor("R3", [D, T_], F32, kind="ExternalOutput").ap()
    ACRS = nc.dram_tensor("acrsD", [64, CH], F32, kind="Internal").ap()

    ypv = YP.rearrange("(t p) f -> p t f", p=128)
    zsv = ZS.rearrange("(t p) f -> p t f", p=128)
    r2v = R2.rearrange("(t p) n -> p t n", p=128)
    r3v = R3.rearrange("(t p) n -> p t n", p=128)
    woutv = WOUTT.rearrange("(t p) f -> p t f", p=128)

    from contextlib import ExitStack
    with ExitStack() as ctx:
        kb = KB(nc)
        kb.init_sems(ctx)
        E = ctx.enter_context
        ones_col = Buf(E(nc.sbuf_tensor("onesc", [128, 1], BF16)))
        ones_row = Buf(E(nc.sbuf_tensor("onesr", [1, 128], F32)))
        ones_rowb = Buf(E(nc.sbuf_tensor("onesrb", [1, 128], BF16)))
        eyeb = Buf(E(nc.sbuf_tensor("eyeb", [128, 128], BF16)))
        kb.dma("y", lambda e: e.dma_start(out=ones_col.ap[:], in_=ONESC), writes=[ones_col])
        kb.dma("y", lambda e: e.dma_start(out=ones_row.ap[:], in_=ONESR), writes=[ones_row])
        kb.dma("y", lambda e: e.dma_start(out=eyeb.ap[:], in_=EYEB), writes=[eyeb])
        kb.v(lambda e: e.memset(ones_rowb.ap[:], 1.0), writes=[ones_rowb])
        epsb = kb.const_col(ctx, 1e-5, 128)

        nwb = Buf(E(nc.sbuf_tensor("nwb", [128, 4096], F32)))
        inst = Buf(E(nc.sbuf_tensor("instD", [64, 4096], F32)))
        with ExitStack() as sub0:
            S0 = sub0.enter_context
            normw = Buf(S0(nc.sbuf_tensor("normw", [1, 4096], F32)))
            kb.dma("y", lambda e: e.dma_start(out=normw.ap[:], in_=NORMW), writes=[normw])
            ps_b = Buf(S0(nc.psum_tensor("ps_bD", [128, 512], F32)))
            for g in range(8):
                kb.t(lambda e, g=g: e.matmul(ps_b.ap[:], ones_row.ap[0:1, :],
                                             normw.ap[0:1, g * 512:(g + 1) * 512],
                                             start=True, stop=True),
                     reads=[normw, ones_row], writes=[ps_b])
                kb.v(lambda e, g=g: e.tensor_copy(out=nwb.ap[:, g * 512:(g + 1) * 512],
                                                  in_=ps_b.ap[:]), reads=[ps_b], writes=[nwb])
            uends = Buf(S0(nc.sbuf_tensor("uends", [64, 3 * 4096], F32)))
            kb.dma("y", lambda e: e.dma_start(
                out=uends.ap[:].rearrange("p (k f) -> p k f", f=4096), in_=UENDS.transpose([1, 0, 2])),
                writes=[uends])
            atots = Buf(S0(nc.sbuf_tensor("atots", [64, 3], F32)))
            kb.dma("y", lambda e: e.dma_start(out=atots.ap[:], in_=ATOTS), writes=[atots])
            ein = Buf(S0(nc.sbuf_tensor("einD", [64, 2], F32)))
            kb.s(lambda e: e.activation(out=ein.ap[:, 0:1], in_=atots.ap[:, 0:1], func=AF.Exp),
                 reads=[atots], writes=[ein])
            kb.v(lambda e: e.tensor_add(out=ein.ap[:, 1:2], in0=atots.ap[:, 0:1],
                                        in1=atots.ap[:, 1:2]), reads=[atots], writes=[ein])
            kb.s(lambda e: e.activation(out=ein.ap[:, 1:2], in_=ein.ap[:, 1:2], func=AF.Exp),
                 reads=[ein], writes=[ein])
            kb.v(lambda e: e.scalar_tensor_tensor(
                out=inst.ap[:], in0=uends.ap[:, 4096:2 * 4096], scalar=ein.ap[:, 0:1],
                in1=uends.ap[:, 0:4096], op0=ALU.mult, op1=ALU.add),
                reads=[uends, ein], writes=[inst])
            kb.v(lambda e: e.scalar_tensor_tensor(
                out=inst.ap[:], in0=uends.ap[:, 2 * 4096:3 * 4096], scalar=ein.ap[:, 1:2],
                in1=inst.ap[:], op0=ALU.mult, op1=ALU.add),
                reads=[uends, ein, inst], writes=[inst])
            kb.phase_barrier()
        achk = Buf(E(nc.sbuf_tensor("achkD", [64, NCH_], F32)))
        acs = Buf(E(nc.sbuf_tensor("acsD", [64, NCH_], F32)))
        acbuf = Buf(E(nc.sbuf_tensor("acbufD", [64, T_], F32)))
        kb.dma("y", lambda e: e.dma_start(out=acbuf.ap[:], in_=AC), writes=[acbuf])
        kb.v(lambda e: e.tensor_copy(out=achk.ap[:], in_=acbuf.ap[:, CH - 1::CH]),
             reads=[acbuf], writes=[achk])
        zch = Buf(E(nc.sbuf_tensor("zchD", [64, NCH_], F32)))
        kb.v(lambda e: e.memset(zch.ap[:], 0.0), writes=[zch])
        kb.v(lambda e: e.tensor_tensor_scan(out=acs.ap[:], data0=achk.ap[:],
                                            data1=zch.ap[:], initial=0.0,
                                            op0=ALU.add, op1=ALU.add),
             reads=[achk, zch], writes=[acs])
        eacs = Buf(E(nc.sbuf_tensor("eacsD", [64, NCH_], F32)))
        kb.v(lambda e: e.memset(eacs.ap[:, 0:1], 0.0), writes=[eacs])
        kb.v(lambda e: e.tensor_copy(out=eacs.ap[:, 1:NCH_], in_=acs.ap[:, 0:NCH_ - 1]),
             reads=[acs], writes=[eacs])
        kb.s(lambda e: e.activation(out=eacs.ap[:], in_=eacs.ap[:], func=AF.Exp),
             reads=[eacs], writes=[eacs])

        with ExitStack() as sub:
            SE = sub.enter_context
            csb = Buf(SE(nc.sbuf_tensor("csbD", [64, T_], BF16)))
            kb.dma("y", lambda e: e.dma_start(out=csb.ap[:], in_=CS), writes=[csb])
            pin = Buf(SE(nc.sbuf_tensor("pinD", [64, 4096], BF16)))
            cd = Buf(SE(nc.sbuf_tensor("cdD", [64, CH], BF16)))
            acrow = Buf(SE(nc.sbuf_tensor("acrowD", [1, 8 * CH], F32)))
            exprow = Buf(SE(nc.sbuf_tensor("exprowD", [1, 8 * CH], BF16)))
            ypc = Buf(SE(nc.sbuf_tensor("ypcD", [128, 4096], F32)))
            zc_ = Buf(SE(nc.sbuf_tensor("zcD", [128, 4096], BF16)))
            sz = Buf(SE(nc.sbuf_tensor("szD", [128, 4096], F32)))
            sq = sz
            rcol = Buf(SE(nc.sbuf_tensor("rcolD", [128, 1], F32)))
            hnb = Buf(SE(nc.sbuf_tensor("hnbD", [128, 4096], BF16)))
            hnT = Buf(SE(nc.sbuf_tensor("hnTD", [128, 32 * 512], BF16)))
            wout = [Buf(SE(nc.sbuf_tensor(f"woutD{i}", [128, 32 * 128], BF16))) for i in range(2)]
            outc = Buf(SE(nc.sbuf_tensor("outcD", [128, 16 * 512], F32)))
            ps_y = Buf(SE(nc.psum_tensor("ps_yD", [128, 512], F32)))
            ps_bc2 = Buf(SE(nc.psum_tensor("ps_bc2D", [64, 512], F32)))
            ps_tr = Buf(SE(nc.psum_tensor("ps_trD", [128, 128], BF16)))
            ps_o = Buf(SE(nc.psum_tensor("ps_oD", [128, 512], F32)))
            lnc = _ln_cache(kb, nc, SE, "lnD", 512)
            acr_d = Buf(None)
            r3buf = Buf(None)
            for quad in range(T_ // 512):
                for lt in range(4):
                    ch = quad * 4 + lt
                    kb.v(lambda e, ch=ch: e.tensor_scalar(
                        out=pin.ap[:], in0=inst.ap[:], scalar1=eacs.ap[:, ch:ch + 1],
                        scalar2=None, op0=ALU.mult),
                        reads=[inst, eacs], writes=[pin])
                    kb.dma("y", lambda e, ch=ch: e.dma_start(
                        out=ACRS, in_=acbuf.ap[:, ch * CH:(ch + 1) * CH]),
                        reads=[acbuf], writes=[acr_d])
                    kb.dma("y", lambda e, ch=ch: e.dma_start(
                        out=ypc.ap[:], in_=ypv[:, ch, :]), writes=[ypc])
                    for h in range(64):
                        hg = h % 8
                        if hg == 0:
                            kb.dma("y", lambda e, h=h: e.dma_start(
                                out=acrow.ap[:],
                                in_=ACRS.rearrange("h l -> (h l)").unsqueeze(0)[:, h * CH:(h + 8) * CH]),
                                reads=[acr_d], writes=[acrow], extra=list(acr_d.w))
                            kb.s(lambda e: e.activation(out=exprow.ap[:], in_=acrow.ap[:],
                                                        func=AF.Exp),
                                 reads=[acrow], writes=[exprow])
                        kb.t(lambda e, hg=hg: e.matmul(
                            ps_bc2.ap[:, 0:CH], ones_rowb.ap[0:1, 0:64],
                            exprow.ap[0:1, hg * CH:(hg + 1) * CH],
                            start=True, stop=True),
                            reads=[exprow, ones_rowb], writes=[ps_bc2])
                        kb.v(lambda e, ch=ch: e.tensor_tensor(
                            out=cd.ap[:], in0=csb.ap[:, ch * CH:(ch + 1) * CH],
                            in1=ps_bc2.ap[:, 0:CH], op=ALU.mult),
                            reads=[csb, ps_bc2], writes=[cd])
                        kb.t(lambda e, h=h, hg=hg: e.matmul(
                            ps_y.ap[:, hg * 64:(hg + 1) * 64], cd.ap[:],
                            pin.ap[:, h * 64:(h + 1) * 64],
                            start=True, stop=True),
                            reads=[cd, pin],
                            writes=[ps_y] if hg == 7 else [],
                            extra=(list(ps_y.w) + list(ps_y.r)) if hg != 7 else ())
                        if hg == 7:
                            g = h // 8
                            kb.v(lambda e, g=g: e.tensor_add(
                                out=ypc.ap[:, g * 512:(g + 1) * 512],
                                in0=ypc.ap[:, g * 512:(g + 1) * 512], in1=ps_y.ap[:]),
                                reads=[ps_y, ypc], writes=[ypc])
                    kb.dma("y", lambda e, ch=ch: e.dma_start(
                        out=zc_.ap[:], in_=zsv[:, ch, :]), writes=[zc_])
                    kb.s(lambda e: e.activation(out=sz.ap[:], in_=zc_.ap[:], func=AF.Silu),
                         reads=[zc_], writes=[sz])
                    kb.v(lambda e: e.tensor_mul(out=ypc.ap[:], in0=ypc.ap[:], in1=sz.ap[:]),
                         reads=[sz, ypc], writes=[ypc])
                    kb.s(lambda e: e.square(out=sq.ap[:], in_=ypc.ap[:]),
                         reads=[ypc], writes=[sq])
                    kb.v(lambda e: e.reduce_sum(out=rcol.ap[:], in_=sq.ap[:], axis=AX.X),
                         reads=[sq], writes=[rcol])
                    kb.v(lambda e: e.tensor_scalar(out=rcol.ap[:], in0=rcol.ap[:],
                                                   scalar1=1.0 / 4096, scalar2=epsb.ap[:, 0:1],
                                                   op0=ALU.mult, op1=ALU.add),
                         reads=[rcol, epsb], writes=[rcol])
                    kb.s(lambda e: e.sqrt(out=rcol.ap[:], in_=rcol.ap[:]),
                         reads=[rcol], writes=[rcol])
                    kb.v(lambda e: e.reciprocal(out=rcol.ap[:], in_=rcol.ap[:]),
                         reads=[rcol], writes=[rcol])
                    kb.v(lambda e: e.tensor_scalar(out=ypc.ap[:], in0=ypc.ap[:],
                                                   scalar1=rcol.ap[:, 0:1], scalar2=None,
                                                   op0=ALU.mult),
                         reads=[rcol, ypc], writes=[ypc])
                    kb.v(lambda e: e.tensor_tensor(out=hnb.ap[:], in0=ypc.ap[:],
                                                   in1=nwb.ap[:], op=ALU.mult),
                         reads=[nwb, ypc], writes=[hnb])
                    for ft in range(32):
                        kb.t(lambda e, ft=ft: e.transpose(
                            ps_tr.ap[:], hnb.ap[:, ft * 128:(ft + 1) * 128], eyeb.ap[:]),
                            reads=[hnb, eyeb], writes=[ps_tr])
                        kb.v(lambda e, ft=ft, lt=lt: e.tensor_copy(
                            out=hnT.ap[:, ft * 512 + lt * 128:ft * 512 + (lt + 1) * 128],
                            in_=ps_tr.ap[:]), reads=[ps_tr], writes=[hnT])
                for j in range(16):
                    wb_ = wout[j % 2]
                    kb.dma("p", lambda e, j=j, wb_=wb_: e.dma_start(
                        out=wb_.ap[:].rearrange("p (t f) -> p t f", f=128),
                        in_=woutv[:, :, j * 128:(j + 1) * 128]), writes=[wb_])
                    for i in range(32):
                        st, sp = (i == 0), (i == 31)
                        kb.accum_matmul(ps_o, wb_, hnT,
                            lambda e, i=i, wb_=wb_, st=st, sp=sp: e.matmul(
                                ps_o.ap[:], wb_.ap[:, i * 128:(i + 1) * 128],
                                hnT.ap[:, i * 512:(i + 1) * 512], start=st, stop=sp), st, sp)
                    kb.v(lambda e, j=j: e.tensor_copy(out=outc.ap[:, j * 512:(j + 1) * 512],
                                                      in_=ps_o.ap[:]),
                         reads=[ps_o], writes=[outc])
                otiles = [outc.view(outc.ap[:, i * 512:(i + 1) * 512]) for i in range(16)]
                _ln_chunk(kb, nc, SE, otiles, otiles, ones_col, ones_row, 1e-5,
                          "", kb_cache=lnc)
                kb.merge_views(outc, otiles)
                kb.dma("y", lambda e, quad=quad: e.dma_start(
                    out=r3v[:, :, quad * 512:(quad + 1) * 512],
                    in_=outc.ap[:].rearrange("p (t n) -> p t n", n=512)),
                    reads=[outc], writes=[r3buf])
                kb.dma("p", lambda e, quad=quad: e.dma_start(
                    out=r3v[:, :, quad * 512:(quad + 1) * 512],
                    in_=r2v[:, :, quad * 512:(quad + 1) * 512],
                    accum_op=ALU.add), reads=[], writes=[r3buf])
        kb.final_wait("y")
    return nc



def ln_free(kb, E_, nc, _a, _b, w, eps, name, epsb):
    mu = Buf(E_(nc.sbuf_tensor(name + "_mu", [128, 1], F32)))
    var = Buf(E_(nc.sbuf_tensor(name + "_var", [128, 1], F32)))
    var4 = Buf(E_(nc.sbuf_tensor(name + "_var4", [128, w // 512], F32)))
    sq = Buf(E_(nc.sbuf_tensor(name + "_sq", [128, 512], F32)))

    def run(xv, ov):
        kb.v(lambda e: e.reduce_sum(out=mu.ap[:], in_=xv.ap[:], axis=AX.X),
             reads=[xv], writes=[mu])
        kb.v(lambda e: e.tensor_scalar(out=mu.ap[:], in0=mu.ap[:], scalar1=1.0 / w,
                                       scalar2=None, op0=ALU.mult), reads=[mu], writes=[mu])
        for q in range(w // 512):
            kb.s(lambda e, q=q: e.square(out=sq.ap[:], in_=xv.ap[:, q * 512:(q + 1) * 512]),
                 reads=[xv], writes=[sq])
            kb.v(lambda e, q=q: e.reduce_sum(out=var4.ap[:, q:q + 1], in_=sq.ap[:], axis=AX.X),
                 reads=[sq], writes=[var4])
        kb.v(lambda e: e.reduce_sum(out=var.ap[:], in_=var4.ap[:], axis=AX.X),
             reads=[var4], writes=[var])
        kb.v(lambda e: e.tensor_scalar(out=var.ap[:], in0=var.ap[:], scalar1=1.0 / w,
                                       scalar2=epsb.ap[:, 0:1], op0=ALU.mult, op1=ALU.add),
             reads=[var, epsb], writes=[var])
        kb.v(lambda e: e.scalar_tensor_tensor(out=var.ap[:], in0=mu.ap[:],
                                              scalar=mu.ap[:, 0:1], in1=var.ap[:],
                                              op0=ALU.mult, op1=ALU.subtract_rev
                                              if hasattr(ALU, "subtract_rev") else ALU.subtract),
             reads=[mu, var], writes=[var]) if False else None
        kb.v(lambda e: e.tensor_mul(out=sq.ap[:, 0:1], in0=mu.ap[:], in1=mu.ap[:]),
             reads=[mu], writes=[sq])
        kb.v(lambda e: e.tensor_tensor(out=var.ap[:], in0=var.ap[:], in1=sq.ap[:, 0:1],
                                       op=ALU.subtract), reads=[sq, var], writes=[var])
        kb.s(lambda e: e.sqrt(out=var.ap[:], in_=var.ap[:]), reads=[var], writes=[var])
        kb.v(lambda e: e.reciprocal(out=var.ap[:], in_=var.ap[:]), reads=[var], writes=[var])
        kb.v(lambda e: e.tensor_scalar(out=ov.ap[:], in0=xv.ap[:], scalar1=mu.ap[:, 0:1],
                                       scalar2=var.ap[:, 0:1], op0=ALU.subtract,
                                       op1=ALU.mult),
             reads=[xv, mu, var], writes=[ov])
    return run


def build_launch_E(T_=T):
    nc = bass.Bass("TRN2", target_bir_lowering=False)
    R3 = nc.dram_tensor("R3", [D, T_], F32, kind="ExternalInput").ap()
    GWT = nc.dram_tensor("GWT", [D, 8], F32, kind="ExternalInput").ap()
    EFC1T = nc.dram_tensor("EFC1T", [8, D, 1408], F32, kind="ExternalInput").ap()
    EFC2T = nc.dram_tensor("EFC2T", [8, D, 1408], F32, kind="ExternalInput").ap()
    EPROJT = nc.dram_tensor("EPROJT", [8, 1408, D], F32, kind="ExternalInput").ap()
    SFC1T = nc.dram_tensor("SFC1T", [D, 5632], F32, kind="ExternalInput").ap()
    SFC2T = nc.dram_tensor("SFC2T", [D, 5632], F32, kind="ExternalInput").ap()
    SPROJT = nc.dram_tensor("SPROJT", [5632, D], F32, kind="ExternalInput").ap()
    SGWT = nc.dram_tensor("SGWT", [D, 1], F32, kind="ExternalInput").ap()
    EYEF = nc.dram_tensor("EYEF", [128, 128], F32, kind="ExternalInput").ap()
    ONESC = nc.dram_tensor("ONESC", [128, 1], BF16, kind="ExternalInput").ap()
    ONESR = nc.dram_tensor("ONESR", [1, 128], F32, kind="ExternalInput").ap()
    XOUT = nc.dram_tensor("XOUT", [T_, D], F32, kind="ExternalOutput").ap()

    r3v = R3.rearrange("(t p) n -> p t n", p=128)
    xoutv = XOUT.rearrange("(t p) f -> p t f", p=128)
    f1vs = [EFC1T[e_].rearrange("(t p) f -> p t f", p=128) for e_ in range(8)]
    f2vs = [EFC2T[e_].rearrange("(t p) f -> p t f", p=128) for e_ in range(8)]
    pjvs = [EPROJT[e_].rearrange("(i p) d -> p i d", p=128) for e_ in range(8)]
    bs1v = SFC1T.rearrange("(t p) f -> p t f", p=128)
    bs2v = SFC2T.rearrange("(t p) f -> p t f", p=128)
    spjv = SPROJT.rearrange("(i p) d -> p i d", p=128)

    NF = 11
    NPASS = 2
    TOKP = T_ // NPASS
    NCC = TOKP // 512
    NTT = TOKP // 128
    experts = [("r", e_) for e_ in range(8)] + [("sq", q) for q in range(4)]

    from contextlib import ExitStack
    with ExitStack() as ctx:
        kb = KB(nc)
        kb.init_sems(ctx)
        E = ctx.enter_context
        ones_col = Buf(E(nc.sbuf_tensor("onesc", [128, 1], BF16)))
        ones_row = Buf(E(nc.sbuf_tensor("onesr", [1, 128], F32)))
        eyef = Buf(E(nc.sbuf_tensor("eyef", [128, 128], F32)))
        kb.dma("y", lambda e: e.dma_start(out=ones_col.ap[:], in_=ONESC), writes=[ones_col])
        kb.dma("y", lambda e: e.dma_start(out=ones_row.ap[:], in_=ONESR), writes=[ones_row])
        kb.dma("y", lambda e: e.dma_start(out=eyef.ap[:], in_=EYEF), writes=[eyef])
        epsb = kb.const_col(ctx, 1e-5, 128)
        gwb = Buf(E(nc.sbuf_tensor("gwb", [128, 16 * 8], BF16)))
        kb.dma("p", lambda e: e.dma_start(
            out=gwb.ap[:].rearrange("p (t f) -> p t f", f=8),
            in_=GWT.rearrange("(t p) f -> p t f", p=128)), writes=[gwb])
        sgwb = Buf(E(nc.sbuf_tensor("sgwb", [128, 16], BF16)))
        kb.dma("p", lambda e: e.dma_start(
            out=sgwb.ap[:].unsqueeze(2),
            in_=SGWT.rearrange("(t p) f -> p t f", p=128),
            allow_slow_non_contiguous=True), writes=[sgwb])

        for p_ in range(NPASS):
            p0 = p_ * TOKP
            with ExitStack() as psc:
                PE_ = psc.enter_context
                xln = Buf(PE_(nc.sbuf_tensor(f"xlnE{p_}", [128, NCC * 16 * 512], BF16)))
                yacc = Buf(PE_(nc.sbuf_tensor(f"yaccE{p_}", [128, NTT * 2048], F32)))
                combT = Buf(PE_(nc.sbuf_tensor(f"combTE{p_}", [128, NTT * 8], F32)))
                gcol = Buf(PE_(nc.sbuf_tensor(f"gcolE{p_}", [128, NTT], F32)))

                with ExitStack() as p1c:
                    P1 = p1c.enter_context
                    r3c = Buf(P1(nc.sbuf_tensor(f"r3cE{p_}", [128, 16 * 512], F32)))
                    logr = Buf(P1(nc.sbuf_tensor(f"logrE{p_}", [8, 512], F32)))
                    logt = Buf(P1(nc.sbuf_tensor(f"logtE{p_}", [128, 8], F32)))
                    m1 = Buf(P1(nc.sbuf_tensor(f"m1E{p_}", [128, 1], F32)))
                    m2 = Buf(P1(nc.sbuf_tensor(f"m2E{p_}", [128, 1], F32)))
                    msk = Buf(P1(nc.sbuf_tensor(f"mskE{p_}", [128, 8], F32)))
                    grow = Buf(P1(nc.sbuf_tensor(f"growE{p_}", [1, 512], F32)))
                    ps_r = Buf(P1(nc.psum_tensor(f"psrE{p_}", [8, 512], F32)))
                    ps_t = Buf(P1(nc.psum_tensor(f"psrtE{p_}", [128, 128], F32)))
                    ps_g = Buf(P1(nc.psum_tensor(f"psgE{p_}", [1, 512], F32)))
                    lnc = _ln_cache(kb, nc, P1, f"lnE{p_}", 512)
                    for cc in range(NCC):
                        c0 = p0 + cc * 512
                        kb.dma("y", lambda e, c0=c0: e.dma_start(
                            out=r3c.ap[:].rearrange("p (t n) -> p t n", n=512),
                            in_=r3v[:, :, c0:c0 + 512]), writes=[r3c])
                        xts = [r3c.view(r3c.ap[:, i * 512:(i + 1) * 512]) for i in range(16)]
                        ots = [xln.view(xln.ap[:, (cc * 16 + i) * 512:(cc * 16 + i + 1) * 512])
                               for i in range(16)]
                        _ln_chunk(kb, nc, P1, xts, ots, ones_col, ones_row, 1e-5, "",
                                  kb_cache=lnc)
                        kb.merge_views(r3c, xts)
                        kb.merge_views(xln, ots)
                        for i in range(16):
                            st, sp = (i == 0), (i == 15)
                            kb.accum_matmul(ps_r, gwb, xln,
                                lambda e, i=i, cc=cc, st=st, sp=sp: e.matmul(
                                    ps_r.ap[:], gwb.ap[:, i * 8:(i + 1) * 8],
                                    xln.ap[:, (cc * 16 + i) * 512:(cc * 16 + i + 1) * 512],
                                    start=st, stop=sp), st, sp)
                        kb.v(lambda e: e.tensor_copy(out=logr.ap[:], in_=ps_r.ap[:]),
                             reads=[ps_r], writes=[logr])
                        for t4 in range(4):
                            tt = cc * 4 + t4
                            kb.t(lambda e, t4=t4: e.transpose(
                                ps_t.ap[:, 0:8], logr.ap[:, t4 * 128:(t4 + 1) * 128],
                                eyef.ap[0:8, 0:8]), reads=[logr, eyef], writes=[ps_t])
                            kb.v(lambda e: e.tensor_copy(out=logt.ap[:], in_=ps_t.ap[:, 0:8]),
                                 reads=[ps_t], writes=[logt])
                            kb.v(lambda e: e.reduce_max(out=m1.ap[:], in_=logt.ap[:], axis=AX.X),
                                 reads=[logt], writes=[m1])
                            kb.v(lambda e: e.tensor_scalar(out=logt.ap[:], in0=logt.ap[:],
                                                           scalar1=m1.ap[:, 0:1], scalar2=None,
                                                           op0=ALU.subtract),
                                 reads=[m1, logt], writes=[logt])
                            kb.s(lambda e: e.activation(out=logt.ap[:], in_=logt.ap[:], func=AF.Exp),
                                 reads=[logt], writes=[logt])
                            kb.v(lambda e: e.reduce_sum(out=m1.ap[:], in_=logt.ap[:], axis=AX.X),
                                 reads=[logt], writes=[m1])
                            kb.v(lambda e: e.reciprocal(out=m1.ap[:], in_=m1.ap[:]),
                                 reads=[m1], writes=[m1])
                            kb.v(lambda e: e.tensor_scalar(out=logt.ap[:], in0=logt.ap[:],
                                                           scalar1=m1.ap[:, 0:1], scalar2=None,
                                                           op0=ALU.mult),
                                 reads=[m1, logt], writes=[logt])
                            kb.v(lambda e: e.reduce_max(out=m1.ap[:], in_=logt.ap[:], axis=AX.X),
                                 reads=[logt], writes=[m1])
                            kb.v(lambda e: e.tensor_scalar(out=msk.ap[:], in0=logt.ap[:],
                                                           scalar1=m1.ap[:, 0:1], scalar2=None,
                                                           op0=ALU.is_ge),
                                 reads=[logt, m1], writes=[msk])
                            kb.v(lambda e: e.scalar_tensor_tensor(out=msk.ap[:], in0=msk.ap[:],
                                                                  scalar=-2.0, in1=logt.ap[:],
                                                                  op0=ALU.mult, op1=ALU.add),
                                 reads=[logt, msk], writes=[msk])
                            kb.v(lambda e: e.reduce_max(out=m2.ap[:], in_=msk.ap[:], axis=AX.X),
                                 reads=[msk], writes=[m2])
                            kb.v(lambda e: e.tensor_scalar(out=msk.ap[:], in0=logt.ap[:],
                                                           scalar1=m2.ap[:, 0:1], scalar2=None,
                                                           op0=ALU.is_ge),
                                 reads=[logt, m2], writes=[msk])
                            kb.v(lambda e, tt=tt: e.tensor_tensor(
                                out=combT.ap[:, tt * 8:(tt + 1) * 8], in0=logt.ap[:],
                                in1=msk.ap[:], op=ALU.mult),
                                reads=[logt, msk], writes=[combT])
                        for i in range(16):
                            st, sp = (i == 0), (i == 15)
                            kb.accum_matmul(ps_g, sgwb, xln,
                                lambda e, i=i, cc=cc, st=st, sp=sp: e.matmul(
                                    ps_g.ap[:], sgwb.ap[:, i:i + 1],
                                    xln.ap[:, (cc * 16 + i) * 512:(cc * 16 + i + 1) * 512],
                                    start=st, stop=sp), st, sp)
                        kb.s(lambda e: e.activation(out=grow.ap[:], in_=ps_g.ap[0:1, :],
                                                    func=AF.Sigmoid), reads=[ps_g], writes=[grow])
                        for t4 in range(4):
                            tt = cc * 4 + t4
                            kb.t(lambda e, t4=t4: e.transpose(
                                ps_t.ap[:, 0:1], grow.ap[:, t4 * 128:(t4 + 1) * 128],
                                eyef.ap[0:1, 0:1]), reads=[grow, eyef], writes=[ps_t])
                            kb.v(lambda e, tt=tt: e.tensor_copy(out=gcol.ap[:, tt:tt + 1],
                                                                in_=ps_t.ap[:, 0:1]),
                                 reads=[ps_t], writes=[gcol])
                    kb.phase_barrier()

                with ExitStack() as p2c:
                    P2 = p2c.enter_context
                    hT = Buf(P2(nc.sbuf_tensor(f"hTE{p_}", [128, NF * TOKP], BF16)))
                    stg1 = [Buf(P2(nc.sbuf_tensor(f"sg1E{p_}{i}", [128, 1024], F32)))
                            for i in range(2)]
                    stg2 = [Buf(P2(nc.sbuf_tensor(f"sg2E{p_}{i}", [128, 1024], F32)))
                            for i in range(2)]
                    w1b = [Buf(P2(nc.sbuf_tensor(f"w1bE{p_}{i}", [128, 2048], BF16)))
                           for i in range(2)]
                    w2b = [Buf(P2(nc.sbuf_tensor(f"w2bE{p_}{i}", [128, 2048], BF16)))
                           for i in range(2)]
                    wpjs = [Buf(P2(nc.sbuf_tensor(f"wpsE{p_}0", [128, NF * 512], F32)))] * 2
                    wpjb = [Buf(P2(nc.sbuf_tensor(f"wpbE{p_}{i}", [128, NF * 512], BF16)))
                            for i in range(2)]
                    silu_t = [Buf(P2(nc.sbuf_tensor(f"silE{p_}{i}", [128, TOKP], F32)))
                              for i in range(2)]
                    ps1 = [Buf(P2(nc.psum_tensor(f"ps1E{p_}{i}", [128, TOKP], F32)))
                           for i in range(2)]
                    ps2 = [Buf(P2(nc.psum_tensor(f"ps2E{p_}{i}", [128, TOKP], F32)))
                           for i in range(2)]

                    def fc_srcs(ei, ft):
                        kind, idx = experts[ei]
                        if kind == "r":
                            return (f1vs[idx][:, :, ft * 128:(ft + 1) * 128],
                                    f2vs[idx][:, :, ft * 128:(ft + 1) * 128])
                        qf0 = idx * 1408
                        return (bs1v[:, :, qf0 + ft * 128:qf0 + (ft + 1) * 128],
                                bs2v[:, :, qf0 + ft * 128:qf0 + (ft + 1) * 128])

                    def fc_load(gft):
                        if gft >= len(experts) * NF:
                            return
                        s1, s2 = fc_srcs(gft // NF, gft % NF)
                        b = gft % 2
                        for h in range(2):
                            kb.dma("y", lambda e, s1=s1, h=h: e.dma_start(
                                out=stg1[h].ap[:].rearrange("p (t f) -> p t f", f=128),
                                in_=s1[:, h * 8:(h + 1) * 8, :]), writes=[stg1[h]])
                            kb.dma("y", lambda e, s2=s2, h=h: e.dma_start(
                                out=stg2[h].ap[:].rearrange("p (t f) -> p t f", f=128),
                                in_=s2[:, h * 8:(h + 1) * 8, :]), writes=[stg2[h]])
                            kb.s(lambda e, b=b, h=h: e.copy(
                                out=w1b[b].ap[:, h * 1024:(h + 1) * 1024], in_=stg1[h].ap[:]),
                                reads=[stg1[h]], writes=[w1b[b]])
                            kb.v(lambda e, b=b, h=h: e.tensor_copy(
                                out=w2b[b].ap[:, h * 1024:(h + 1) * 1024], in_=stg2[h].ap[:]),
                                reads=[stg2[h]], writes=[w2b[b]])

                    def pj_dma(ei, dchunk):
                        if dchunk >= 4:
                            return
                        kind, idx = experts[ei]
                        i0 = idx * NF if kind == "sq" else 0
                        pv = spjv if kind == "sq" else pjvs[idx]
                        sb = wpjs[dchunk % 2]
                        kb.dma("y", lambda e, pv=pv, i0=i0, dchunk=dchunk, sb=sb: e.dma_start(
                            out=sb.ap[:].rearrange("p (i d) -> p i d", d=512),
                            in_=pv[:, i0:i0 + NF, dchunk * 512:(dchunk + 1) * 512]),
                            writes=[sb])

                    def pj_cast(dchunk):
                        if dchunk >= 4:
                            return
                        sb = wpjs[dchunk % 2]
                        wb = wpjb[dchunk % 2]
                        kb.v(lambda e, sb=sb, wb=wb: e.tensor_copy(
                            out=wb.ap[:], in_=sb.ap[:]), reads=[sb], writes=[wb])

                    fc_load(0)
                    for ei, (kind, idx) in enumerate(experts):
                        for ft in range(NF):
                            gft = ei * NF + ft
                            fc_load(gft + 1)
                            if ft == NF - 3:
                                pj_dma(ei, 0)
                            b = gft % 2
                            p1, p2 = ps1[b], ps2[b]
                            for i in range(16):
                                for nch in range(NCC):
                                    st = (i == 0)
                                    sp = (i == 15 and nch == NCC - 1)
                                    xsl = (nch * 16 + i) * 512
                                    kb.accum_matmul(p1, w1b[b], xln,
                                        lambda e, i=i, nch=nch, b=b, xsl=xsl, st=st, sp=False,
                                               p1=p1: e.matmul(
                                            p1.ap[:, nch * 512:(nch + 1) * 512],
                                            w1b[b].ap[:, i * 128:(i + 1) * 128],
                                            xln.ap[:, xsl:xsl + 512],
                                            start=st, stop=(i == 15)), st, sp)
                                    kb.accum_matmul(p2, w2b[b], xln,
                                        lambda e, i=i, nch=nch, b=b, xsl=xsl, st=st,
                                               p2=p2: e.matmul(
                                            p2.ap[:, nch * 512:(nch + 1) * 512],
                                            w2b[b].ap[:, i * 128:(i + 1) * 128],
                                            xln.ap[:, xsl:xsl + 512],
                                            start=st, stop=(i == 15)), st, sp)
                            kb.s(lambda e, b=b, p1=p1: e.activation(
                                out=silu_t[b].ap[:], in_=p1.ap[:], func=AF.Silu),
                                reads=[p1], writes=[silu_t[b]])
                            kb.v(lambda e, ft=ft, b=b, p2=p2: e.tensor_mul(
                                out=hT.ap[:, ft * TOKP:(ft + 1) * TOKP], in0=silu_t[b].ap[:],
                                in1=p2.ap[:]), reads=[silu_t[b], p2], writes=[hT])
                            if ft == NF - 2:
                                pj_cast(0)
                        for dchunk in range(4):
                            wb = wpjb[dchunk % 2]
                            pj_dma(ei, dchunk + 1)
                            for tt in range(NTT):
                                pp = ps2[tt % 2]
                                for i in range(NF):
                                    st, sp = (i == 0), (i == NF - 1)
                                    kb.accum_matmul(pp, hT, wb,
                                        lambda e, i=i, tt=tt, pp=pp, wb=wb, st=st, sp=sp: e.matmul(
                                            pp.ap[:, 0:512],
                                            hT.ap[:, i * TOKP + tt * 128:i * TOKP + (tt + 1) * 128],
                                            wb.ap[:, i * 512:(i + 1) * 512],
                                            start=st, stop=sp), st, sp)
                                if kind == "r":
                                    ccol = combT.ap[:, tt * 8 + idx:tt * 8 + idx + 1]
                                    csrc = combT
                                else:
                                    ccol = gcol.ap[:, tt:tt + 1]
                                    csrc = gcol
                                y0 = tt * 2048 + dchunk * 512
                                if ei == 0:
                                    kb.v(lambda e, y0=y0, pp=pp, ccol=ccol: e.tensor_scalar(
                                        out=yacc.ap[:, y0:y0 + 512], in0=pp.ap[:, 0:512],
                                        scalar1=ccol, scalar2=None, op0=ALU.mult),
                                        reads=[pp, csrc], writes=[yacc])
                                else:
                                    kb.v(lambda e, y0=y0, pp=pp, ccol=ccol: e.scalar_tensor_tensor(
                                        out=yacc.ap[:, y0:y0 + 512], in0=pp.ap[:, 0:512],
                                        scalar=ccol, in1=yacc.ap[:, y0:y0 + 512],
                                        op0=ALU.mult, op1=ALU.add),
                                        reads=[pp, csrc, yacc], writes=[yacc])
                                if tt == 1:
                                    pj_cast(dchunk + 1)
                    kb.phase_barrier()

                with ExitStack() as p3c:
                    P3 = p3c.enter_context
                    r3t = [Buf(P3(nc.sbuf_tensor(f"r3tE{p_}{i}", [128, 2048], F32)))
                           for i in range(2)]
                    r3tok = [Buf(P3(nc.sbuf_tensor(f"r3kE{p_}{i}", [128, 2048], F32)))
                             for i in range(2)]
                    ps_t2 = [Buf(P3(nc.psum_tensor(f"pstE{p_}{i}", [128, 128], F32)))
                             for i in range(4)]
                    lnfree = ln_free(kb, P3, nc, None, None, 2048, 1e-5, f"lnfE{p_}", epsb)
                    xout_b = Buf(None)

                    def r3_load(tt):
                        if tt >= NTT:
                            return
                        tok0 = p0 + tt * 128
                        kb.dma("y", lambda e, tok0=tok0, tt=tt: e.dma_start(
                            out=r3t[tt % 2].ap[:].rearrange("p (t n) -> p t n", n=128),
                            in_=r3v[:, :, tok0:tok0 + 128]), writes=[r3t[tt % 2]])

                    r3_load(0)
                    for tt in range(NTT):
                        r3_load(tt + 1)
                        yv = yacc.view(yacc.ap[:, tt * 2048:(tt + 1) * 2048])
                        ov = yacc.view(yacc.ap[:, tt * 2048:(tt + 1) * 2048])
                        lnfree(yv, ov)
                        kb.merge_views(yacc, [yv, ov])
                        rt = r3t[tt % 2]
                        rk = r3tok[tt % 2]
                        for i in range(16):
                            pst = ps_t2[i % 4]
                            kb.t(lambda e, i=i, rt=rt, pst=pst: e.transpose(
                                pst.ap[:], rt.ap[:, i * 128:(i + 1) * 128], eyef.ap[:]),
                                reads=[rt, eyef], writes=[pst])
                            cpeng = kb.v if i % 2 == 0 else kb.s
                            if i % 2 == 0:
                                cpeng(lambda e, i=i, pst=pst, rk=rk: e.tensor_copy(
                                    out=rk.ap[:, i * 128:(i + 1) * 128], in_=pst.ap[:]),
                                    reads=[pst], writes=[rk])
                            else:
                                cpeng(lambda e, i=i, pst=pst, rk=rk: e.copy(
                                    out=rk.ap[:, i * 128:(i + 1) * 128], in_=pst.ap[:]),
                                    reads=[pst], writes=[rk])
                        kb.v(lambda e, tt=tt, rk=rk: e.tensor_add(
                            out=rk.ap[:], in0=rk.ap[:],
                            in1=yacc.ap[:, tt * 2048:(tt + 1) * 2048]),
                            reads=[yacc, rk], writes=[rk])
                        gtt = p_ * NTT + tt
                        kb.dma("s", lambda e, gtt=gtt, rk=rk: e.dma_start(
                            out=xoutv[:, gtt, :], in_=rk.ap[:]),
                            reads=[rk], writes=[xout_b])
                    kb.phase_barrier()
        kb.final_wait("y")
    return nc



_ONESC = np.ones((128, 1), dtype=BF16NP)
_ONESR = np.ones((1, 128), dtype=np.float32)
_EYEB = np.eye(128, dtype=BF16NP)
_EYEF = np.eye(128, dtype=np.float32)
_TRIADD = np.where(np.tril(np.ones((128, 128))).T > 0, 0.0, -1e30).astype(np.float32)


def _attn_masks(own0, S_=8192, T_=T):
    NJ, NI = T_ // 512, S_ // 128
    m = np.zeros((NJ, NI, 128, 512), dtype=BF16NP)
    kk = np.arange(128)[:, None]
    qq = np.arange(512)[None, :]
    for j in range(NJ):
        q0 = own0 + 512 * j
        for i in range(NI):
            m[j, i] = ((128 * i + kk) <= (q0 + qq)).astype(BF16NP)
    return m


_MASK_CACHE = {}


def kernel(**inputs):
    x = np.asarray(inputs["L0_attn_x"], dtype=np.float32)
    B0, S_, D_ = x.shape
    cores = list(range(NCORES))

    whgT, whpT = host_quant_launch(inputs)

    ncA = build_launch_A(S_, T)
    wo_t = np.ascontiguousarray(np.asarray(inputs["L0_attn_W_o"]).T)
    xT_b = [np.ascontiguousarray(x[b].T) for b in range(B0)]
    in_maps = []
    for c in cores:
        b, q = c // 4, c % 4
        if q not in _MASK_CACHE:
            _MASK_CACHE[q] = _attn_masks(q * T, S_, T)
        in_maps.append({
            "XCTX": xT_b[b],
            "WDQ": np.asarray(inputs["L0_attn_W_dq"]),
            "WUQ": np.asarray(inputs["L0_attn_W_uq"]),
            "WDKV": np.asarray(inputs["L0_attn_W_dkv"]),
            "WUKV": np.asarray(inputs["L0_attn_W_ukv"]),
            "WOT": wo_t,
            "QLNG": np.asarray(inputs["L0_attn_q_ln_g"])[:, None],
            "QLNB": np.asarray(inputs["L0_attn_q_ln_b"])[:, None],
            "KVLNG": np.asarray(inputs["L0_attn_kv_ln_g"])[:, None],
            "KVLNB": np.asarray(inputs["L0_attn_kv_ln_b"])[:, None],
            "MASKS": _MASK_CACHE[q],
            "IDXQ": np.array([[q]], dtype=np.int32),
            "ONESC": _ONESC, "ONESR": _ONESR,
        })
    resA = run_bass_kernel_spmd(ncA, in_maps, core_ids=cores)
    R1 = [resA.results[c]["R1"] for c in cores]

    ncB = build_launch_B(T)
    in_maps = [{
        "R1": R1[c], "WHG": whgT, "WHP": whpT,
        "ONESC": _ONESC, "ONESR": _ONESR,
    } for c in cores]
    resB = run_bass_kernel_spmd(ncB, in_maps, core_ids=cores)
    R2 = [resB.results[c]["R2"] for c in cores]

    ncC = build_launch_C(T)
    win_t = np.ascontiguousarray(np.asarray(inputs["L1_attn_W_in"]).T)
    drow = np.repeat(np.asarray(inputs["L1_attn_D"]), 64)[None, :].astype(np.float32)
    in_maps = []
    for c in cores:
        q = c % 4
        ext = (np.zeros((D_, EXT), np.float32) if q == 0
               else np.ascontiguousarray(R2[c - 1][:, -EXT:]))
        r2e = np.concatenate([ext, R2[c]], axis=1)
        in_maps.append({
            "R2E": np.ascontiguousarray(r2e),
            "WINT": win_t,
            "CONVW": np.asarray(inputs["L1_attn_conv_w"]),
            "CONVB": np.asarray(inputs["L1_attn_conv_b"])[:, None],
            "DTB": np.asarray(inputs["L1_attn_dt_bias"])[:, None],
            "ALOG": np.asarray(inputs["L1_attn_A_log"])[:, None],
            "DROW": drow, "TRI": _TRIADD, "EYE": _EYEB, "EYEF": _EYEF,
            "ONESC": _ONESC, "ONESR": _ONESR,
        })
    resC = run_bass_kernel_spmd(ncC, in_maps, core_ids=cores)
    outsC = [resC.results[c] for c in cores]

    ncD = build_launch_D(T)
    wout_t = np.ascontiguousarray(np.asarray(inputs["L1_attn_W_out"]).T)
    normw = np.asarray(inputs["L1_attn_norm_w"])[None, :].astype(np.float32)
    in_maps = []
    for c in cores:
        q = c % 4
        uends = np.zeros((3, 64, 4096), np.float32)
        atots = np.zeros((64, 3), np.float32)
        for k in range(3):
            p = c - 1 - k
            if p // 4 == c // 4 and p >= 0:
                uends[k] = outsC[p]["UEND"]
                atots[:, k] = outsC[p]["ATOT"][:, 0]
        in_maps.append({
            "YP": outsC[c]["YP"], "ZS": outsC[c]["ZS"], "CS": outsC[c]["CS"],
            "AC": outsC[c]["AC"], "UENDS": uends, "ATOTS": atots,
            "R2": R2[c], "WOUTT": wout_t, "NORMW": normw,
            "EYE": _EYEB, "ONESC": _ONESC, "ONESR": _ONESR,
        })
    resD = run_bass_kernel_spmd(ncD, in_maps, core_ids=cores)
    R3 = [resD.results[c]["R3"] for c in cores]

    ncE = build_launch_E(T)
    gwt = np.ascontiguousarray(np.asarray(inputs["L1_ffn_gate_w"]).T)
    efc1t = np.ascontiguousarray(np.asarray(inputs["L1_ffn_expert_fc1"]).transpose(0, 2, 1))
    efc2t = np.ascontiguousarray(np.asarray(inputs["L1_ffn_expert_fc2"]).transpose(0, 2, 1))
    eprojt = np.ascontiguousarray(np.asarray(inputs["L1_ffn_expert_proj"]).transpose(0, 2, 1))
    sfc1t = np.ascontiguousarray(np.asarray(inputs["L1_ffn_shared_fc1"]).T)
    sfc2t = np.ascontiguousarray(np.asarray(inputs["L1_ffn_shared_fc2"]).T)
    sprojt = np.ascontiguousarray(np.asarray(inputs["L1_ffn_shared_proj"]).T)
    sgwt = np.ascontiguousarray(np.asarray(inputs["L1_ffn_shared_gate_w"]).T)
    in_maps = [{
        "R3": R3[c], "GWT": gwt, "EFC1T": efc1t, "EFC2T": efc2t,
        "EPROJT": eprojt, "SFC1T": sfc1t, "SFC2T": sfc2t, "SPROJT": sprojt,
        "SGWT": sgwt, "EYEF": _EYEF, "ONESC": _ONESC, "ONESR": _ONESR,
    } for c in cores]
    resE = run_bass_kernel_spmd(ncE, in_maps, core_ids=cores)

    out = np.empty((B0, S_, D_), dtype=np.float32)
    for c in cores:
        b, q = c // 4, c % 4
        out[b, q * T:(q + 1) * T] = resE.results[c]["XOUT"]
    return out
